# version 20
# baseline (speedup 1.0000x reference)
import numpy as np
import ml_dtypes
from contextlib import ExitStack

import concourse.bass as bass
import concourse.mybir as mybir
from concourse.bass_utils import run_bass_kernel_spmd

F32 = mybir.dt.float32
BF16 = mybir.dt.bfloat16
I32 = mybir.dt.int32
AF = mybir.ActivationFunctionType
ALU = mybir.AluOpType
AX = mybir.AxisListType
BF16NP = ml_dtypes.bfloat16

D = 1024
T = 2048
NCORES = 8
NT = D // 128
LN_EPS = 1e-5


ENGINES = ("sync", "tensor", "vector", "scalar", "gpsimd")


class Prog:
    def __init__(self, nc):
        self.nc = nc
        self.stack = ExitStack()
        self._alloc_stack = None
        self.sems = {}
        self.cnt = {}
        self.queue = {e: [] for e in ENGINES}
        self.last_w = {}
        self.readers = {}
        self.waited = {e: {} for e in ENGINES}
        self.bufs = {}

    def sem(self, name):
        if name not in self.sems:
            self.sems[name] = self.stack.enter_context(self.nc.semaphore(name))
            self.cnt[name] = 0
        return self.sems[name]

    def sbuf(self, name, shape, dtype):
        st = self._alloc_stack if self._alloc_stack is not None else self.stack
        t = st.enter_context(self.nc.sbuf_tensor(name, list(shape), dtype))
        self.bufs[name] = t
        return t

    def scope(self):
        from contextlib import contextmanager

        @contextmanager
        def _cm():
            st = ExitStack()
            old = self._alloc_stack
            self._alloc_stack = st
            try:
                yield
            finally:
                self._alloc_stack = old
                st.close()
        return _cm()

    def barrier(self):
        targets = {sname: v for sname, v in self.cnt.items() if v > 0}
        for eng in ENGINES:
            w = {}
            for sname, v in targets.items():
                if sname == eng:
                    continue
                if self.waited[eng].get(sname, -1) >= v:
                    continue
                w[sname] = v
            if w:
                self.queue[eng].append((w, None, None))
                for sname, v in w.items():
                    self.waited[eng][sname] = max(self.waited[eng].get(sname, -1), v)
        self.last_w.clear()
        self.readers.clear()

    def psum(self, name, shape, dtype=F32):
        t = self.stack.enter_context(self.nc.psum_tensor(name, list(shape), dtype))
        self.bufs[name] = t
        return t

    def make_pool(self):
        self.pspool = self.psum("pspool", [128, 8, 512], F32)

    def psb(self, bank):
        return self.pspool[:, bank, :]

    def psbk(self, b0, n=1):
        return [f"psb{i}" for i in range(b0, b0 + n)]

    def psw(self, b0, n):
        return self.pspool[:, b0:b0 + n, :].rearrange("p a b -> p (a b)")

    def _waits_for(self, eng, reads, writes):
        need = {}
        def add(swv):
            s, v = swv
            if need.get(s, 0) < v:
                need[s] = v
        for k in reads:
            if k in self.last_w:
                add(self.last_w[k])
        for k in writes:
            if k in self.last_w:
                add(self.last_w[k])
            for swv in self.readers.get(k, ()):
                add(swv)
        out = {}
        for s, v in need.items():
            if s == eng and v > self.cnt.get(eng, 0):
                continue
            if self.waited[eng].get(s, -1) >= v:
                continue
            out[s] = v
        return out

    def _record(self, eng_sem, val, reads, writes):
        for k in writes:
            self.last_w[k] = (eng_sem, val)
            self.readers[k] = []
        for k in reads:
            self.readers.setdefault(k, []).append((eng_sem, val))

    def op(self, eng, fn, reads=(), writes=(), inc=True):
        self.sem(eng)
        waits = self._waits_for(eng, reads, writes)
        for s, v in waits.items():
            self.waited[eng][s] = max(self.waited[eng].get(s, -1), v)
        if inc:
            self.cnt[eng] += 1
            incspec = (eng, 1)
            val = self.cnt[eng]
        else:
            incspec = None
            val = self.cnt[eng] + 1
        self._record(eng, val, reads, writes)
        self.queue[eng].append((dict(waits), fn, incspec))

    def dma(self, qeng, chan, fn, reads=(), writes=()):
        chan = "dq_" + chan
        self.sem(chan)
        self.sem(qeng)
        waits = self._waits_for(qeng, reads, writes)
        for s, v in waits.items():
            self.waited[qeng][s] = max(self.waited[qeng].get(s, -1), v)
        self.cnt[chan] += 16
        self._record(chan, self.cnt[chan], reads, writes)
        self.queue[qeng].append((dict(waits), fn, (chan, 16)))

    def final_wait(self, qeng, chans):
        waits = {}
        for c in chans:
            c = "dq_" + c
            if self.cnt.get(c, 0) > 0:
                waits[c] = self.cnt[c]
        self.queue[qeng].append((waits, None, None))

    def build(self):
        block = self.stack.enter_context(self.nc.Block())
        for eng_name in ENGINES:
            items = self.queue[eng_name]
            if not items:
                continue
            def make_body(items):
                def body(engine):
                    for waits, fn, incspec in items:
                        for s, v in waits.items():
                            engine.wait_ge(self.sems[s], v)
                        if fn is None:
                            continue
                        inst = fn(engine)
                        if incspec is not None:
                            s, by = incspec
                            inst.then_inc(self.sems[s], by)
                return body
            getattr(block, eng_name)(make_body(items))

    def close(self):
        self.stack.close()


class Common:

    def __init__(self, P: Prog):
        self.P = P
        self.ones_f = P.sbuf("c_ones_f", [128, 1], F32)
        self.ones_row = P.sbuf("c_ones_row", [1, 128], F32)
        self.eps_row = P.sbuf("c_eps_row", [1, 1], F32)
        P.op("vector", lambda v: v.memset(self.ones_f[:], 1.0), writes=["c_ones_f"])
        P.op("vector", lambda v: v.memset(self.ones_row[:], 1.0), writes=["c_ones_row"])
        P.op("vector", lambda v: v.memset(self.eps_row[:], LN_EPS), writes=["c_eps_row"])



def alloc_ln(P: Prog, tag):
    ln = {}
    ln["rows"] = P.sbuf(f"ln_rows_{tag}", [1, 3, 512], F32)
    ln["ps_a"] = P.pspool[0:1, 2, :]
    ln["ps_b"] = P.pspool[0:1, 3, :]
    ln["bc0"] = P.psb(4)
    ln["bc1"] = P.psb(5)
    ln["k"] = tag
    return ln


def emit_ln_stats(P, C, ln, src, src_key, sq, sq_key, ntile):
    k = ln["k"]
    rows, rk = ln["rows"], f"ln_rows_{k}"
    P.op("scalar", lambda e: e.activation(out=sq[:], in_=src[:], func=AF.Square),
         reads=[src_key], writes=[sq_key])
    for g in range(ntile):
        P.op("tensor", lambda e, g=g: e.matmul(
            out=ln["ps_a"], lhsT=C.ones_f[:], rhs=src[:, g, :],
            start=(g == 0), stop=(g == ntile - 1)),
            reads=[src_key, "c_ones_f"], writes=["psb2"], inc=(g == ntile - 1))
    for g in range(ntile):
        P.op("tensor", lambda e, g=g: e.matmul(
            out=ln["ps_b"], lhsT=C.ones_f[:], rhs=sq[:, g, :],
            start=(g == 0), stop=(g == ntile - 1)),
            reads=[sq_key, "c_ones_f"], writes=["psb3"], inc=(g == ntile - 1))
    nfeat = float(ntile * 128)
    P.op("vector", lambda e: e.tensor_scalar(
        out=rows[0:1, 0, :], in0=ln["ps_a"], scalar1=1.0 / nfeat, scalar2=None,
        op0=ALU.mult), reads=["psb2"], writes=[rk])
    P.op("vector", lambda e: e.tensor_scalar(
        out=rows[0:1, 1, :], in0=ln["ps_b"], scalar1=1.0 / nfeat, scalar2=None,
        op0=ALU.mult), reads=["psb3"], writes=[rk])
    P.op("vector", lambda e: e.tensor_tensor(
        out=rows[0:1, 2, :], in0=rows[0:1, 0, :], in1=rows[0:1, 0, :], op=ALU.mult),
        reads=[rk], writes=[rk])
    P.op("vector", lambda e: e.tensor_sub(
        out=rows[0:1, 1, :], in0=rows[0:1, 1, :], in1=rows[0:1, 2, :]),
        reads=[rk], writes=[rk])
    P.op("scalar", lambda e: e.activation(
        out=rows[0:1, 1, :], in_=rows[0:1, 1, :], func=AF.Sqrt, bias=C.eps_row[:]),
        reads=[rk, "c_eps_row"], writes=[rk])
    P.op("vector", lambda e: e.reciprocal(
        out=rows[0:1, 1, :], in_=rows[0:1, 1, :]), reads=[rk], writes=[rk])
    P.op("vector", lambda e: e.tensor_mul(
        out=rows[0:1, 2, :], in0=rows[0:1, 0, :], in1=rows[0:1, 1, :]),
        reads=[rk], writes=[rk])
    P.op("tensor", lambda e: e.matmul(
        out=ln["bc0"], lhsT=C.ones_row[:], rhs=rows[0:1, 1, :],
        start=True, stop=True), reads=[rk, "c_ones_row"], writes=["psb4"])
    P.op("tensor", lambda e: e.matmul(
        out=ln["bc1"], lhsT=C.ones_row[:], rhs=rows[0:1, 2, :],
        start=True, stop=True), reads=[rk, "c_ones_row"],
        writes=["psb5"], inc=True)


def emit_ln_apply(P, C, ln, src, src_key, dst, dst_key, tmp, tmp_key, ntile,
                  gamma_col=None, beta_col=None, gb_keys=(), add_to=None,
                  add_key=None):
    k = ln["k"]
    for g in range(ntile):
        P.op("vector", lambda e, g=g: e.tensor_mul(
            out=tmp[:, g, :], in0=src[:, g, :], in1=ln["bc0"]),
            reads=[src_key, "psb4"], writes=[tmp_key])
    if gamma_col is None and add_to is None:
        for g in range(ntile):
            P.op("vector", lambda e, g=g: e.tensor_sub(
                out=dst[:, g, :], in0=tmp[:, g, :], in1=ln["bc1"]),
                reads=[tmp_key, "psb5"], writes=[dst_key])
    elif gamma_col is not None:
        for g in range(ntile):
            P.op("vector", lambda e, g=g: e.tensor_sub(
                out=tmp[:, g, :], in0=tmp[:, g, :], in1=ln["bc1"]),
                reads=[tmp_key, "psb5"], writes=[tmp_key])
        for g in range(ntile):
            P.op("vector", lambda e, g=g: e.tensor_scalar(
                out=dst[:, g, :], in0=tmp[:, g, :],
                scalar1=gamma_col[:, g:g + 1], scalar2=beta_col[:, g:g + 1],
                op0=ALU.mult, op1=ALU.add),
                reads=[tmp_key] + list(gb_keys), writes=[dst_key])
    else:
        for g in range(ntile):
            P.op("vector", lambda e, g=g: e.tensor_sub(
                out=tmp[:, g, :], in0=tmp[:, g, :], in1=ln["bc1"]),
                reads=[tmp_key, "psb5"], writes=[tmp_key])
        for g in range(ntile):
            P.op("vector", lambda e, g=g: e.tensor_add(
                out=dst[:, g, :], in0=tmp[:, g, :], in1=add_to[:, g, :]),
                reads=[tmp_key, add_key], writes=[dst_key])


def load_x_chunk(P, xsrc_ap, xsrc_key, xc, xc_key, c0, TC):
    P.dma("sync", xc_key,
          lambda e: e.dma_start(
              out=xc[:], in_=xsrc_ap[:, c0:c0 + TC].rearrange("(g p) t -> p g t", p=128)),
          reads=[xsrc_key], writes=[xc_key])


def store_x_chunk(P, xdst_ap, xdst_key, xc, xc_key, c0, TC, chan):
    P.dma("sync", chan,
          lambda e: e.dma_start(
              out=xdst_ap[:, c0:c0 + TC].rearrange("(g p) t -> p g t", p=128), in_=xc[:]),
          reads=[xc_key], writes=[xdst_key])



def build_l1ffn(P: Prog, C: Common, din, dout, xin_key="dram_xin",
                xout_key="dram_xout"):
    NH = 32
    TC = 512

    s1c = P.sbuf("f_s1c", [128, NH], F32)
    s2c = P.sbuf("f_s2c", [128, NH], F32)
    sh1c = P.sbuf("f_sh1c", [128, NH], F32)
    sh2c = P.sbuf("f_sh2c", [128, NH], F32)
    s3c = P.sbuf("f_s3c", [128, NT], F32)
    for nm in ("s1c", "s2c", "sh1c", "sh2c", "s3c"):
        P.dma("sync", "f_" + nm,
              lambda e, nm=nm: e.dma_start(out=P.bufs["f_" + nm][:], in_=din[nm][:]),
              reads=[("dram", nm)], writes=["f_" + nm])

    xc = P.sbuf("f_xc", [128, NT, TC], F32)
    tmp = P.sbuf("f_tmp", [128, NT, TC], F32)
    xhat = P.sbuf("f_xhat", [128, NT, TC], BF16)
    t1 = P.sbuf("f_t1", [128, NH, TC], BF16)
    t2 = P.sbuf("f_t2", [128, NH, TC], BF16)
    y = P.sbuf("f_y", [128, NT, TC], F32)
    silu_t = P.sbuf("f_silu", [128, TC], F32)
    wbuf = [P.sbuf(f"f_wbuf{i}", [128, 32, 128], BF16) for i in range(2)]
    ps_mm = [P.psb(0), P.psb(1)]
    ln = alloc_ln(P, "f")

    for c0 in range(0, T, TC):
        load_x_chunk(P, din["xin"], xin_key, xc, "f_xc", c0, TC)
        emit_ln_stats(P, C, ln, xc, "f_xc", tmp, "f_tmp", NT)
        emit_ln_apply(P, C, ln, xc, "f_xc", xhat, "f_xhat", tmp, "f_tmp", NT)

        for ot in range(NH):
            wb, wbk = wbuf[ot % 2], f"f_wbuf{ot % 2}"
            P.dma("gpsimd", wbk + "a",
                  lambda e, ot=ot, wb=wb: e.dma_start(
                      out=wb[:, 0:8, :], in_=din["w1Tt"][ot]),
                  reads=[("dram", "w1Tt")], writes=[wbk + "a"])
            P.dma("gpsimd", wbk + "b",
                  lambda e, ot=ot, wb=wb: e.dma_start(
                      out=wb[:, 8:16, :], in_=din["w2Tt"][ot]),
                  reads=[("dram", "w2Tt")], writes=[wbk + "b"])
            for g in range(NT):
                P.op("tensor", lambda e, g=g, wb=wb: e.matmul(
                    out=ps_mm[0], lhsT=wb[:, g, :], rhs=xhat[:, g, :],
                    start=(g == 0), stop=(g == NT - 1)),
                    reads=[wbk + "a", "f_xhat"], writes=["psb0"], inc=(g == NT - 1))
            for g in range(NT):
                P.op("tensor", lambda e, g=g, wb=wb: e.matmul(
                    out=ps_mm[1], lhsT=wb[:, 8 + g, :], rhs=xhat[:, g, :],
                    start=(g == 0), stop=(g == NT - 1)),
                    reads=[wbk + "b", "f_xhat"], writes=["psb1"], inc=(g == NT - 1))
            P.op("scalar", lambda e, ot=ot: e.activation(
                out=silu_t[:], in_=ps_mm[0], func=AF.Silu, scale=s1c[:, ot:ot + 1]),
                reads=["psb0", "f_s1c"], writes=["f_silu"])
            P.op("vector", lambda e, ot=ot: e.scalar_tensor_tensor(
                out=t1[:, ot, :], in0=ps_mm[1], scalar=s2c[:, ot:ot + 1],
                in1=silu_t[:], op0=ALU.mult, op1=ALU.mult),
                reads=["psb1", "f_s2c", "f_silu"], writes=["f_t1"])

        for ot in range(NH):
            wb, wbk = wbuf[ot % 2], f"f_wbuf{ot % 2}"
            bk_ = ot % 2
            P.dma("gpsimd", wbk + "a",
                  lambda e, ot=ot, wb=wb: e.dma_start(
                      out=wb[:], in_=din["h1Tt"][ot]),
                  reads=[("dram", "h1Tt")], writes=[wbk + "a", wbk + "b"])
            for g in range(NH):
                P.op("tensor", lambda e, g=g, wb=wb, bk_=bk_: e.matmul(
                    out=ps_mm[bk_], lhsT=wb[:, g, :], rhs=t1[:, g, :],
                    start=(g == 0), stop=(g == NH - 1)),
                    reads=[wbk + "a", wbk + "b", "f_t1"], writes=[f"psb{bk_}"],
                    inc=(g == NH - 1))
            P.op("scalar", lambda e, ot=ot, bk_=bk_: e.activation(
                out=t2[:, ot, :], in_=ps_mm[bk_], func=AF.Gelu,
                scale=sh1c[:, ot:ot + 1]),
                reads=[f"psb{bk_}", "f_sh1c"], writes=["f_t2"])

        for ot in range(NH):
            wb, wbk = wbuf[ot % 2], f"f_wbuf{ot % 2}"
            bk_ = ot % 2
            P.dma("gpsimd", wbk + "a",
                  lambda e, ot=ot, wb=wb: e.dma_start(
                      out=wb[:], in_=din["h2Tt"][ot]),
                  reads=[("dram", "h2Tt")], writes=[wbk + "a", wbk + "b"])
            for g in range(NH):
                P.op("tensor", lambda e, g=g, wb=wb, bk_=bk_: e.matmul(
                    out=ps_mm[bk_], lhsT=wb[:, g, :], rhs=t2[:, g, :],
                    start=(g == 0), stop=(g == NH - 1)),
                    reads=[wbk + "a", wbk + "b", "f_t2"], writes=[f"psb{bk_}"],
                    inc=(g == NH - 1))
            P.op("scalar", lambda e, ot=ot, bk_=bk_: e.activation(
                out=t1[:, ot, :], in_=ps_mm[bk_], func=AF.Gelu,
                scale=sh2c[:, ot:ot + 1]),
                reads=[f"psb{bk_}", "f_sh2c"], writes=["f_t1"])

        for ot in range(NT):
            wb, wbk = wbuf[ot % 2], f"f_wbuf{ot % 2}"
            bk_ = ot % 2
            P.dma("gpsimd", wbk + "a",
                  lambda e, ot=ot, wb=wb: e.dma_start(
                      out=wb[:], in_=din["w3Tt"][ot]),
                  reads=[("dram", "w3Tt")], writes=[wbk + "a", wbk + "b"])
            for g in range(NH):
                P.op("tensor", lambda e, g=g, wb=wb, bk_=bk_: e.matmul(
                    out=ps_mm[bk_], lhsT=wb[:, g, :], rhs=t1[:, g, :],
                    start=(g == 0), stop=(g == NH - 1)),
                    reads=[wbk + "a", wbk + "b", "f_t1"], writes=[f"psb{bk_}"],
                    inc=(g == NH - 1))
            P.op("vector", lambda e, ot=ot, bk_=bk_: e.tensor_scalar(
                out=y[:, ot, :], in0=ps_mm[bk_], scalar1=s3c[:, ot:ot + 1],
                scalar2=None, op0=ALU.mult),
                reads=[f"psb{bk_}", "f_s3c"], writes=["f_y"])

        emit_ln_stats(P, C, ln, y, "f_y", tmp, "f_tmp", NT)
        emit_ln_apply(P, C, ln, y, "f_y", xc, "f_xc", tmp, "f_tmp", NT,
                      add_to=xc, add_key="f_xc")
        store_x_chunk(P, dout["xout"], xout_key, xc, "f_xc", c0, TC, "st_xout")


def _tile_lhs(wT):
    K, M = wT.shape
    return np.ascontiguousarray(
        wT.reshape(K // 128, 128, M // 128, 128).transpose(2, 1, 0, 3))


def host_prep_l1ffn(inputs):
    f32, c = np.float32, np.ascontiguousarray
    d = {}
    d["w1Tt"] = _tile_lhs(inputs["L1_ffn_w1_w"].T.astype(f32))
    d["w2Tt"] = _tile_lhs(inputs["L1_ffn_w2_w"].T.astype(f32))
    d["h1Tt"] = _tile_lhs(inputs["L1_ffn_h1_w"].T.astype(f32))
    d["h2Tt"] = _tile_lhs(inputs["L1_ffn_h2_w"].T.astype(f32))
    d["w3Tt"] = _tile_lhs(inputs["L1_ffn_w3_w"].T.astype(f32))
    d["s1c"] = c(np.asarray(inputs["L1_ffn_w1_s"], f32).reshape(32, 128).T)
    d["s2c"] = c(np.asarray(inputs["L1_ffn_w2_s"], f32).reshape(32, 128).T)
    d["sh1c"] = c(np.asarray(inputs["L1_ffn_h1_s"], f32).reshape(32, 128).T)
    d["sh2c"] = c(np.asarray(inputs["L1_ffn_h2_s"], f32).reshape(32, 128).T)
    d["s3c"] = c(np.asarray(inputs["L1_ffn_w3_s"], f32).reshape(8, 128).T)
    return d


L1FFN_SHAPES = {
    "w1Tt": [32, 128, 8, 128], "w2Tt": [32, 128, 8, 128],
    "h1Tt": [32, 128, 32, 128], "h2Tt": [32, 128, 32, 128],
    "w3Tt": [8, 128, 32, 128], "s1c": [128, 32],
    "s2c": [128, 32], "sh1c": [128, 32], "sh2c": [128, 32], "s3c": [128, 8],
}


def build_l1ffn_program():
    nc = bass.Bass("TRN2", target_bir_lowering=False)
    din = {"xin": nc.dram_tensor("xin", [D, T], F32, kind="ExternalInput").ap()}
    for nm, shp in L1FFN_SHAPES.items():
        din[nm] = nc.dram_tensor(nm, shp, F32, kind="ExternalInput").ap()
    dout = {"xout": nc.dram_tensor("xout", [D, T], F32, kind="ExternalOutput").ap()}
    P = Prog(nc)
    P.make_pool()
    C = Common(P)
    build_l1ffn(P, C, din, dout)
    P.final_wait("sync", ["st_xout"])
    P.build()
    return nc





def build_l0ffn(P: Prog, C: Common, din, dout, xin_key="dram_xin",
                xout_key="dram_xout"):
    TC = 512

    xc = P.sbuf("q_xc", [128, NT, TC], F32)
    tmp = P.sbuf("q_tmp", [128, NT, TC], F32)
    hbf = P.sbuf("q_hbf", [128, NT, T], BF16)
    y1bf = P.sbuf("q_y1bf", [128, NT, T], BF16)
    yc = P.sbuf("q_yc", [128, NT, TC], F32)
    qraw = P.sbuf("q_qraw", [128, 1024], I32)
    sbc = P.sbuf("q_sbc", [128, 1024], F32)
    wbf = [P.sbuf(f"q_wbf{i}", [128, NT, 1024], BF16) for i in range(2)]
    u = P.sbuf("q_u", [32, T], BF16)
    lat = P.sbuf("q_lat", [128, NT, 32], BF16)
    lbt = P.sbuf("q_lbt", [32, NT, 128], BF16)
    bcol = P.sbuf("q_bcol", [128, 2, NT], F32)
    gcol = P.sbuf("q_gcol", [128, NT], F32)
    becol = P.sbuf("q_becol", [128, NT], F32)
    ps_u = P.pspool[0:32, 6, :]
    ps_mm = [P.psb(0), P.psb(1)]
    ln = alloc_ln(P, "q")

    for ci, c0 in enumerate(range(0, T, TC)):
        load_x_chunk(P, din["xin"], xin_key, xc, "q_xc", c0, TC)
        emit_ln_stats(P, C, ln, xc, "q_xc", tmp, "q_tmp", NT)
        emit_ln_apply(P, C, ln, xc, "q_xc", yc, "q_yc", tmp, "q_tmp", NT)
        for g in range(NT):
            P.op("vector", lambda e, g=g, c0=c0: e.tensor_copy(
                out=hbf[:, g, c0:c0 + TC], in_=yc[:, g, :]),
                reads=["q_yc"], writes=["q_hbf"])
        store_x_chunk(P, din["hdram"], ("dram", "hdram"), yc, "q_yc", c0, TC, "st_h")

    def dequant(layer, wslot):
        wk = f"q_wbf{wslot}"
        for c in range(NT):
            P.dma("sync", "q_qraw",
                  lambda e, layer=layer, c=c: e.dma_start(
                      out=qraw[:], in_=din["qT"][layer, c * 128:(c + 1) * 128, :]),
                  reads=[("dram", "qT")], writes=["q_qraw"])
            P.dma("sync", "q_sbc",
                  lambda e, layer=layer, c=c: e.dma_start(
                      out=sbc[:],
                      in_=din["sT"][layer, c * 8:(c + 1) * 8, :][:, None, :]
                          .broadcast_to([8, 16, 1024])),
                  reads=[("dram", "sT")], writes=["q_sbc"])
            P.op("vector", lambda e, c=c: e.tensor_scalar(
                out=tmp[:, 0:2, :].rearrange("p a b -> p (a b)"), in0=qraw[:],
                scalar1=1.0 / 7.5, scalar2=-1.0,
                op0=ALU.mult, op1=ALU.add),
                reads=["q_qraw"], writes=["q_tmp"])
            P.op("vector", lambda e, c=c, wslot=wslot: e.tensor_mul(
                out=wbf[wslot][:, c, :], in0=tmp[:, 0:2, :].rearrange("p a b -> p (a b)"),
                in1=sbc[:]),
                reads=["q_tmp", "q_sbc"], writes=[wk])

    def load_lora(layer):
        P.dma("gpsimd", "q_lat",
              lambda e, layer=layer: e.dma_start(
                  out=lat[:], in_=din["laTt"][layer]),
              reads=[("dram", "laTt")], writes=["q_lat"])
        P.dma("gpsimd", "q_lbt",
              lambda e, layer=layer: e.dma_start(
                  out=lbt[:], in_=din["lbT"][layer].rearrange("r (c p) -> r c p", p=128)),
              reads=[("dram", "lbT")], writes=["q_lbt"])

    def lora_mid(rhs, rhs_key):
        for sl in range(T // 512):
            for g in range(NT):
                P.op("tensor", lambda e, sl=sl, g=g: e.matmul(
                    out=ps_u, lhsT=lat[:, g, :], rhs=rhs[:, g, sl * 512:(sl + 1) * 512],
                    start=(g == 0), stop=(g == NT - 1)),
                    reads=["q_lat", rhs_key], writes=["psb6"], inc=(g == NT - 1))
            P.op("scalar", lambda e, sl=sl: e.copy(
                out=u[:, sl * 512:(sl + 1) * 512], in_=ps_u),
                reads=["psb6"], writes=["q_u"])

    def qlora_mm(wslot, rhs, rhs_key, c0, bslot):
        wk = f"q_wbf{wslot}"
        for sl in range(TC // 512):
            s0 = c0 + sl * 512
            for ot in range(NT):
                pm = ps_mm[ot % 2]
                pmk = f"psb{ot % 2}"
                for g in range(NT):
                    P.op("tensor", lambda e, g=g, ot=ot, s0=s0, pm=pm, wslot=wslot: e.matmul(
                        out=pm, lhsT=wbf[wslot][:, g, ot * 128:(ot + 1) * 128],
                        rhs=rhs[:, g, s0:s0 + 512],
                        start=(g == 0), stop=False),
                        reads=[wk, rhs_key], writes=[pmk], inc=False)
                P.op("tensor", lambda e, ot=ot, s0=s0, pm=pm: e.matmul(
                    out=pm, lhsT=lbt[:, ot, :], rhs=u[:, s0:s0 + 512],
                    start=False, stop=True),
                    reads=["q_lbt", "q_u", pmk], writes=[pmk], inc=True)
                P.op("vector", lambda e, ot=ot, sl=sl, pm=pm, bslot=bslot: e.tensor_scalar(
                    out=yc[:, ot, sl * 512:(sl + 1) * 512], in0=pm,
                    scalar1=bcol[:, bslot, ot:ot + 1], scalar2=None, op0=ALU.add),
                    reads=[pmk, "q_bcol"], writes=["q_yc"])

    for blk in range(3):
        l0, l1_ = 2 * blk, 2 * blk + 1
        dequant(l0, 0)
        load_lora(l0)
        P.dma("gpsimd", "q_bcol",
              lambda e, l0=l0: e.dma_start(
                  out=bcol[:], in_=din["bcol"][l0:l0 + 2].rearrange("a p c -> p a c")),
              reads=[("dram", "bcol")], writes=["q_bcol"])
        P.dma("gpsimd", "q_gcol",
              lambda e, blk=blk: e.dma_start(out=gcol[:], in_=din["gcol"][blk]),
              reads=[("dram", "gcol")], writes=["q_gcol"])
        P.dma("gpsimd", "q_becol",
              lambda e, blk=blk: e.dma_start(out=becol[:], in_=din["becol"][blk]),
              reads=[("dram", "becol")], writes=["q_becol"])
        lora_mid(hbf, "q_hbf")
        for c0 in range(0, T, TC):
            qlora_mm(0, hbf, "q_hbf", c0, 0)
            emit_ln_stats(P, C, ln, yc, "q_yc", tmp, "q_tmp", NT)
            emit_ln_apply(P, C, ln, yc, "q_yc", yc, "q_yc", tmp, "q_tmp", NT,
                          gamma_col=gcol, beta_col=becol,
                          gb_keys=["q_gcol", "q_becol"])
            for g in range(NT):
                P.op("scalar", lambda e, g=g, c0=c0: e.activation(
                    out=y1bf[:, g, c0:c0 + TC], in_=yc[:, g, :], func=AF.Gelu),
                    reads=["q_yc"], writes=["q_y1bf"])
        dequant(l1_, 1)
        load_lora(l1_)
        lora_mid(y1bf, "q_y1bf")
        last = blk == 2
        for c0 in range(0, T, TC):
            qlora_mm(1, y1bf, "q_y1bf", c0, 1)
            load_x_chunk(P, din["hdram"], ("dram", "hdram"), xc, "q_xc", c0, TC)
            for g in range(NT):
                P.op("vector", lambda e, g=g: e.tensor_add(
                    out=yc[:, g, :], in0=yc[:, g, :], in1=xc[:, g, :]),
                    reads=["q_yc", "q_xc"], writes=["q_yc"])
            if not last:
                for g in range(NT):
                    P.op("vector", lambda e, g=g, c0=c0: e.tensor_copy(
                        out=hbf[:, g, c0:c0 + TC], in_=yc[:, g, :]),
                        reads=["q_yc"], writes=["q_hbf"])
                store_x_chunk(P, din["hdram"], ("dram", "hdram"), yc, "q_yc", c0,
                              TC, "st_h")
            else:
                emit_ln_stats(P, C, ln, yc, "q_yc", tmp, "q_tmp", NT)
                load_x_chunk(P, din["xin"], xin_key, xc, "q_xc", c0, TC)
                emit_ln_apply(P, C, ln, yc, "q_yc", xc, "q_xc", tmp, "q_tmp", NT,
                              add_to=xc, add_key="q_xc")
                store_x_chunk(P, dout["xout"], xout_key, xc, "q_xc", c0, TC,
                              "st_xout")


def host_prep_l0ffn(inputs):
    f32, c = np.float32, np.ascontiguousarray
    d = {}
    d["qT"] = c(np.asarray(inputs["L0_ffn_qweights"], np.int32).transpose(0, 2, 1))
    d["sT"] = c(np.asarray(inputs["L0_ffn_scales"], f32).transpose(0, 2, 1))
    d["bcol"] = c(np.asarray(inputs["L0_ffn_biases"], f32).reshape(6, 8, 128)
                  .transpose(0, 2, 1))
    laT = np.asarray(inputs["L0_ffn_lora_a"], f32).transpose(0, 2, 1)
    d["laTt"] = c(laT.reshape(6, 8, 128, 32).transpose(0, 2, 1, 3))
    d["lbT"] = c(np.asarray(inputs["L0_ffn_lora_b"], f32).transpose(0, 2, 1))
    d["gcol"] = c(np.asarray(inputs["L0_ffn_ln_gamma"], f32).reshape(3, 8, 128)
                  .transpose(0, 2, 1))
    d["becol"] = c(np.asarray(inputs["L0_ffn_ln_beta"], f32).reshape(3, 8, 128)
                   .transpose(0, 2, 1))
    return d


L0FFN_SHAPES = {
    "qT": ([6, 1024, 1024], I32), "sT": ([6, 64, 1024], F32),
    "bcol": ([6, 128, 8], F32), "laTt": ([6, 128, 8, 32], F32),
    "lbT": ([6, 32, 1024], F32), "gcol": ([3, 128, 8], F32),
    "becol": ([3, 128, 8], F32),
}


def build_l0ffn_program():
    nc = bass.Bass("TRN2", target_bir_lowering=False)
    din = {"xin": nc.dram_tensor("xin", [D, T], F32, kind="ExternalInput").ap()}
    for nm, (shp, dt) in L0FFN_SHAPES.items():
        din[nm] = nc.dram_tensor(nm, shp, dt, kind="ExternalInput").ap()
    din["hdram"] = nc.dram_tensor("hdram", [D, T], F32, kind="Internal").ap()
    dout = {"xout": nc.dram_tensor("xout", [D, T], F32, kind="ExternalOutput").ap()}
    P = Prog(nc)
    P.make_pool()
    C = Common(P)
    build_l0ffn(P, C, din, dout)
    P.final_wait("sync", ["st_xout"])
    P.build()
    return nc





def build_l1attn(P: Prog, C: Common, din, dout, xin_key="dram_xin",
                 xout_key="dram_xout"):
    TC = 512
    NIC = T // 128
    W = 2176

    xc = P.sbuf("a_xc", [128, NT, TC], F32)
    tmp = P.sbuf("a_tmp", [128, NT, TC], F32)
    xhat = P.sbuf("a_xhat", [128, NT, T], BF16)
    wo_sb = P.sbuf("a_wo", [128, NT, 1024], BF16)
    wp3 = P.sbuf("a_wp3", [128, NT, 3, 128], BF16)
    qp = P.sbuf("a_qp", [128, T], BF16)
    kp = P.sbuf("a_kp", [128, T], BF16)
    vp = P.sbuf("a_vp", [128, NIC, 128], BF16)
    tab2 = P.sbuf("a_tab2", [128, 4096], BF16)
    relw = P.sbuf("a_relw", [128, 2, W], BF16)
    rpe = P.sbuf("a_rpe", [128, 2, T], BF16)
    praw = P.sbuf("a_praw", [128, 2, T], BF16)
    ptb4 = P.sbuf("a_ptb4", [128, NIC, 512], BF16)
    mrow = P.sbuf("a_mrow", [128, 2, 8], F32)
    ctx = P.sbuf("a_ctx", [128, NT, T], BF16)
    yc = P.sbuf("a_yc", [128, NT, TC], F32)
    sfull = [yc[:, 0:4, :].rearrange("p a b -> p (a b)"),
             yc[:, 4:8, :].rearrange("p a b -> p (a b)")]
    bqc = P.sbuf("a_bqc", [128, 8], F32)
    bkc = P.sbuf("a_bkc", [128, 8], F32)
    boc = P.sbuf("a_boc", [128, 8], F32)
    bvb = P.sbuf("a_bvb", [128, 1024], F32)
    identbf = P.sbuf("a_ident", [128, 128], BF16)
    ln = alloc_ln(P, "a")

    for nm, buf in (("bqc", bqc), ("bkc", bkc), ("boc", boc)):
        P.dma("sync", "a_" + nm,
              lambda e, nm=nm, buf=buf: e.dma_start(out=buf[:], in_=din[nm][:]),
              reads=[("dram", nm)], writes=["a_" + nm])
    P.dma("sync", "a_bvb",
          lambda e: e.dma_start(out=bvb[:], in_=din["bvrow"][:].broadcast_to([128, 1024])),
          reads=[("dram", "bvrow")], writes=["a_bvb"])
    P.dma("sync", "a_ident",
          lambda e: e.dma_start(out=identbf[:], in_=din["identbf"][:]),
          reads=[("dram", "identbf")], writes=["a_ident"])
    P.dma("gpsimd", "a_wo",
          lambda e: e.dma_start(out=wo_sb[:], in_=din["woTt"][:]),
          reads=[("dram", "woTt")], writes=["a_wo"])
    P.op("vector", lambda e: e.memset(tab2[:], 0.0), writes=["a_tab2"])

    for c0 in range(0, T, TC):
        load_x_chunk(P, din["xin"], xin_key, xc, "a_xc", c0, TC)
        emit_ln_stats(P, C, ln, xc, "a_xc", tmp, "a_tmp", NT)
        emit_ln_apply(P, C, ln, xc, "a_xc",
                      xhat[:, :, c0:c0 + TC], "a_xhat", tmp, "a_tmp", NT)

    for hp in range(8):
        for wi, wnm in enumerate(("wqTt", "wkTt", "wvTt")):
            P.dma("gpsimd", "a_wp3",
                  lambda e, wi=wi, wnm=wnm, hp=hp: e.dma_start(
                      out=wp3[:, :, wi, :], in_=din[wnm][hp]),
                  reads=[("dram", wnm)], writes=["a_wp3"])
        for hh in range(2):
            h = hp * 2 + hh
            P.dma("gpsimd", "a_tab2",
                  lambda e, h=h, hh=hh: e.dma_start(
                      out=tab2[64 * hh:64 * hh + 64, 0:4095], in_=din["tabTrev"][h]),
                  reads=[("dram", "tabTrev")], writes=["a_tab2"])
        for ts in range(4):
            for g in range(NT):
                P.op("tensor", lambda e, g=g, ts=ts: e.matmul(
                    out=P.psb(0), lhsT=wp3[:, g, 0, :], rhs=xhat[:, g, ts * 512:(ts + 1) * 512],
                    start=(g == 0), stop=(g == NT - 1)),
                    reads=["a_wp3", "a_xhat"], writes=["psb0"], inc=(g == NT - 1))
            P.op("vector", lambda e, ts=ts, hp=hp: e.tensor_scalar(
                out=qp[:, ts * 512:(ts + 1) * 512], in0=P.psb(0),
                scalar1=bqc[:, hp:hp + 1], scalar2=None, op0=ALU.add),
                reads=["psb0", "a_bqc"], writes=["a_qp"])
            for g in range(NT):
                P.op("tensor", lambda e, g=g, ts=ts: e.matmul(
                    out=P.psb(1), lhsT=wp3[:, g, 1, :], rhs=xhat[:, g, ts * 512:(ts + 1) * 512],
                    start=(g == 0), stop=(g == NT - 1)),
                    reads=["a_wp3", "a_xhat"], writes=["psb1"], inc=(g == NT - 1))
            P.op("vector", lambda e, ts=ts, hp=hp: e.tensor_scalar(
                out=kp[:, ts * 512:(ts + 1) * 512], in0=P.psb(1),
                scalar1=bkc[:, hp:hp + 1], scalar2=0.125, op0=ALU.add, op1=ALU.mult),
                reads=["psb1", "a_bkc"], writes=["a_kp"])
        for tc in range(NIC):
            for g in range(NT):
                P.op("tensor", lambda e, g=g, tc=tc: e.matmul(
                    out=P.pspool[:, 2, 0:128], lhsT=xhat[:, g, tc * 128:(tc + 1) * 128],
                    rhs=wp3[:, g, 2, :],
                    start=(g == 0), stop=(g == NT - 1)),
                    reads=["a_wp3", "a_xhat"], writes=["psb2"], inc=(g == NT - 1))
            P.op("vector", lambda e, tc=tc, hp=hp: e.tensor_add(
                out=vp[:, tc, :], in0=P.pspool[:, 2, 0:128],
                in1=bvb[:, hp * 128:(hp + 1) * 128]),
                reads=["psb2", "a_bvb"], writes=["a_vp"])

        for hh in range(2):
            h = hp * 2 + hh
            e0 = 64 * hh

            def emit_relw(ic, e0=e0):
                i0 = ic * 128
                s0w = 1920 - i0
                pi = ic % 2
                for rs in range(5):
                    w = 512 if rs < 4 else 128
                    bk_ = 4 + rs % 2
                    P.op("tensor", lambda e, rs=rs, w=w, bk_=bk_, e0=e0, i0=i0, s0w=s0w: e.matmul(
                        out=P.pspool[:, bk_, 0:w],
                        lhsT=qp[e0:e0 + 64, i0:i0 + 128],
                        rhs=tab2[e0:e0 + 64, s0w + rs * 512:s0w + rs * 512 + w],
                        start=True, stop=True),
                        reads=["a_qp", "a_tab2"], writes=[f"psb{bk_}"])
                    P.op("scalar", lambda e, rs=rs, w=w, bk_=bk_, pi=pi: e.copy(
                        out=relw[:, pi, rs * 512:rs * 512 + w], in_=P.pspool[:, bk_, 0:w]),
                        reads=[f"psb{bk_}"], writes=[f"a_relw{pi}"])
                P.dma("sync", f"a_reld{pi}",
                      lambda e, pi=pi: e.dma_start(out=din["reld"][pi], in_=relw[:, pi, :]),
                      reads=[f"a_relw{pi}"], writes=[("dram", f"reld{pi}")])
                shear = bass.AP(tensor=din["reld"].tensor, offset=pi * 128 * W + 127,
                                ap=[[W - 1, 128], [1, T]])
                P.dma("sync", f"a_rpe{pi}",
                      lambda e, shear=shear, pi=pi: e.dma_start(
                          out=rpe[:, pi, :], in_=shear),
                      reads=[("dram", f"reld{pi}")], writes=[f"a_rpe{pi}"])

            def emit_scores(ic, e0=e0):
                i0 = ic * 128
                pi = ic % 2
                sf, sfk = sfull[pi], f"a_sf{pi}"
                for js in range(4):
                    b = js % 2
                    P.op("tensor", lambda e, js=js, i0=i0, e0=e0, b=b: e.matmul(
                        out=P.psb(b), lhsT=qp[e0:e0 + 64, i0:i0 + 128],
                        rhs=kp[e0:e0 + 64, js * 512:(js + 1) * 512],
                        start=True, stop=True),
                        reads=["a_qp", "a_kp"], writes=[f"psb{b}"])
                    P.op("vector", lambda e, js=js, pi=pi, b=b, sf=sf: e.tensor_tensor(
                        out=sf[:, js * 512:(js + 1) * 512], in0=P.psb(b),
                        in1=rpe[:, pi, js * 512:(js + 1) * 512], op=ALU.add),
                        reads=[f"psb{b}", f"a_rpe{pi}"], writes=[sfk])
                P.op("vector", lambda e, pi=pi, sf=sf: e.reduce_max(
                    out=mrow[:, pi, 4:5], in_=sf[:], axis=AX.X),
                    reads=[sfk], writes=[f"a_mrow{pi}"])
                P.op("vector", lambda e, pi=pi: e.tensor_scalar(
                    out=mrow[:, pi, 4:5], in0=mrow[:, pi, 4:5], scalar1=-1.0,
                    scalar2=None, op0=ALU.mult),
                    reads=[f"a_mrow{pi}"], writes=[f"a_mrow{pi}"])
                P.op("scalar", lambda e, pi=pi, sf=sf: e.activation(
                    out=praw[:, pi, :], in_=sf[:], func=AF.Exp,
                    bias=mrow[:, pi, 4:5], accum_out=mrow[:, pi, 6:7]),
                    reads=[sfk, f"a_mrow{pi}"],
                    writes=[f"a_praw{pi}", f"a_mrow{pi}"])

            def emit_post(ic):
                pi = ic % 2
                P.op("vector", lambda e, pi=pi: e.reciprocal(
                    out=mrow[:, pi, 7:8], in_=mrow[:, pi, 6:7]),
                    reads=[f"a_mrow{pi}"], writes=[f"a_mrow{pi}"])
                P.op("vector", lambda e, pi=pi: e.tensor_scalar(
                    out=praw[:, pi, :], in0=praw[:, pi, :], scalar1=mrow[:, pi, 7:8],
                    scalar2=None, op0=ALU.mult),
                    reads=[f"a_praw{pi}", f"a_mrow{pi}"], writes=[f"a_praw{pi}"])

            def emit_trans(ic):
                pi = ic % 2
                icq = ic % 4
                for kc in range(NIC):
                    tb = 2 + (kc // 4) % 2
                    P.op("tensor", lambda e, kc=kc, pi=pi, tb=tb: e.matmul(
                        out=P.pspool[:, tb, (kc % 4) * 128:(kc % 4) * 128 + 128],
                        lhsT=praw[:, pi, kc * 128:(kc + 1) * 128], rhs=identbf[:],
                        start=True, stop=True),
                        reads=[f"a_praw{pi}", "a_ident"], writes=[f"psb{tb}"],
                        inc=(kc % 4 == 3))
                    if kc % 4 == 3:
                        g4 = kc // 4
                        P.op("scalar", lambda e, g4=g4, icq=icq, tb=tb: e.copy(
                            out=ptb4[:, 4 * g4:4 * g4 + 4, icq * 128:icq * 128 + 128],
                            in_=P.pspool[:, tb, :].rearrange("p (a b) -> p a b", a=4)),
                            reads=[f"psb{tb}"], writes=["a_ptb4"])

            def emit_ctx(icg, e0=e0, hp=hp):
                for jc in range(NIC):
                    P.op("tensor", lambda e, jc=jc, e0=e0: e.matmul(
                        out=P.pspool[0:64, 6, :], lhsT=vp[:, jc, e0:e0 + 64],
                        rhs=ptb4[:, jc, :],
                        start=(jc == 0), stop=(jc == NIC - 1)),
                        reads=["a_vp", "a_ptb4"], writes=["psb6"],
                        inc=(jc == NIC - 1))
                P.op("scalar", lambda e, icg=icg, e0=e0, hp=hp: e.copy(
                    out=ctx[e0:e0 + 64, hp, icg * 512:(icg + 1) * 512],
                    in_=P.pspool[0:64, 6, :]),
                    reads=["psb6"], writes=["a_ctx"])

            for ic in range(-1, NIC + 1):
                if ic >= 1:
                    emit_post(ic - 1)
                if 0 <= ic < NIC:
                    emit_scores(ic)
                if ic + 1 < NIC:
                    emit_relw(ic + 1)
                if ic >= 1:
                    emit_trans(ic - 1)
                    if (ic - 1) % 4 == 3:
                        emit_ctx((ic - 1) // 4)

    for ci, c0 in enumerate(range(0, T, TC)):
        for et in range(NT):
            bk_ = et % 2
            for g in range(NT):
                P.op("tensor", lambda e, g=g, et=et, ci=ci, bk_=bk_: e.matmul(
                    out=P.psb(bk_), lhsT=wo_sb[:, g, et * 128:(et + 1) * 128],
                    rhs=ctx[:, g, ci * 512:(ci + 1) * 512],
                    start=(g == 0), stop=(g == NT - 1)),
                    reads=["a_wo", "a_ctx"], writes=[f"psb{bk_}"], inc=(g == NT - 1))
            P.op("vector", lambda e, et=et, bk_=bk_: e.tensor_scalar(
                out=yc[:, et, :], in0=P.psb(bk_), scalar1=boc[:, et:et + 1],
                scalar2=None, op0=ALU.add),
                reads=[f"psb{bk_}", "a_boc"], writes=["a_yc"])
        emit_ln_stats(P, C, ln, yc, "a_yc", tmp, "a_tmp", NT)
        load_x_chunk(P, din["xin"], xin_key, xc, "a_xc", c0, TC)
        emit_ln_apply(P, C, ln, yc, "a_yc", xc, "a_xc", tmp, "a_tmp", NT,
                      add_to=xc, add_key="a_xc")
        store_x_chunk(P, dout["xout"], xout_key, xc, "a_xc", c0, TC, "st_xout")


def host_prep_l1attn(inputs):
    f32, c = np.float32, np.ascontiguousarray
    d = {}
    d["wqTt"] = _tile_lhs(np.asarray(inputs["L1_attn_wq"], f32).T)
    d["wkTt"] = _tile_lhs(np.asarray(inputs["L1_attn_wk"], f32).T)
    d["wvTt"] = _tile_lhs(np.asarray(inputs["L1_attn_wv"], f32).T)
    woT = np.asarray(inputs["L1_attn_wo"], f32).T
    d["woTt"] = c(woT.reshape(8, 128, 1024).transpose(1, 0, 2))
    d["bqc"] = c(np.asarray(inputs["L1_attn_bq"], f32).reshape(8, 128).T)
    d["bkc"] = c(np.asarray(inputs["L1_attn_bk"], f32).reshape(8, 128).T)
    d["boc"] = c(np.asarray(inputs["L1_attn_bo"], f32).reshape(8, 128).T)
    d["bvrow"] = c(np.asarray(inputs["L1_attn_bv"], f32).reshape(1, 1024))
    tab = np.asarray(inputs["L1_attn_rpe_table"], f32)[:, :2 * T - 1]
    d["tabTrev"] = c(tab[:, ::-1].transpose(0, 2, 1))
    d["identbf"] = np.eye(128, dtype=BF16NP)
    return d


L1ATTN_SHAPES = {
    "wqTt": ([8, 128, 8, 128], F32), "wkTt": ([8, 128, 8, 128], F32),
    "wvTt": ([8, 128, 8, 128], F32), "woTt": ([128, 8, 1024], F32),
    "bqc": ([128, 8], F32), "bkc": ([128, 8], F32), "boc": ([128, 8], F32),
    "bvrow": ([1, 1024], F32), "tabTrev": ([16, 64, 4095], F32),
    "identbf": ([128, 128], BF16),
}


def build_l1attn_program():
    nc = bass.Bass("TRN2", target_bir_lowering=False)
    din = {"xin": nc.dram_tensor("xin", [D, T], F32, kind="ExternalInput").ap()}
    for nm, (shp, dt) in L1ATTN_SHAPES.items():
        din[nm] = nc.dram_tensor(nm, shp, dt, kind="ExternalInput").ap()
    din["reld"] = nc.dram_tensor("reld", [2, 128, 2176], BF16, kind="Internal").ap()
    dout = {"xout": nc.dram_tensor("xout", [D, T], F32, kind="ExternalOutput").ap()}
    P = Prog(nc)
    P.make_pool()
    C = Common(P)
    build_l1attn(P, C, din, dout)
    P.final_wait("sync", ["st_xout"])
    P.build()
    return nc





PI = float(np.pi)


def build_l0attn(P: Prog, C: Common, din, dout, xin_key="dram_xin",
                 xout_key="dram_xout"):
    TCC = 512
    NCH = T // TCC
    CT = 17

    xc = P.sbuf("g_xc", [128, NT, TCC], F32)
    tmp = P.sbuf("g_tmp", [128, NT, TCC], F32)
    xhat = P.sbuf("g_xhat", [128, NT, TCC], BF16)
    wproj = P.sbuf("g_wproj", [128, NT, 2080], BF16)
    outw = [P.sbuf(f"g_outw{i}", [128, NT, 128], BF16) for i in range(2)]
    zbuf = P.sbuf("g_zbuf", [128, CT, TCC + 3], F32)
    z2c = P.sbuf("g_z2c", [128, 9, TCC], F32)
    gate = P.sbuf("g_gate", [128, NT, TCC], BF16)
    scr = P.sbuf("g_scr", [128, 2, TCC], F32)
    sqt = P.sbuf("g_sqt", [128, 2, TCC], F32)
    convw = P.sbuf("g_convw", [128, CT, 4], F32)
    convb = P.sbuf("g_convb", [128, CT], F32)
    thc = P.sbuf("g_thc", [128, 4, NT], F32)
    slc = P.sbuf("g_slc", [32, 4], F32)
    sscol = P.sbuf("g_sscol", [32, 1], F32)
    dlog = P.sbuf("g_dlog", [1, 4], F32)
    wcols = P.sbuf("g_wcols", [128, 4], F32)
    nscol = P.sbuf("g_nscol", [32, 8], F32)
    wre = P.sbuf("g_wre", [32, 8, 32], F32)
    bd = P.sbuf("g_bd", [128, 8, 128], BF16)
    selg = P.sbuf("g_selg", [32, NT, 128], F32)
    sel4 = P.sbuf("g_sel4", [4, 128], F32)
    o32 = P.sbuf("g_o32", [128, 4], F32)
    dist = P.sbuf("g_dist", [32, 512], F32)
    twc = P.sbuf("g_twc", [32, TCC], F32)
    pw = P.sbuf("g_pw", [32, TCC], F32)
    denc = P.sbuf("g_denc", [32, TCC], F32)
    dstage = P.sbuf("g_dstage", [4, TCC], F32)
    sstage = P.sbuf("g_sstage", [4, 2, TCC], F32)
    dcarry = P.sbuf("g_dcarry", [32, 2], F32)
    carry = P.sbuf("g_carry", [128, 64], F32)
    numf = P.sbuf("g_numf", [128, 8, TCC], F32)
    numbf = P.sbuf("g_numbf", [128, 8, TCC], BF16)
    ygate = P.sbuf("g_ygate", [128, NT, TCC], BF16)
    zeros = P.sbuf("g_zeros", [128, TCC], F32)
    negpi = P.sbuf("g_negpi", [128, 1], F32)
    pihalf = P.sbuf("g_pihalf", [128, 1], F32)
    onecol = P.sbuf("g_onecol", [32, 1], F32)

    P.dma("sync", "g_convw", lambda e: e.dma_start(
        out=convw[:], in_=din["convT"].rearrange("(c p) j -> p c j", p=128)),
        reads=[("dram", "convT")], writes=["g_convw"])
    P.dma("sync", "g_convb", lambda e: e.dma_start(
        out=convb[:], in_=din["convb"][:]),
        reads=[("dram", "convb")], writes=["g_convb"])
    P.dma("sync", "g_thc", lambda e: e.dma_start(
        out=thc[:], in_=din["thetac"][:]),
        reads=[("dram", "thetac")], writes=["g_thc"])
    P.dma("sync", "g_slc", lambda e: e.dma_start(
        out=slc[:, 0:1], in_=din["slopes"][:]),
        reads=[("dram", "slopes")], writes=["g_slc"])
    P.dma("sync", "g_sscol", lambda e: e.dma_start(
        out=sscol[:], in_=din["sscol"][:]),
        reads=[("dram", "sscol")], writes=["g_sscol"])
    P.dma("sync", "g_dlog", lambda e: e.dma_start(
        out=dlog[0:1, 0:3], in_=din["dlog"][:]),
        reads=[("dram", "dlog")], writes=["g_dlog"])
    P.dma("sync", "g_nscol", lambda e: e.dma_start(
        out=nscol[:], in_=din["nsc"][:]),
        reads=[("dram", "nsc")], writes=["g_nscol"])
    P.dma("sync", "g_wre", lambda e: e.dma_start(
        out=wre[:], in_=din["wrim"].rearrange("r m h o -> h (r m) o")),
        reads=[("dram", "wrim")], writes=["g_wre"])
    P.dma("sync", "g_selg", lambda e: e.dma_start(
        out=selg[:], in_=din["selg"].rearrange("g k p -> k g p")),
        reads=[("dram", "selg")], writes=["g_selg"])
    P.dma("sync", "g_sel4", lambda e: e.dma_start(
        out=sel4[:], in_=din["sel4"][:]), reads=[("dram", "sel4")], writes=["g_sel4"])
    P.dma("sync", "g_o32", lambda e: e.dma_start(
        out=o32[:], in_=din["o32"][:]), reads=[("dram", "o32")], writes=["g_o32"])

    P.dma("gpsimd", "g_wproj", lambda e: e.dma_start(
        out=wproj[:], in_=din["inprojw"][:]),
        reads=[("dram", "inprojw")], writes=["g_wproj"])

    P.op("vector", lambda e: e.memset(zeros[:], 0.0), writes=["g_zeros"])
    P.op("vector", lambda e: e.memset(negpi[:], -PI), writes=["g_negpi"])
    P.op("vector", lambda e: e.memset(pihalf[:], 0.5 * PI), writes=["g_pihalf"])
    P.op("vector", lambda e: e.memset(onecol[:], 1.0), writes=["g_onecol"])
    P.op("vector", lambda e: e.memset(zbuf[:], 0.0), writes=["g_zbuf"])
    P.op("vector", lambda e: e.memset(carry[:], 0.0), writes=["g_carry"])
    P.op("vector", lambda e: e.memset(dcarry[:], 0.0), writes=["g_dcarry"])
    P.op("vector", lambda e: e.memset(bd[:], 0.0), writes=["g_bd"])

    P.op("scalar", lambda e: e.activation(
        out=slc[:, 2:3], in_=slc[:, 0:1], func=AF.Abs),
        reads=["g_slc"], writes=["g_slc"])
    P.op("scalar", lambda e: e.activation(
        out=slc[:, 2:3], in_=slc[:, 2:3], func=AF.Exp, scale=-1.0),
        reads=["g_slc"], writes=["g_slc"])
    P.op("scalar", lambda e: e.activation(
        out=slc[:, 2:3], in_=slc[:, 2:3], func=AF.Ln, bias=onecol[:]),
        reads=["g_slc", "g_onecol"], writes=["g_slc"])
    P.op("vector", lambda e: e.tensor_scalar(
        out=slc[:, 3:4], in0=slc[:, 0:1], scalar1=0.0, scalar2=None,
        op0=ALU.max), reads=["g_slc"], writes=["g_slc"])
    P.op("vector", lambda e: e.tensor_tensor(
        out=slc[:, 1:2], in0=slc[:, 2:3], in1=slc[:, 3:4], op=ALU.add),
        reads=["g_slc"], writes=["g_slc"])
    P.op("vector", lambda e: e.tensor_scalar(
        out=slc[:, 1:2], in0=slc[:, 1:2], scalar1=-1.0, scalar2=None,
        op0=ALU.mult), reads=["g_slc"], writes=["g_slc"])

    P.op("vector", lambda e: e.reduce_max(
        out=dlog[0:1, 3:4], in_=dlog[0:1, 0:3], axis=AX.X),
        reads=["g_dlog"], writes=["g_dlog"])
    P.op("vector", lambda e: e.tensor_scalar(
        out=dlog[0:1, 3:4], in0=dlog[0:1, 3:4], scalar1=-1.0, scalar2=None,
        op0=ALU.mult), reads=["g_dlog"], writes=["g_dlog"])
    P.op("scalar", lambda e: e.activation(
        out=dlog[0:1, 0:3], in_=dlog[0:1, 0:3], func=AF.Exp, bias=dlog[0:1, 3:4]),
        reads=["g_dlog"], writes=["g_dlog"])
    P.op("vector", lambda e: e.reduce_sum(
        out=dlog[0:1, 3:4], in_=dlog[0:1, 0:3], axis=AX.X),
        reads=["g_dlog"], writes=["g_dlog"])
    P.op("vector", lambda e: e.reciprocal(
        out=dlog[0:1, 3:4], in_=dlog[0:1, 3:4]), reads=["g_dlog"], writes=["g_dlog"])
    P.op("vector", lambda e: e.tensor_scalar(
        out=dlog[0:1, 0:3], in0=dlog[0:1, 0:3], scalar1=dlog[0:1, 3:4],
        scalar2=None, op0=ALU.mult), reads=["g_dlog"], writes=["g_dlog"])
    P.op("tensor", lambda e: e.matmul(
        out=P.pspool[:, 7, 0:3], lhsT=C.ones_row[:], rhs=dlog[0:1, 0:3],
        start=True, stop=True), reads=["g_dlog", "c_ones_row"], writes=["psb7"])
    P.op("vector", lambda e: e.tensor_copy(
        out=wcols[:, 0:3], in_=P.pspool[:, 7, 0:3]), reads=["psb7"], writes=["g_wcols"])
    P.op("vector", lambda e: e.tensor_scalar(
        out=wcols[:, 3:4], in0=wcols[:, 2:3], scalar1=-1.0, scalar2=None,
        op0=ALU.mult), reads=["g_wcols"], writes=["g_wcols"])

    for rim in range(8):
        P.op("vector", lambda e, rim=rim: e.tensor_scalar(
            out=wre[:, rim, :], in0=wre[:, rim, :], scalar1=nscol[:, rim:rim + 1],
            scalar2=None, op0=ALU.mult), reads=["g_wre", "g_nscol"], writes=["g_wre"])
    for rim in range(8):
        for kq in range(4):
            P.dma("gpsimd", "g_bd",
                  lambda e, rim=rim, kq=kq: e.dma_start(
                      out=bd[32 * kq:32 * kq + 32, rim, 32 * kq:32 * kq + 32],
                      in_=wre[:, rim, :]),
                  reads=["g_wre"], writes=["g_bd"])

    ln = alloc_ln(P, "g")

    for ci in range(NCH):
        c0 = ci * TCC
        load_x_chunk(P, din["xin"], xin_key, xc, "g_xc", c0, TCC)
        emit_ln_stats(P, C, ln, xc, "g_xc", tmp, "g_tmp", NT)
        emit_ln_apply(P, C, ln, xc, "g_xc", xhat, "g_xhat", tmp, "g_tmp", NT)

        for ct in range(CT):
            ncol = 128 if ct < 16 else 32
            bk_ = ct % 2
            for g in range(NT):
                P.op("tensor", lambda e, ct=ct, g=g, bk_=bk_, ncol=ncol: e.matmul(
                    out=P.pspool[0:ncol, bk_, :], lhsT=wproj[:, g, ct * 128:ct * 128 + ncol],
                    rhs=xhat[:, g, :], start=(g == 0), stop=(g == NT - 1)),
                    reads=["g_wproj", "g_xhat"], writes=[f"psb{bk_}"], inc=(g == NT - 1))
            P.op("scalar", lambda e, ct=ct, bk_=bk_, ncol=ncol: e.copy(
                out=zbuf[0:ncol, ct, 3:3 + TCC], in_=P.pspool[0:ncol, bk_, :]),
                reads=[f"psb{bk_}"], writes=["g_zbuf"])

        for ct in range(CT):
            nrow = 128 if ct < 16 else 32
            if ct < 8:
                dst, dstk = z2c, "g_z2c"
                dsl = (slice(0, nrow), ct, slice(0, TCC))
            elif ct < 16:
                dst, dstk = scr, f"g_scr{ct % 2}"
                dsl = (slice(0, nrow), ct % 2, slice(0, TCC))
            else:
                dst, dstk = z2c, "g_z2c"
                dsl = (slice(0, nrow), 8, slice(0, TCC))
            P.op("vector", lambda e, ct=ct, nrow=nrow, dst=dst, dsl=dsl: e.tensor_scalar(
                out=dst[dsl], in0=zbuf[0:nrow, ct, 0:TCC],
                scalar1=convw[0:nrow, ct, 0:1], scalar2=convb[0:nrow, ct:ct + 1],
                op0=ALU.mult, op1=ALU.add),
                reads=["g_zbuf", "g_convw", "g_convb"], writes=[dstk])
            for j in (1, 2, 3):
                P.op("vector", lambda e, ct=ct, j=j, nrow=nrow, dst=dst, dsl=dsl:
                     e.scalar_tensor_tensor(
                         out=dst[dsl], in0=zbuf[0:nrow, ct, j:j + TCC],
                         scalar=convw[0:nrow, ct, j:j + 1], in1=dst[dsl],
                         op0=ALU.mult, op1=ALU.add),
                     reads=["g_zbuf", "g_convw", dstk], writes=[dstk])
            if 8 <= ct < 16:
                P.op("scalar", lambda e, ct=ct: e.activation(
                    out=gate[:, ct - 8, :], in_=scr[:, ct % 2, :], func=AF.Silu),
                    reads=[f"g_scr{ct % 2}"], writes=["g_gate"])
        P.op("vector", lambda e: e.tensor_copy(
            out=zbuf[:, :, 0:3], in_=zbuf[:, :, TCC:TCC + 3]),
            reads=["g_zbuf"], writes=["g_zbuf"])

        P.dma("sync", "g_dist", lambda e, c0=c0: e.dma_start(
            out=dist[:], in_=din["dist"][:, c0:c0 + TCC]),
            reads=[("dram", "dist")], writes=["g_dist"])
        P.op("scalar", lambda e: e.activation(
            out=twc[:], in_=dist[:], func=AF.Exp, scale=slc[:, 1:2]),
            reads=["g_dist", "g_slc"], writes=["g_twc"])
        P.op("vector", lambda e: e.tensor_scalar(
            out=pw[:], in0=z2c[0:32, 8, :], scalar1=sscol[:], scalar2=20.0,
            op0=ALU.mult, op1=ALU.min), reads=["g_z2c", "g_sscol"], writes=["g_pw"])
        P.op("vector", lambda e: e.tensor_scalar(
            out=pw[:], in0=pw[:], scalar1=-20.0, scalar2=None,
            op0=ALU.max), reads=["g_pw"], writes=["g_pw"])
        P.op("scalar", lambda e: e.activation(
            out=pw[:], in_=pw[:], func=AF.Exp), reads=["g_pw"], writes=["g_pw"])
        P.op("vector", lambda e: e.tensor_mul(
            out=pw[:], in0=pw[:], in1=twc[:]), reads=["g_pw", "g_twc"], writes=["g_pw"])
        P.op("vector", lambda e: e.tensor_tensor_scan(
            out=denc[:], data0=pw[:], data1=zeros[0:32, :], initial=dcarry[:, 0:1],
            op0=ALU.add, op1=ALU.add),
            reads=["g_pw", "g_zeros", "g_dcarry"], writes=["g_denc"])
        P.op("vector", lambda e: e.tensor_copy(
            out=dcarry[:, 0:1], in_=denc[:, TCC - 1:TCC]),
            reads=["g_denc"], writes=["g_dcarry"])

        for g in range(NT):
            gp = g % 2
            sck = f"g_scr{gp}"
            P.op("tensor", lambda e, g=g: e.matmul(
                out=P.psb(6), lhsT=selg[:, g, :], rhs=pw[:],
                start=True, stop=True), reads=["g_selg", "g_pw"], writes=["psb6"])
            P.op("vector", lambda e, g=g, gp=gp: e.tensor_scalar(
                out=scr[:, gp, :], in0=z2c[:, g, :], scalar1=wcols[:, 3:4],
                scalar2=wcols[:, 1:2], op0=ALU.mult, op1=ALU.add),
                reads=["g_z2c", "g_wcols"], writes=[sck])
            P.op("vector", lambda e, g=g, gp=gp: e.tensor_mul(
                out=scr[:, gp, :], in0=scr[:, gp, :], in1=z2c[:, g, :]),
                reads=[sck, "g_z2c"], writes=[sck])
            P.op("vector", lambda e, gp=gp: e.tensor_scalar(
                out=scr[:, gp, :], in0=scr[:, gp, :], scalar1=wcols[:, 0:1],
                scalar2=None, op0=ALU.add), reads=[sck, "g_wcols"], writes=[sck])
            P.op("vector", lambda e, gp=gp: e.tensor_mul(
                out=scr[:, gp, :], in0=scr[:, gp, :], in1=P.psb(6)),
                reads=[sck, "psb6"], writes=[sck])
            for m in range(4):
                mp = m % 2
                tk = f"g_tmpA{mp}"
                sqk = f"g_sqt{mp}"
                t0, t1, t2 = 4 * mp + 0, 4 * mp + 1, 4 * mp + 2
                sq = sqt[:, mp, :]
                P.op("vector", lambda e, g=g, m=m, sq=sq: e.tensor_scalar(
                    out=sq, in0=z2c[:, g, :], scalar1=thc[:, m, g:g + 1],
                    scalar2=None, op0=ALU.mult),
                    reads=["g_z2c", "g_thc"], writes=[sqk])
                P.op("vector", lambda e, sq=sq, t2=t2: e.tensor_scalar(
                    out=tmp[:, t2, :], in0=sq, scalar1=PI, scalar2=None,
                    op0=ALU.is_gt), reads=[sqk], writes=[tk])
                P.op("vector", lambda e, sq=sq, t1=t1: e.tensor_scalar(
                    out=tmp[:, t1, :], in0=sq, scalar1=-PI, scalar2=None,
                    op0=ALU.is_lt), reads=[sqk], writes=[tk])
                P.op("vector", lambda e, t1=t1, t2=t2: e.tensor_sub(
                    out=tmp[:, t2, :], in0=tmp[:, t2, :], in1=tmp[:, t1, :]),
                    reads=[tk], writes=[tk])
                P.op("vector", lambda e, sq=sq, t1=t1, t2=t2: e.scalar_tensor_tensor(
                    out=tmp[:, t1, :], in0=tmp[:, t2, :], scalar=-2 * PI,
                    in1=sq, op0=ALU.mult, op1=ALU.add),
                    reads=[tk, sqk], writes=[tk])
                P.op("vector", lambda e, t1=t1: e.tensor_scalar(
                    out=tmp[:, t1, :], in0=tmp[:, t1, :], scalar1=PI,
                    scalar2=-PI, op0=ALU.min, op1=ALU.max),
                    reads=[tk], writes=[tk])
                P.op("scalar", lambda e, t0=t0, t1=t1: e.activation(
                    out=tmp[:, t0, :], in_=tmp[:, t1, :], func=AF.Sin),
                    reads=[tk], writes=[tk])
                P.op("scalar", lambda e, t1=t1, t2=t2: e.activation(
                    out=tmp[:, t2, :], in_=tmp[:, t1, :], func=AF.Sin, scale=0.5),
                    reads=[tk], writes=[tk])
                P.op("scalar", lambda e, t2=t2: e.activation(
                    out=tmp[:, t2, :], in_=tmp[:, t2, :], func=AF.Square),
                    reads=[tk], writes=[tk])
                P.op("vector", lambda e, t1=t1, t2=t2: e.tensor_scalar(
                    out=tmp[:, t1, :], in0=tmp[:, t2, :], scalar1=-2.0,
                    scalar2=1.0, op0=ALU.mult, op1=ALU.add),
                    reads=[tk], writes=[tk])
                P.op("vector", lambda e, t1=t1, gp=gp: e.tensor_mul(
                    out=tmp[:, t1, :], in0=tmp[:, t1, :], in1=scr[:, gp, :]),
                    reads=[tk, sck], writes=[tk])
                P.op("vector", lambda e, t0=t0, gp=gp: e.tensor_mul(
                    out=tmp[:, t0, :], in0=tmp[:, t0, :], in1=scr[:, gp, :]),
                    reads=[tk, sck], writes=[tk])
                for ri in range(2):
                    src = (t1, t0)[ri]
                    cc = g * 8 + m * 2 + ri
                    P.op("vector", lambda e, m=m, ri=ri, src=src, cc=cc: e.tensor_tensor_scan(
                        out=numf[:, 2 * m + ri, :], data0=tmp[:, src, :],
                        data1=zeros[:], initial=carry[:, cc:cc + 1],
                        op0=ALU.add, op1=ALU.add),
                        reads=[tk, "g_zeros", "g_carry"], writes=["g_numf"])
                    sqs = (3, 7)[ri]
                    P.op("scalar", lambda e, m=m, ri=ri, sqs=sqs: e.activation(
                        out=tmp[:, sqs, :], in_=numf[:, 2 * m + ri, :], func=AF.Square),
                        reads=["g_numf"], writes=[f"g_sqb{ri}"])
                    for sub in range(4):
                        first = (m == 0 and ri == 0 and sub == 0)
                        last = (m == 3 and ri == 1 and sub == 3)
                        P.op("tensor", lambda e, sub=sub, sqs=sqs, first=first, last=last: e.matmul(
                            out=P.pspool[0:4, 7, sub * 128:(sub + 1) * 128],
                            lhsT=o32[:], rhs=tmp[:, sqs, sub * 128:(sub + 1) * 128],
                            start=first, stop=last),
                            reads=["g_o32", f"g_sqb{ri}"], writes=["psb7"],
                            inc=(sub == 3))
            P.op("vector", lambda e, g=g: e.tensor_copy(
                out=carry[:, g * 8:g * 8 + 8], in_=numf[:, 0:8, TCC - 1:TCC]),
                reads=["g_numf"], writes=["g_carry"])
            P.op("scalar", lambda e: e.activation(
                out=sstage[:, 0, :], in_=P.pspool[0:4, 7, :],
                func=AF.Copy, scale=1.0 / 256.0),
                reads=["psb7"], writes=["g_sstage"])
            P.dma("sync", "g_dstage",
                  lambda e, g=g: e.dma_start(out=dstage[:], in_=denc[4 * g:4 * g + 4, :]),
                  reads=["g_denc"], writes=["g_dstage"])
            P.op("vector", lambda e: e.tensor_scalar(
                out=dstage[:], in0=dstage[:], scalar1=1e-4, scalar2=None,
                op0=ALU.max), reads=["g_dstage"], writes=["g_dstage"])
            P.op("scalar", lambda e: e.activation(
                out=dstage[:], in_=dstage[:], func=AF.Square, scale=float(np.sqrt(1e-5))),
                reads=["g_dstage"], writes=["g_dstage"])
            P.op("vector", lambda e: e.tensor_add(
                out=sstage[:, 1, :], in0=sstage[:, 0, :], in1=dstage[:]),
                reads=["g_sstage", "g_dstage"], writes=["g_sstage"])
            P.op("scalar", lambda e: e.activation(
                out=sstage[:, 1, :], in_=sstage[:, 1, :], func=AF.Sqrt),
                reads=["g_sstage"], writes=["g_sstage"])
            P.op("vector", lambda e: e.reciprocal(
                out=sstage[:, 1, :], in_=sstage[:, 1, :]),
                reads=["g_sstage"], writes=["g_sstage"])
            P.op("tensor", lambda e: e.matmul(
                out=P.psb(6), lhsT=sel4[:], rhs=sstage[:, 1, :],
                start=True, stop=True), reads=["g_sel4", "g_sstage"], writes=["psb6"])
            for mm in range(8):
                P.op("vector", lambda e, mm=mm: e.tensor_mul(
                    out=numbf[:, mm, :], in0=numf[:, mm, :], in1=P.psb(6)),
                    reads=["g_numf", "psb6"], writes=["g_numbf"])
            for rim in range(8):
                ri, m = rim // 4, rim % 4
                P.op("tensor", lambda e, rim=rim, ri=ri, m=m: e.matmul(
                    out=P.psb(2), lhsT=bd[:, rim, :], rhs=numbf[:, 2 * m + ri, :],
                    start=(rim == 0), stop=(rim == 7)),
                    reads=["g_bd", "g_numbf"], writes=["psb2"], inc=(rim == 7))
            P.op("vector", lambda e, g=g: e.tensor_mul(
                out=ygate[:, g, :], in0=P.psb(2), in1=gate[:, g, :]),
                reads=["psb2", "g_gate"], writes=["g_ygate"])

        for et in range(NT):
            P.dma("gpsimd", f"g_outw{et % 2}",
                  lambda e, et=et: e.dma_start(
                      out=outw[et % 2][:], in_=din["outprojw"][et]),
                  reads=[("dram", "outprojw")], writes=[f"g_outw{et % 2}"])
            for g in range(NT):
                P.op("tensor", lambda e, et=et, g=g: e.matmul(
                    out=P.psb(3), lhsT=outw[et % 2][:, g, :], rhs=ygate[:, g, :],
                    start=(g == 0), stop=(g == NT - 1)),
                    reads=[f"g_outw{et % 2}", "g_ygate"], writes=["psb3"],
                    inc=(g == NT - 1))
            P.op("vector", lambda e, et=et: e.tensor_copy(
                out=numf[:, et, :], in_=P.psb(3)), reads=["psb3"], writes=["g_numf"])

        emit_ln_stats(P, C, ln, numf, "g_numf", tmp, "g_tmp", NT)
        load_x_chunk(P, din["xin"], xin_key, xc, "g_xc", c0, TCC)
        emit_ln_apply(P, C, ln, numf, "g_numf", xc, "g_xc", tmp, "g_tmp", NT,
                      add_to=xc, add_key="g_xc")
        store_x_chunk(P, dout["xout"], xout_key, xc, "g_xc", c0, TCC, "st_xout")


def host_prep_l0attn(inputs):
    f32, c = np.float32, np.ascontiguousarray
    d = {}
    ipw = np.asarray(inputs["L0_attn_in_proj_w"], f32)
    d["inprojw"] = c(ipw.reshape(8, 128, 2080).transpose(1, 0, 2))
    convT = np.zeros((2176, 4), f32)
    convT[:2080] = np.asarray(inputs["L0_attn_conv_w"], f32)[:, 0, :].T
    d["convT"] = convT
    convb = np.zeros((2176,), f32)
    convb[:2080] = np.asarray(inputs["L0_attn_conv_b"], f32)
    d["convb"] = c(convb.reshape(17, 128).T)
    th = np.asarray(inputs["L0_attn_theta"], f32).reshape(32, 32, 4)
    d["thetac"] = c(th.transpose(2, 0, 1).reshape(4, 8, 128).transpose(2, 0, 1))
    d["slopes"] = c(np.concatenate([
        np.asarray(inputs["L0_attn_decay_slopes"], f32),
        np.asarray(inputs["L0_attn_anchor_slopes"], f32)]).reshape(32, 1))
    d["sscol"] = c(np.asarray(inputs["L0_attn_score_scale"], f32).reshape(32, 1))
    d["dlog"] = c(np.asarray(inputs["L0_attn_deriv_logits"], f32).reshape(1, 3))
    ns = np.asarray(inputs["L0_attn_norm_scale"], f32)
    nsc = np.stack([ns[:128].reshape(32, 4).T, ns[128:].reshape(32, 4).T])
    d["nsc"] = c(nsc.reshape(8, 32).T)
    wr = np.asarray(inputs["L0_attn_W_re"], f32).reshape(32, 4, 32).transpose(1, 0, 2)
    wi = np.asarray(inputs["L0_attn_W_im"], f32).reshape(32, 4, 32).transpose(1, 0, 2)
    d["wrim"] = c(np.stack([wr, wi]))
    d["outprojw"] = _tile_lhs(np.asarray(inputs["L0_attn_out_proj_w"], f32))
    p = np.arange(128)
    d["sel4"] = c((p[None, :] // 32 == np.arange(4)[:, None]).astype(f32))
    d["o32"] = c(d["sel4"].T)
    selg = np.zeros((8, 32, 128), f32)
    for g in range(8):
        selg[g, g * 4 + p // 32, p] = 1.0
    d["selg"] = selg
    t = np.arange(2048, dtype=f32)
    dist = np.empty((32, 2048), f32)
    dist[:28] = 2047.0 - t
    dist[28:] = t
    d["dist"] = dist
    return d


L0ATTN_SHAPES = {
    "inprojw": ([128, 8, 2080], F32), "convT": ([2176, 4], F32),
    "convb": ([128, 17], F32), "thetac": ([128, 4, 8], F32),
    "slopes": ([32, 1], F32), "sscol": ([32, 1], F32), "dlog": ([1, 3], F32),
    "nsc": ([32, 8], F32), "wrim": ([2, 4, 32, 32], F32),
    "outprojw": ([8, 128, 8, 128], F32), "sel4": ([4, 128], F32),
    "o32": ([128, 4], F32), "selg": ([8, 32, 128], F32), "dist": ([32, 2048], F32),
}


def build_l0attn_program():
    nc = bass.Bass("TRN2", target_bir_lowering=False)
    din = {"xin": nc.dram_tensor("xin", [D, T], F32, kind="ExternalInput").ap()}
    for nm, (shp, dt) in L0ATTN_SHAPES.items():
        din[nm] = nc.dram_tensor(nm, shp, dt, kind="ExternalInput").ap()
    dout = {"xout": nc.dram_tensor("xout", [D, T], F32, kind="ExternalOutput").ap()}
    P = Prog(nc)
    P.make_pool()
    C = Common(P)
    build_l0attn(P, C, din, dout)
    P.final_wait("sync", ["st_xout"])
    P.build()
    return nc




def build_fused_program():
    nc = bass.Bass("TRN2", target_bir_lowering=False)
    din = {"xin": nc.dram_tensor("xin", [D, T], F32, kind="ExternalInput").ap()}
    for shapes in (L0ATTN_SHAPES,):
        for nm, (shp, dt) in shapes.items():
            din[nm] = nc.dram_tensor(nm, shp, dt, kind="ExternalInput").ap()
    for nm, (shp, dt) in L0FFN_SHAPES.items():
        din[nm] = nc.dram_tensor(nm, shp, dt, kind="ExternalInput").ap()
    for nm, (shp, dt) in L1ATTN_SHAPES.items():
        din[nm] = nc.dram_tensor(nm, shp, dt, kind="ExternalInput").ap()
    for nm, shp in L1FFN_SHAPES.items():
        din[nm] = nc.dram_tensor(nm, shp, F32, kind="ExternalInput").ap()
    din["hdram"] = nc.dram_tensor("hdram", [D, T], F32, kind="Internal").ap()
    din["reld"] = nc.dram_tensor("reld", [2, 128, 2176], BF16, kind="Internal").ap()
    xra = nc.dram_tensor("xra", [D, T], F32, kind="Internal").ap()
    xrb = nc.dram_tensor("xrb", [D, T], F32, kind="Internal").ap()
    xout = nc.dram_tensor("xout", [D, T], F32, kind="ExternalOutput").ap()

    P = Prog(nc)
    P.make_pool()
    C = Common(P)
    with P.scope():
        build_l0attn(P, C, din, {"xout": xra},
                     xin_key=("dram", "xin"), xout_key=("dram", "xra"))
    P.barrier()
    with P.scope():
        build_l0ffn(P, C, dict(din, xin=xra), {"xout": xrb},
                    xin_key=("dram", "xra"), xout_key=("dram", "xrb"))
    P.barrier()
    with P.scope():
        build_l1attn(P, C, dict(din, xin=xrb), {"xout": xra},
                     xin_key=("dram", "xrb"), xout_key=("dram", "xra2"))
    P.barrier()
    with P.scope():
        build_l1ffn(P, C, dict(din, xin=xra), {"xout": xout},
                    xin_key=("dram", "xra2"), xout_key=("dram", "xout"))
    P.final_wait("sync", ["st_xout"])
    P.build()
    return nc



_PROGS = {}


def _get_prog(name, builder):
    if name not in _PROGS:
        _PROGS[name] = builder()
    return _PROGS[name]


def kernel(**inputs):
    x = np.asarray(inputs["L0_attn_x"], np.float32)
    nc = _get_prog("fused", build_fused_program)
    hp = {}
    hp.update(host_prep_l0attn(inputs))
    hp.update(host_prep_l0ffn(inputs))
    hp.update(host_prep_l1attn(inputs))
    hp.update(host_prep_l1ffn(inputs))
    in_maps = []
    for i in range(NCORES):
        m = {"xin": np.ascontiguousarray(x[i].T)}
        m.update(hp)
        in_maps.append(m)
    res = run_bass_kernel_spmd(nc, in_maps, core_ids=list(range(NCORES)))
    out = np.stack([np.asarray(res.results[i]["xout"]).T for i in range(NCORES)])
    return out.astype(np.float32)


# version 21
# speedup vs baseline: 1.0140x; 1.0140x over previous
import numpy as np
import ml_dtypes
from contextlib import ExitStack

import concourse.bass as bass
import concourse.mybir as mybir
from concourse.bass_utils import run_bass_kernel_spmd

F32 = mybir.dt.float32
BF16 = mybir.dt.bfloat16
I32 = mybir.dt.int32
AF = mybir.ActivationFunctionType
ALU = mybir.AluOpType
AX = mybir.AxisListType
BF16NP = ml_dtypes.bfloat16

D = 1024
T = 2048
NCORES = 8
NT = D // 128
LN_EPS = 1e-5


ENGINES = ("sync", "tensor", "vector", "scalar", "gpsimd")


class Prog:
    def __init__(self, nc):
        self.nc = nc
        self.stack = ExitStack()
        self._alloc_stack = None
        self.sems = {}
        self.cnt = {}
        self.queue = {e: [] for e in ENGINES}
        self.last_w = {}
        self.readers = {}
        self.waited = {e: {} for e in ENGINES}
        self.bufs = {}

    def sem(self, name):
        if name not in self.sems:
            self.sems[name] = self.stack.enter_context(self.nc.semaphore(name))
            self.cnt[name] = 0
        return self.sems[name]

    def sbuf(self, name, shape, dtype):
        st = self._alloc_stack if self._alloc_stack is not None else self.stack
        t = st.enter_context(self.nc.sbuf_tensor(name, list(shape), dtype))
        self.bufs[name] = t
        return t

    def scope(self):
        from contextlib import contextmanager

        @contextmanager
        def _cm():
            st = ExitStack()
            old = self._alloc_stack
            self._alloc_stack = st
            try:
                yield
            finally:
                self._alloc_stack = old
                st.close()
        return _cm()

    def barrier(self):
        targets = {sname: v for sname, v in self.cnt.items() if v > 0}
        for eng in ENGINES:
            w = {}
            for sname, v in targets.items():
                if sname == eng:
                    continue
                if self.waited[eng].get(sname, -1) >= v:
                    continue
                w[sname] = v
            if w:
                self.queue[eng].append((w, None, None))
                for sname, v in w.items():
                    self.waited[eng][sname] = max(self.waited[eng].get(sname, -1), v)
        self.last_w.clear()
        self.readers.clear()

    def psum(self, name, shape, dtype=F32):
        t = self.stack.enter_context(self.nc.psum_tensor(name, list(shape), dtype))
        self.bufs[name] = t
        return t

    def make_pool(self):
        self.pspool = self.psum("pspool", [128, 8, 512], F32)

    def psb(self, bank):
        return self.pspool[:, bank, :]

    def psbk(self, b0, n=1):
        return [f"psb{i}" for i in range(b0, b0 + n)]

    def psw(self, b0, n):
        return self.pspool[:, b0:b0 + n, :].rearrange("p a b -> p (a b)")

    def _waits_for(self, eng, reads, writes):
        need = {}
        def add(swv):
            s, v = swv
            if need.get(s, 0) < v:
                need[s] = v
        for k in reads:
            if k in self.last_w:
                add(self.last_w[k])
        for k in writes:
            if k in self.last_w:
                add(self.last_w[k])
            for swv in self.readers.get(k, ()):
                add(swv)
        out = {}
        for s, v in need.items():
            if s == eng and v > self.cnt.get(eng, 0):
                continue
            if self.waited[eng].get(s, -1) >= v:
                continue
            out[s] = v
        return out

    def _record(self, eng_sem, val, reads, writes):
        for k in writes:
            self.last_w[k] = (eng_sem, val)
            self.readers[k] = []
        for k in reads:
            self.readers.setdefault(k, []).append((eng_sem, val))

    def op(self, eng, fn, reads=(), writes=(), inc=True):
        self.sem(eng)
        waits = self._waits_for(eng, reads, writes)
        for s, v in waits.items():
            self.waited[eng][s] = max(self.waited[eng].get(s, -1), v)
        if inc:
            self.cnt[eng] += 1
            incspec = (eng, 1)
            val = self.cnt[eng]
        else:
            incspec = None
            val = self.cnt[eng] + 1
        self._record(eng, val, reads, writes)
        self.queue[eng].append((dict(waits), fn, incspec))

    def dma(self, qeng, chan, fn, reads=(), writes=()):
        chan = "dq_" + chan
        self.sem(chan)
        self.sem(qeng)
        waits = self._waits_for(qeng, reads, writes)
        for s, v in waits.items():
            self.waited[qeng][s] = max(self.waited[qeng].get(s, -1), v)
        self.cnt[chan] += 16
        self._record(chan, self.cnt[chan], reads, writes)
        self.queue[qeng].append((dict(waits), fn, (chan, 16)))

    def final_wait(self, qeng, chans):
        waits = {}
        for c in chans:
            c = "dq_" + c
            if self.cnt.get(c, 0) > 0:
                waits[c] = self.cnt[c]
        self.queue[qeng].append((waits, None, None))

    def build(self):
        block = self.stack.enter_context(self.nc.Block())
        for eng_name in ENGINES:
            items = self.queue[eng_name]
            if not items:
                continue
            def make_body(items):
                def body(engine):
                    for waits, fn, incspec in items:
                        for s, v in waits.items():
                            engine.wait_ge(self.sems[s], v)
                        if fn is None:
                            continue
                        inst = fn(engine)
                        if incspec is not None:
                            s, by = incspec
                            inst.then_inc(self.sems[s], by)
                return body
            getattr(block, eng_name)(make_body(items))

    def close(self):
        self.stack.close()


class Common:

    def __init__(self, P: Prog):
        self.P = P
        self.ones_f = P.sbuf("c_ones_f", [128, 1], F32)
        self.ones_row = P.sbuf("c_ones_row", [1, 128], F32)
        self.eps_row = P.sbuf("c_eps_row", [1, 1], F32)
        P.op("vector", lambda v: v.memset(self.ones_f[:], 1.0), writes=["c_ones_f"])
        P.op("vector", lambda v: v.memset(self.ones_row[:], 1.0), writes=["c_ones_row"])
        P.op("vector", lambda v: v.memset(self.eps_row[:], LN_EPS), writes=["c_eps_row"])



def alloc_ln(P: Prog, tag):
    ln = {}
    ln["rows"] = P.sbuf(f"ln_rows_{tag}", [1, 3, 512], F32)
    ln["ps_a"] = P.pspool[0:1, 2, :]
    ln["ps_b"] = P.pspool[0:1, 3, :]
    ln["bc0"] = P.psb(4)
    ln["bc1"] = P.psb(5)
    ln["k"] = tag
    return ln


def emit_ln_stats(P, C, ln, src, src_key, sq, sq_key, ntile):
    k = ln["k"]
    rows, rk = ln["rows"], f"ln_rows_{k}"
    P.op("scalar", lambda e: e.activation(out=sq[:], in_=src[:], func=AF.Square),
         reads=[src_key], writes=[sq_key])
    for g in range(ntile):
        P.op("tensor", lambda e, g=g: e.matmul(
            out=ln["ps_a"], lhsT=C.ones_f[:], rhs=src[:, g, :],
            start=(g == 0), stop=(g == ntile - 1)),
            reads=[src_key, "c_ones_f"], writes=["psb2"], inc=(g == ntile - 1))
    for g in range(ntile):
        P.op("tensor", lambda e, g=g: e.matmul(
            out=ln["ps_b"], lhsT=C.ones_f[:], rhs=sq[:, g, :],
            start=(g == 0), stop=(g == ntile - 1)),
            reads=[sq_key, "c_ones_f"], writes=["psb3"], inc=(g == ntile - 1))
    nfeat = float(ntile * 128)
    P.op("vector", lambda e: e.tensor_scalar(
        out=rows[0:1, 0, :], in0=ln["ps_a"], scalar1=1.0 / nfeat, scalar2=None,
        op0=ALU.mult), reads=["psb2"], writes=[rk])
    P.op("vector", lambda e: e.tensor_scalar(
        out=rows[0:1, 1, :], in0=ln["ps_b"], scalar1=1.0 / nfeat, scalar2=None,
        op0=ALU.mult), reads=["psb3"], writes=[rk])
    P.op("vector", lambda e: e.tensor_tensor(
        out=rows[0:1, 2, :], in0=rows[0:1, 0, :], in1=rows[0:1, 0, :], op=ALU.mult),
        reads=[rk], writes=[rk])
    P.op("vector", lambda e: e.tensor_sub(
        out=rows[0:1, 1, :], in0=rows[0:1, 1, :], in1=rows[0:1, 2, :]),
        reads=[rk], writes=[rk])
    P.op("scalar", lambda e: e.activation(
        out=rows[0:1, 1, :], in_=rows[0:1, 1, :], func=AF.Sqrt, bias=C.eps_row[:]),
        reads=[rk, "c_eps_row"], writes=[rk])
    P.op("vector", lambda e: e.reciprocal(
        out=rows[0:1, 1, :], in_=rows[0:1, 1, :]), reads=[rk], writes=[rk])
    P.op("vector", lambda e: e.tensor_mul(
        out=rows[0:1, 2, :], in0=rows[0:1, 0, :], in1=rows[0:1, 1, :]),
        reads=[rk], writes=[rk])
    P.op("tensor", lambda e: e.matmul(
        out=ln["bc0"], lhsT=C.ones_row[:], rhs=rows[0:1, 1, :],
        start=True, stop=True), reads=[rk, "c_ones_row"], writes=["psb4"])
    P.op("tensor", lambda e: e.matmul(
        out=ln["bc1"], lhsT=C.ones_row[:], rhs=rows[0:1, 2, :],
        start=True, stop=True), reads=[rk, "c_ones_row"],
        writes=["psb5"], inc=True)


def emit_ln_apply(P, C, ln, src, src_key, dst, dst_key, tmp, tmp_key, ntile,
                  gamma_col=None, beta_col=None, gb_keys=(), add_to=None,
                  add_key=None):
    k = ln["k"]
    for g in range(ntile):
        P.op("vector", lambda e, g=g: e.tensor_mul(
            out=tmp[:, g, :], in0=src[:, g, :], in1=ln["bc0"]),
            reads=[src_key, "psb4"], writes=[tmp_key])
    if gamma_col is None and add_to is None:
        for g in range(ntile):
            P.op("vector", lambda e, g=g: e.tensor_sub(
                out=dst[:, g, :], in0=tmp[:, g, :], in1=ln["bc1"]),
                reads=[tmp_key, "psb5"], writes=[dst_key])
    elif gamma_col is not None:
        for g in range(ntile):
            P.op("vector", lambda e, g=g: e.tensor_sub(
                out=tmp[:, g, :], in0=tmp[:, g, :], in1=ln["bc1"]),
                reads=[tmp_key, "psb5"], writes=[tmp_key])
        for g in range(ntile):
            P.op("vector", lambda e, g=g: e.tensor_scalar(
                out=dst[:, g, :], in0=tmp[:, g, :],
                scalar1=gamma_col[:, g:g + 1], scalar2=beta_col[:, g:g + 1],
                op0=ALU.mult, op1=ALU.add),
                reads=[tmp_key] + list(gb_keys), writes=[dst_key])
    else:
        for g in range(ntile):
            P.op("vector", lambda e, g=g: e.tensor_sub(
                out=tmp[:, g, :], in0=tmp[:, g, :], in1=ln["bc1"]),
                reads=[tmp_key, "psb5"], writes=[tmp_key])
        for g in range(ntile):
            P.op("vector", lambda e, g=g: e.tensor_add(
                out=dst[:, g, :], in0=tmp[:, g, :], in1=add_to[:, g, :]),
                reads=[tmp_key, add_key], writes=[dst_key])


def load_x_chunk(P, xsrc_ap, xsrc_key, xc, xc_key, c0, TC):
    P.dma("sync", xc_key,
          lambda e: e.dma_start(
              out=xc[:], in_=xsrc_ap[:, c0:c0 + TC].rearrange("(g p) t -> p g t", p=128)),
          reads=[xsrc_key], writes=[xc_key])


def store_x_chunk(P, xdst_ap, xdst_key, xc, xc_key, c0, TC, chan):
    P.dma("sync", chan,
          lambda e: e.dma_start(
              out=xdst_ap[:, c0:c0 + TC].rearrange("(g p) t -> p g t", p=128), in_=xc[:]),
          reads=[xc_key], writes=[xdst_key])



def build_l1ffn(P: Prog, C: Common, din, dout, xin_key="dram_xin",
                xout_key="dram_xout"):
    NH = 32
    TC = 512

    s1c = P.sbuf("f_s1c", [128, NH], F32)
    s2c = P.sbuf("f_s2c", [128, NH], F32)
    sh1c = P.sbuf("f_sh1c", [128, NH], F32)
    sh2c = P.sbuf("f_sh2c", [128, NH], F32)
    s3c = P.sbuf("f_s3c", [128, NT], F32)
    for nm in ("s1c", "s2c", "sh1c", "sh2c", "s3c"):
        P.dma("sync", "f_" + nm,
              lambda e, nm=nm: e.dma_start(out=P.bufs["f_" + nm][:], in_=din[nm][:]),
              reads=[("dram", nm)], writes=["f_" + nm])

    xc = P.sbuf("f_xc", [128, NT, TC], F32)
    tmp = P.sbuf("f_tmp", [128, NT, TC], F32)
    xhat = P.sbuf("f_xhat", [128, NT, TC], BF16)
    t1 = P.sbuf("f_t1", [128, NH, TC], BF16)
    t2 = P.sbuf("f_t2", [128, NH, TC], BF16)
    y = P.sbuf("f_y", [128, NT, TC], F32)
    silu_t = P.sbuf("f_silu", [128, TC], F32)
    wbuf = [P.sbuf(f"f_wbuf{i}", [128, 32, 128], BF16) for i in range(2)]
    ps_mm = [P.psb(0), P.psb(1)]
    ln = alloc_ln(P, "f")

    for c0 in range(0, T, TC):
        load_x_chunk(P, din["xin"], xin_key, xc, "f_xc", c0, TC)
        emit_ln_stats(P, C, ln, xc, "f_xc", tmp, "f_tmp", NT)
        emit_ln_apply(P, C, ln, xc, "f_xc", xhat, "f_xhat", tmp, "f_tmp", NT)

        for ot in range(NH):
            wb, wbk = wbuf[ot % 2], f"f_wbuf{ot % 2}"
            P.dma("gpsimd", wbk + "a",
                  lambda e, ot=ot, wb=wb: e.dma_start(
                      out=wb[:, 0:8, :], in_=din["w1Tt"][ot]),
                  reads=[("dram", "w1Tt")], writes=[wbk + "a"])
            P.dma("gpsimd", wbk + "b",
                  lambda e, ot=ot, wb=wb: e.dma_start(
                      out=wb[:, 8:16, :], in_=din["w2Tt"][ot]),
                  reads=[("dram", "w2Tt")], writes=[wbk + "b"])
            for g in range(NT):
                P.op("tensor", lambda e, g=g, wb=wb: e.matmul(
                    out=ps_mm[0], lhsT=wb[:, g, :], rhs=xhat[:, g, :],
                    start=(g == 0), stop=(g == NT - 1)),
                    reads=[wbk + "a", "f_xhat"], writes=["psb0"], inc=(g == NT - 1))
            for g in range(NT):
                P.op("tensor", lambda e, g=g, wb=wb: e.matmul(
                    out=ps_mm[1], lhsT=wb[:, 8 + g, :], rhs=xhat[:, g, :],
                    start=(g == 0), stop=(g == NT - 1)),
                    reads=[wbk + "b", "f_xhat"], writes=["psb1"], inc=(g == NT - 1))
            P.op("scalar", lambda e, ot=ot: e.activation(
                out=silu_t[:], in_=ps_mm[0], func=AF.Silu, scale=s1c[:, ot:ot + 1]),
                reads=["psb0", "f_s1c"], writes=["f_silu"])
            P.op("vector", lambda e, ot=ot: e.scalar_tensor_tensor(
                out=t1[:, ot, :], in0=ps_mm[1], scalar=s2c[:, ot:ot + 1],
                in1=silu_t[:], op0=ALU.mult, op1=ALU.mult),
                reads=["psb1", "f_s2c", "f_silu"], writes=["f_t1"])

        for ot in range(NH):
            wb, wbk = wbuf[ot % 2], f"f_wbuf{ot % 2}"
            bk_ = ot % 2
            P.dma("gpsimd", wbk + "a",
                  lambda e, ot=ot, wb=wb: e.dma_start(
                      out=wb[:], in_=din["h1Tt"][ot]),
                  reads=[("dram", "h1Tt")], writes=[wbk + "a", wbk + "b"])
            for g in range(NH):
                P.op("tensor", lambda e, g=g, wb=wb, bk_=bk_: e.matmul(
                    out=ps_mm[bk_], lhsT=wb[:, g, :], rhs=t1[:, g, :],
                    start=(g == 0), stop=(g == NH - 1)),
                    reads=[wbk + "a", wbk + "b", "f_t1"], writes=[f"psb{bk_}"],
                    inc=(g == NH - 1))
            P.op("scalar", lambda e, ot=ot, bk_=bk_: e.activation(
                out=t2[:, ot, :], in_=ps_mm[bk_], func=AF.Gelu,
                scale=sh1c[:, ot:ot + 1]),
                reads=[f"psb{bk_}", "f_sh1c"], writes=["f_t2"])

        for ot in range(NH):
            wb, wbk = wbuf[ot % 2], f"f_wbuf{ot % 2}"
            bk_ = ot % 2
            P.dma("gpsimd", wbk + "a",
                  lambda e, ot=ot, wb=wb: e.dma_start(
                      out=wb[:], in_=din["h2Tt"][ot]),
                  reads=[("dram", "h2Tt")], writes=[wbk + "a", wbk + "b"])
            for g in range(NH):
                P.op("tensor", lambda e, g=g, wb=wb, bk_=bk_: e.matmul(
                    out=ps_mm[bk_], lhsT=wb[:, g, :], rhs=t2[:, g, :],
                    start=(g == 0), stop=(g == NH - 1)),
                    reads=[wbk + "a", wbk + "b", "f_t2"], writes=[f"psb{bk_}"],
                    inc=(g == NH - 1))
            P.op("scalar", lambda e, ot=ot, bk_=bk_: e.activation(
                out=t1[:, ot, :], in_=ps_mm[bk_], func=AF.Gelu,
                scale=sh2c[:, ot:ot + 1]),
                reads=[f"psb{bk_}", "f_sh2c"], writes=["f_t1"])

        for ot in range(NT):
            wb, wbk = wbuf[ot % 2], f"f_wbuf{ot % 2}"
            bk_ = ot % 2
            P.dma("gpsimd", wbk + "a",
                  lambda e, ot=ot, wb=wb: e.dma_start(
                      out=wb[:], in_=din["w3Tt"][ot]),
                  reads=[("dram", "w3Tt")], writes=[wbk + "a", wbk + "b"])
            for g in range(NH):
                P.op("tensor", lambda e, g=g, wb=wb, bk_=bk_: e.matmul(
                    out=ps_mm[bk_], lhsT=wb[:, g, :], rhs=t1[:, g, :],
                    start=(g == 0), stop=(g == NH - 1)),
                    reads=[wbk + "a", wbk + "b", "f_t1"], writes=[f"psb{bk_}"],
                    inc=(g == NH - 1))
            P.op("vector", lambda e, ot=ot, bk_=bk_: e.tensor_scalar(
                out=y[:, ot, :], in0=ps_mm[bk_], scalar1=s3c[:, ot:ot + 1],
                scalar2=None, op0=ALU.mult),
                reads=[f"psb{bk_}", "f_s3c"], writes=["f_y"])

        emit_ln_stats(P, C, ln, y, "f_y", tmp, "f_tmp", NT)
        emit_ln_apply(P, C, ln, y, "f_y", xc, "f_xc", tmp, "f_tmp", NT,
                      add_to=xc, add_key="f_xc")
        store_x_chunk(P, dout["xout"], xout_key, xc, "f_xc", c0, TC, "st_xout")


def _tile_lhs(wT):
    K, M = wT.shape
    return np.ascontiguousarray(
        wT.reshape(K // 128, 128, M // 128, 128).transpose(2, 1, 0, 3))


def host_prep_l1ffn(inputs):
    f32, c = np.float32, np.ascontiguousarray
    d = {}
    d["w1Tt"] = _tile_lhs(inputs["L1_ffn_w1_w"].T.astype(f32))
    d["w2Tt"] = _tile_lhs(inputs["L1_ffn_w2_w"].T.astype(f32))
    d["h1Tt"] = _tile_lhs(inputs["L1_ffn_h1_w"].T.astype(f32))
    d["h2Tt"] = _tile_lhs(inputs["L1_ffn_h2_w"].T.astype(f32))
    d["w3Tt"] = _tile_lhs(inputs["L1_ffn_w3_w"].T.astype(f32))
    d["s1c"] = c(np.asarray(inputs["L1_ffn_w1_s"], f32).reshape(32, 128).T)
    d["s2c"] = c(np.asarray(inputs["L1_ffn_w2_s"], f32).reshape(32, 128).T)
    d["sh1c"] = c(np.asarray(inputs["L1_ffn_h1_s"], f32).reshape(32, 128).T)
    d["sh2c"] = c(np.asarray(inputs["L1_ffn_h2_s"], f32).reshape(32, 128).T)
    d["s3c"] = c(np.asarray(inputs["L1_ffn_w3_s"], f32).reshape(8, 128).T)
    return d


L1FFN_SHAPES = {
    "w1Tt": [32, 128, 8, 128], "w2Tt": [32, 128, 8, 128],
    "h1Tt": [32, 128, 32, 128], "h2Tt": [32, 128, 32, 128],
    "w3Tt": [8, 128, 32, 128], "s1c": [128, 32],
    "s2c": [128, 32], "sh1c": [128, 32], "sh2c": [128, 32], "s3c": [128, 8],
}


def build_l1ffn_program():
    nc = bass.Bass("TRN2", target_bir_lowering=False)
    din = {"xin": nc.dram_tensor("xin", [D, T], F32, kind="ExternalInput").ap()}
    for nm, shp in L1FFN_SHAPES.items():
        din[nm] = nc.dram_tensor(nm, shp, F32, kind="ExternalInput").ap()
    dout = {"xout": nc.dram_tensor("xout", [D, T], F32, kind="ExternalOutput").ap()}
    P = Prog(nc)
    P.make_pool()
    C = Common(P)
    build_l1ffn(P, C, din, dout)
    P.final_wait("sync", ["st_xout"])
    P.build()
    return nc





def build_l0ffn(P: Prog, C: Common, din, dout, xin_key="dram_xin",
                xout_key="dram_xout"):
    TC = 512

    xc = P.sbuf("q_xc", [128, NT, TC], F32)
    tmp = P.sbuf("q_tmp", [128, NT, TC], F32)
    hbf = P.sbuf("q_hbf", [128, NT, T], BF16)
    y1bf = P.sbuf("q_y1bf", [128, NT, T], BF16)
    yc = P.sbuf("q_yc", [128, NT, TC], F32)
    qraw = P.sbuf("q_qraw", [128, 1024], I32)
    sbc = P.sbuf("q_sbc", [128, 1024], F32)
    wbf = [P.sbuf(f"q_wbf{i}", [128, NT, 1024], BF16) for i in range(2)]
    u = P.sbuf("q_u", [32, T], BF16)
    lat = P.sbuf("q_lat", [128, NT, 32], BF16)
    lbt = P.sbuf("q_lbt", [32, NT, 128], BF16)
    bcol = P.sbuf("q_bcol", [128, 2, NT], F32)
    gcol = P.sbuf("q_gcol", [128, NT], F32)
    becol = P.sbuf("q_becol", [128, NT], F32)
    ps_u = P.pspool[0:32, 6, :]
    ps_mm = [P.psb(0), P.psb(1)]
    ln = alloc_ln(P, "q")

    for ci, c0 in enumerate(range(0, T, TC)):
        load_x_chunk(P, din["xin"], xin_key, xc, "q_xc", c0, TC)
        emit_ln_stats(P, C, ln, xc, "q_xc", tmp, "q_tmp", NT)
        emit_ln_apply(P, C, ln, xc, "q_xc", yc, "q_yc", tmp, "q_tmp", NT)
        for g in range(NT):
            P.op("vector", lambda e, g=g, c0=c0: e.tensor_copy(
                out=hbf[:, g, c0:c0 + TC], in_=yc[:, g, :]),
                reads=["q_yc"], writes=["q_hbf"])
        store_x_chunk(P, din["hdram"], ("dram", "hdram"), yc, "q_yc", c0, TC, "st_h")

    def dequant(layer, wslot):
        wk = f"q_wbf{wslot}"
        for c in range(NT):
            P.dma("sync", "q_qraw",
                  lambda e, layer=layer, c=c: e.dma_start(
                      out=qraw[:], in_=din["qT"][layer, c * 128:(c + 1) * 128, :]),
                  reads=[("dram", "qT")], writes=["q_qraw"])
            P.dma("sync", "q_sbc",
                  lambda e, layer=layer, c=c: e.dma_start(
                      out=sbc[:],
                      in_=din["sT"][layer, c * 8:(c + 1) * 8, :][:, None, :]
                          .broadcast_to([8, 16, 1024])),
                  reads=[("dram", "sT")], writes=["q_sbc"])
            P.op("vector", lambda e, c=c: e.tensor_scalar(
                out=tmp[:, 0:2, :].rearrange("p a b -> p (a b)"), in0=qraw[:],
                scalar1=1.0 / 7.5, scalar2=-1.0,
                op0=ALU.mult, op1=ALU.add),
                reads=["q_qraw"], writes=["q_tmp"])
            P.op("vector", lambda e, c=c, wslot=wslot: e.tensor_mul(
                out=wbf[wslot][:, c, :], in0=tmp[:, 0:2, :].rearrange("p a b -> p (a b)"),
                in1=sbc[:]),
                reads=["q_tmp", "q_sbc"], writes=[wk])

    def load_lora(layer):
        P.dma("gpsimd", "q_lat",
              lambda e, layer=layer: e.dma_start(
                  out=lat[:], in_=din["laTt"][layer]),
              reads=[("dram", "laTt")], writes=["q_lat"])
        P.dma("gpsimd", "q_lbt",
              lambda e, layer=layer: e.dma_start(
                  out=lbt[:], in_=din["lbT"][layer].rearrange("r (c p) -> r c p", p=128)),
              reads=[("dram", "lbT")], writes=["q_lbt"])

    def lora_mid(rhs, rhs_key):
        for sl in range(T // 512):
            for g in range(NT):
                P.op("tensor", lambda e, sl=sl, g=g: e.matmul(
                    out=ps_u, lhsT=lat[:, g, :], rhs=rhs[:, g, sl * 512:(sl + 1) * 512],
                    start=(g == 0), stop=(g == NT - 1)),
                    reads=["q_lat", rhs_key], writes=["psb6"], inc=(g == NT - 1))
            P.op("scalar", lambda e, sl=sl: e.copy(
                out=u[:, sl * 512:(sl + 1) * 512], in_=ps_u),
                reads=["psb6"], writes=["q_u"])

    def qlora_mm(wslot, rhs, rhs_key, c0, bslot):
        wk = f"q_wbf{wslot}"
        for sl in range(TC // 512):
            s0 = c0 + sl * 512
            for ot in range(NT):
                pm = ps_mm[ot % 2]
                pmk = f"psb{ot % 2}"
                for g in range(NT):
                    P.op("tensor", lambda e, g=g, ot=ot, s0=s0, pm=pm, wslot=wslot: e.matmul(
                        out=pm, lhsT=wbf[wslot][:, g, ot * 128:(ot + 1) * 128],
                        rhs=rhs[:, g, s0:s0 + 512],
                        start=(g == 0), stop=False),
                        reads=[wk, rhs_key], writes=[pmk], inc=False)
                P.op("tensor", lambda e, ot=ot, s0=s0, pm=pm: e.matmul(
                    out=pm, lhsT=lbt[:, ot, :], rhs=u[:, s0:s0 + 512],
                    start=False, stop=True),
                    reads=["q_lbt", "q_u", pmk], writes=[pmk], inc=True)
                P.op("vector", lambda e, ot=ot, sl=sl, pm=pm, bslot=bslot: e.tensor_scalar(
                    out=yc[:, ot, sl * 512:(sl + 1) * 512], in0=pm,
                    scalar1=bcol[:, bslot, ot:ot + 1], scalar2=None, op0=ALU.add),
                    reads=[pmk, "q_bcol"], writes=["q_yc"])

    for blk in range(3):
        l0, l1_ = 2 * blk, 2 * blk + 1
        dequant(l0, 0)
        load_lora(l0)
        P.dma("gpsimd", "q_bcol",
              lambda e, l0=l0: e.dma_start(
                  out=bcol[:], in_=din["bcol"][l0:l0 + 2].rearrange("a p c -> p a c")),
              reads=[("dram", "bcol")], writes=["q_bcol"])
        P.dma("gpsimd", "q_gcol",
              lambda e, blk=blk: e.dma_start(out=gcol[:], in_=din["gcol"][blk]),
              reads=[("dram", "gcol")], writes=["q_gcol"])
        P.dma("gpsimd", "q_becol",
              lambda e, blk=blk: e.dma_start(out=becol[:], in_=din["becol"][blk]),
              reads=[("dram", "becol")], writes=["q_becol"])
        lora_mid(hbf, "q_hbf")
        for c0 in range(0, T, TC):
            qlora_mm(0, hbf, "q_hbf", c0, 0)
            emit_ln_stats(P, C, ln, yc, "q_yc", tmp, "q_tmp", NT)
            emit_ln_apply(P, C, ln, yc, "q_yc", yc, "q_yc", tmp, "q_tmp", NT,
                          gamma_col=gcol, beta_col=becol,
                          gb_keys=["q_gcol", "q_becol"])
            for g in range(NT):
                P.op("scalar", lambda e, g=g, c0=c0: e.activation(
                    out=y1bf[:, g, c0:c0 + TC], in_=yc[:, g, :], func=AF.Gelu),
                    reads=["q_yc"], writes=["q_y1bf"])
        dequant(l1_, 1)
        load_lora(l1_)
        lora_mid(y1bf, "q_y1bf")
        last = blk == 2
        for c0 in range(0, T, TC):
            qlora_mm(1, y1bf, "q_y1bf", c0, 1)
            load_x_chunk(P, din["hdram"], ("dram", "hdram"), xc, "q_xc", c0, TC)
            for g in range(NT):
                P.op("vector", lambda e, g=g: e.tensor_add(
                    out=yc[:, g, :], in0=yc[:, g, :], in1=xc[:, g, :]),
                    reads=["q_yc", "q_xc"], writes=["q_yc"])
            if not last:
                for g in range(NT):
                    P.op("vector", lambda e, g=g, c0=c0: e.tensor_copy(
                        out=hbf[:, g, c0:c0 + TC], in_=yc[:, g, :]),
                        reads=["q_yc"], writes=["q_hbf"])
                store_x_chunk(P, din["hdram"], ("dram", "hdram"), yc, "q_yc", c0,
                              TC, "st_h")
            else:
                emit_ln_stats(P, C, ln, yc, "q_yc", tmp, "q_tmp", NT)
                load_x_chunk(P, din["xin"], xin_key, xc, "q_xc", c0, TC)
                emit_ln_apply(P, C, ln, yc, "q_yc", xc, "q_xc", tmp, "q_tmp", NT,
                              add_to=xc, add_key="q_xc")
                store_x_chunk(P, dout["xout"], xout_key, xc, "q_xc", c0, TC,
                              "st_xout")


def host_prep_l0ffn(inputs):
    f32, c = np.float32, np.ascontiguousarray
    d = {}
    d["qT"] = c(np.asarray(inputs["L0_ffn_qweights"], np.int32).transpose(0, 2, 1))
    d["sT"] = c(np.asarray(inputs["L0_ffn_scales"], f32).transpose(0, 2, 1))
    d["bcol"] = c(np.asarray(inputs["L0_ffn_biases"], f32).reshape(6, 8, 128)
                  .transpose(0, 2, 1))
    laT = np.asarray(inputs["L0_ffn_lora_a"], f32).transpose(0, 2, 1)
    d["laTt"] = c(laT.reshape(6, 8, 128, 32).transpose(0, 2, 1, 3))
    d["lbT"] = c(np.asarray(inputs["L0_ffn_lora_b"], f32).transpose(0, 2, 1))
    d["gcol"] = c(np.asarray(inputs["L0_ffn_ln_gamma"], f32).reshape(3, 8, 128)
                  .transpose(0, 2, 1))
    d["becol"] = c(np.asarray(inputs["L0_ffn_ln_beta"], f32).reshape(3, 8, 128)
                   .transpose(0, 2, 1))
    return d


L0FFN_SHAPES = {
    "qT": ([6, 1024, 1024], I32), "sT": ([6, 64, 1024], F32),
    "bcol": ([6, 128, 8], F32), "laTt": ([6, 128, 8, 32], F32),
    "lbT": ([6, 32, 1024], F32), "gcol": ([3, 128, 8], F32),
    "becol": ([3, 128, 8], F32),
}


def build_l0ffn_program():
    nc = bass.Bass("TRN2", target_bir_lowering=False)
    din = {"xin": nc.dram_tensor("xin", [D, T], F32, kind="ExternalInput").ap()}
    for nm, (shp, dt) in L0FFN_SHAPES.items():
        din[nm] = nc.dram_tensor(nm, shp, dt, kind="ExternalInput").ap()
    din["hdram"] = nc.dram_tensor("hdram", [D, T], F32, kind="Internal").ap()
    dout = {"xout": nc.dram_tensor("xout", [D, T], F32, kind="ExternalOutput").ap()}
    P = Prog(nc)
    P.make_pool()
    C = Common(P)
    build_l0ffn(P, C, din, dout)
    P.final_wait("sync", ["st_xout"])
    P.build()
    return nc





def build_l1attn(P: Prog, C: Common, din, dout, xin_key="dram_xin",
                 xout_key="dram_xout"):
    TC = 512
    NIC = T // 128
    W = 2176

    xc = P.sbuf("a_xc", [128, NT, TC], F32)
    tmp = P.sbuf("a_tmp", [128, NT, TC], F32)
    xhat = P.sbuf("a_xhat", [128, NT, T], BF16)
    wo_sb = P.sbuf("a_wo", [128, NT, 1024], BF16)
    wp3 = P.sbuf("a_wp3", [128, NT, 3, 128], BF16)
    qp = P.sbuf("a_qp", [128, T], BF16)
    kp = P.sbuf("a_kp", [128, T], BF16)
    vp = P.sbuf("a_vp", [128, NIC, 128], BF16)
    tab2 = P.sbuf("a_tab2", [128, 4096], BF16)
    relw = P.sbuf("a_relw", [128, 2, W], BF16)
    rpe = P.sbuf("a_rpe", [128, 2, T], BF16)
    praw = P.sbuf("a_praw", [128, 2, T], BF16)
    ptb4 = P.sbuf("a_ptb4", [128, NIC, 512], BF16)
    mrow = P.sbuf("a_mrow", [128, 2, 8], F32)
    ctx = P.sbuf("a_ctx", [128, NT, T], BF16)
    yc = P.sbuf("a_yc", [128, NT, TC], F32)
    sfull = [yc[:, 0:4, :].rearrange("p a b -> p (a b)"),
             yc[:, 4:8, :].rearrange("p a b -> p (a b)")]
    bqc = P.sbuf("a_bqc", [128, 8], F32)
    bkc = P.sbuf("a_bkc", [128, 8], F32)
    boc = P.sbuf("a_boc", [128, 8], F32)
    bvb = P.sbuf("a_bvb", [128, 1024], F32)
    identbf = P.sbuf("a_ident", [128, 128], BF16)
    ln = alloc_ln(P, "a")

    for nm, buf in (("bqc", bqc), ("bkc", bkc), ("boc", boc)):
        P.dma("sync", "a_" + nm,
              lambda e, nm=nm, buf=buf: e.dma_start(out=buf[:], in_=din[nm][:]),
              reads=[("dram", nm)], writes=["a_" + nm])
    P.dma("sync", "a_bvb",
          lambda e: e.dma_start(out=bvb[:], in_=din["bvrow"][:].broadcast_to([128, 1024])),
          reads=[("dram", "bvrow")], writes=["a_bvb"])
    P.dma("sync", "a_ident",
          lambda e: e.dma_start(out=identbf[:], in_=din["identbf"][:]),
          reads=[("dram", "identbf")], writes=["a_ident"])
    P.dma("gpsimd", "a_wo",
          lambda e: e.dma_start(out=wo_sb[:], in_=din["woTt"][:]),
          reads=[("dram", "woTt")], writes=["a_wo"])
    P.op("vector", lambda e: e.memset(tab2[:], 0.0), writes=["a_tab2"])

    for c0 in range(0, T, TC):
        load_x_chunk(P, din["xin"], xin_key, xc, "a_xc", c0, TC)
        emit_ln_stats(P, C, ln, xc, "a_xc", tmp, "a_tmp", NT)
        emit_ln_apply(P, C, ln, xc, "a_xc",
                      xhat[:, :, c0:c0 + TC], "a_xhat", tmp, "a_tmp", NT)

    for hp in range(8):
        for wi, wnm in enumerate(("wqTt", "wkTt", "wvTt")):
            P.dma("gpsimd", "a_wp3",
                  lambda e, wi=wi, wnm=wnm, hp=hp: e.dma_start(
                      out=wp3[:, :, wi, :], in_=din[wnm][hp]),
                  reads=[("dram", wnm)], writes=["a_wp3"])
        for hh in range(2):
            h = hp * 2 + hh
            P.dma("gpsimd", "a_tab2",
                  lambda e, h=h, hh=hh: e.dma_start(
                      out=tab2[64 * hh:64 * hh + 64, 0:4095], in_=din["tabTrev"][h]),
                  reads=[("dram", "tabTrev")], writes=["a_tab2"])
        for ts in range(4):
            for g in range(NT):
                P.op("tensor", lambda e, g=g, ts=ts: e.matmul(
                    out=P.psb(0), lhsT=wp3[:, g, 0, :], rhs=xhat[:, g, ts * 512:(ts + 1) * 512],
                    start=(g == 0), stop=(g == NT - 1)),
                    reads=["a_wp3", "a_xhat"], writes=["psb0"], inc=(g == NT - 1))
            P.op("vector", lambda e, ts=ts, hp=hp: e.tensor_scalar(
                out=qp[:, ts * 512:(ts + 1) * 512], in0=P.psb(0),
                scalar1=bqc[:, hp:hp + 1], scalar2=None, op0=ALU.add),
                reads=["psb0", "a_bqc"], writes=["a_qp"])
            for g in range(NT):
                P.op("tensor", lambda e, g=g, ts=ts: e.matmul(
                    out=P.psb(1), lhsT=wp3[:, g, 1, :], rhs=xhat[:, g, ts * 512:(ts + 1) * 512],
                    start=(g == 0), stop=(g == NT - 1)),
                    reads=["a_wp3", "a_xhat"], writes=["psb1"], inc=(g == NT - 1))
            P.op("vector", lambda e, ts=ts, hp=hp: e.tensor_scalar(
                out=kp[:, ts * 512:(ts + 1) * 512], in0=P.psb(1),
                scalar1=bkc[:, hp:hp + 1], scalar2=0.125, op0=ALU.add, op1=ALU.mult),
                reads=["psb1", "a_bkc"], writes=["a_kp"])
        for tc in range(NIC):
            for g in range(NT):
                P.op("tensor", lambda e, g=g, tc=tc: e.matmul(
                    out=P.pspool[:, 2, 0:128], lhsT=xhat[:, g, tc * 128:(tc + 1) * 128],
                    rhs=wp3[:, g, 2, :],
                    start=(g == 0), stop=(g == NT - 1)),
                    reads=["a_wp3", "a_xhat"], writes=["psb2"], inc=(g == NT - 1))
            P.op("vector", lambda e, tc=tc, hp=hp: e.tensor_add(
                out=vp[:, tc, :], in0=P.pspool[:, 2, 0:128],
                in1=bvb[:, hp * 128:(hp + 1) * 128]),
                reads=["psb2", "a_bvb"], writes=["a_vp"])

        for hh in range(2):
            h = hp * 2 + hh
            e0 = 64 * hh

            def emit_relw(ic, e0=e0):
                i0 = ic * 128
                s0w = 1920 - i0
                pi = ic % 2
                for rs in range(5):
                    w = 512 if rs < 4 else 128
                    bk_ = 4 + rs % 2
                    P.op("tensor", lambda e, rs=rs, w=w, bk_=bk_, e0=e0, i0=i0, s0w=s0w: e.matmul(
                        out=P.pspool[:, bk_, 0:w],
                        lhsT=qp[e0:e0 + 64, i0:i0 + 128],
                        rhs=tab2[e0:e0 + 64, s0w + rs * 512:s0w + rs * 512 + w],
                        start=True, stop=True),
                        reads=["a_qp", "a_tab2"], writes=[f"psb{bk_}"])
                    P.op("scalar", lambda e, rs=rs, w=w, bk_=bk_, pi=pi: e.copy(
                        out=relw[:, pi, rs * 512:rs * 512 + w], in_=P.pspool[:, bk_, 0:w]),
                        reads=[f"psb{bk_}"], writes=[f"a_relw{pi}"])
                P.dma("sync", f"a_reld{pi}",
                      lambda e, pi=pi: e.dma_start(out=din["reld"][pi], in_=relw[:, pi, :]),
                      reads=[f"a_relw{pi}"], writes=[("dram", f"reld{pi}")])
                shear = bass.AP(tensor=din["reld"].tensor, offset=pi * 128 * W + 127,
                                ap=[[W - 1, 128], [1, T]])
                P.dma("sync", f"a_rpe{pi}",
                      lambda e, shear=shear, pi=pi: e.dma_start(
                          out=rpe[:, pi, :], in_=shear),
                      reads=[("dram", f"reld{pi}")], writes=[f"a_rpe{pi}"])

            def emit_scores(ic, e0=e0):
                i0 = ic * 128
                pi = ic % 2
                for js in range(4):
                    P.op("tensor", lambda e, js=js, i0=i0, e0=e0: e.matmul(
                        out=P.psb(js), lhsT=qp[e0:e0 + 64, i0:i0 + 128],
                        rhs=kp[e0:e0 + 64, js * 512:(js + 1) * 512],
                        start=True, stop=False),
                        reads=["a_qp", "a_kp"], writes=[f"psb{js}"], inc=False)
                    P.op("tensor", lambda e, js=js, pi=pi: e.matmul(
                        out=P.psb(js), lhsT=identbf[:],
                        rhs=rpe[:, pi, js * 512:(js + 1) * 512],
                        start=False, stop=True),
                        reads=["a_ident", f"a_rpe{pi}", f"psb{js}"], writes=[f"psb{js}"])
                    P.op("vector", lambda e, js=js, pi=pi: e.reduce_max(
                        out=mrow[:, pi, js:js + 1], in_=P.psb(js), axis=AX.X),
                        reads=[f"psb{js}"], writes=[f"a_mrow{pi}"])
                P.op("vector", lambda e, pi=pi: e.tensor_max(
                    out=mrow[:, pi, 4:5], in0=mrow[:, pi, 0:1], in1=mrow[:, pi, 1:2]),
                    reads=[f"a_mrow{pi}"], writes=[f"a_mrow{pi}"])
                P.op("vector", lambda e, pi=pi: e.tensor_max(
                    out=mrow[:, pi, 5:6], in0=mrow[:, pi, 2:3], in1=mrow[:, pi, 3:4]),
                    reads=[f"a_mrow{pi}"], writes=[f"a_mrow{pi}"])
                P.op("vector", lambda e, pi=pi: e.tensor_max(
                    out=mrow[:, pi, 4:5], in0=mrow[:, pi, 4:5], in1=mrow[:, pi, 5:6]),
                    reads=[f"a_mrow{pi}"], writes=[f"a_mrow{pi}"])
                P.op("vector", lambda e, pi=pi: e.tensor_scalar(
                    out=mrow[:, pi, 4:5], in0=mrow[:, pi, 4:5], scalar1=-1.0,
                    scalar2=None, op0=ALU.mult),
                    reads=[f"a_mrow{pi}"], writes=[f"a_mrow{pi}"])
                P.op("scalar", lambda e, pi=pi: e.activation(
                    out=praw[:, pi, :], in_=P.psw(0, 4), func=AF.Exp,
                    bias=mrow[:, pi, 4:5], accum_out=mrow[:, pi, 6:7]),
                    reads=["psb0", "psb1", "psb2", "psb3", f"a_mrow{pi}"],
                    writes=[f"a_praw{pi}", f"a_mrow{pi}"])

            def emit_post(ic):
                pi = ic % 2
                P.op("vector", lambda e, pi=pi: e.reciprocal(
                    out=mrow[:, pi, 7:8], in_=mrow[:, pi, 6:7]),
                    reads=[f"a_mrow{pi}"], writes=[f"a_mrow{pi}"])
                P.op("vector", lambda e, pi=pi: e.tensor_scalar(
                    out=praw[:, pi, :], in0=praw[:, pi, :], scalar1=mrow[:, pi, 7:8],
                    scalar2=None, op0=ALU.mult),
                    reads=[f"a_praw{pi}", f"a_mrow{pi}"], writes=[f"a_praw{pi}"])

            def emit_trans(ic):
                pi = ic % 2
                icq = ic % 4
                for kc in range(NIC):
                    tb = 6 + (kc // 4) % 2
                    P.op("tensor", lambda e, kc=kc, pi=pi, tb=tb: e.matmul(
                        out=P.pspool[:, tb, (kc % 4) * 128:(kc % 4) * 128 + 128],
                        lhsT=praw[:, pi, kc * 128:(kc + 1) * 128], rhs=identbf[:],
                        start=True, stop=True),
                        reads=[f"a_praw{pi}", "a_ident"], writes=[f"psb{tb}"],
                        inc=(kc % 4 == 3))
                    if kc % 4 == 3:
                        g4 = kc // 4
                        P.op("vector", lambda e, g4=g4, icq=icq, tb=tb: e.tensor_copy(
                            out=ptb4[:, 4 * g4:4 * g4 + 4, icq * 128:icq * 128 + 128],
                            in_=P.pspool[:, tb, :].rearrange("p (a b) -> p a b", a=4)),
                            reads=[f"psb{tb}"], writes=["a_ptb4"])

            def emit_ctx(icg, e0=e0, hp=hp):
                for jc in range(NIC):
                    P.op("tensor", lambda e, jc=jc, e0=e0: e.matmul(
                        out=P.pspool[0:64, 6, 0:512], lhsT=vp[:, jc, e0:e0 + 64],
                        rhs=ptb4[:, jc, :],
                        start=(jc == 0), stop=(jc == NIC - 1)),
                        reads=["a_vp", "a_ptb4"], writes=["psb6"],
                        inc=(jc == NIC - 1))
                P.op("scalar", lambda e, icg=icg, e0=e0, hp=hp: e.copy(
                    out=ctx[e0:e0 + 64, hp, icg * 512:(icg + 1) * 512],
                    in_=P.pspool[0:64, 6, 0:512]),
                    reads=["psb6"], writes=["a_ctx"])

            for ic in range(-1, NIC + 1):
                if ic >= 1:
                    emit_post(ic - 1)
                if 0 <= ic < NIC:
                    emit_scores(ic)
                if ic + 1 < NIC:
                    emit_relw(ic + 1)
                if ic >= 1:
                    emit_trans(ic - 1)
                    if (ic - 1) % 4 == 3:
                        emit_ctx((ic - 1) // 4)

    for ci, c0 in enumerate(range(0, T, TC)):
        for et in range(NT):
            bk_ = et % 2
            for g in range(NT):
                P.op("tensor", lambda e, g=g, et=et, ci=ci, bk_=bk_: e.matmul(
                    out=P.psb(bk_), lhsT=wo_sb[:, g, et * 128:(et + 1) * 128],
                    rhs=ctx[:, g, ci * 512:(ci + 1) * 512],
                    start=(g == 0), stop=(g == NT - 1)),
                    reads=["a_wo", "a_ctx"], writes=[f"psb{bk_}"], inc=(g == NT - 1))
            P.op("vector", lambda e, et=et, bk_=bk_: e.tensor_scalar(
                out=yc[:, et, :], in0=P.psb(bk_), scalar1=boc[:, et:et + 1],
                scalar2=None, op0=ALU.add),
                reads=[f"psb{bk_}", "a_boc"], writes=["a_yc"])
        emit_ln_stats(P, C, ln, yc, "a_yc", tmp, "a_tmp", NT)
        load_x_chunk(P, din["xin"], xin_key, xc, "a_xc", c0, TC)
        emit_ln_apply(P, C, ln, yc, "a_yc", xc, "a_xc", tmp, "a_tmp", NT,
                      add_to=xc, add_key="a_xc")
        store_x_chunk(P, dout["xout"], xout_key, xc, "a_xc", c0, TC, "st_xout")


def host_prep_l1attn(inputs):
    f32, c = np.float32, np.ascontiguousarray
    d = {}
    d["wqTt"] = _tile_lhs(np.asarray(inputs["L1_attn_wq"], f32).T)
    d["wkTt"] = _tile_lhs(np.asarray(inputs["L1_attn_wk"], f32).T)
    d["wvTt"] = _tile_lhs(np.asarray(inputs["L1_attn_wv"], f32).T)
    woT = np.asarray(inputs["L1_attn_wo"], f32).T
    d["woTt"] = c(woT.reshape(8, 128, 1024).transpose(1, 0, 2))
    d["bqc"] = c(np.asarray(inputs["L1_attn_bq"], f32).reshape(8, 128).T)
    d["bkc"] = c(np.asarray(inputs["L1_attn_bk"], f32).reshape(8, 128).T)
    d["boc"] = c(np.asarray(inputs["L1_attn_bo"], f32).reshape(8, 128).T)
    d["bvrow"] = c(np.asarray(inputs["L1_attn_bv"], f32).reshape(1, 1024))
    tab = np.asarray(inputs["L1_attn_rpe_table"], f32)[:, :2 * T - 1]
    d["tabTrev"] = c(tab[:, ::-1].transpose(0, 2, 1))
    d["identbf"] = np.eye(128, dtype=BF16NP)
    return d


L1ATTN_SHAPES = {
    "wqTt": ([8, 128, 8, 128], F32), "wkTt": ([8, 128, 8, 128], F32),
    "wvTt": ([8, 128, 8, 128], F32), "woTt": ([128, 8, 1024], F32),
    "bqc": ([128, 8], F32), "bkc": ([128, 8], F32), "boc": ([128, 8], F32),
    "bvrow": ([1, 1024], F32), "tabTrev": ([16, 64, 4095], F32),
    "identbf": ([128, 128], BF16),
}


def build_l1attn_program():
    nc = bass.Bass("TRN2", target_bir_lowering=False)
    din = {"xin": nc.dram_tensor("xin", [D, T], F32, kind="ExternalInput").ap()}
    for nm, (shp, dt) in L1ATTN_SHAPES.items():
        din[nm] = nc.dram_tensor(nm, shp, dt, kind="ExternalInput").ap()
    din["reld"] = nc.dram_tensor("reld", [2, 128, 2176], BF16, kind="Internal").ap()
    dout = {"xout": nc.dram_tensor("xout", [D, T], F32, kind="ExternalOutput").ap()}
    P = Prog(nc)
    P.make_pool()
    C = Common(P)
    build_l1attn(P, C, din, dout)
    P.final_wait("sync", ["st_xout"])
    P.build()
    return nc





PI = float(np.pi)


def build_l0attn(P: Prog, C: Common, din, dout, xin_key="dram_xin",
                 xout_key="dram_xout"):
    TCC = 512
    NCH = T // TCC
    CT = 17

    xc = P.sbuf("g_xc", [128, NT, TCC], F32)
    tmp = P.sbuf("g_tmp", [128, NT, TCC], F32)
    xhat = P.sbuf("g_xhat", [128, NT, TCC], BF16)
    wproj = P.sbuf("g_wproj", [128, NT, 2080], BF16)
    outw = [P.sbuf(f"g_outw{i}", [128, NT, 128], BF16) for i in range(2)]
    zbuf = P.sbuf("g_zbuf", [128, CT, TCC + 3], F32)
    z2c = P.sbuf("g_z2c", [128, 9, TCC], F32)
    gate = P.sbuf("g_gate", [128, NT, TCC], BF16)
    scr = P.sbuf("g_scr", [128, 2, TCC], F32)
    sqt = P.sbuf("g_sqt", [128, 2, TCC], F32)
    convw = P.sbuf("g_convw", [128, CT, 4], F32)
    convb = P.sbuf("g_convb", [128, CT], F32)
    thc = P.sbuf("g_thc", [128, 4, NT], F32)
    slc = P.sbuf("g_slc", [32, 4], F32)
    sscol = P.sbuf("g_sscol", [32, 1], F32)
    dlog = P.sbuf("g_dlog", [1, 4], F32)
    wcols = P.sbuf("g_wcols", [128, 4], F32)
    nscol = P.sbuf("g_nscol", [32, 8], F32)
    wre = P.sbuf("g_wre", [32, 8, 32], F32)
    bd = P.sbuf("g_bd", [128, 8, 128], BF16)
    selg = P.sbuf("g_selg", [32, NT, 128], F32)
    sel4 = P.sbuf("g_sel4", [4, 128], F32)
    o32 = P.sbuf("g_o32", [128, 4], F32)
    dist = P.sbuf("g_dist", [32, 512], F32)
    twc = P.sbuf("g_twc", [32, TCC], F32)
    pw = P.sbuf("g_pw", [32, TCC], F32)
    denc = P.sbuf("g_denc", [32, TCC], F32)
    dstage = P.sbuf("g_dstage", [4, TCC], F32)
    sstage = P.sbuf("g_sstage", [4, 2, TCC], F32)
    dcarry = P.sbuf("g_dcarry", [32, 2], F32)
    carry = P.sbuf("g_carry", [128, 64], F32)
    numf = P.sbuf("g_numf", [128, 8, TCC], F32)
    numbf = P.sbuf("g_numbf", [128, 8, TCC], BF16)
    ygate = P.sbuf("g_ygate", [128, NT, TCC], BF16)
    zeros = P.sbuf("g_zeros", [128, TCC], F32)
    negpi = P.sbuf("g_negpi", [128, 1], F32)
    pihalf = P.sbuf("g_pihalf", [128, 1], F32)
    onecol = P.sbuf("g_onecol", [32, 1], F32)

    P.dma("sync", "g_convw", lambda e: e.dma_start(
        out=convw[:], in_=din["convT"].rearrange("(c p) j -> p c j", p=128)),
        reads=[("dram", "convT")], writes=["g_convw"])
    P.dma("sync", "g_convb", lambda e: e.dma_start(
        out=convb[:], in_=din["convb"][:]),
        reads=[("dram", "convb")], writes=["g_convb"])
    P.dma("sync", "g_thc", lambda e: e.dma_start(
        out=thc[:], in_=din["thetac"][:]),
        reads=[("dram", "thetac")], writes=["g_thc"])
    P.dma("sync", "g_slc", lambda e: e.dma_start(
        out=slc[:, 0:1], in_=din["slopes"][:]),
        reads=[("dram", "slopes")], writes=["g_slc"])
    P.dma("sync", "g_sscol", lambda e: e.dma_start(
        out=sscol[:], in_=din["sscol"][:]),
        reads=[("dram", "sscol")], writes=["g_sscol"])
    P.dma("sync", "g_dlog", lambda e: e.dma_start(
        out=dlog[0:1, 0:3], in_=din["dlog"][:]),
        reads=[("dram", "dlog")], writes=["g_dlog"])
    P.dma("sync", "g_nscol", lambda e: e.dma_start(
        out=nscol[:], in_=din["nsc"][:]),
        reads=[("dram", "nsc")], writes=["g_nscol"])
    P.dma("sync", "g_wre", lambda e: e.dma_start(
        out=wre[:], in_=din["wrim"].rearrange("r m h o -> h (r m) o")),
        reads=[("dram", "wrim")], writes=["g_wre"])
    P.dma("sync", "g_selg", lambda e: e.dma_start(
        out=selg[:], in_=din["selg"].rearrange("g k p -> k g p")),
        reads=[("dram", "selg")], writes=["g_selg"])
    P.dma("sync", "g_sel4", lambda e: e.dma_start(
        out=sel4[:], in_=din["sel4"][:]), reads=[("dram", "sel4")], writes=["g_sel4"])
    P.dma("sync", "g_o32", lambda e: e.dma_start(
        out=o32[:], in_=din["o32"][:]), reads=[("dram", "o32")], writes=["g_o32"])

    P.dma("gpsimd", "g_wproj", lambda e: e.dma_start(
        out=wproj[:], in_=din["inprojw"][:]),
        reads=[("dram", "inprojw")], writes=["g_wproj"])

    P.op("vector", lambda e: e.memset(zeros[:], 0.0), writes=["g_zeros"])
    P.op("vector", lambda e: e.memset(negpi[:], -PI), writes=["g_negpi"])
    P.op("vector", lambda e: e.memset(pihalf[:], 0.5 * PI), writes=["g_pihalf"])
    P.op("vector", lambda e: e.memset(onecol[:], 1.0), writes=["g_onecol"])
    P.op("vector", lambda e: e.memset(zbuf[:], 0.0), writes=["g_zbuf"])
    P.op("vector", lambda e: e.memset(carry[:], 0.0), writes=["g_carry"])
    P.op("vector", lambda e: e.memset(dcarry[:], 0.0), writes=["g_dcarry"])
    P.op("vector", lambda e: e.memset(bd[:], 0.0), writes=["g_bd"])

    P.op("scalar", lambda e: e.activation(
        out=slc[:, 2:3], in_=slc[:, 0:1], func=AF.Abs),
        reads=["g_slc"], writes=["g_slc"])
    P.op("scalar", lambda e: e.activation(
        out=slc[:, 2:3], in_=slc[:, 2:3], func=AF.Exp, scale=-1.0),
        reads=["g_slc"], writes=["g_slc"])
    P.op("scalar", lambda e: e.activation(
        out=slc[:, 2:3], in_=slc[:, 2:3], func=AF.Ln, bias=onecol[:]),
        reads=["g_slc", "g_onecol"], writes=["g_slc"])
    P.op("vector", lambda e: e.tensor_scalar(
        out=slc[:, 3:4], in0=slc[:, 0:1], scalar1=0.0, scalar2=None,
        op0=ALU.max), reads=["g_slc"], writes=["g_slc"])
    P.op("vector", lambda e: e.tensor_tensor(
        out=slc[:, 1:2], in0=slc[:, 2:3], in1=slc[:, 3:4], op=ALU.add),
        reads=["g_slc"], writes=["g_slc"])
    P.op("vector", lambda e: e.tensor_scalar(
        out=slc[:, 1:2], in0=slc[:, 1:2], scalar1=-1.0, scalar2=None,
        op0=ALU.mult), reads=["g_slc"], writes=["g_slc"])

    P.op("vector", lambda e: e.reduce_max(
        out=dlog[0:1, 3:4], in_=dlog[0:1, 0:3], axis=AX.X),
        reads=["g_dlog"], writes=["g_dlog"])
    P.op("vector", lambda e: e.tensor_scalar(
        out=dlog[0:1, 3:4], in0=dlog[0:1, 3:4], scalar1=-1.0, scalar2=None,
        op0=ALU.mult), reads=["g_dlog"], writes=["g_dlog"])
    P.op("scalar", lambda e: e.activation(
        out=dlog[0:1, 0:3], in_=dlog[0:1, 0:3], func=AF.Exp, bias=dlog[0:1, 3:4]),
        reads=["g_dlog"], writes=["g_dlog"])
    P.op("vector", lambda e: e.reduce_sum(
        out=dlog[0:1, 3:4], in_=dlog[0:1, 0:3], axis=AX.X),
        reads=["g_dlog"], writes=["g_dlog"])
    P.op("vector", lambda e: e.reciprocal(
        out=dlog[0:1, 3:4], in_=dlog[0:1, 3:4]), reads=["g_dlog"], writes=["g_dlog"])
    P.op("vector", lambda e: e.tensor_scalar(
        out=dlog[0:1, 0:3], in0=dlog[0:1, 0:3], scalar1=dlog[0:1, 3:4],
        scalar2=None, op0=ALU.mult), reads=["g_dlog"], writes=["g_dlog"])
    P.op("tensor", lambda e: e.matmul(
        out=P.pspool[:, 7, 0:3], lhsT=C.ones_row[:], rhs=dlog[0:1, 0:3],
        start=True, stop=True), reads=["g_dlog", "c_ones_row"], writes=["psb7"])
    P.op("vector", lambda e: e.tensor_copy(
        out=wcols[:, 0:3], in_=P.pspool[:, 7, 0:3]), reads=["psb7"], writes=["g_wcols"])
    P.op("vector", lambda e: e.tensor_scalar(
        out=wcols[:, 3:4], in0=wcols[:, 2:3], scalar1=-1.0, scalar2=None,
        op0=ALU.mult), reads=["g_wcols"], writes=["g_wcols"])

    for rim in range(8):
        P.op("vector", lambda e, rim=rim: e.tensor_scalar(
            out=wre[:, rim, :], in0=wre[:, rim, :], scalar1=nscol[:, rim:rim + 1],
            scalar2=None, op0=ALU.mult), reads=["g_wre", "g_nscol"], writes=["g_wre"])
    for rim in range(8):
        for kq in range(4):
            P.dma("gpsimd", "g_bd",
                  lambda e, rim=rim, kq=kq: e.dma_start(
                      out=bd[32 * kq:32 * kq + 32, rim, 32 * kq:32 * kq + 32],
                      in_=wre[:, rim, :]),
                  reads=["g_wre"], writes=["g_bd"])

    ln = alloc_ln(P, "g")

    for ci in range(NCH):
        c0 = ci * TCC
        load_x_chunk(P, din["xin"], xin_key, xc, "g_xc", c0, TCC)
        emit_ln_stats(P, C, ln, xc, "g_xc", tmp, "g_tmp", NT)
        emit_ln_apply(P, C, ln, xc, "g_xc", xhat, "g_xhat", tmp, "g_tmp", NT)

        for ct in range(CT):
            ncol = 128 if ct < 16 else 32
            bk_ = ct % 2
            for g in range(NT):
                P.op("tensor", lambda e, ct=ct, g=g, bk_=bk_, ncol=ncol: e.matmul(
                    out=P.pspool[0:ncol, bk_, :], lhsT=wproj[:, g, ct * 128:ct * 128 + ncol],
                    rhs=xhat[:, g, :], start=(g == 0), stop=(g == NT - 1)),
                    reads=["g_wproj", "g_xhat"], writes=[f"psb{bk_}"], inc=(g == NT - 1))
            P.op("scalar", lambda e, ct=ct, bk_=bk_, ncol=ncol: e.copy(
                out=zbuf[0:ncol, ct, 3:3 + TCC], in_=P.pspool[0:ncol, bk_, :]),
                reads=[f"psb{bk_}"], writes=["g_zbuf"])

        for ct in range(CT):
            nrow = 128 if ct < 16 else 32
            if ct < 8:
                dst, dstk = z2c, "g_z2c"
                dsl = (slice(0, nrow), ct, slice(0, TCC))
            elif ct < 16:
                dst, dstk = scr, f"g_scr{ct % 2}"
                dsl = (slice(0, nrow), ct % 2, slice(0, TCC))
            else:
                dst, dstk = z2c, "g_z2c"
                dsl = (slice(0, nrow), 8, slice(0, TCC))
            P.op("vector", lambda e, ct=ct, nrow=nrow, dst=dst, dsl=dsl: e.tensor_scalar(
                out=dst[dsl], in0=zbuf[0:nrow, ct, 0:TCC],
                scalar1=convw[0:nrow, ct, 0:1], scalar2=convb[0:nrow, ct:ct + 1],
                op0=ALU.mult, op1=ALU.add),
                reads=["g_zbuf", "g_convw", "g_convb"], writes=[dstk])
            for j in (1, 2, 3):
                P.op("vector", lambda e, ct=ct, j=j, nrow=nrow, dst=dst, dsl=dsl:
                     e.scalar_tensor_tensor(
                         out=dst[dsl], in0=zbuf[0:nrow, ct, j:j + TCC],
                         scalar=convw[0:nrow, ct, j:j + 1], in1=dst[dsl],
                         op0=ALU.mult, op1=ALU.add),
                     reads=["g_zbuf", "g_convw", dstk], writes=[dstk])
            if 8 <= ct < 16:
                P.op("scalar", lambda e, ct=ct: e.activation(
                    out=gate[:, ct - 8, :], in_=scr[:, ct % 2, :], func=AF.Silu),
                    reads=[f"g_scr{ct % 2}"], writes=["g_gate"])
        P.op("vector", lambda e: e.tensor_copy(
            out=zbuf[:, :, 0:3], in_=zbuf[:, :, TCC:TCC + 3]),
            reads=["g_zbuf"], writes=["g_zbuf"])

        P.dma("sync", "g_dist", lambda e, c0=c0: e.dma_start(
            out=dist[:], in_=din["dist"][:, c0:c0 + TCC]),
            reads=[("dram", "dist")], writes=["g_dist"])
        P.op("scalar", lambda e: e.activation(
            out=twc[:], in_=dist[:], func=AF.Exp, scale=slc[:, 1:2]),
            reads=["g_dist", "g_slc"], writes=["g_twc"])
        P.op("vector", lambda e: e.tensor_scalar(
            out=pw[:], in0=z2c[0:32, 8, :], scalar1=sscol[:], scalar2=20.0,
            op0=ALU.mult, op1=ALU.min), reads=["g_z2c", "g_sscol"], writes=["g_pw"])
        P.op("vector", lambda e: e.tensor_scalar(
            out=pw[:], in0=pw[:], scalar1=-20.0, scalar2=None,
            op0=ALU.max), reads=["g_pw"], writes=["g_pw"])
        P.op("scalar", lambda e: e.activation(
            out=pw[:], in_=pw[:], func=AF.Exp), reads=["g_pw"], writes=["g_pw"])
        P.op("vector", lambda e: e.tensor_mul(
            out=pw[:], in0=pw[:], in1=twc[:]), reads=["g_pw", "g_twc"], writes=["g_pw"])
        P.op("vector", lambda e: e.tensor_tensor_scan(
            out=denc[:], data0=pw[:], data1=zeros[0:32, :], initial=dcarry[:, 0:1],
            op0=ALU.add, op1=ALU.add),
            reads=["g_pw", "g_zeros", "g_dcarry"], writes=["g_denc"])
        P.op("vector", lambda e: e.tensor_copy(
            out=dcarry[:, 0:1], in_=denc[:, TCC - 1:TCC]),
            reads=["g_denc"], writes=["g_dcarry"])

        for g in range(NT):
            gp = g % 2
            sck = f"g_scr{gp}"
            P.op("tensor", lambda e, g=g: e.matmul(
                out=P.psb(6), lhsT=selg[:, g, :], rhs=pw[:],
                start=True, stop=True), reads=["g_selg", "g_pw"], writes=["psb6"])
            P.op("vector", lambda e, g=g, gp=gp: e.tensor_scalar(
                out=scr[:, gp, :], in0=z2c[:, g, :], scalar1=wcols[:, 3:4],
                scalar2=wcols[:, 1:2], op0=ALU.mult, op1=ALU.add),
                reads=["g_z2c", "g_wcols"], writes=[sck])
            P.op("vector", lambda e, g=g, gp=gp: e.tensor_mul(
                out=scr[:, gp, :], in0=scr[:, gp, :], in1=z2c[:, g, :]),
                reads=[sck, "g_z2c"], writes=[sck])
            P.op("vector", lambda e, gp=gp: e.tensor_scalar(
                out=scr[:, gp, :], in0=scr[:, gp, :], scalar1=wcols[:, 0:1],
                scalar2=None, op0=ALU.add), reads=[sck, "g_wcols"], writes=[sck])
            P.op("vector", lambda e, gp=gp: e.tensor_mul(
                out=scr[:, gp, :], in0=scr[:, gp, :], in1=P.psb(6)),
                reads=[sck, "psb6"], writes=[sck])
            for m in range(4):
                mp = m % 2
                tk = f"g_tmpA{mp}"
                sqk = f"g_sqt{mp}"
                t0, t1, t2 = 4 * mp + 0, 4 * mp + 1, 4 * mp + 2
                sq = sqt[:, mp, :]
                P.op("vector", lambda e, g=g, m=m, sq=sq: e.tensor_scalar(
                    out=sq, in0=z2c[:, g, :], scalar1=thc[:, m, g:g + 1],
                    scalar2=None, op0=ALU.mult),
                    reads=["g_z2c", "g_thc"], writes=[sqk])
                P.op("vector", lambda e, sq=sq, t2=t2: e.tensor_scalar(
                    out=tmp[:, t2, :], in0=sq, scalar1=PI, scalar2=None,
                    op0=ALU.is_gt), reads=[sqk], writes=[tk])
                P.op("vector", lambda e, sq=sq, t1=t1: e.tensor_scalar(
                    out=tmp[:, t1, :], in0=sq, scalar1=-PI, scalar2=None,
                    op0=ALU.is_lt), reads=[sqk], writes=[tk])
                P.op("vector", lambda e, t1=t1, t2=t2: e.tensor_sub(
                    out=tmp[:, t2, :], in0=tmp[:, t2, :], in1=tmp[:, t1, :]),
                    reads=[tk], writes=[tk])
                P.op("vector", lambda e, sq=sq, t1=t1, t2=t2: e.scalar_tensor_tensor(
                    out=tmp[:, t1, :], in0=tmp[:, t2, :], scalar=-2 * PI,
                    in1=sq, op0=ALU.mult, op1=ALU.add),
                    reads=[tk, sqk], writes=[tk])
                P.op("vector", lambda e, t1=t1: e.tensor_scalar(
                    out=tmp[:, t1, :], in0=tmp[:, t1, :], scalar1=PI,
                    scalar2=-PI, op0=ALU.min, op1=ALU.max),
                    reads=[tk], writes=[tk])
                P.op("scalar", lambda e, t0=t0, t1=t1: e.activation(
                    out=tmp[:, t0, :], in_=tmp[:, t1, :], func=AF.Sin),
                    reads=[tk], writes=[tk])
                P.op("scalar", lambda e, t1=t1, t2=t2: e.activation(
                    out=tmp[:, t2, :], in_=tmp[:, t1, :], func=AF.Sin, scale=0.5),
                    reads=[tk], writes=[tk])
                P.op("scalar", lambda e, t2=t2: e.activation(
                    out=tmp[:, t2, :], in_=tmp[:, t2, :], func=AF.Square),
                    reads=[tk], writes=[tk])
                P.op("vector", lambda e, t1=t1, t2=t2: e.tensor_scalar(
                    out=tmp[:, t1, :], in0=tmp[:, t2, :], scalar1=-2.0,
                    scalar2=1.0, op0=ALU.mult, op1=ALU.add),
                    reads=[tk], writes=[tk])
                P.op("vector", lambda e, t1=t1, gp=gp: e.tensor_mul(
                    out=tmp[:, t1, :], in0=tmp[:, t1, :], in1=scr[:, gp, :]),
                    reads=[tk, sck], writes=[tk])
                P.op("vector", lambda e, t0=t0, gp=gp: e.tensor_mul(
                    out=tmp[:, t0, :], in0=tmp[:, t0, :], in1=scr[:, gp, :]),
                    reads=[tk, sck], writes=[tk])
                for ri in range(2):
                    src = (t1, t0)[ri]
                    cc = g * 8 + m * 2 + ri
                    P.op("vector", lambda e, m=m, ri=ri, src=src, cc=cc: e.tensor_tensor_scan(
                        out=numf[:, 2 * m + ri, :], data0=tmp[:, src, :],
                        data1=zeros[:], initial=carry[:, cc:cc + 1],
                        op0=ALU.add, op1=ALU.add),
                        reads=[tk, "g_zeros", "g_carry"], writes=["g_numf"])
                    sqs = (3, 7)[ri]
                    P.op("scalar", lambda e, m=m, ri=ri, sqs=sqs: e.activation(
                        out=tmp[:, sqs, :], in_=numf[:, 2 * m + ri, :], func=AF.Square),
                        reads=["g_numf"], writes=[f"g_sqb{ri}"])
                    for sub in range(4):
                        first = (m == 0 and ri == 0 and sub == 0)
                        last = (m == 3 and ri == 1 and sub == 3)
                        P.op("tensor", lambda e, sub=sub, sqs=sqs, first=first, last=last: e.matmul(
                            out=P.pspool[0:4, 7, sub * 128:(sub + 1) * 128],
                            lhsT=o32[:], rhs=tmp[:, sqs, sub * 128:(sub + 1) * 128],
                            start=first, stop=last),
                            reads=["g_o32", f"g_sqb{ri}"], writes=["psb7"],
                            inc=(sub == 3))
            P.op("vector", lambda e, g=g: e.tensor_copy(
                out=carry[:, g * 8:g * 8 + 8], in_=numf[:, 0:8, TCC - 1:TCC]),
                reads=["g_numf"], writes=["g_carry"])
            P.op("scalar", lambda e: e.activation(
                out=sstage[:, 0, :], in_=P.pspool[0:4, 7, :],
                func=AF.Copy, scale=1.0 / 256.0),
                reads=["psb7"], writes=["g_sstage"])
            P.dma("sync", "g_dstage",
                  lambda e, g=g: e.dma_start(out=dstage[:], in_=denc[4 * g:4 * g + 4, :]),
                  reads=["g_denc"], writes=["g_dstage"])
            P.op("vector", lambda e: e.tensor_scalar(
                out=dstage[:], in0=dstage[:], scalar1=1e-4, scalar2=None,
                op0=ALU.max), reads=["g_dstage"], writes=["g_dstage"])
            P.op("scalar", lambda e: e.activation(
                out=dstage[:], in_=dstage[:], func=AF.Square, scale=float(np.sqrt(1e-5))),
                reads=["g_dstage"], writes=["g_dstage"])
            P.op("vector", lambda e: e.tensor_add(
                out=sstage[:, 1, :], in0=sstage[:, 0, :], in1=dstage[:]),
                reads=["g_sstage", "g_dstage"], writes=["g_sstage"])
            P.op("scalar", lambda e: e.activation(
                out=sstage[:, 1, :], in_=sstage[:, 1, :], func=AF.Sqrt),
                reads=["g_sstage"], writes=["g_sstage"])
            P.op("vector", lambda e: e.reciprocal(
                out=sstage[:, 1, :], in_=sstage[:, 1, :]),
                reads=["g_sstage"], writes=["g_sstage"])
            P.op("tensor", lambda e: e.matmul(
                out=P.psb(6), lhsT=sel4[:], rhs=sstage[:, 1, :],
                start=True, stop=True), reads=["g_sel4", "g_sstage"], writes=["psb6"])
            for mm in range(8):
                P.op("vector", lambda e, mm=mm: e.tensor_mul(
                    out=numbf[:, mm, :], in0=numf[:, mm, :], in1=P.psb(6)),
                    reads=["g_numf", "psb6"], writes=["g_numbf"])
            for rim in range(8):
                ri, m = rim // 4, rim % 4
                P.op("tensor", lambda e, rim=rim, ri=ri, m=m: e.matmul(
                    out=P.psb(2), lhsT=bd[:, rim, :], rhs=numbf[:, 2 * m + ri, :],
                    start=(rim == 0), stop=(rim == 7)),
                    reads=["g_bd", "g_numbf"], writes=["psb2"], inc=(rim == 7))
            P.op("vector", lambda e, g=g: e.tensor_mul(
                out=ygate[:, g, :], in0=P.psb(2), in1=gate[:, g, :]),
                reads=["psb2", "g_gate"], writes=["g_ygate"])

        for et in range(NT):
            P.dma("gpsimd", f"g_outw{et % 2}",
                  lambda e, et=et: e.dma_start(
                      out=outw[et % 2][:], in_=din["outprojw"][et]),
                  reads=[("dram", "outprojw")], writes=[f"g_outw{et % 2}"])
            for g in range(NT):
                P.op("tensor", lambda e, et=et, g=g: e.matmul(
                    out=P.psb(3), lhsT=outw[et % 2][:, g, :], rhs=ygate[:, g, :],
                    start=(g == 0), stop=(g == NT - 1)),
                    reads=[f"g_outw{et % 2}", "g_ygate"], writes=["psb3"],
                    inc=(g == NT - 1))
            P.op("vector", lambda e, et=et: e.tensor_copy(
                out=numf[:, et, :], in_=P.psb(3)), reads=["psb3"], writes=["g_numf"])

        emit_ln_stats(P, C, ln, numf, "g_numf", tmp, "g_tmp", NT)
        load_x_chunk(P, din["xin"], xin_key, xc, "g_xc", c0, TCC)
        emit_ln_apply(P, C, ln, numf, "g_numf", xc, "g_xc", tmp, "g_tmp", NT,
                      add_to=xc, add_key="g_xc")
        store_x_chunk(P, dout["xout"], xout_key, xc, "g_xc", c0, TCC, "st_xout")


def host_prep_l0attn(inputs):
    f32, c = np.float32, np.ascontiguousarray
    d = {}
    ipw = np.asarray(inputs["L0_attn_in_proj_w"], f32)
    d["inprojw"] = c(ipw.reshape(8, 128, 2080).transpose(1, 0, 2))
    convT = np.zeros((2176, 4), f32)
    convT[:2080] = np.asarray(inputs["L0_attn_conv_w"], f32)[:, 0, :].T
    d["convT"] = convT
    convb = np.zeros((2176,), f32)
    convb[:2080] = np.asarray(inputs["L0_attn_conv_b"], f32)
    d["convb"] = c(convb.reshape(17, 128).T)
    th = np.asarray(inputs["L0_attn_theta"], f32).reshape(32, 32, 4)
    d["thetac"] = c(th.transpose(2, 0, 1).reshape(4, 8, 128).transpose(2, 0, 1))
    d["slopes"] = c(np.concatenate([
        np.asarray(inputs["L0_attn_decay_slopes"], f32),
        np.asarray(inputs["L0_attn_anchor_slopes"], f32)]).reshape(32, 1))
    d["sscol"] = c(np.asarray(inputs["L0_attn_score_scale"], f32).reshape(32, 1))
    d["dlog"] = c(np.asarray(inputs["L0_attn_deriv_logits"], f32).reshape(1, 3))
    ns = np.asarray(inputs["L0_attn_norm_scale"], f32)
    nsc = np.stack([ns[:128].reshape(32, 4).T, ns[128:].reshape(32, 4).T])
    d["nsc"] = c(nsc.reshape(8, 32).T)
    wr = np.asarray(inputs["L0_attn_W_re"], f32).reshape(32, 4, 32).transpose(1, 0, 2)
    wi = np.asarray(inputs["L0_attn_W_im"], f32).reshape(32, 4, 32).transpose(1, 0, 2)
    d["wrim"] = c(np.stack([wr, wi]))
    d["outprojw"] = _tile_lhs(np.asarray(inputs["L0_attn_out_proj_w"], f32))
    p = np.arange(128)
    d["sel4"] = c((p[None, :] // 32 == np.arange(4)[:, None]).astype(f32))
    d["o32"] = c(d["sel4"].T)
    selg = np.zeros((8, 32, 128), f32)
    for g in range(8):
        selg[g, g * 4 + p // 32, p] = 1.0
    d["selg"] = selg
    t = np.arange(2048, dtype=f32)
    dist = np.empty((32, 2048), f32)
    dist[:28] = 2047.0 - t
    dist[28:] = t
    d["dist"] = dist
    return d


L0ATTN_SHAPES = {
    "inprojw": ([128, 8, 2080], F32), "convT": ([2176, 4], F32),
    "convb": ([128, 17], F32), "thetac": ([128, 4, 8], F32),
    "slopes": ([32, 1], F32), "sscol": ([32, 1], F32), "dlog": ([1, 3], F32),
    "nsc": ([32, 8], F32), "wrim": ([2, 4, 32, 32], F32),
    "outprojw": ([8, 128, 8, 128], F32), "sel4": ([4, 128], F32),
    "o32": ([128, 4], F32), "selg": ([8, 32, 128], F32), "dist": ([32, 2048], F32),
}


def build_l0attn_program():
    nc = bass.Bass("TRN2", target_bir_lowering=False)
    din = {"xin": nc.dram_tensor("xin", [D, T], F32, kind="ExternalInput").ap()}
    for nm, (shp, dt) in L0ATTN_SHAPES.items():
        din[nm] = nc.dram_tensor(nm, shp, dt, kind="ExternalInput").ap()
    dout = {"xout": nc.dram_tensor("xout", [D, T], F32, kind="ExternalOutput").ap()}
    P = Prog(nc)
    P.make_pool()
    C = Common(P)
    build_l0attn(P, C, din, dout)
    P.final_wait("sync", ["st_xout"])
    P.build()
    return nc




def build_fused_program():
    nc = bass.Bass("TRN2", target_bir_lowering=False)
    din = {"xin": nc.dram_tensor("xin", [D, T], F32, kind="ExternalInput").ap()}
    for shapes in (L0ATTN_SHAPES,):
        for nm, (shp, dt) in shapes.items():
            din[nm] = nc.dram_tensor(nm, shp, dt, kind="ExternalInput").ap()
    for nm, (shp, dt) in L0FFN_SHAPES.items():
        din[nm] = nc.dram_tensor(nm, shp, dt, kind="ExternalInput").ap()
    for nm, (shp, dt) in L1ATTN_SHAPES.items():
        din[nm] = nc.dram_tensor(nm, shp, dt, kind="ExternalInput").ap()
    for nm, shp in L1FFN_SHAPES.items():
        din[nm] = nc.dram_tensor(nm, shp, F32, kind="ExternalInput").ap()
    din["hdram"] = nc.dram_tensor("hdram", [D, T], F32, kind="Internal").ap()
    din["reld"] = nc.dram_tensor("reld", [2, 128, 2176], BF16, kind="Internal").ap()
    xra = nc.dram_tensor("xra", [D, T], F32, kind="Internal").ap()
    xrb = nc.dram_tensor("xrb", [D, T], F32, kind="Internal").ap()
    xout = nc.dram_tensor("xout", [D, T], F32, kind="ExternalOutput").ap()

    P = Prog(nc)
    P.make_pool()
    C = Common(P)
    with P.scope():
        build_l0attn(P, C, din, {"xout": xra},
                     xin_key=("dram", "xin"), xout_key=("dram", "xra"))
    P.barrier()
    with P.scope():
        build_l0ffn(P, C, dict(din, xin=xra), {"xout": xrb},
                    xin_key=("dram", "xra"), xout_key=("dram", "xrb"))
    P.barrier()
    with P.scope():
        build_l1attn(P, C, dict(din, xin=xrb), {"xout": xra},
                     xin_key=("dram", "xrb"), xout_key=("dram", "xra2"))
    P.barrier()
    with P.scope():
        build_l1ffn(P, C, dict(din, xin=xra), {"xout": xout},
                    xin_key=("dram", "xra2"), xout_key=("dram", "xout"))
    P.final_wait("sync", ["st_xout"])
    P.build()
    return nc



_PROGS = {}


def _get_prog(name, builder):
    if name not in _PROGS:
        _PROGS[name] = builder()
    return _PROGS[name]


def kernel(**inputs):
    x = np.asarray(inputs["L0_attn_x"], np.float32)
    nc = _get_prog("fused", build_fused_program)
    hp = {}
    hp.update(host_prep_l0attn(inputs))
    hp.update(host_prep_l0ffn(inputs))
    hp.update(host_prep_l1attn(inputs))
    hp.update(host_prep_l1ffn(inputs))
    in_maps = []
    for i in range(NCORES):
        m = {"xin": np.ascontiguousarray(x[i].T)}
        m.update(hp)
        in_maps.append(m)
    res = run_bass_kernel_spmd(nc, in_maps, core_ids=list(range(NCORES)))
    out = np.stack([np.asarray(res.results[i]["xout"]).T for i in range(NCORES)])
    return out.astype(np.float32)


# version 23
# speedup vs baseline: 1.1217x; 1.1063x over previous
import numpy as np
import ml_dtypes
from contextlib import ExitStack

import concourse.bass as bass
import concourse.mybir as mybir
from concourse.bass_utils import run_bass_kernel_spmd

F32 = mybir.dt.float32
BF16 = mybir.dt.bfloat16
I32 = mybir.dt.int32
AF = mybir.ActivationFunctionType
ALU = mybir.AluOpType
AX = mybir.AxisListType
BF16NP = ml_dtypes.bfloat16

D = 1024
T = 2048
NCORES = 8
NT = D // 128
LN_EPS = 1e-5


ENGINES = ("sync", "tensor", "vector", "scalar", "gpsimd")


class Prog:
    def __init__(self, nc):
        self.nc = nc
        self.stack = ExitStack()
        self._alloc_stack = None
        self.sems = {}
        self.cnt = {}
        self.queue = {e: [] for e in ENGINES}
        self.last_w = {}
        self.readers = {}
        self.waited = {e: {} for e in ENGINES}
        self.bufs = {}

    def sem(self, name):
        if name not in self.sems:
            self.sems[name] = self.stack.enter_context(self.nc.semaphore(name))
            self.cnt[name] = 0
        return self.sems[name]

    def sbuf(self, name, shape, dtype):
        st = self._alloc_stack if self._alloc_stack is not None else self.stack
        t = st.enter_context(self.nc.sbuf_tensor(name, list(shape), dtype))
        self.bufs[name] = t
        return t

    def scope(self):
        from contextlib import contextmanager

        @contextmanager
        def _cm():
            st = ExitStack()
            old = self._alloc_stack
            self._alloc_stack = st
            try:
                yield
            finally:
                self._alloc_stack = old
                st.close()
        return _cm()

    def barrier(self):
        targets = {sname: v for sname, v in self.cnt.items() if v > 0}
        for eng in ENGINES:
            w = {}
            for sname, v in targets.items():
                if sname == eng:
                    continue
                if self.waited[eng].get(sname, -1) >= v:
                    continue
                w[sname] = v
            if w:
                self.queue[eng].append((w, None, None))
                for sname, v in w.items():
                    self.waited[eng][sname] = max(self.waited[eng].get(sname, -1), v)
        self.last_w.clear()
        self.readers.clear()

    def psum(self, name, shape, dtype=F32):
        t = self.stack.enter_context(self.nc.psum_tensor(name, list(shape), dtype))
        self.bufs[name] = t
        return t

    def make_pool(self):
        self.pspool = self.psum("pspool", [128, 8, 512], F32)

    def psb(self, bank):
        return self.pspool[:, bank, :]

    def psbk(self, b0, n=1):
        return [f"psb{i}" for i in range(b0, b0 + n)]

    def psw(self, b0, n):
        return self.pspool[:, b0:b0 + n, :].rearrange("p a b -> p (a b)")

    def _waits_for(self, eng, reads, writes):
        need = {}
        def add(swv):
            s, v = swv
            if need.get(s, 0) < v:
                need[s] = v
        for k in reads:
            if k in self.last_w:
                add(self.last_w[k])
        for k in writes:
            if k in self.last_w:
                add(self.last_w[k])
            for swv in self.readers.get(k, ()):
                add(swv)
        out = {}
        for s, v in need.items():
            if s == eng and v > self.cnt.get(eng, 0):
                continue
            if self.waited[eng].get(s, -1) >= v:
                continue
            out[s] = v
        return out

    def _record(self, eng_sem, val, reads, writes):
        for k in writes:
            self.last_w[k] = (eng_sem, val)
            self.readers[k] = []
        for k in reads:
            self.readers.setdefault(k, []).append((eng_sem, val))

    def op(self, eng, fn, reads=(), writes=(), inc=True):
        self.sem(eng)
        waits = self._waits_for(eng, reads, writes)
        for s, v in waits.items():
            self.waited[eng][s] = max(self.waited[eng].get(s, -1), v)
        if inc:
            self.cnt[eng] += 1
            incspec = (eng, 1)
            val = self.cnt[eng]
        else:
            incspec = None
            val = self.cnt[eng] + 1
        self._record(eng, val, reads, writes)
        self.queue[eng].append((dict(waits), fn, incspec))

    def dma(self, qeng, chan, fn, reads=(), writes=()):
        chan = "dq_" + chan
        self.sem(chan)
        self.sem(qeng)
        waits = self._waits_for(qeng, reads, writes)
        for s, v in waits.items():
            self.waited[qeng][s] = max(self.waited[qeng].get(s, -1), v)
        self.cnt[chan] += 16
        self._record(chan, self.cnt[chan], reads, writes)
        self.queue[qeng].append((dict(waits), fn, (chan, 16)))

    def final_wait(self, qeng, chans):
        waits = {}
        for c in chans:
            c = "dq_" + c
            if self.cnt.get(c, 0) > 0:
                waits[c] = self.cnt[c]
        self.queue[qeng].append((waits, None, None))

    def build(self):
        block = self.stack.enter_context(self.nc.Block())
        for eng_name in ENGINES:
            items = self.queue[eng_name]
            if not items:
                continue
            def make_body(items):
                def body(engine):
                    for waits, fn, incspec in items:
                        for s, v in waits.items():
                            engine.wait_ge(self.sems[s], v)
                        if fn is None:
                            continue
                        inst = fn(engine)
                        if incspec is not None:
                            s, by = incspec
                            inst.then_inc(self.sems[s], by)
                return body
            getattr(block, eng_name)(make_body(items))

    def close(self):
        self.stack.close()


class Common:

    def __init__(self, P: Prog):
        self.P = P
        self.ones_f = P.sbuf("c_ones_f", [128, 1], F32)
        self.ones_row = P.sbuf("c_ones_row", [1, 128], F32)
        self.eps_row = P.sbuf("c_eps_row", [1, 1], F32)
        P.op("vector", lambda v: v.memset(self.ones_f[:], 1.0), writes=["c_ones_f"])
        P.op("vector", lambda v: v.memset(self.ones_row[:], 1.0), writes=["c_ones_row"])
        P.op("vector", lambda v: v.memset(self.eps_row[:], LN_EPS), writes=["c_eps_row"])



def alloc_ln(P: Prog, tag):
    ln = {}
    ln["rows"] = P.sbuf(f"ln_rows_{tag}", [1, 3, 512], F32)
    ln["ps_a"] = P.pspool[0:1, 2, :]
    ln["ps_b"] = P.pspool[0:1, 3, :]
    ln["bc0"] = P.psb(4)
    ln["bc1"] = P.psb(5)
    ln["k"] = tag
    return ln


def emit_ln_stats(P, C, ln, src, src_key, sq, sq_key, ntile):
    k = ln["k"]
    rows, rk = ln["rows"], f"ln_rows_{k}"
    P.op("scalar", lambda e: e.activation(out=sq[:], in_=src[:], func=AF.Square),
         reads=[src_key], writes=[sq_key])
    for g in range(ntile):
        P.op("tensor", lambda e, g=g: e.matmul(
            out=ln["ps_a"], lhsT=C.ones_f[:], rhs=src[:, g, :],
            start=(g == 0), stop=(g == ntile - 1)),
            reads=[src_key, "c_ones_f"], writes=["psb2"], inc=(g == ntile - 1))
    for g in range(ntile):
        P.op("tensor", lambda e, g=g: e.matmul(
            out=ln["ps_b"], lhsT=C.ones_f[:], rhs=sq[:, g, :],
            start=(g == 0), stop=(g == ntile - 1)),
            reads=[sq_key, "c_ones_f"], writes=["psb3"], inc=(g == ntile - 1))
    nfeat = float(ntile * 128)
    P.op("vector", lambda e: e.tensor_scalar(
        out=rows[0:1, 0, :], in0=ln["ps_a"], scalar1=1.0 / nfeat, scalar2=None,
        op0=ALU.mult), reads=["psb2"], writes=[rk])
    P.op("vector", lambda e: e.tensor_scalar(
        out=rows[0:1, 1, :], in0=ln["ps_b"], scalar1=1.0 / nfeat, scalar2=None,
        op0=ALU.mult), reads=["psb3"], writes=[rk])
    P.op("vector", lambda e: e.tensor_tensor(
        out=rows[0:1, 2, :], in0=rows[0:1, 0, :], in1=rows[0:1, 0, :], op=ALU.mult),
        reads=[rk], writes=[rk])
    P.op("vector", lambda e: e.tensor_sub(
        out=rows[0:1, 1, :], in0=rows[0:1, 1, :], in1=rows[0:1, 2, :]),
        reads=[rk], writes=[rk])
    P.op("scalar", lambda e: e.activation(
        out=rows[0:1, 1, :], in_=rows[0:1, 1, :], func=AF.Sqrt, bias=C.eps_row[:]),
        reads=[rk, "c_eps_row"], writes=[rk])
    P.op("vector", lambda e: e.reciprocal(
        out=rows[0:1, 1, :], in_=rows[0:1, 1, :]), reads=[rk], writes=[rk])
    P.op("vector", lambda e: e.tensor_mul(
        out=rows[0:1, 2, :], in0=rows[0:1, 0, :], in1=rows[0:1, 1, :]),
        reads=[rk], writes=[rk])
    P.op("tensor", lambda e: e.matmul(
        out=ln["bc0"], lhsT=C.ones_row[:], rhs=rows[0:1, 1, :],
        start=True, stop=True), reads=[rk, "c_ones_row"], writes=["psb4"])
    P.op("tensor", lambda e: e.matmul(
        out=ln["bc1"], lhsT=C.ones_row[:], rhs=rows[0:1, 2, :],
        start=True, stop=True), reads=[rk, "c_ones_row"],
        writes=["psb5"], inc=True)


def emit_ln_apply(P, C, ln, src, src_key, dst, dst_key, tmp, tmp_key, ntile,
                  gamma_col=None, beta_col=None, gb_keys=(), add_to=None,
                  add_key=None):
    k = ln["k"]
    for g in range(ntile):
        P.op("vector", lambda e, g=g: e.tensor_mul(
            out=tmp[:, g, :], in0=src[:, g, :], in1=ln["bc0"]),
            reads=[src_key, "psb4"], writes=[tmp_key])
    if gamma_col is None and add_to is None:
        for g in range(ntile):
            P.op("vector", lambda e, g=g: e.tensor_sub(
                out=dst[:, g, :], in0=tmp[:, g, :], in1=ln["bc1"]),
                reads=[tmp_key, "psb5"], writes=[dst_key])
    elif gamma_col is not None:
        for g in range(ntile):
            P.op("vector", lambda e, g=g: e.tensor_sub(
                out=tmp[:, g, :], in0=tmp[:, g, :], in1=ln["bc1"]),
                reads=[tmp_key, "psb5"], writes=[tmp_key])
        for g in range(ntile):
            P.op("vector", lambda e, g=g: e.tensor_scalar(
                out=dst[:, g, :], in0=tmp[:, g, :],
                scalar1=gamma_col[:, g:g + 1], scalar2=beta_col[:, g:g + 1],
                op0=ALU.mult, op1=ALU.add),
                reads=[tmp_key] + list(gb_keys), writes=[dst_key])
    else:
        for g in range(ntile):
            P.op("vector", lambda e, g=g: e.tensor_sub(
                out=tmp[:, g, :], in0=tmp[:, g, :], in1=ln["bc1"]),
                reads=[tmp_key, "psb5"], writes=[tmp_key])
        for g in range(ntile):
            P.op("vector", lambda e, g=g: e.tensor_add(
                out=dst[:, g, :], in0=tmp[:, g, :], in1=add_to[:, g, :]),
                reads=[tmp_key, add_key], writes=[dst_key])


def load_x_chunk(P, xsrc_ap, xsrc_key, xc, xc_key, c0, TC):
    P.dma("sync", xc_key,
          lambda e: e.dma_start(
              out=xc[:], in_=xsrc_ap[:, c0:c0 + TC].rearrange("(g p) t -> p g t", p=128)),
          reads=[xsrc_key], writes=[xc_key])


def store_x_chunk(P, xdst_ap, xdst_key, xc, xc_key, c0, TC, chan):
    P.dma("sync", chan,
          lambda e: e.dma_start(
              out=xdst_ap[:, c0:c0 + TC].rearrange("(g p) t -> p g t", p=128), in_=xc[:]),
          reads=[xc_key], writes=[xdst_key])



def build_l1ffn(P: Prog, C: Common, din, dout, xin_key="dram_xin",
                xout_key="dram_xout"):
    NH = 32
    TC = 512

    s1c = P.sbuf("f_s1c", [128, NH], F32)
    s2c = P.sbuf("f_s2c", [128, NH], F32)
    sh1c = P.sbuf("f_sh1c", [128, NH], F32)
    sh2c = P.sbuf("f_sh2c", [128, NH], F32)
    s3c = P.sbuf("f_s3c", [128, NT], F32)
    for nm in ("s1c", "s2c", "sh1c", "sh2c", "s3c"):
        P.dma("sync", "f_" + nm,
              lambda e, nm=nm: e.dma_start(out=P.bufs["f_" + nm][:], in_=din[nm][:]),
              reads=[("dram", nm)], writes=["f_" + nm])

    xc = P.sbuf("f_xc", [128, NT, TC], F32)
    tmp = P.sbuf("f_tmp", [128, NT, TC], F32)
    xhat = P.sbuf("f_xhat", [128, NT, TC], BF16)
    t1 = P.sbuf("f_t1", [128, NH, TC], BF16)
    t2 = P.sbuf("f_t2", [128, NH, TC], BF16)
    y = P.sbuf("f_y", [128, NT, TC], F32)
    silu_t = P.sbuf("f_silu", [128, TC], F32)
    wbuf = [P.sbuf(f"f_wbuf{i}", [128, 32, 128], BF16) for i in range(2)]
    ps_mm = [P.psb(0), P.psb(1)]
    ln = alloc_ln(P, "f")

    for c0 in range(0, T, TC):
        load_x_chunk(P, din["xin"], xin_key, xc, "f_xc", c0, TC)
        emit_ln_stats(P, C, ln, xc, "f_xc", tmp, "f_tmp", NT)
        emit_ln_apply(P, C, ln, xc, "f_xc", xhat, "f_xhat", tmp, "f_tmp", NT)

        for ot in range(NH):
            wb, wbk = wbuf[ot % 2], f"f_wbuf{ot % 2}"
            P.dma("gpsimd", wbk + "a",
                  lambda e, ot=ot, wb=wb: e.dma_start(
                      out=wb[:, 0:8, :], in_=din["w1Tt"][ot]),
                  reads=[("dram", "w1Tt")], writes=[wbk + "a"])
            P.dma("gpsimd", wbk + "b",
                  lambda e, ot=ot, wb=wb: e.dma_start(
                      out=wb[:, 8:16, :], in_=din["w2Tt"][ot]),
                  reads=[("dram", "w2Tt")], writes=[wbk + "b"])
            for g in range(NT):
                P.op("tensor", lambda e, g=g, wb=wb: e.matmul(
                    out=ps_mm[0], lhsT=wb[:, g, :], rhs=xhat[:, g, :],
                    start=(g == 0), stop=(g == NT - 1)),
                    reads=[wbk + "a", "f_xhat"], writes=["psb0"], inc=(g == NT - 1))
            for g in range(NT):
                P.op("tensor", lambda e, g=g, wb=wb: e.matmul(
                    out=ps_mm[1], lhsT=wb[:, 8 + g, :], rhs=xhat[:, g, :],
                    start=(g == 0), stop=(g == NT - 1)),
                    reads=[wbk + "b", "f_xhat"], writes=["psb1"], inc=(g == NT - 1))
            P.op("scalar", lambda e, ot=ot: e.activation(
                out=silu_t[:], in_=ps_mm[0], func=AF.Silu, scale=s1c[:, ot:ot + 1]),
                reads=["psb0", "f_s1c"], writes=["f_silu"])
            P.op("vector", lambda e, ot=ot: e.scalar_tensor_tensor(
                out=t1[:, ot, :], in0=ps_mm[1], scalar=s2c[:, ot:ot + 1],
                in1=silu_t[:], op0=ALU.mult, op1=ALU.mult),
                reads=["psb1", "f_s2c", "f_silu"], writes=["f_t1"])

        for ot in range(NH):
            wb, wbk = wbuf[ot % 2], f"f_wbuf{ot % 2}"
            bk_ = ot % 2
            P.dma("gpsimd", wbk + "a",
                  lambda e, ot=ot, wb=wb: e.dma_start(
                      out=wb[:], in_=din["h1Tt"][ot]),
                  reads=[("dram", "h1Tt")], writes=[wbk + "a", wbk + "b"])
            for g in range(NH):
                P.op("tensor", lambda e, g=g, wb=wb, bk_=bk_: e.matmul(
                    out=ps_mm[bk_], lhsT=wb[:, g, :], rhs=t1[:, g, :],
                    start=(g == 0), stop=(g == NH - 1)),
                    reads=[wbk + "a", wbk + "b", "f_t1"], writes=[f"psb{bk_}"],
                    inc=(g == NH - 1))
            P.op("scalar", lambda e, ot=ot, bk_=bk_: e.activation(
                out=t2[:, ot, :], in_=ps_mm[bk_], func=AF.Gelu,
                scale=sh1c[:, ot:ot + 1]),
                reads=[f"psb{bk_}", "f_sh1c"], writes=["f_t2"])

        for ot in range(NH):
            wb, wbk = wbuf[ot % 2], f"f_wbuf{ot % 2}"
            bk_ = ot % 2
            P.dma("gpsimd", wbk + "a",
                  lambda e, ot=ot, wb=wb: e.dma_start(
                      out=wb[:], in_=din["h2Tt"][ot]),
                  reads=[("dram", "h2Tt")], writes=[wbk + "a", wbk + "b"])
            for g in range(NH):
                P.op("tensor", lambda e, g=g, wb=wb, bk_=bk_: e.matmul(
                    out=ps_mm[bk_], lhsT=wb[:, g, :], rhs=t2[:, g, :],
                    start=(g == 0), stop=(g == NH - 1)),
                    reads=[wbk + "a", wbk + "b", "f_t2"], writes=[f"psb{bk_}"],
                    inc=(g == NH - 1))
            P.op("scalar", lambda e, ot=ot, bk_=bk_: e.activation(
                out=t1[:, ot, :], in_=ps_mm[bk_], func=AF.Gelu,
                scale=sh2c[:, ot:ot + 1]),
                reads=[f"psb{bk_}", "f_sh2c"], writes=["f_t1"])

        for ot in range(NT):
            wb, wbk = wbuf[ot % 2], f"f_wbuf{ot % 2}"
            bk_ = ot % 2
            P.dma("gpsimd", wbk + "a",
                  lambda e, ot=ot, wb=wb: e.dma_start(
                      out=wb[:], in_=din["w3Tt"][ot]),
                  reads=[("dram", "w3Tt")], writes=[wbk + "a", wbk + "b"])
            for g in range(NH):
                P.op("tensor", lambda e, g=g, wb=wb, bk_=bk_: e.matmul(
                    out=ps_mm[bk_], lhsT=wb[:, g, :], rhs=t1[:, g, :],
                    start=(g == 0), stop=(g == NH - 1)),
                    reads=[wbk + "a", wbk + "b", "f_t1"], writes=[f"psb{bk_}"],
                    inc=(g == NH - 1))
            P.op("vector", lambda e, ot=ot, bk_=bk_: e.tensor_scalar(
                out=y[:, ot, :], in0=ps_mm[bk_], scalar1=s3c[:, ot:ot + 1],
                scalar2=None, op0=ALU.mult),
                reads=[f"psb{bk_}", "f_s3c"], writes=["f_y"])

        emit_ln_stats(P, C, ln, y, "f_y", tmp, "f_tmp", NT)
        emit_ln_apply(P, C, ln, y, "f_y", xc, "f_xc", tmp, "f_tmp", NT,
                      add_to=xc, add_key="f_xc")
        store_x_chunk(P, dout["xout"], xout_key, xc, "f_xc", c0, TC, "st_xout")


def _tile_lhs(wT):
    K, M = wT.shape
    return np.ascontiguousarray(
        wT.reshape(K // 128, 128, M // 128, 128).transpose(2, 1, 0, 3))


def host_prep_l1ffn(inputs):
    f32, c = np.float32, np.ascontiguousarray
    d = {}
    d["w1Tt"] = _tile_lhs(inputs["L1_ffn_w1_w"].T.astype(f32))
    d["w2Tt"] = _tile_lhs(inputs["L1_ffn_w2_w"].T.astype(f32))
    d["h1Tt"] = _tile_lhs(inputs["L1_ffn_h1_w"].T.astype(f32))
    d["h2Tt"] = _tile_lhs(inputs["L1_ffn_h2_w"].T.astype(f32))
    d["w3Tt"] = _tile_lhs(inputs["L1_ffn_w3_w"].T.astype(f32))
    d["s1c"] = c(np.asarray(inputs["L1_ffn_w1_s"], f32).reshape(32, 128).T)
    d["s2c"] = c(np.asarray(inputs["L1_ffn_w2_s"], f32).reshape(32, 128).T)
    d["sh1c"] = c(np.asarray(inputs["L1_ffn_h1_s"], f32).reshape(32, 128).T)
    d["sh2c"] = c(np.asarray(inputs["L1_ffn_h2_s"], f32).reshape(32, 128).T)
    d["s3c"] = c(np.asarray(inputs["L1_ffn_w3_s"], f32).reshape(8, 128).T)
    return d


L1FFN_SHAPES = {
    "w1Tt": [32, 128, 8, 128], "w2Tt": [32, 128, 8, 128],
    "h1Tt": [32, 128, 32, 128], "h2Tt": [32, 128, 32, 128],
    "w3Tt": [8, 128, 32, 128], "s1c": [128, 32],
    "s2c": [128, 32], "sh1c": [128, 32], "sh2c": [128, 32], "s3c": [128, 8],
}


def build_l1ffn_program():
    nc = bass.Bass("TRN2", target_bir_lowering=False)
    din = {"xin": nc.dram_tensor("xin", [D, T], F32, kind="ExternalInput").ap()}
    for nm, shp in L1FFN_SHAPES.items():
        din[nm] = nc.dram_tensor(nm, shp, F32, kind="ExternalInput").ap()
    dout = {"xout": nc.dram_tensor("xout", [D, T], F32, kind="ExternalOutput").ap()}
    P = Prog(nc)
    P.make_pool()
    C = Common(P)
    build_l1ffn(P, C, din, dout)
    P.final_wait("sync", ["st_xout"])
    P.build()
    return nc





def build_l0ffn(P: Prog, C: Common, din, dout, xin_key="dram_xin",
                xout_key="dram_xout"):
    TC = 512

    xc = P.sbuf("q_xc", [128, NT, TC], F32)
    tmp = P.sbuf("q_tmp", [128, NT, TC], F32)
    hbf = P.sbuf("q_hbf", [128, NT, T], BF16)
    y1bf = P.sbuf("q_y1bf", [128, NT, T], BF16)
    yc = P.sbuf("q_yc", [128, NT, TC], F32)
    qraw = P.sbuf("q_qraw", [128, 1024], I32)
    sbc = P.sbuf("q_sbc", [128, 1024], F32)
    wbf = [P.sbuf(f"q_wbf{i}", [128, NT, 1024], BF16) for i in range(2)]
    u = P.sbuf("q_u", [32, T], BF16)
    lat = P.sbuf("q_lat", [128, NT, 32], BF16)
    lbt = P.sbuf("q_lbt", [32, NT, 128], BF16)
    bcol = P.sbuf("q_bcol", [128, 2, NT], F32)
    gcol = P.sbuf("q_gcol", [128, NT], F32)
    becol = P.sbuf("q_becol", [128, NT], F32)
    ps_u = P.pspool[0:32, 6, :]
    ps_mm = [P.psb(0), P.psb(1)]
    ln = alloc_ln(P, "q")

    for ci, c0 in enumerate(range(0, T, TC)):
        load_x_chunk(P, din["xin"], xin_key, xc, "q_xc", c0, TC)
        emit_ln_stats(P, C, ln, xc, "q_xc", tmp, "q_tmp", NT)
        emit_ln_apply(P, C, ln, xc, "q_xc", yc, "q_yc", tmp, "q_tmp", NT)
        for g in range(NT):
            P.op("vector", lambda e, g=g, c0=c0: e.tensor_copy(
                out=hbf[:, g, c0:c0 + TC], in_=yc[:, g, :]),
                reads=["q_yc"], writes=["q_hbf"])
        store_x_chunk(P, din["hdram"], ("dram", "hdram"), yc, "q_yc", c0, TC, "st_h")

    def dequant(layer, wslot):
        wk = f"q_wbf{wslot}"
        for c in range(NT):
            P.dma("sync", "q_qraw",
                  lambda e, layer=layer, c=c: e.dma_start(
                      out=qraw[:], in_=din["qT"][layer, c * 128:(c + 1) * 128, :]),
                  reads=[("dram", "qT")], writes=["q_qraw"])
            P.dma("sync", "q_sbc",
                  lambda e, layer=layer, c=c: e.dma_start(
                      out=sbc[:],
                      in_=din["sT"][layer, c * 8:(c + 1) * 8, :][:, None, :]
                          .broadcast_to([8, 16, 1024])),
                  reads=[("dram", "sT")], writes=["q_sbc"])
            P.op("vector", lambda e, c=c: e.tensor_scalar(
                out=tmp[:, 0:2, :].rearrange("p a b -> p (a b)"), in0=qraw[:],
                scalar1=1.0 / 7.5, scalar2=-1.0,
                op0=ALU.mult, op1=ALU.add),
                reads=["q_qraw"], writes=["q_tmp"])
            P.op("vector", lambda e, c=c, wslot=wslot: e.tensor_mul(
                out=wbf[wslot][:, c, :], in0=tmp[:, 0:2, :].rearrange("p a b -> p (a b)"),
                in1=sbc[:]),
                reads=["q_tmp", "q_sbc"], writes=[wk])

    def load_lora(layer):
        P.dma("gpsimd", "q_lat",
              lambda e, layer=layer: e.dma_start(
                  out=lat[:], in_=din["laTt"][layer]),
              reads=[("dram", "laTt")], writes=["q_lat"])
        P.dma("gpsimd", "q_lbt",
              lambda e, layer=layer: e.dma_start(
                  out=lbt[:], in_=din["lbT"][layer].rearrange("r (c p) -> r c p", p=128)),
              reads=[("dram", "lbT")], writes=["q_lbt"])

    def lora_mid(rhs, rhs_key):
        for sl in range(T // 512):
            for g in range(NT):
                P.op("tensor", lambda e, sl=sl, g=g: e.matmul(
                    out=ps_u, lhsT=lat[:, g, :], rhs=rhs[:, g, sl * 512:(sl + 1) * 512],
                    start=(g == 0), stop=(g == NT - 1)),
                    reads=["q_lat", rhs_key], writes=["psb6"], inc=(g == NT - 1))
            P.op("scalar", lambda e, sl=sl: e.copy(
                out=u[:, sl * 512:(sl + 1) * 512], in_=ps_u),
                reads=["psb6"], writes=["q_u"])

    def qlora_mm(wslot, rhs, rhs_key, c0, bslot):
        wk = f"q_wbf{wslot}"
        for sl in range(TC // 512):
            s0 = c0 + sl * 512
            for ot in range(NT):
                pm = ps_mm[ot % 2]
                pmk = f"psb{ot % 2}"
                for g in range(NT):
                    P.op("tensor", lambda e, g=g, ot=ot, s0=s0, pm=pm, wslot=wslot: e.matmul(
                        out=pm, lhsT=wbf[wslot][:, g, ot * 128:(ot + 1) * 128],
                        rhs=rhs[:, g, s0:s0 + 512],
                        start=(g == 0), stop=False),
                        reads=[wk, rhs_key], writes=[pmk], inc=False)
                P.op("tensor", lambda e, ot=ot, s0=s0, pm=pm: e.matmul(
                    out=pm, lhsT=lbt[:, ot, :], rhs=u[:, s0:s0 + 512],
                    start=False, stop=True),
                    reads=["q_lbt", "q_u", pmk], writes=[pmk], inc=True)
                P.op("vector", lambda e, ot=ot, sl=sl, pm=pm, bslot=bslot: e.tensor_scalar(
                    out=yc[:, ot, sl * 512:(sl + 1) * 512], in0=pm,
                    scalar1=bcol[:, bslot, ot:ot + 1], scalar2=None, op0=ALU.add),
                    reads=[pmk, "q_bcol"], writes=["q_yc"])

    for blk in range(3):
        l0, l1_ = 2 * blk, 2 * blk + 1
        dequant(l0, 0)
        load_lora(l0)
        P.dma("gpsimd", "q_bcol",
              lambda e, l0=l0: e.dma_start(
                  out=bcol[:], in_=din["bcol"][l0:l0 + 2].rearrange("a p c -> p a c")),
              reads=[("dram", "bcol")], writes=["q_bcol"])
        P.dma("gpsimd", "q_gcol",
              lambda e, blk=blk: e.dma_start(out=gcol[:], in_=din["gcol"][blk]),
              reads=[("dram", "gcol")], writes=["q_gcol"])
        P.dma("gpsimd", "q_becol",
              lambda e, blk=blk: e.dma_start(out=becol[:], in_=din["becol"][blk]),
              reads=[("dram", "becol")], writes=["q_becol"])
        lora_mid(hbf, "q_hbf")
        for c0 in range(0, T, TC):
            qlora_mm(0, hbf, "q_hbf", c0, 0)
            emit_ln_stats(P, C, ln, yc, "q_yc", tmp, "q_tmp", NT)
            emit_ln_apply(P, C, ln, yc, "q_yc", yc, "q_yc", tmp, "q_tmp", NT,
                          gamma_col=gcol, beta_col=becol,
                          gb_keys=["q_gcol", "q_becol"])
            for g in range(NT):
                P.op("scalar", lambda e, g=g, c0=c0: e.activation(
                    out=y1bf[:, g, c0:c0 + TC], in_=yc[:, g, :], func=AF.Gelu),
                    reads=["q_yc"], writes=["q_y1bf"])
        dequant(l1_, 1)
        load_lora(l1_)
        lora_mid(y1bf, "q_y1bf")
        last = blk == 2
        for c0 in range(0, T, TC):
            qlora_mm(1, y1bf, "q_y1bf", c0, 1)
            load_x_chunk(P, din["hdram"], ("dram", "hdram"), xc, "q_xc", c0, TC)
            for g in range(NT):
                P.op("vector", lambda e, g=g: e.tensor_add(
                    out=yc[:, g, :], in0=yc[:, g, :], in1=xc[:, g, :]),
                    reads=["q_yc", "q_xc"], writes=["q_yc"])
            if not last:
                for g in range(NT):
                    P.op("vector", lambda e, g=g, c0=c0: e.tensor_copy(
                        out=hbf[:, g, c0:c0 + TC], in_=yc[:, g, :]),
                        reads=["q_yc"], writes=["q_hbf"])
                store_x_chunk(P, din["hdram"], ("dram", "hdram"), yc, "q_yc", c0,
                              TC, "st_h")
            else:
                emit_ln_stats(P, C, ln, yc, "q_yc", tmp, "q_tmp", NT)
                load_x_chunk(P, din["xin"], xin_key, xc, "q_xc", c0, TC)
                emit_ln_apply(P, C, ln, yc, "q_yc", xc, "q_xc", tmp, "q_tmp", NT,
                              add_to=xc, add_key="q_xc")
                store_x_chunk(P, dout["xout"], xout_key, xc, "q_xc", c0, TC,
                              "st_xout")


def host_prep_l0ffn(inputs):
    f32, c = np.float32, np.ascontiguousarray
    d = {}
    d["qT"] = c(np.asarray(inputs["L0_ffn_qweights"], np.int32).transpose(0, 2, 1))
    d["sT"] = c(np.asarray(inputs["L0_ffn_scales"], f32).transpose(0, 2, 1))
    d["bcol"] = c(np.asarray(inputs["L0_ffn_biases"], f32).reshape(6, 8, 128)
                  .transpose(0, 2, 1))
    laT = np.asarray(inputs["L0_ffn_lora_a"], f32).transpose(0, 2, 1)
    d["laTt"] = c(laT.reshape(6, 8, 128, 32).transpose(0, 2, 1, 3))
    d["lbT"] = c(np.asarray(inputs["L0_ffn_lora_b"], f32).transpose(0, 2, 1))
    d["gcol"] = c(np.asarray(inputs["L0_ffn_ln_gamma"], f32).reshape(3, 8, 128)
                  .transpose(0, 2, 1))
    d["becol"] = c(np.asarray(inputs["L0_ffn_ln_beta"], f32).reshape(3, 8, 128)
                   .transpose(0, 2, 1))
    return d


L0FFN_SHAPES = {
    "qT": ([6, 1024, 1024], I32), "sT": ([6, 64, 1024], F32),
    "bcol": ([6, 128, 8], F32), "laTt": ([6, 128, 8, 32], F32),
    "lbT": ([6, 32, 1024], F32), "gcol": ([3, 128, 8], F32),
    "becol": ([3, 128, 8], F32),
}


def build_l0ffn_program():
    nc = bass.Bass("TRN2", target_bir_lowering=False)
    din = {"xin": nc.dram_tensor("xin", [D, T], F32, kind="ExternalInput").ap()}
    for nm, (shp, dt) in L0FFN_SHAPES.items():
        din[nm] = nc.dram_tensor(nm, shp, dt, kind="ExternalInput").ap()
    din["hdram"] = nc.dram_tensor("hdram", [D, T], F32, kind="Internal").ap()
    dout = {"xout": nc.dram_tensor("xout", [D, T], F32, kind="ExternalOutput").ap()}
    P = Prog(nc)
    P.make_pool()
    C = Common(P)
    build_l0ffn(P, C, din, dout)
    P.final_wait("sync", ["st_xout"])
    P.build()
    return nc





def build_l1attn(P: Prog, C: Common, din, dout, xin_key="dram_xin",
                 xout_key="dram_xout"):
    TC = 512
    NIC = T // 128
    W = 2176

    xc = P.sbuf("a_xc", [128, NT, TC], F32)
    tmp = P.sbuf("a_tmp", [128, NT, TC], F32)
    xhat = P.sbuf("a_xhat", [128, NT, T], BF16)
    wp3 = P.sbuf("a_wp3", [128, NT, 3, 128], BF16)
    qp = P.sbuf("a_qp", [128, T], BF16)
    kp = P.sbuf("a_kp", [128, T], BF16)
    vp = P.sbuf("a_vp", [128, NIC, 128], BF16)
    tab2 = P.sbuf("a_tab2", [128, 4096], BF16)
    relw = P.sbuf("a_relw", [128, 2, W], BF16)
    rpe = P.sbuf("a_rpe", [128, 4, T], BF16)
    praw = P.sbuf("a_praw", [128, 2, T], BF16)
    ptb4 = P.sbuf("a_ptb4", [128, NIC, 512], BF16)
    wo_sb = ptb4.rearrange("p a b -> p (a b)")
    mrow = P.sbuf("a_mrow", [128, 2, 8], F32)
    ctx = P.sbuf("a_ctx", [128, NT, T], BF16)
    yc = P.sbuf("a_yc", [128, NT, TC], F32)
    sfull = [yc[:, 0:4, :].rearrange("p a b -> p (a b)"),
             yc[:, 4:8, :].rearrange("p a b -> p (a b)")]
    bqc = P.sbuf("a_bqc", [128, 8], F32)
    bkc = P.sbuf("a_bkc", [128, 8], F32)
    boc = P.sbuf("a_boc", [128, 8], F32)
    bvb = P.sbuf("a_bvb", [128, 1024], F32)
    identbf = P.sbuf("a_ident", [128, 128], BF16)
    ln = alloc_ln(P, "a")

    for nm, buf in (("bqc", bqc), ("bkc", bkc), ("boc", boc)):
        P.dma("sync", "a_" + nm,
              lambda e, nm=nm, buf=buf: e.dma_start(out=buf[:], in_=din[nm][:]),
              reads=[("dram", nm)], writes=["a_" + nm])
    P.dma("sync", "a_bvb",
          lambda e: e.dma_start(out=bvb[:], in_=din["bvrow"][:].broadcast_to([128, 1024])),
          reads=[("dram", "bvrow")], writes=["a_bvb"])
    P.dma("sync", "a_ident",
          lambda e: e.dma_start(out=identbf[:], in_=din["identbf"][:]),
          reads=[("dram", "identbf")], writes=["a_ident"])
    P.op("vector", lambda e: e.memset(tab2[:], 0.0), writes=["a_tab2"])

    for c0 in range(0, T, TC):
        load_x_chunk(P, din["xin"], xin_key, xc, "a_xc", c0, TC)
        emit_ln_stats(P, C, ln, xc, "a_xc", tmp, "a_tmp", NT)
        emit_ln_apply(P, C, ln, xc, "a_xc",
                      xhat[:, :, c0:c0 + TC], "a_xhat", tmp, "a_tmp", NT)

    for hp in range(8):
        for wi, wnm in enumerate(("wqTt", "wkTt", "wvTt")):
            P.dma("gpsimd", "a_wp3",
                  lambda e, wi=wi, wnm=wnm, hp=hp: e.dma_start(
                      out=wp3[:, :, wi, :], in_=din[wnm][hp]),
                  reads=[("dram", wnm)], writes=["a_wp3"])
        for hh in range(2):
            h = hp * 2 + hh
            P.dma("gpsimd", "a_tab2",
                  lambda e, h=h, hh=hh: e.dma_start(
                      out=tab2[64 * hh:64 * hh + 64, 0:4095], in_=din["tabTrev"][h]),
                  reads=[("dram", "tabTrev")], writes=["a_tab2"])
        for ts in range(4):
            for g in range(NT):
                P.op("tensor", lambda e, g=g, ts=ts: e.matmul(
                    out=P.psb(0), lhsT=wp3[:, g, 0, :], rhs=xhat[:, g, ts * 512:(ts + 1) * 512],
                    start=(g == 0), stop=(g == NT - 1)),
                    reads=["a_wp3", "a_xhat"], writes=["psb0"], inc=(g == NT - 1))
            P.op("vector", lambda e, ts=ts, hp=hp: e.tensor_scalar(
                out=qp[:, ts * 512:(ts + 1) * 512], in0=P.psb(0),
                scalar1=bqc[:, hp:hp + 1], scalar2=None, op0=ALU.add),
                reads=["psb0", "a_bqc"], writes=["a_qp"])
            for g in range(NT):
                P.op("tensor", lambda e, g=g, ts=ts: e.matmul(
                    out=P.psb(1), lhsT=wp3[:, g, 1, :], rhs=xhat[:, g, ts * 512:(ts + 1) * 512],
                    start=(g == 0), stop=(g == NT - 1)),
                    reads=["a_wp3", "a_xhat"], writes=["psb1"], inc=(g == NT - 1))
            P.op("vector", lambda e, ts=ts, hp=hp: e.tensor_scalar(
                out=kp[:, ts * 512:(ts + 1) * 512], in0=P.psb(1),
                scalar1=bkc[:, hp:hp + 1], scalar2=0.125, op0=ALU.add, op1=ALU.mult),
                reads=["psb1", "a_bkc"], writes=["a_kp"])
        for tc in range(NIC):
            for g in range(NT):
                P.op("tensor", lambda e, g=g, tc=tc: e.matmul(
                    out=P.pspool[:, 2, 0:128], lhsT=xhat[:, g, tc * 128:(tc + 1) * 128],
                    rhs=wp3[:, g, 2, :],
                    start=(g == 0), stop=(g == NT - 1)),
                    reads=["a_wp3", "a_xhat"], writes=["psb2"], inc=(g == NT - 1))
            P.op("vector", lambda e, tc=tc, hp=hp: e.tensor_add(
                out=vp[:, tc, :], in0=P.pspool[:, 2, 0:128],
                in1=bvb[:, hp * 128:(hp + 1) * 128]),
                reads=["psb2", "a_bvb"], writes=["a_vp"])

        for hh in range(2):
            h = hp * 2 + hh
            e0 = 64 * hh

            def emit_relw(ic, e0=e0):
                i0 = ic * 128
                s0w = 1920 - i0
                pi = ic % 2
                pr = ic % 4
                for rs in range(5):
                    w = 512 if rs < 4 else 128
                    bk_ = 4 + rs % 2
                    P.op("tensor", lambda e, rs=rs, w=w, bk_=bk_, e0=e0, i0=i0, s0w=s0w: e.matmul(
                        out=P.pspool[:, bk_, 0:w],
                        lhsT=qp[e0:e0 + 64, i0:i0 + 128],
                        rhs=tab2[e0:e0 + 64, s0w + rs * 512:s0w + rs * 512 + w],
                        start=True, stop=True),
                        reads=["a_qp", "a_tab2"], writes=[f"psb{bk_}"])
                    P.op("vector", lambda e, rs=rs, w=w, bk_=bk_, pi=pi: e.tensor_copy(
                        out=relw[:, pi, rs * 512:rs * 512 + w], in_=P.pspool[:, bk_, 0:w]),
                        reads=[f"psb{bk_}"], writes=[f"a_relw{pi}"])
                P.dma("sync", f"a_reld{pi}",
                      lambda e, pi=pi: e.dma_start(out=din["reld"][pi], in_=relw[:, pi, :]),
                      reads=[f"a_relw{pi}"], writes=[("dram", f"reld{pi}")])
                shear = bass.AP(tensor=din["reld"].tensor, offset=pi * 128 * W + 127,
                                ap=[[W - 1, 128], [1, T]])
                P.dma("sync", f"a_rpe{pr}",
                      lambda e, shear=shear, pr=pr: e.dma_start(
                          out=rpe[:, pr, :], in_=shear),
                      reads=[("dram", f"reld{pi}")], writes=[f"a_rpe{pr}"])

            def emit_scores(ic, e0=e0):
                i0 = ic * 128
                pi = ic % 2
                pr = ic % 4
                for js in range(4):
                    P.op("tensor", lambda e, js=js, i0=i0, e0=e0: e.matmul(
                        out=P.psb(js), lhsT=qp[e0:e0 + 64, i0:i0 + 128],
                        rhs=kp[e0:e0 + 64, js * 512:(js + 1) * 512],
                        start=True, stop=False),
                        reads=["a_qp", "a_kp"], writes=[f"psb{js}"], inc=False)
                    P.op("tensor", lambda e, js=js, pr=pr: e.matmul(
                        out=P.psb(js), lhsT=identbf[:],
                        rhs=rpe[:, pr, js * 512:(js + 1) * 512],
                        start=False, stop=True),
                        reads=["a_ident", f"a_rpe{pr}", f"psb{js}"], writes=[f"psb{js}"])
                    P.op("vector", lambda e, js=js, pi=pi: e.reduce_max(
                        out=mrow[:, pi, js:js + 1], in_=P.psb(js), axis=AX.X),
                        reads=[f"psb{js}"], writes=[f"a_mrow{pi}"])
                P.op("vector", lambda e, pi=pi: e.tensor_max(
                    out=mrow[:, pi, 4:5], in0=mrow[:, pi, 0:1], in1=mrow[:, pi, 1:2]),
                    reads=[f"a_mrow{pi}"], writes=[f"a_mrow{pi}"])
                P.op("vector", lambda e, pi=pi: e.tensor_max(
                    out=mrow[:, pi, 5:6], in0=mrow[:, pi, 2:3], in1=mrow[:, pi, 3:4]),
                    reads=[f"a_mrow{pi}"], writes=[f"a_mrow{pi}"])
                P.op("vector", lambda e, pi=pi: e.tensor_max(
                    out=mrow[:, pi, 4:5], in0=mrow[:, pi, 4:5], in1=mrow[:, pi, 5:6]),
                    reads=[f"a_mrow{pi}"], writes=[f"a_mrow{pi}"])
                P.op("vector", lambda e, pi=pi: e.tensor_scalar(
                    out=mrow[:, pi, 4:5], in0=mrow[:, pi, 4:5], scalar1=-1.0,
                    scalar2=None, op0=ALU.mult),
                    reads=[f"a_mrow{pi}"], writes=[f"a_mrow{pi}"])
                P.op("scalar", lambda e, pi=pi: e.activation(
                    out=praw[:, pi, :], in_=P.psw(0, 4), func=AF.Exp,
                    bias=mrow[:, pi, 4:5], accum_out=mrow[:, pi, 6:7]),
                    reads=["psb0", "psb1", "psb2", "psb3", f"a_mrow{pi}"],
                    writes=[f"a_praw{pi}", f"a_mrow{pi}"])

            def emit_post(ic):
                pi = ic % 2
                P.op("vector", lambda e, pi=pi: e.reciprocal(
                    out=mrow[:, pi, 7:8], in_=mrow[:, pi, 6:7]),
                    reads=[f"a_mrow{pi}"], writes=[f"a_mrow{pi}"])
                P.op("vector", lambda e, pi=pi: e.tensor_scalar(
                    out=praw[:, pi, :], in0=praw[:, pi, :], scalar1=mrow[:, pi, 7:8],
                    scalar2=None, op0=ALU.mult),
                    reads=[f"a_praw{pi}", f"a_mrow{pi}"], writes=[f"a_praw{pi}"])

            def emit_trans(ic):
                pi = ic % 2
                icq = ic % 4
                for kc in range(NIC):
                    tb = 6 + (kc // 4) % 2
                    P.op("tensor", lambda e, kc=kc, pi=pi, tb=tb: e.matmul(
                        out=P.pspool[:, tb, (kc % 4) * 128:(kc % 4) * 128 + 128],
                        lhsT=praw[:, pi, kc * 128:(kc + 1) * 128], rhs=identbf[:],
                        start=True, stop=True),
                        reads=[f"a_praw{pi}", "a_ident"], writes=[f"psb{tb}"],
                        inc=(kc % 4 == 3))
                    if kc % 4 == 3:
                        g4 = kc // 4
                        P.op("scalar", lambda e, g4=g4, icq=icq, tb=tb: e.copy(
                            out=ptb4[:, 4 * g4:4 * g4 + 4, icq * 128:icq * 128 + 128],
                            in_=P.pspool[:, tb, :].rearrange("p (a b) -> p a b", a=4)),
                            reads=[f"psb{tb}"], writes=["a_ptb4"])

            def emit_ctx(icg, e0=e0, hp=hp):
                for jc in range(NIC):
                    P.op("tensor", lambda e, jc=jc, e0=e0: e.matmul(
                        out=P.pspool[0:64, 6, 0:512], lhsT=vp[:, jc, e0:e0 + 64],
                        rhs=ptb4[:, jc, :],
                        start=(jc == 0), stop=(jc == NIC - 1)),
                        reads=["a_vp", "a_ptb4"], writes=["psb6"],
                        inc=(jc == NIC - 1))
                P.op("scalar", lambda e, icg=icg, e0=e0, hp=hp: e.copy(
                    out=ctx[e0:e0 + 64, hp, icg * 512:(icg + 1) * 512],
                    in_=P.pspool[0:64, 6, 0:512]),
                    reads=["psb6"], writes=["a_ctx"])

            for ic in range(-2, NIC + 1):
                if ic >= 1:
                    emit_post(ic - 1)
                if 0 <= ic < NIC:
                    emit_scores(ic)
                if ic + 2 < NIC:
                    emit_relw(ic + 2)
                if ic >= 1:
                    emit_trans(ic - 1)
                    if (ic - 1) % 4 == 3:
                        emit_ctx((ic - 1) // 4)

    P.dma("gpsimd", "a_wo",
          lambda e: e.dma_start(out=wo_sb[:], in_=din["woTt"][:]),
          reads=[("dram", "woTt"), "a_ptb4"], writes=["a_wo", "a_ptb4"])
    for ci, c0 in enumerate(range(0, T, TC)):
        for et in range(NT):
            bk_ = et % 2
            for g in range(NT):
                P.op("tensor", lambda e, g=g, et=et, ci=ci, bk_=bk_: e.matmul(
                    out=P.psb(bk_),
                    lhsT=wo_sb[:, g * 1024 + et * 128:g * 1024 + et * 128 + 128],
                    rhs=ctx[:, g, ci * 512:(ci + 1) * 512],
                    start=(g == 0), stop=(g == NT - 1)),
                    reads=["a_wo", "a_ctx"], writes=[f"psb{bk_}"], inc=(g == NT - 1))
            P.op("vector", lambda e, et=et, bk_=bk_: e.tensor_scalar(
                out=yc[:, et, :], in0=P.psb(bk_), scalar1=boc[:, et:et + 1],
                scalar2=None, op0=ALU.add),
                reads=[f"psb{bk_}", "a_boc"], writes=["a_yc"])
        emit_ln_stats(P, C, ln, yc, "a_yc", tmp, "a_tmp", NT)
        load_x_chunk(P, din["xin"], xin_key, xc, "a_xc", c0, TC)
        emit_ln_apply(P, C, ln, yc, "a_yc", xc, "a_xc", tmp, "a_tmp", NT,
                      add_to=xc, add_key="a_xc")
        store_x_chunk(P, dout["xout"], xout_key, xc, "a_xc", c0, TC, "st_xout")


def host_prep_l1attn(inputs):
    f32, c = np.float32, np.ascontiguousarray
    d = {}
    d["wqTt"] = _tile_lhs(np.asarray(inputs["L1_attn_wq"], f32).T)
    d["wkTt"] = _tile_lhs(np.asarray(inputs["L1_attn_wk"], f32).T)
    d["wvTt"] = _tile_lhs(np.asarray(inputs["L1_attn_wv"], f32).T)
    woT = np.asarray(inputs["L1_attn_wo"], f32).T
    d["woTt"] = c(woT.reshape(8, 128, 1024).transpose(1, 0, 2))
    d["bqc"] = c(np.asarray(inputs["L1_attn_bq"], f32).reshape(8, 128).T)
    d["bkc"] = c(np.asarray(inputs["L1_attn_bk"], f32).reshape(8, 128).T)
    d["boc"] = c(np.asarray(inputs["L1_attn_bo"], f32).reshape(8, 128).T)
    d["bvrow"] = c(np.asarray(inputs["L1_attn_bv"], f32).reshape(1, 1024))
    tab = np.asarray(inputs["L1_attn_rpe_table"], f32)[:, :2 * T - 1]
    d["tabTrev"] = c(tab[:, ::-1].transpose(0, 2, 1))
    d["identbf"] = np.eye(128, dtype=BF16NP)
    return d


L1ATTN_SHAPES = {
    "wqTt": ([8, 128, 8, 128], F32), "wkTt": ([8, 128, 8, 128], F32),
    "wvTt": ([8, 128, 8, 128], F32), "woTt": ([128, 8, 1024], F32),
    "bqc": ([128, 8], F32), "bkc": ([128, 8], F32), "boc": ([128, 8], F32),
    "bvrow": ([1, 1024], F32), "tabTrev": ([16, 64, 4095], F32),
    "identbf": ([128, 128], BF16),
}


def build_l1attn_program():
    nc = bass.Bass("TRN2", target_bir_lowering=False)
    din = {"xin": nc.dram_tensor("xin", [D, T], F32, kind="ExternalInput").ap()}
    for nm, (shp, dt) in L1ATTN_SHAPES.items():
        din[nm] = nc.dram_tensor(nm, shp, dt, kind="ExternalInput").ap()
    din["reld"] = nc.dram_tensor("reld", [2, 128, 2176], BF16, kind="Internal").ap()
    dout = {"xout": nc.dram_tensor("xout", [D, T], F32, kind="ExternalOutput").ap()}
    P = Prog(nc)
    P.make_pool()
    C = Common(P)
    build_l1attn(P, C, din, dout)
    P.final_wait("sync", ["st_xout"])
    P.build()
    return nc





PI = float(np.pi)


def build_l0attn(P: Prog, C: Common, din, dout, xin_key="dram_xin",
                 xout_key="dram_xout"):
    TCC = 512
    NCH = T // TCC
    CT = 17

    xc = P.sbuf("g_xc", [128, NT, TCC], F32)
    tmp = P.sbuf("g_tmp", [128, NT, TCC], F32)
    xhat = P.sbuf("g_xhat", [128, NT, TCC], BF16)
    wproj = P.sbuf("g_wproj", [128, NT, 2080], BF16)
    outw = [P.sbuf(f"g_outw{i}", [128, NT, 128], BF16) for i in range(2)]
    zbuf = P.sbuf("g_zbuf", [128, CT, TCC + 3], F32)
    z2c = P.sbuf("g_z2c", [128, 9, TCC], F32)
    gate = P.sbuf("g_gate", [128, NT, TCC], BF16)
    scr = P.sbuf("g_scr", [128, 2, TCC], F32)
    sqt = P.sbuf("g_sqt", [128, 2, TCC], F32)
    convw = P.sbuf("g_convw", [128, CT, 4], F32)
    convb = P.sbuf("g_convb", [128, CT], F32)
    thc = P.sbuf("g_thc", [128, 4, NT], F32)
    slc = P.sbuf("g_slc", [32, 4], F32)
    sscol = P.sbuf("g_sscol", [32, 1], F32)
    dlog = P.sbuf("g_dlog", [1, 4], F32)
    wcols = P.sbuf("g_wcols", [128, 4], F32)
    nscol = P.sbuf("g_nscol", [32, 8], F32)
    wre = P.sbuf("g_wre", [32, 8, 32], F32)
    bd = P.sbuf("g_bd", [128, 8, 128], BF16)
    selg = P.sbuf("g_selg", [32, NT, 128], F32)
    sel4 = P.sbuf("g_sel4", [4, 128], F32)
    o32 = P.sbuf("g_o32", [128, 4], F32)
    dist = P.sbuf("g_dist", [32, 512], F32)
    twc = P.sbuf("g_twc", [32, TCC], F32)
    pw = P.sbuf("g_pw", [32, TCC], F32)
    denc = P.sbuf("g_denc", [32, TCC], F32)
    dstage = P.sbuf("g_dstage", [4, TCC], F32)
    sstage = P.sbuf("g_sstage", [4, 2, TCC], F32)
    dcarry = P.sbuf("g_dcarry", [32, 2], F32)
    carry = P.sbuf("g_carry", [128, 64], F32)
    numf = P.sbuf("g_numf", [128, 8, TCC], F32)
    numbf = P.sbuf("g_numbf", [128, 8, TCC], BF16)
    ygate = P.sbuf("g_ygate", [128, NT, TCC], BF16)
    zeros = P.sbuf("g_zeros", [128, TCC], F32)
    negpi = P.sbuf("g_negpi", [128, 1], F32)
    pihalf = P.sbuf("g_pihalf", [128, 1], F32)
    onecol = P.sbuf("g_onecol", [32, 1], F32)

    P.dma("sync", "g_convw", lambda e: e.dma_start(
        out=convw[:], in_=din["convT"].rearrange("(c p) j -> p c j", p=128)),
        reads=[("dram", "convT")], writes=["g_convw"])
    P.dma("sync", "g_convb", lambda e: e.dma_start(
        out=convb[:], in_=din["convb"][:]),
        reads=[("dram", "convb")], writes=["g_convb"])
    P.dma("sync", "g_thc", lambda e: e.dma_start(
        out=thc[:], in_=din["thetac"][:]),
        reads=[("dram", "thetac")], writes=["g_thc"])
    P.dma("sync", "g_slc", lambda e: e.dma_start(
        out=slc[:, 0:1], in_=din["slopes"][:]),
        reads=[("dram", "slopes")], writes=["g_slc"])
    P.dma("sync", "g_sscol", lambda e: e.dma_start(
        out=sscol[:], in_=din["sscol"][:]),
        reads=[("dram", "sscol")], writes=["g_sscol"])
    P.dma("sync", "g_dlog", lambda e: e.dma_start(
        out=dlog[0:1, 0:3], in_=din["dlog"][:]),
        reads=[("dram", "dlog")], writes=["g_dlog"])
    P.dma("sync", "g_nscol", lambda e: e.dma_start(
        out=nscol[:], in_=din["nsc"][:]),
        reads=[("dram", "nsc")], writes=["g_nscol"])
    P.dma("sync", "g_wre", lambda e: e.dma_start(
        out=wre[:], in_=din["wrim"].rearrange("r m h o -> h (r m) o")),
        reads=[("dram", "wrim")], writes=["g_wre"])
    P.dma("sync", "g_selg", lambda e: e.dma_start(
        out=selg[:], in_=din["selg"].rearrange("g k p -> k g p")),
        reads=[("dram", "selg")], writes=["g_selg"])
    P.dma("sync", "g_sel4", lambda e: e.dma_start(
        out=sel4[:], in_=din["sel4"][:]), reads=[("dram", "sel4")], writes=["g_sel4"])
    P.dma("sync", "g_o32", lambda e: e.dma_start(
        out=o32[:], in_=din["o32"][:]), reads=[("dram", "o32")], writes=["g_o32"])

    P.dma("gpsimd", "g_wproj", lambda e: e.dma_start(
        out=wproj[:], in_=din["inprojw"][:]),
        reads=[("dram", "inprojw")], writes=["g_wproj"])

    P.op("vector", lambda e: e.memset(zeros[:], 0.0), writes=["g_zeros"])
    P.op("vector", lambda e: e.memset(negpi[:], -PI), writes=["g_negpi"])
    P.op("vector", lambda e: e.memset(pihalf[:], 0.5 * PI), writes=["g_pihalf"])
    P.op("vector", lambda e: e.memset(onecol[:], 1.0), writes=["g_onecol"])
    P.op("vector", lambda e: e.memset(zbuf[:], 0.0), writes=["g_zbuf"])
    P.op("vector", lambda e: e.memset(carry[:], 0.0), writes=["g_carry"])
    P.op("vector", lambda e: e.memset(dcarry[:], 0.0), writes=["g_dcarry"])
    P.op("vector", lambda e: e.memset(bd[:], 0.0), writes=["g_bd"])

    P.op("scalar", lambda e: e.activation(
        out=slc[:, 2:3], in_=slc[:, 0:1], func=AF.Abs),
        reads=["g_slc"], writes=["g_slc"])
    P.op("scalar", lambda e: e.activation(
        out=slc[:, 2:3], in_=slc[:, 2:3], func=AF.Exp, scale=-1.0),
        reads=["g_slc"], writes=["g_slc"])
    P.op("scalar", lambda e: e.activation(
        out=slc[:, 2:3], in_=slc[:, 2:3], func=AF.Ln, bias=onecol[:]),
        reads=["g_slc", "g_onecol"], writes=["g_slc"])
    P.op("vector", lambda e: e.tensor_scalar(
        out=slc[:, 3:4], in0=slc[:, 0:1], scalar1=0.0, scalar2=None,
        op0=ALU.max), reads=["g_slc"], writes=["g_slc"])
    P.op("vector", lambda e: e.tensor_tensor(
        out=slc[:, 1:2], in0=slc[:, 2:3], in1=slc[:, 3:4], op=ALU.add),
        reads=["g_slc"], writes=["g_slc"])
    P.op("vector", lambda e: e.tensor_scalar(
        out=slc[:, 1:2], in0=slc[:, 1:2], scalar1=-1.0, scalar2=None,
        op0=ALU.mult), reads=["g_slc"], writes=["g_slc"])

    P.op("vector", lambda e: e.reduce_max(
        out=dlog[0:1, 3:4], in_=dlog[0:1, 0:3], axis=AX.X),
        reads=["g_dlog"], writes=["g_dlog"])
    P.op("vector", lambda e: e.tensor_scalar(
        out=dlog[0:1, 3:4], in0=dlog[0:1, 3:4], scalar1=-1.0, scalar2=None,
        op0=ALU.mult), reads=["g_dlog"], writes=["g_dlog"])
    P.op("scalar", lambda e: e.activation(
        out=dlog[0:1, 0:3], in_=dlog[0:1, 0:3], func=AF.Exp, bias=dlog[0:1, 3:4]),
        reads=["g_dlog"], writes=["g_dlog"])
    P.op("vector", lambda e: e.reduce_sum(
        out=dlog[0:1, 3:4], in_=dlog[0:1, 0:3], axis=AX.X),
        reads=["g_dlog"], writes=["g_dlog"])
    P.op("vector", lambda e: e.reciprocal(
        out=dlog[0:1, 3:4], in_=dlog[0:1, 3:4]), reads=["g_dlog"], writes=["g_dlog"])
    P.op("vector", lambda e: e.tensor_scalar(
        out=dlog[0:1, 0:3], in0=dlog[0:1, 0:3], scalar1=dlog[0:1, 3:4],
        scalar2=None, op0=ALU.mult), reads=["g_dlog"], writes=["g_dlog"])
    P.op("tensor", lambda e: e.matmul(
        out=P.pspool[:, 7, 0:3], lhsT=C.ones_row[:], rhs=dlog[0:1, 0:3],
        start=True, stop=True), reads=["g_dlog", "c_ones_row"], writes=["psb7"])
    P.op("vector", lambda e: e.tensor_copy(
        out=wcols[:, 0:3], in_=P.pspool[:, 7, 0:3]), reads=["psb7"], writes=["g_wcols"])
    P.op("vector", lambda e: e.tensor_scalar(
        out=wcols[:, 3:4], in0=wcols[:, 2:3], scalar1=-1.0, scalar2=None,
        op0=ALU.mult), reads=["g_wcols"], writes=["g_wcols"])

    for rim in range(8):
        P.op("vector", lambda e, rim=rim: e.tensor_scalar(
            out=wre[:, rim, :], in0=wre[:, rim, :], scalar1=nscol[:, rim:rim + 1],
            scalar2=None, op0=ALU.mult), reads=["g_wre", "g_nscol"], writes=["g_wre"])
    for rim in range(8):
        for kq in range(4):
            P.dma("gpsimd", "g_bd",
                  lambda e, rim=rim, kq=kq: e.dma_start(
                      out=bd[32 * kq:32 * kq + 32, rim, 32 * kq:32 * kq + 32],
                      in_=wre[:, rim, :]),
                  reads=["g_wre"], writes=["g_bd"])

    ln = alloc_ln(P, "g")

    for ci in range(NCH):
        c0 = ci * TCC
        load_x_chunk(P, din["xin"], xin_key, xc, "g_xc", c0, TCC)
        emit_ln_stats(P, C, ln, xc, "g_xc", tmp, "g_tmp", NT)
        emit_ln_apply(P, C, ln, xc, "g_xc", xhat, "g_xhat", tmp, "g_tmp", NT)

        for ct in range(CT):
            ncol = 128 if ct < 16 else 32
            bk_ = ct % 2
            for g in range(NT):
                P.op("tensor", lambda e, ct=ct, g=g, bk_=bk_, ncol=ncol: e.matmul(
                    out=P.pspool[0:ncol, bk_, :], lhsT=wproj[:, g, ct * 128:ct * 128 + ncol],
                    rhs=xhat[:, g, :], start=(g == 0), stop=(g == NT - 1)),
                    reads=["g_wproj", "g_xhat"], writes=[f"psb{bk_}"], inc=(g == NT - 1))
            P.op("scalar", lambda e, ct=ct, bk_=bk_, ncol=ncol: e.copy(
                out=zbuf[0:ncol, ct, 3:3 + TCC], in_=P.pspool[0:ncol, bk_, :]),
                reads=[f"psb{bk_}"], writes=["g_zbuf"])

        for ct in range(CT):
            nrow = 128 if ct < 16 else 32
            if ct < 8:
                dst, dstk = z2c, "g_z2c"
                dsl = (slice(0, nrow), ct, slice(0, TCC))
            elif ct < 16:
                dst, dstk = scr, f"g_scr{ct % 2}"
                dsl = (slice(0, nrow), ct % 2, slice(0, TCC))
            else:
                dst, dstk = z2c, "g_z2c"
                dsl = (slice(0, nrow), 8, slice(0, TCC))
            P.op("vector", lambda e, ct=ct, nrow=nrow, dst=dst, dsl=dsl: e.tensor_scalar(
                out=dst[dsl], in0=zbuf[0:nrow, ct, 0:TCC],
                scalar1=convw[0:nrow, ct, 0:1], scalar2=convb[0:nrow, ct:ct + 1],
                op0=ALU.mult, op1=ALU.add),
                reads=["g_zbuf", "g_convw", "g_convb"], writes=[dstk])
            for j in (1, 2, 3):
                P.op("vector", lambda e, ct=ct, j=j, nrow=nrow, dst=dst, dsl=dsl:
                     e.scalar_tensor_tensor(
                         out=dst[dsl], in0=zbuf[0:nrow, ct, j:j + TCC],
                         scalar=convw[0:nrow, ct, j:j + 1], in1=dst[dsl],
                         op0=ALU.mult, op1=ALU.add),
                     reads=["g_zbuf", "g_convw", dstk], writes=[dstk])
            if 8 <= ct < 16:
                P.op("scalar", lambda e, ct=ct: e.activation(
                    out=gate[:, ct - 8, :], in_=scr[:, ct % 2, :], func=AF.Silu),
                    reads=[f"g_scr{ct % 2}"], writes=["g_gate"])
        P.op("vector", lambda e: e.tensor_copy(
            out=zbuf[:, :, 0:3], in_=zbuf[:, :, TCC:TCC + 3]),
            reads=["g_zbuf"], writes=["g_zbuf"])

        P.dma("sync", "g_dist", lambda e, c0=c0: e.dma_start(
            out=dist[:], in_=din["dist"][:, c0:c0 + TCC]),
            reads=[("dram", "dist")], writes=["g_dist"])
        P.op("scalar", lambda e: e.activation(
            out=twc[:], in_=dist[:], func=AF.Exp, scale=slc[:, 1:2]),
            reads=["g_dist", "g_slc"], writes=["g_twc"])
        P.op("vector", lambda e: e.tensor_scalar(
            out=pw[:], in0=z2c[0:32, 8, :], scalar1=sscol[:], scalar2=20.0,
            op0=ALU.mult, op1=ALU.min), reads=["g_z2c", "g_sscol"], writes=["g_pw"])
        P.op("vector", lambda e: e.tensor_scalar(
            out=pw[:], in0=pw[:], scalar1=-20.0, scalar2=None,
            op0=ALU.max), reads=["g_pw"], writes=["g_pw"])
        P.op("scalar", lambda e: e.activation(
            out=pw[:], in_=pw[:], func=AF.Exp), reads=["g_pw"], writes=["g_pw"])
        P.op("vector", lambda e: e.tensor_mul(
            out=pw[:], in0=pw[:], in1=twc[:]), reads=["g_pw", "g_twc"], writes=["g_pw"])
        P.op("vector", lambda e: e.tensor_tensor_scan(
            out=denc[:], data0=pw[:], data1=zeros[0:32, :], initial=dcarry[:, 0:1],
            op0=ALU.add, op1=ALU.add),
            reads=["g_pw", "g_zeros", "g_dcarry"], writes=["g_denc"])
        P.op("vector", lambda e: e.tensor_copy(
            out=dcarry[:, 0:1], in_=denc[:, TCC - 1:TCC]),
            reads=["g_denc"], writes=["g_dcarry"])

        for g in range(NT):
            gp = g % 2
            sck = f"g_scr{gp}"
            P.op("tensor", lambda e, g=g: e.matmul(
                out=P.psb(6), lhsT=selg[:, g, :], rhs=pw[:],
                start=True, stop=True), reads=["g_selg", "g_pw"], writes=["psb6"])
            P.op("vector", lambda e, g=g, gp=gp: e.tensor_scalar(
                out=scr[:, gp, :], in0=z2c[:, g, :], scalar1=wcols[:, 3:4],
                scalar2=wcols[:, 1:2], op0=ALU.mult, op1=ALU.add),
                reads=["g_z2c", "g_wcols"], writes=[sck])
            P.op("vector", lambda e, g=g, gp=gp: e.tensor_mul(
                out=scr[:, gp, :], in0=scr[:, gp, :], in1=z2c[:, g, :]),
                reads=[sck, "g_z2c"], writes=[sck])
            P.op("vector", lambda e, gp=gp: e.tensor_scalar(
                out=scr[:, gp, :], in0=scr[:, gp, :], scalar1=wcols[:, 0:1],
                scalar2=None, op0=ALU.add), reads=[sck, "g_wcols"], writes=[sck])
            P.op("vector", lambda e, gp=gp: e.tensor_mul(
                out=scr[:, gp, :], in0=scr[:, gp, :], in1=P.psb(6)),
                reads=[sck, "psb6"], writes=[sck])
            for m in range(4):
                mp = m % 2
                tk = f"g_tmpA{mp}"
                sqk = f"g_sqt{mp}"
                t0, t1, t2 = 4 * mp + 0, 4 * mp + 1, 4 * mp + 2
                sq = sqt[:, mp, :]
                P.op("vector", lambda e, g=g, m=m, sq=sq: e.tensor_scalar(
                    out=sq, in0=z2c[:, g, :], scalar1=thc[:, m, g:g + 1],
                    scalar2=None, op0=ALU.mult),
                    reads=["g_z2c", "g_thc"], writes=[sqk])
                P.op("vector", lambda e, sq=sq, t2=t2: e.tensor_scalar(
                    out=tmp[:, t2, :], in0=sq, scalar1=PI, scalar2=None,
                    op0=ALU.is_gt), reads=[sqk], writes=[tk])
                P.op("vector", lambda e, sq=sq, t1=t1: e.tensor_scalar(
                    out=tmp[:, t1, :], in0=sq, scalar1=-PI, scalar2=None,
                    op0=ALU.is_lt), reads=[sqk], writes=[tk])
                P.op("vector", lambda e, t1=t1, t2=t2: e.tensor_sub(
                    out=tmp[:, t2, :], in0=tmp[:, t2, :], in1=tmp[:, t1, :]),
                    reads=[tk], writes=[tk])
                P.op("vector", lambda e, sq=sq, t1=t1, t2=t2: e.scalar_tensor_tensor(
                    out=tmp[:, t1, :], in0=tmp[:, t2, :], scalar=-2 * PI,
                    in1=sq, op0=ALU.mult, op1=ALU.add),
                    reads=[tk, sqk], writes=[tk])
                P.op("vector", lambda e, t1=t1: e.tensor_scalar(
                    out=tmp[:, t1, :], in0=tmp[:, t1, :], scalar1=PI,
                    scalar2=-PI, op0=ALU.min, op1=ALU.max),
                    reads=[tk], writes=[tk])
                P.op("scalar", lambda e, t0=t0, t1=t1: e.activation(
                    out=tmp[:, t0, :], in_=tmp[:, t1, :], func=AF.Sin),
                    reads=[tk], writes=[tk])
                P.op("scalar", lambda e, t1=t1, t2=t2: e.activation(
                    out=tmp[:, t2, :], in_=tmp[:, t1, :], func=AF.Sin, scale=0.5),
                    reads=[tk], writes=[tk])
                P.op("scalar", lambda e, t2=t2: e.activation(
                    out=tmp[:, t2, :], in_=tmp[:, t2, :], func=AF.Square),
                    reads=[tk], writes=[tk])
                P.op("vector", lambda e, t1=t1, t2=t2: e.tensor_scalar(
                    out=tmp[:, t1, :], in0=tmp[:, t2, :], scalar1=-2.0,
                    scalar2=1.0, op0=ALU.mult, op1=ALU.add),
                    reads=[tk], writes=[tk])
                P.op("vector", lambda e, t1=t1, gp=gp: e.tensor_mul(
                    out=tmp[:, t1, :], in0=tmp[:, t1, :], in1=scr[:, gp, :]),
                    reads=[tk, sck], writes=[tk])
                P.op("vector", lambda e, t0=t0, gp=gp: e.tensor_mul(
                    out=tmp[:, t0, :], in0=tmp[:, t0, :], in1=scr[:, gp, :]),
                    reads=[tk, sck], writes=[tk])
                for ri in range(2):
                    src = (t1, t0)[ri]
                    cc = g * 8 + m * 2 + ri
                    P.op("vector", lambda e, m=m, ri=ri, src=src, cc=cc: e.tensor_tensor_scan(
                        out=numf[:, 2 * m + ri, :], data0=tmp[:, src, :],
                        data1=zeros[:], initial=carry[:, cc:cc + 1],
                        op0=ALU.add, op1=ALU.add),
                        reads=[tk, "g_zeros", "g_carry"], writes=["g_numf"])
                    sqs = (3, 7)[ri]
                    P.op("scalar", lambda e, m=m, ri=ri, sqs=sqs: e.activation(
                        out=tmp[:, sqs, :], in_=numf[:, 2 * m + ri, :], func=AF.Square),
                        reads=["g_numf"], writes=[f"g_sqb{ri}"])
                    for sub in range(4):
                        first = (m == 0 and ri == 0 and sub == 0)
                        last = (m == 3 and ri == 1 and sub == 3)
                        P.op("tensor", lambda e, sub=sub, sqs=sqs, first=first, last=last: e.matmul(
                            out=P.pspool[0:4, 7, sub * 128:(sub + 1) * 128],
                            lhsT=o32[:], rhs=tmp[:, sqs, sub * 128:(sub + 1) * 128],
                            start=first, stop=last),
                            reads=["g_o32", f"g_sqb{ri}"], writes=["psb7"],
                            inc=(sub == 3))
            P.op("vector", lambda e, g=g: e.tensor_copy(
                out=carry[:, g * 8:g * 8 + 8], in_=numf[:, 0:8, TCC - 1:TCC]),
                reads=["g_numf"], writes=["g_carry"])
            P.op("scalar", lambda e: e.activation(
                out=sstage[:, 0, :], in_=P.pspool[0:4, 7, :],
                func=AF.Copy, scale=1.0 / 256.0),
                reads=["psb7"], writes=["g_sstage"])
            P.dma("sync", "g_dstage",
                  lambda e, g=g: e.dma_start(out=dstage[:], in_=denc[4 * g:4 * g + 4, :]),
                  reads=["g_denc"], writes=["g_dstage"])
            P.op("vector", lambda e: e.tensor_scalar(
                out=dstage[:], in0=dstage[:], scalar1=1e-4, scalar2=None,
                op0=ALU.max), reads=["g_dstage"], writes=["g_dstage"])
            P.op("scalar", lambda e: e.activation(
                out=dstage[:], in_=dstage[:], func=AF.Square, scale=float(np.sqrt(1e-5))),
                reads=["g_dstage"], writes=["g_dstage"])
            P.op("vector", lambda e: e.tensor_add(
                out=sstage[:, 1, :], in0=sstage[:, 0, :], in1=dstage[:]),
                reads=["g_sstage", "g_dstage"], writes=["g_sstage"])
            P.op("scalar", lambda e: e.activation(
                out=sstage[:, 1, :], in_=sstage[:, 1, :], func=AF.Sqrt),
                reads=["g_sstage"], writes=["g_sstage"])
            P.op("vector", lambda e: e.reciprocal(
                out=sstage[:, 1, :], in_=sstage[:, 1, :]),
                reads=["g_sstage"], writes=["g_sstage"])
            P.op("tensor", lambda e: e.matmul(
                out=P.psb(6), lhsT=sel4[:], rhs=sstage[:, 1, :],
                start=True, stop=True), reads=["g_sel4", "g_sstage"], writes=["psb6"])
            for mm in range(8):
                P.op("vector", lambda e, mm=mm: e.tensor_mul(
                    out=numbf[:, mm, :], in0=numf[:, mm, :], in1=P.psb(6)),
                    reads=["g_numf", "psb6"], writes=["g_numbf"])
            for rim in range(8):
                ri, m = rim // 4, rim % 4
                P.op("tensor", lambda e, rim=rim, ri=ri, m=m: e.matmul(
                    out=P.psb(2), lhsT=bd[:, rim, :], rhs=numbf[:, 2 * m + ri, :],
                    start=(rim == 0), stop=(rim == 7)),
                    reads=["g_bd", "g_numbf"], writes=["psb2"], inc=(rim == 7))
            P.op("vector", lambda e, g=g: e.tensor_mul(
                out=ygate[:, g, :], in0=P.psb(2), in1=gate[:, g, :]),
                reads=["psb2", "g_gate"], writes=["g_ygate"])

        for et in range(NT):
            P.dma("gpsimd", f"g_outw{et % 2}",
                  lambda e, et=et: e.dma_start(
                      out=outw[et % 2][:], in_=din["outprojw"][et]),
                  reads=[("dram", "outprojw")], writes=[f"g_outw{et % 2}"])
            for g in range(NT):
                P.op("tensor", lambda e, et=et, g=g: e.matmul(
                    out=P.psb(3), lhsT=outw[et % 2][:, g, :], rhs=ygate[:, g, :],
                    start=(g == 0), stop=(g == NT - 1)),
                    reads=[f"g_outw{et % 2}", "g_ygate"], writes=["psb3"],
                    inc=(g == NT - 1))
            P.op("vector", lambda e, et=et: e.tensor_copy(
                out=numf[:, et, :], in_=P.psb(3)), reads=["psb3"], writes=["g_numf"])

        emit_ln_stats(P, C, ln, numf, "g_numf", tmp, "g_tmp", NT)
        load_x_chunk(P, din["xin"], xin_key, xc, "g_xc", c0, TCC)
        emit_ln_apply(P, C, ln, numf, "g_numf", xc, "g_xc", tmp, "g_tmp", NT,
                      add_to=xc, add_key="g_xc")
        store_x_chunk(P, dout["xout"], xout_key, xc, "g_xc", c0, TCC, "st_xout")


def host_prep_l0attn(inputs):
    f32, c = np.float32, np.ascontiguousarray
    d = {}
    ipw = np.asarray(inputs["L0_attn_in_proj_w"], f32)
    d["inprojw"] = c(ipw.reshape(8, 128, 2080).transpose(1, 0, 2))
    convT = np.zeros((2176, 4), f32)
    convT[:2080] = np.asarray(inputs["L0_attn_conv_w"], f32)[:, 0, :].T
    d["convT"] = convT
    convb = np.zeros((2176,), f32)
    convb[:2080] = np.asarray(inputs["L0_attn_conv_b"], f32)
    d["convb"] = c(convb.reshape(17, 128).T)
    th = np.asarray(inputs["L0_attn_theta"], f32).reshape(32, 32, 4)
    d["thetac"] = c(th.transpose(2, 0, 1).reshape(4, 8, 128).transpose(2, 0, 1))
    d["slopes"] = c(np.concatenate([
        np.asarray(inputs["L0_attn_decay_slopes"], f32),
        np.asarray(inputs["L0_attn_anchor_slopes"], f32)]).reshape(32, 1))
    d["sscol"] = c(np.asarray(inputs["L0_attn_score_scale"], f32).reshape(32, 1))
    d["dlog"] = c(np.asarray(inputs["L0_attn_deriv_logits"], f32).reshape(1, 3))
    ns = np.asarray(inputs["L0_attn_norm_scale"], f32)
    nsc = np.stack([ns[:128].reshape(32, 4).T, ns[128:].reshape(32, 4).T])
    d["nsc"] = c(nsc.reshape(8, 32).T)
    wr = np.asarray(inputs["L0_attn_W_re"], f32).reshape(32, 4, 32).transpose(1, 0, 2)
    wi = np.asarray(inputs["L0_attn_W_im"], f32).reshape(32, 4, 32).transpose(1, 0, 2)
    d["wrim"] = c(np.stack([wr, wi]))
    d["outprojw"] = _tile_lhs(np.asarray(inputs["L0_attn_out_proj_w"], f32))
    p = np.arange(128)
    d["sel4"] = c((p[None, :] // 32 == np.arange(4)[:, None]).astype(f32))
    d["o32"] = c(d["sel4"].T)
    selg = np.zeros((8, 32, 128), f32)
    for g in range(8):
        selg[g, g * 4 + p // 32, p] = 1.0
    d["selg"] = selg
    t = np.arange(2048, dtype=f32)
    dist = np.empty((32, 2048), f32)
    dist[:28] = 2047.0 - t
    dist[28:] = t
    d["dist"] = dist
    return d


L0ATTN_SHAPES = {
    "inprojw": ([128, 8, 2080], F32), "convT": ([2176, 4], F32),
    "convb": ([128, 17], F32), "thetac": ([128, 4, 8], F32),
    "slopes": ([32, 1], F32), "sscol": ([32, 1], F32), "dlog": ([1, 3], F32),
    "nsc": ([32, 8], F32), "wrim": ([2, 4, 32, 32], F32),
    "outprojw": ([8, 128, 8, 128], F32), "sel4": ([4, 128], F32),
    "o32": ([128, 4], F32), "selg": ([8, 32, 128], F32), "dist": ([32, 2048], F32),
}


def build_l0attn_program():
    nc = bass.Bass("TRN2", target_bir_lowering=False)
    din = {"xin": nc.dram_tensor("xin", [D, T], F32, kind="ExternalInput").ap()}
    for nm, (shp, dt) in L0ATTN_SHAPES.items():
        din[nm] = nc.dram_tensor(nm, shp, dt, kind="ExternalInput").ap()
    dout = {"xout": nc.dram_tensor("xout", [D, T], F32, kind="ExternalOutput").ap()}
    P = Prog(nc)
    P.make_pool()
    C = Common(P)
    build_l0attn(P, C, din, dout)
    P.final_wait("sync", ["st_xout"])
    P.build()
    return nc




def build_fused_program():
    nc = bass.Bass("TRN2", target_bir_lowering=False)
    din = {"xin": nc.dram_tensor("xin", [D, T], F32, kind="ExternalInput").ap()}
    for shapes in (L0ATTN_SHAPES,):
        for nm, (shp, dt) in shapes.items():
            din[nm] = nc.dram_tensor(nm, shp, dt, kind="ExternalInput").ap()
    for nm, (shp, dt) in L0FFN_SHAPES.items():
        din[nm] = nc.dram_tensor(nm, shp, dt, kind="ExternalInput").ap()
    for nm, (shp, dt) in L1ATTN_SHAPES.items():
        din[nm] = nc.dram_tensor(nm, shp, dt, kind="ExternalInput").ap()
    for nm, shp in L1FFN_SHAPES.items():
        din[nm] = nc.dram_tensor(nm, shp, F32, kind="ExternalInput").ap()
    din["hdram"] = nc.dram_tensor("hdram", [D, T], F32, kind="Internal").ap()
    din["reld"] = nc.dram_tensor("reld", [2, 128, 2176], BF16, kind="Internal").ap()
    xra = nc.dram_tensor("xra", [D, T], F32, kind="Internal").ap()
    xrb = nc.dram_tensor("xrb", [D, T], F32, kind="Internal").ap()
    xout = nc.dram_tensor("xout", [D, T], F32, kind="ExternalOutput").ap()

    P = Prog(nc)
    P.make_pool()
    C = Common(P)
    with P.scope():
        build_l0attn(P, C, din, {"xout": xra},
                     xin_key=("dram", "xin"), xout_key=("dram", "xra"))
    P.barrier()
    with P.scope():
        build_l0ffn(P, C, dict(din, xin=xra), {"xout": xrb},
                    xin_key=("dram", "xra"), xout_key=("dram", "xrb"))
    P.barrier()
    with P.scope():
        build_l1attn(P, C, dict(din, xin=xrb), {"xout": xra},
                     xin_key=("dram", "xrb"), xout_key=("dram", "xra2"))
    P.barrier()
    with P.scope():
        build_l1ffn(P, C, dict(din, xin=xra), {"xout": xout},
                    xin_key=("dram", "xra2"), xout_key=("dram", "xout"))
    P.final_wait("sync", ["st_xout"])
    P.build()
    return nc



_PROGS = {}


def _get_prog(name, builder):
    if name not in _PROGS:
        _PROGS[name] = builder()
    return _PROGS[name]


def kernel(**inputs):
    x = np.asarray(inputs["L0_attn_x"], np.float32)
    nc = _get_prog("fused", build_fused_program)
    hp = {}
    hp.update(host_prep_l0attn(inputs))
    hp.update(host_prep_l0ffn(inputs))
    hp.update(host_prep_l1attn(inputs))
    hp.update(host_prep_l1ffn(inputs))
    in_maps = []
    for i in range(NCORES):
        m = {"xin": np.ascontiguousarray(x[i].T)}
        m.update(hp)
        in_maps.append(m)
    res = run_bass_kernel_spmd(nc, in_maps, core_ids=list(range(NCORES)))
    out = np.stack([np.asarray(res.results[i]["xout"]).T for i in range(NCORES)])
    return out.astype(np.float32)


# version 24 (final)
# speedup vs baseline: 1.1244x; 1.0024x over previous
import numpy as np
import ml_dtypes
from contextlib import ExitStack

import concourse.bass as bass
import concourse.mybir as mybir
from concourse.bass_utils import run_bass_kernel_spmd

F32 = mybir.dt.float32
BF16 = mybir.dt.bfloat16
I32 = mybir.dt.int32
AF = mybir.ActivationFunctionType
ALU = mybir.AluOpType
AX = mybir.AxisListType
BF16NP = ml_dtypes.bfloat16

D = 1024
T = 2048
NCORES = 8
NT = D // 128
LN_EPS = 1e-5


ENGINES = ("sync", "tensor", "vector", "scalar", "gpsimd")


class Prog:
    def __init__(self, nc):
        self.nc = nc
        self.stack = ExitStack()
        self._alloc_stack = None
        self.sems = {}
        self.cnt = {}
        self.queue = {e: [] for e in ENGINES}
        self.last_w = {}
        self.readers = {}
        self.waited = {e: {} for e in ENGINES}
        self.bufs = {}

    def sem(self, name):
        if name not in self.sems:
            self.sems[name] = self.stack.enter_context(self.nc.semaphore(name))
            self.cnt[name] = 0
        return self.sems[name]

    def sbuf(self, name, shape, dtype):
        st = self._alloc_stack if self._alloc_stack is not None else self.stack
        t = st.enter_context(self.nc.sbuf_tensor(name, list(shape), dtype))
        self.bufs[name] = t
        return t

    def scope(self):
        from contextlib import contextmanager

        @contextmanager
        def _cm():
            st = ExitStack()
            old = self._alloc_stack
            self._alloc_stack = st
            try:
                yield
            finally:
                self._alloc_stack = old
                st.close()
        return _cm()

    def barrier(self):
        targets = {sname: v for sname, v in self.cnt.items() if v > 0}
        for eng in ENGINES:
            w = {}
            for sname, v in targets.items():
                if sname == eng:
                    continue
                if self.waited[eng].get(sname, -1) >= v:
                    continue
                w[sname] = v
            if w:
                self.queue[eng].append((w, None, None))
                for sname, v in w.items():
                    self.waited[eng][sname] = max(self.waited[eng].get(sname, -1), v)
        self.last_w.clear()
        self.readers.clear()

    def psum(self, name, shape, dtype=F32):
        t = self.stack.enter_context(self.nc.psum_tensor(name, list(shape), dtype))
        self.bufs[name] = t
        return t

    def make_pool(self):
        self.pspool = self.psum("pspool", [128, 8, 512], F32)

    def psb(self, bank):
        return self.pspool[:, bank, :]

    def psbk(self, b0, n=1):
        return [f"psb{i}" for i in range(b0, b0 + n)]

    def psw(self, b0, n):
        return self.pspool[:, b0:b0 + n, :].rearrange("p a b -> p (a b)")

    def _waits_for(self, eng, reads, writes):
        need = {}
        def add(swv):
            s, v = swv
            if need.get(s, 0) < v:
                need[s] = v
        for k in reads:
            if k in self.last_w:
                add(self.last_w[k])
        for k in writes:
            if k in self.last_w:
                add(self.last_w[k])
            for swv in self.readers.get(k, ()):
                add(swv)
        out = {}
        for s, v in need.items():
            if s == eng and v > self.cnt.get(eng, 0):
                continue
            if self.waited[eng].get(s, -1) >= v:
                continue
            out[s] = v
        return out

    def _record(self, eng_sem, val, reads, writes):
        for k in writes:
            self.last_w[k] = (eng_sem, val)
            self.readers[k] = []
        for k in reads:
            self.readers.setdefault(k, []).append((eng_sem, val))

    def op(self, eng, fn, reads=(), writes=(), inc=True):
        self.sem(eng)
        waits = self._waits_for(eng, reads, writes)
        for s, v in waits.items():
            self.waited[eng][s] = max(self.waited[eng].get(s, -1), v)
        if inc:
            self.cnt[eng] += 1
            incspec = (eng, 1)
            val = self.cnt[eng]
        else:
            incspec = None
            val = self.cnt[eng] + 1
        self._record(eng, val, reads, writes)
        self.queue[eng].append((dict(waits), fn, incspec))

    def dma(self, qeng, chan, fn, reads=(), writes=()):
        chan = "dq_" + chan
        self.sem(chan)
        self.sem(qeng)
        waits = self._waits_for(qeng, reads, writes)
        for s, v in waits.items():
            self.waited[qeng][s] = max(self.waited[qeng].get(s, -1), v)
        self.cnt[chan] += 16
        self._record(chan, self.cnt[chan], reads, writes)
        self.queue[qeng].append((dict(waits), fn, (chan, 16)))

    def final_wait(self, qeng, chans):
        waits = {}
        for c in chans:
            c = "dq_" + c
            if self.cnt.get(c, 0) > 0:
                waits[c] = self.cnt[c]
        self.queue[qeng].append((waits, None, None))

    def build(self):
        block = self.stack.enter_context(self.nc.Block())
        for eng_name in ENGINES:
            items = self.queue[eng_name]
            if not items:
                continue
            def make_body(items):
                def body(engine):
                    for waits, fn, incspec in items:
                        for s, v in waits.items():
                            engine.wait_ge(self.sems[s], v)
                        if fn is None:
                            continue
                        inst = fn(engine)
                        if incspec is not None:
                            s, by = incspec
                            inst.then_inc(self.sems[s], by)
                return body
            getattr(block, eng_name)(make_body(items))

    def close(self):
        self.stack.close()


class Common:

    def __init__(self, P: Prog):
        self.P = P
        self.ones_f = P.sbuf("c_ones_f", [128, 1], F32)
        self.ones_row = P.sbuf("c_ones_row", [1, 128], F32)
        self.eps_row = P.sbuf("c_eps_row", [1, 1], F32)
        P.op("vector", lambda v: v.memset(self.ones_f[:], 1.0), writes=["c_ones_f"])
        P.op("vector", lambda v: v.memset(self.ones_row[:], 1.0), writes=["c_ones_row"])
        P.op("vector", lambda v: v.memset(self.eps_row[:], LN_EPS), writes=["c_eps_row"])



def alloc_ln(P: Prog, tag):
    ln = {}
    ln["rows"] = P.sbuf(f"ln_rows_{tag}", [1, 3, 512], F32)
    ln["ps_a"] = P.pspool[0:1, 2, :]
    ln["ps_b"] = P.pspool[0:1, 3, :]
    ln["bc0"] = P.psb(4)
    ln["bc1"] = P.psb(5)
    ln["k"] = tag
    return ln


def emit_ln_stats(P, C, ln, src, src_key, sq, sq_key, ntile):
    k = ln["k"]
    rows, rk = ln["rows"], f"ln_rows_{k}"
    P.op("scalar", lambda e: e.activation(out=sq[:], in_=src[:], func=AF.Square),
         reads=[src_key], writes=[sq_key])
    for g in range(ntile):
        P.op("tensor", lambda e, g=g: e.matmul(
            out=ln["ps_a"], lhsT=C.ones_f[:], rhs=src[:, g, :],
            start=(g == 0), stop=(g == ntile - 1)),
            reads=[src_key, "c_ones_f"], writes=["psb2"], inc=(g == ntile - 1))
    for g in range(ntile):
        P.op("tensor", lambda e, g=g: e.matmul(
            out=ln["ps_b"], lhsT=C.ones_f[:], rhs=sq[:, g, :],
            start=(g == 0), stop=(g == ntile - 1)),
            reads=[sq_key, "c_ones_f"], writes=["psb3"], inc=(g == ntile - 1))
    nfeat = float(ntile * 128)
    P.op("vector", lambda e: e.tensor_scalar(
        out=rows[0:1, 0, :], in0=ln["ps_a"], scalar1=1.0 / nfeat, scalar2=None,
        op0=ALU.mult), reads=["psb2"], writes=[rk])
    P.op("vector", lambda e: e.tensor_scalar(
        out=rows[0:1, 1, :], in0=ln["ps_b"], scalar1=1.0 / nfeat, scalar2=None,
        op0=ALU.mult), reads=["psb3"], writes=[rk])
    P.op("vector", lambda e: e.tensor_tensor(
        out=rows[0:1, 2, :], in0=rows[0:1, 0, :], in1=rows[0:1, 0, :], op=ALU.mult),
        reads=[rk], writes=[rk])
    P.op("vector", lambda e: e.tensor_sub(
        out=rows[0:1, 1, :], in0=rows[0:1, 1, :], in1=rows[0:1, 2, :]),
        reads=[rk], writes=[rk])
    P.op("scalar", lambda e: e.activation(
        out=rows[0:1, 1, :], in_=rows[0:1, 1, :], func=AF.Sqrt, bias=C.eps_row[:]),
        reads=[rk, "c_eps_row"], writes=[rk])
    P.op("vector", lambda e: e.reciprocal(
        out=rows[0:1, 1, :], in_=rows[0:1, 1, :]), reads=[rk], writes=[rk])
    P.op("vector", lambda e: e.tensor_mul(
        out=rows[0:1, 2, :], in0=rows[0:1, 0, :], in1=rows[0:1, 1, :]),
        reads=[rk], writes=[rk])
    P.op("tensor", lambda e: e.matmul(
        out=ln["bc0"], lhsT=C.ones_row[:], rhs=rows[0:1, 1, :],
        start=True, stop=True), reads=[rk, "c_ones_row"], writes=["psb4"])
    P.op("tensor", lambda e: e.matmul(
        out=ln["bc1"], lhsT=C.ones_row[:], rhs=rows[0:1, 2, :],
        start=True, stop=True), reads=[rk, "c_ones_row"],
        writes=["psb5"], inc=True)


def emit_ln_apply(P, C, ln, src, src_key, dst, dst_key, tmp, tmp_key, ntile,
                  gamma_col=None, beta_col=None, gb_keys=(), add_to=None,
                  add_key=None):
    k = ln["k"]
    for g in range(ntile):
        P.op("vector", lambda e, g=g: e.tensor_mul(
            out=tmp[:, g, :], in0=src[:, g, :], in1=ln["bc0"]),
            reads=[src_key, "psb4"], writes=[tmp_key])
    if gamma_col is None and add_to is None:
        for g in range(ntile):
            P.op("vector", lambda e, g=g: e.tensor_sub(
                out=dst[:, g, :], in0=tmp[:, g, :], in1=ln["bc1"]),
                reads=[tmp_key, "psb5"], writes=[dst_key])
    elif gamma_col is not None:
        for g in range(ntile):
            P.op("vector", lambda e, g=g: e.tensor_sub(
                out=tmp[:, g, :], in0=tmp[:, g, :], in1=ln["bc1"]),
                reads=[tmp_key, "psb5"], writes=[tmp_key])
        for g in range(ntile):
            P.op("vector", lambda e, g=g: e.tensor_scalar(
                out=dst[:, g, :], in0=tmp[:, g, :],
                scalar1=gamma_col[:, g:g + 1], scalar2=beta_col[:, g:g + 1],
                op0=ALU.mult, op1=ALU.add),
                reads=[tmp_key] + list(gb_keys), writes=[dst_key])
    else:
        for g in range(ntile):
            P.op("vector", lambda e, g=g: e.tensor_sub(
                out=tmp[:, g, :], in0=tmp[:, g, :], in1=ln["bc1"]),
                reads=[tmp_key, "psb5"], writes=[tmp_key])
        for g in range(ntile):
            P.op("vector", lambda e, g=g: e.tensor_add(
                out=dst[:, g, :], in0=tmp[:, g, :], in1=add_to[:, g, :]),
                reads=[tmp_key, add_key], writes=[dst_key])


def load_x_chunk(P, xsrc_ap, xsrc_key, xc, xc_key, c0, TC):
    P.dma("sync", xc_key,
          lambda e: e.dma_start(
              out=xc[:], in_=xsrc_ap[:, c0:c0 + TC].rearrange("(g p) t -> p g t", p=128)),
          reads=[xsrc_key], writes=[xc_key])


def store_x_chunk(P, xdst_ap, xdst_key, xc, xc_key, c0, TC, chan):
    P.dma("sync", chan,
          lambda e: e.dma_start(
              out=xdst_ap[:, c0:c0 + TC].rearrange("(g p) t -> p g t", p=128), in_=xc[:]),
          reads=[xc_key], writes=[xdst_key])



def build_l1ffn(P: Prog, C: Common, din, dout, xin_key="dram_xin",
                xout_key="dram_xout"):
    NH = 32
    TC = 512

    s1c = P.sbuf("f_s1c", [128, NH], F32)
    s2c = P.sbuf("f_s2c", [128, NH], F32)
    sh1c = P.sbuf("f_sh1c", [128, NH], F32)
    sh2c = P.sbuf("f_sh2c", [128, NH], F32)
    s3c = P.sbuf("f_s3c", [128, NT], F32)
    for nm in ("s1c", "s2c", "sh1c", "sh2c", "s3c"):
        P.dma("sync", "f_" + nm,
              lambda e, nm=nm: e.dma_start(out=P.bufs["f_" + nm][:], in_=din[nm][:]),
              reads=[("dram", nm)], writes=["f_" + nm])

    xc = P.sbuf("f_xc", [128, NT, TC], F32)
    tmp = P.sbuf("f_tmp", [128, NT, TC], F32)
    xhat = P.sbuf("f_xhat", [128, NT, TC], BF16)
    t1 = P.sbuf("f_t1", [128, NH, TC], BF16)
    t2 = P.sbuf("f_t2", [128, NH, TC], BF16)
    y = P.sbuf("f_y", [128, NT, TC], F32)
    silu_t = P.sbuf("f_silu", [128, TC], F32)
    wbuf = [P.sbuf(f"f_wbuf{i}", [128, 32, 128], BF16) for i in range(2)]
    ps_mm = [P.psb(0), P.psb(1)]
    ln = alloc_ln(P, "f")

    for c0 in range(0, T, TC):
        load_x_chunk(P, din["xin"], xin_key, xc, "f_xc", c0, TC)
        emit_ln_stats(P, C, ln, xc, "f_xc", tmp, "f_tmp", NT)
        emit_ln_apply(P, C, ln, xc, "f_xc", xhat, "f_xhat", tmp, "f_tmp", NT)

        for ot in range(NH):
            wb, wbk = wbuf[ot % 2], f"f_wbuf{ot % 2}"
            P.dma("gpsimd", wbk + "a",
                  lambda e, ot=ot, wb=wb: e.dma_start(
                      out=wb[:, 0:8, :], in_=din["w1Tt"][ot]),
                  reads=[("dram", "w1Tt")], writes=[wbk + "a"])
            P.dma("gpsimd", wbk + "b",
                  lambda e, ot=ot, wb=wb: e.dma_start(
                      out=wb[:, 8:16, :], in_=din["w2Tt"][ot]),
                  reads=[("dram", "w2Tt")], writes=[wbk + "b"])
            for g in range(NT):
                P.op("tensor", lambda e, g=g, wb=wb: e.matmul(
                    out=ps_mm[0], lhsT=wb[:, g, :], rhs=xhat[:, g, :],
                    start=(g == 0), stop=(g == NT - 1)),
                    reads=[wbk + "a", "f_xhat"], writes=["psb0"], inc=(g == NT - 1))
            for g in range(NT):
                P.op("tensor", lambda e, g=g, wb=wb: e.matmul(
                    out=ps_mm[1], lhsT=wb[:, 8 + g, :], rhs=xhat[:, g, :],
                    start=(g == 0), stop=(g == NT - 1)),
                    reads=[wbk + "b", "f_xhat"], writes=["psb1"], inc=(g == NT - 1))
            P.op("scalar", lambda e, ot=ot: e.activation(
                out=silu_t[:], in_=ps_mm[0], func=AF.Silu, scale=s1c[:, ot:ot + 1]),
                reads=["psb0", "f_s1c"], writes=["f_silu"])
            P.op("vector", lambda e, ot=ot: e.scalar_tensor_tensor(
                out=t1[:, ot, :], in0=ps_mm[1], scalar=s2c[:, ot:ot + 1],
                in1=silu_t[:], op0=ALU.mult, op1=ALU.mult),
                reads=["psb1", "f_s2c", "f_silu"], writes=["f_t1"])

        for ot in range(NH):
            wb, wbk = wbuf[ot % 2], f"f_wbuf{ot % 2}"
            bk_ = ot % 2
            P.dma("gpsimd", wbk + "a",
                  lambda e, ot=ot, wb=wb: e.dma_start(
                      out=wb[:], in_=din["h1Tt"][ot]),
                  reads=[("dram", "h1Tt")], writes=[wbk + "a", wbk + "b"])
            for g in range(NH):
                P.op("tensor", lambda e, g=g, wb=wb, bk_=bk_: e.matmul(
                    out=ps_mm[bk_], lhsT=wb[:, g, :], rhs=t1[:, g, :],
                    start=(g == 0), stop=(g == NH - 1)),
                    reads=[wbk + "a", wbk + "b", "f_t1"], writes=[f"psb{bk_}"],
                    inc=(g == NH - 1))
            P.op("scalar", lambda e, ot=ot, bk_=bk_: e.activation(
                out=t2[:, ot, :], in_=ps_mm[bk_], func=AF.Gelu,
                scale=sh1c[:, ot:ot + 1]),
                reads=[f"psb{bk_}", "f_sh1c"], writes=["f_t2"])

        for ot in range(NH):
            wb, wbk = wbuf[ot % 2], f"f_wbuf{ot % 2}"
            bk_ = ot % 2
            P.dma("gpsimd", wbk + "a",
                  lambda e, ot=ot, wb=wb: e.dma_start(
                      out=wb[:], in_=din["h2Tt"][ot]),
                  reads=[("dram", "h2Tt")], writes=[wbk + "a", wbk + "b"])
            for g in range(NH):
                P.op("tensor", lambda e, g=g, wb=wb, bk_=bk_: e.matmul(
                    out=ps_mm[bk_], lhsT=wb[:, g, :], rhs=t2[:, g, :],
                    start=(g == 0), stop=(g == NH - 1)),
                    reads=[wbk + "a", wbk + "b", "f_t2"], writes=[f"psb{bk_}"],
                    inc=(g == NH - 1))
            P.op("scalar", lambda e, ot=ot, bk_=bk_: e.activation(
                out=t1[:, ot, :], in_=ps_mm[bk_], func=AF.Gelu,
                scale=sh2c[:, ot:ot + 1]),
                reads=[f"psb{bk_}", "f_sh2c"], writes=["f_t1"])

        for ot in range(NT):
            wb, wbk = wbuf[ot % 2], f"f_wbuf{ot % 2}"
            bk_ = ot % 2
            P.dma("gpsimd", wbk + "a",
                  lambda e, ot=ot, wb=wb: e.dma_start(
                      out=wb[:], in_=din["w3Tt"][ot]),
                  reads=[("dram", "w3Tt")], writes=[wbk + "a", wbk + "b"])
            for g in range(NH):
                P.op("tensor", lambda e, g=g, wb=wb, bk_=bk_: e.matmul(
                    out=ps_mm[bk_], lhsT=wb[:, g, :], rhs=t1[:, g, :],
                    start=(g == 0), stop=(g == NH - 1)),
                    reads=[wbk + "a", wbk + "b", "f_t1"], writes=[f"psb{bk_}"],
                    inc=(g == NH - 1))
            P.op("vector", lambda e, ot=ot, bk_=bk_: e.tensor_scalar(
                out=y[:, ot, :], in0=ps_mm[bk_], scalar1=s3c[:, ot:ot + 1],
                scalar2=None, op0=ALU.mult),
                reads=[f"psb{bk_}", "f_s3c"], writes=["f_y"])

        emit_ln_stats(P, C, ln, y, "f_y", tmp, "f_tmp", NT)
        emit_ln_apply(P, C, ln, y, "f_y", xc, "f_xc", tmp, "f_tmp", NT,
                      add_to=xc, add_key="f_xc")
        store_x_chunk(P, dout["xout"], xout_key, xc, "f_xc", c0, TC, "st_xout")


def _tile_lhs(wT):
    K, M = wT.shape
    return np.ascontiguousarray(
        wT.reshape(K // 128, 128, M // 128, 128).transpose(2, 1, 0, 3))


def host_prep_l1ffn(inputs):
    f32, c = np.float32, np.ascontiguousarray
    d = {}
    d["w1Tt"] = _tile_lhs(inputs["L1_ffn_w1_w"].T.astype(f32))
    d["w2Tt"] = _tile_lhs(inputs["L1_ffn_w2_w"].T.astype(f32))
    d["h1Tt"] = _tile_lhs(inputs["L1_ffn_h1_w"].T.astype(f32))
    d["h2Tt"] = _tile_lhs(inputs["L1_ffn_h2_w"].T.astype(f32))
    d["w3Tt"] = _tile_lhs(inputs["L1_ffn_w3_w"].T.astype(f32))
    d["s1c"] = c(np.asarray(inputs["L1_ffn_w1_s"], f32).reshape(32, 128).T)
    d["s2c"] = c(np.asarray(inputs["L1_ffn_w2_s"], f32).reshape(32, 128).T)
    d["sh1c"] = c(np.asarray(inputs["L1_ffn_h1_s"], f32).reshape(32, 128).T)
    d["sh2c"] = c(np.asarray(inputs["L1_ffn_h2_s"], f32).reshape(32, 128).T)
    d["s3c"] = c(np.asarray(inputs["L1_ffn_w3_s"], f32).reshape(8, 128).T)
    return d


L1FFN_SHAPES = {
    "w1Tt": [32, 128, 8, 128], "w2Tt": [32, 128, 8, 128],
    "h1Tt": [32, 128, 32, 128], "h2Tt": [32, 128, 32, 128],
    "w3Tt": [8, 128, 32, 128], "s1c": [128, 32],
    "s2c": [128, 32], "sh1c": [128, 32], "sh2c": [128, 32], "s3c": [128, 8],
}


def build_l1ffn_program():
    nc = bass.Bass("TRN2", target_bir_lowering=False)
    din = {"xin": nc.dram_tensor("xin", [D, T], F32, kind="ExternalInput").ap()}
    for nm, shp in L1FFN_SHAPES.items():
        din[nm] = nc.dram_tensor(nm, shp, F32, kind="ExternalInput").ap()
    dout = {"xout": nc.dram_tensor("xout", [D, T], F32, kind="ExternalOutput").ap()}
    P = Prog(nc)
    P.make_pool()
    C = Common(P)
    build_l1ffn(P, C, din, dout)
    P.final_wait("sync", ["st_xout"])
    P.build()
    return nc





def build_l0ffn(P: Prog, C: Common, din, dout, xin_key="dram_xin",
                xout_key="dram_xout"):
    TC = 512

    xc = P.sbuf("q_xc", [128, NT, TC], F32)
    tmp = P.sbuf("q_tmp", [128, NT, TC], F32)
    hbf = P.sbuf("q_hbf", [128, NT, T], BF16)
    y1bf = P.sbuf("q_y1bf", [128, NT, T], BF16)
    yc = P.sbuf("q_yc", [128, NT, TC], F32)
    qraw = P.sbuf("q_qraw", [128, 1024], I32)
    sbc = P.sbuf("q_sbc", [128, 1024], F32)
    wbf = [P.sbuf(f"q_wbf{i}", [128, NT, 1024], BF16) for i in range(2)]
    u = P.sbuf("q_u", [32, T], BF16)
    lat = P.sbuf("q_lat", [128, NT, 32], BF16)
    lbt = P.sbuf("q_lbt", [32, NT, 128], BF16)
    bcol = P.sbuf("q_bcol", [128, 2, NT], F32)
    gcol = P.sbuf("q_gcol", [128, NT], F32)
    becol = P.sbuf("q_becol", [128, NT], F32)
    ps_u = P.pspool[0:32, 6, :]
    ps_mm = [P.psb(0), P.psb(1)]
    ln = alloc_ln(P, "q")

    for ci, c0 in enumerate(range(0, T, TC)):
        load_x_chunk(P, din["xin"], xin_key, xc, "q_xc", c0, TC)
        emit_ln_stats(P, C, ln, xc, "q_xc", tmp, "q_tmp", NT)
        emit_ln_apply(P, C, ln, xc, "q_xc", yc, "q_yc", tmp, "q_tmp", NT)
        for g in range(NT):
            P.op("vector", lambda e, g=g, c0=c0: e.tensor_copy(
                out=hbf[:, g, c0:c0 + TC], in_=yc[:, g, :]),
                reads=["q_yc"], writes=["q_hbf"])
        store_x_chunk(P, din["hdram"], ("dram", "hdram"), yc, "q_yc", c0, TC, "st_h")

    def dequant(layer, wslot):
        wk = f"q_wbf{wslot}"
        for c in range(NT):
            P.dma("sync", "q_qraw",
                  lambda e, layer=layer, c=c: e.dma_start(
                      out=qraw[:], in_=din["qT"][layer, c * 128:(c + 1) * 128, :]),
                  reads=[("dram", "qT")], writes=["q_qraw"])
            P.dma("sync", "q_sbc",
                  lambda e, layer=layer, c=c: e.dma_start(
                      out=sbc[:],
                      in_=din["sT"][layer, c * 8:(c + 1) * 8, :][:, None, :]
                          .broadcast_to([8, 16, 1024])),
                  reads=[("dram", "sT")], writes=["q_sbc"])
            P.op("vector", lambda e, c=c: e.tensor_scalar(
                out=tmp[:, 0:2, :].rearrange("p a b -> p (a b)"), in0=qraw[:],
                scalar1=1.0 / 7.5, scalar2=-1.0,
                op0=ALU.mult, op1=ALU.add),
                reads=["q_qraw"], writes=["q_tmp"])
            P.op("vector", lambda e, c=c, wslot=wslot: e.tensor_mul(
                out=wbf[wslot][:, c, :], in0=tmp[:, 0:2, :].rearrange("p a b -> p (a b)"),
                in1=sbc[:]),
                reads=["q_tmp", "q_sbc"], writes=[wk])

    def load_lora(layer):
        P.dma("gpsimd", "q_lat",
              lambda e, layer=layer: e.dma_start(
                  out=lat[:], in_=din["laTt"][layer]),
              reads=[("dram", "laTt")], writes=["q_lat"])
        P.dma("gpsimd", "q_lbt",
              lambda e, layer=layer: e.dma_start(
                  out=lbt[:], in_=din["lbT"][layer].rearrange("r (c p) -> r c p", p=128)),
              reads=[("dram", "lbT")], writes=["q_lbt"])

    def lora_mid(rhs, rhs_key):
        for sl in range(T // 512):
            for g in range(NT):
                P.op("tensor", lambda e, sl=sl, g=g: e.matmul(
                    out=ps_u, lhsT=lat[:, g, :], rhs=rhs[:, g, sl * 512:(sl + 1) * 512],
                    start=(g == 0), stop=(g == NT - 1)),
                    reads=["q_lat", rhs_key], writes=["psb6"], inc=(g == NT - 1))
            P.op("scalar", lambda e, sl=sl: e.copy(
                out=u[:, sl * 512:(sl + 1) * 512], in_=ps_u),
                reads=["psb6"], writes=["q_u"])

    def qlora_mm(wslot, rhs, rhs_key, c0, bslot):
        wk = f"q_wbf{wslot}"
        for sl in range(TC // 512):
            s0 = c0 + sl * 512
            for ot in range(NT):
                pm = ps_mm[ot % 2]
                pmk = f"psb{ot % 2}"
                for g in range(NT):
                    P.op("tensor", lambda e, g=g, ot=ot, s0=s0, pm=pm, wslot=wslot: e.matmul(
                        out=pm, lhsT=wbf[wslot][:, g, ot * 128:(ot + 1) * 128],
                        rhs=rhs[:, g, s0:s0 + 512],
                        start=(g == 0), stop=False),
                        reads=[wk, rhs_key], writes=[pmk], inc=False)
                P.op("tensor", lambda e, ot=ot, s0=s0, pm=pm: e.matmul(
                    out=pm, lhsT=lbt[:, ot, :], rhs=u[:, s0:s0 + 512],
                    start=False, stop=True),
                    reads=["q_lbt", "q_u", pmk], writes=[pmk], inc=True)
                P.op("vector", lambda e, ot=ot, sl=sl, pm=pm, bslot=bslot: e.tensor_scalar(
                    out=yc[:, ot, sl * 512:(sl + 1) * 512], in0=pm,
                    scalar1=bcol[:, bslot, ot:ot + 1], scalar2=None, op0=ALU.add),
                    reads=[pmk, "q_bcol"], writes=["q_yc"])

    for blk in range(3):
        l0, l1_ = 2 * blk, 2 * blk + 1
        dequant(l0, 0)
        load_lora(l0)
        P.dma("gpsimd", "q_bcol",
              lambda e, l0=l0: e.dma_start(
                  out=bcol[:], in_=din["bcol"][l0:l0 + 2].rearrange("a p c -> p a c")),
              reads=[("dram", "bcol")], writes=["q_bcol"])
        P.dma("gpsimd", "q_gcol",
              lambda e, blk=blk: e.dma_start(out=gcol[:], in_=din["gcol"][blk]),
              reads=[("dram", "gcol")], writes=["q_gcol"])
        P.dma("gpsimd", "q_becol",
              lambda e, blk=blk: e.dma_start(out=becol[:], in_=din["becol"][blk]),
              reads=[("dram", "becol")], writes=["q_becol"])
        lora_mid(hbf, "q_hbf")
        for c0 in range(0, T, TC):
            qlora_mm(0, hbf, "q_hbf", c0, 0)
            emit_ln_stats(P, C, ln, yc, "q_yc", tmp, "q_tmp", NT)
            emit_ln_apply(P, C, ln, yc, "q_yc", yc, "q_yc", tmp, "q_tmp", NT,
                          gamma_col=gcol, beta_col=becol,
                          gb_keys=["q_gcol", "q_becol"])
            for g in range(NT):
                P.op("scalar", lambda e, g=g, c0=c0: e.activation(
                    out=y1bf[:, g, c0:c0 + TC], in_=yc[:, g, :], func=AF.Gelu),
                    reads=["q_yc"], writes=["q_y1bf"])
        dequant(l1_, 1)
        load_lora(l1_)
        lora_mid(y1bf, "q_y1bf")
        last = blk == 2
        for c0 in range(0, T, TC):
            qlora_mm(1, y1bf, "q_y1bf", c0, 1)
            load_x_chunk(P, din["hdram"], ("dram", "hdram"), xc, "q_xc", c0, TC)
            for g in range(NT):
                P.op("vector", lambda e, g=g: e.tensor_add(
                    out=yc[:, g, :], in0=yc[:, g, :], in1=xc[:, g, :]),
                    reads=["q_yc", "q_xc"], writes=["q_yc"])
            if not last:
                for g in range(NT):
                    P.op("vector", lambda e, g=g, c0=c0: e.tensor_copy(
                        out=hbf[:, g, c0:c0 + TC], in_=yc[:, g, :]),
                        reads=["q_yc"], writes=["q_hbf"])
                store_x_chunk(P, din["hdram"], ("dram", "hdram"), yc, "q_yc", c0,
                              TC, "st_h")
            else:
                emit_ln_stats(P, C, ln, yc, "q_yc", tmp, "q_tmp", NT)
                load_x_chunk(P, din["xin"], xin_key, xc, "q_xc", c0, TC)
                emit_ln_apply(P, C, ln, yc, "q_yc", xc, "q_xc", tmp, "q_tmp", NT,
                              add_to=xc, add_key="q_xc")
                store_x_chunk(P, dout["xout"], xout_key, xc, "q_xc", c0, TC,
                              "st_xout")


def host_prep_l0ffn(inputs):
    f32, c = np.float32, np.ascontiguousarray
    d = {}
    d["qT"] = c(np.asarray(inputs["L0_ffn_qweights"], np.int32).transpose(0, 2, 1))
    d["sT"] = c(np.asarray(inputs["L0_ffn_scales"], f32).transpose(0, 2, 1))
    d["bcol"] = c(np.asarray(inputs["L0_ffn_biases"], f32).reshape(6, 8, 128)
                  .transpose(0, 2, 1))
    laT = np.asarray(inputs["L0_ffn_lora_a"], f32).transpose(0, 2, 1)
    d["laTt"] = c(laT.reshape(6, 8, 128, 32).transpose(0, 2, 1, 3))
    d["lbT"] = c(np.asarray(inputs["L0_ffn_lora_b"], f32).transpose(0, 2, 1))
    d["gcol"] = c(np.asarray(inputs["L0_ffn_ln_gamma"], f32).reshape(3, 8, 128)
                  .transpose(0, 2, 1))
    d["becol"] = c(np.asarray(inputs["L0_ffn_ln_beta"], f32).reshape(3, 8, 128)
                   .transpose(0, 2, 1))
    return d


L0FFN_SHAPES = {
    "qT": ([6, 1024, 1024], I32), "sT": ([6, 64, 1024], F32),
    "bcol": ([6, 128, 8], F32), "laTt": ([6, 128, 8, 32], F32),
    "lbT": ([6, 32, 1024], F32), "gcol": ([3, 128, 8], F32),
    "becol": ([3, 128, 8], F32),
}


def build_l0ffn_program():
    nc = bass.Bass("TRN2", target_bir_lowering=False)
    din = {"xin": nc.dram_tensor("xin", [D, T], F32, kind="ExternalInput").ap()}
    for nm, (shp, dt) in L0FFN_SHAPES.items():
        din[nm] = nc.dram_tensor(nm, shp, dt, kind="ExternalInput").ap()
    din["hdram"] = nc.dram_tensor("hdram", [D, T], F32, kind="Internal").ap()
    dout = {"xout": nc.dram_tensor("xout", [D, T], F32, kind="ExternalOutput").ap()}
    P = Prog(nc)
    P.make_pool()
    C = Common(P)
    build_l0ffn(P, C, din, dout)
    P.final_wait("sync", ["st_xout"])
    P.build()
    return nc





def build_l1attn(P: Prog, C: Common, din, dout, xin_key="dram_xin",
                 xout_key="dram_xout"):
    TC = 512
    NIC = T // 128
    W = 2176

    xc = P.sbuf("a_xc", [128, NT, TC], F32)
    tmp = P.sbuf("a_tmp", [128, NT, TC], F32)
    xhat = P.sbuf("a_xhat", [128, NT, T], BF16)
    wp3 = P.sbuf("a_wp3", [128, NT, 3, 128], BF16)
    qp = P.sbuf("a_qp", [128, T], BF16)
    kp = P.sbuf("a_kp", [128, T], BF16)
    vp = P.sbuf("a_vp", [128, NIC, 128], BF16)
    tab2 = P.sbuf("a_tab2", [128, 4096], BF16)
    relw = P.sbuf("a_relw", [128, 2, W], BF16)
    rpe = P.sbuf("a_rpe", [128, 4, T], BF16)
    praw = P.sbuf("a_praw", [128, 2, T], BF16)
    ptb4 = P.sbuf("a_ptb4", [128, NIC, 512], BF16)
    wo_sb = ptb4.rearrange("p a b -> p (a b)")
    mrow = P.sbuf("a_mrow", [128, 2, 8], F32)
    ctx = P.sbuf("a_ctx", [128, NT, T], BF16)
    yc = P.sbuf("a_yc", [128, NT, TC], F32)
    sfull = [yc[:, 0:4, :].rearrange("p a b -> p (a b)"),
             yc[:, 4:8, :].rearrange("p a b -> p (a b)")]
    bqc = P.sbuf("a_bqc", [128, 8], F32)
    bkc = P.sbuf("a_bkc", [128, 8], F32)
    boc = P.sbuf("a_boc", [128, 8], F32)
    bvb = P.sbuf("a_bvb", [128, 1024], F32)
    identbf = P.sbuf("a_ident", [128, 128], BF16)
    ln = alloc_ln(P, "a")

    for nm, buf in (("bqc", bqc), ("bkc", bkc), ("boc", boc)):
        P.dma("sync", "a_" + nm,
              lambda e, nm=nm, buf=buf: e.dma_start(out=buf[:], in_=din[nm][:]),
              reads=[("dram", nm)], writes=["a_" + nm])
    P.dma("sync", "a_bvb",
          lambda e: e.dma_start(out=bvb[:], in_=din["bvrow"][:].broadcast_to([128, 1024])),
          reads=[("dram", "bvrow")], writes=["a_bvb"])
    P.dma("sync", "a_ident",
          lambda e: e.dma_start(out=identbf[:], in_=din["identbf"][:]),
          reads=[("dram", "identbf")], writes=["a_ident"])
    P.op("vector", lambda e: e.memset(tab2[:], 0.0), writes=["a_tab2"])

    for c0 in range(0, T, TC):
        load_x_chunk(P, din["xin"], xin_key, xc, "a_xc", c0, TC)
        emit_ln_stats(P, C, ln, xc, "a_xc", tmp, "a_tmp", NT)
        emit_ln_apply(P, C, ln, xc, "a_xc",
                      xhat[:, :, c0:c0 + TC], "a_xhat", tmp, "a_tmp", NT)

    for hp in range(8):
        for wi, wnm in enumerate(("wqTt", "wkTt", "wvTt")):
            P.dma("gpsimd", "a_wp3",
                  lambda e, wi=wi, wnm=wnm, hp=hp: e.dma_start(
                      out=wp3[:, :, wi, :], in_=din[wnm][hp]),
                  reads=[("dram", wnm)], writes=["a_wp3"])
        for hh in range(2):
            h = hp * 2 + hh
            P.dma("gpsimd", "a_tab2",
                  lambda e, h=h, hh=hh: e.dma_start(
                      out=tab2[64 * hh:64 * hh + 64, 0:4095], in_=din["tabTrev"][h]),
                  reads=[("dram", "tabTrev")], writes=["a_tab2"])
        for ts in range(4):
            for g in range(NT):
                P.op("tensor", lambda e, g=g, ts=ts: e.matmul(
                    out=P.psb(0), lhsT=wp3[:, g, 0, :], rhs=xhat[:, g, ts * 512:(ts + 1) * 512],
                    start=(g == 0), stop=(g == NT - 1)),
                    reads=["a_wp3", "a_xhat"], writes=["psb0"], inc=(g == NT - 1))
            P.op("vector", lambda e, ts=ts, hp=hp: e.tensor_scalar(
                out=qp[:, ts * 512:(ts + 1) * 512], in0=P.psb(0),
                scalar1=bqc[:, hp:hp + 1], scalar2=None, op0=ALU.add),
                reads=["psb0", "a_bqc"], writes=["a_qp"])
            for g in range(NT):
                P.op("tensor", lambda e, g=g, ts=ts: e.matmul(
                    out=P.psb(1), lhsT=wp3[:, g, 1, :], rhs=xhat[:, g, ts * 512:(ts + 1) * 512],
                    start=(g == 0), stop=(g == NT - 1)),
                    reads=["a_wp3", "a_xhat"], writes=["psb1"], inc=(g == NT - 1))
            P.op("vector", lambda e, ts=ts, hp=hp: e.tensor_scalar(
                out=kp[:, ts * 512:(ts + 1) * 512], in0=P.psb(1),
                scalar1=bkc[:, hp:hp + 1], scalar2=0.125, op0=ALU.add, op1=ALU.mult),
                reads=["psb1", "a_bkc"], writes=["a_kp"])
        for tc in range(NIC):
            for g in range(NT):
                P.op("tensor", lambda e, g=g, tc=tc: e.matmul(
                    out=P.pspool[:, 2, 0:128], lhsT=xhat[:, g, tc * 128:(tc + 1) * 128],
                    rhs=wp3[:, g, 2, :],
                    start=(g == 0), stop=(g == NT - 1)),
                    reads=["a_wp3", "a_xhat"], writes=["psb2"], inc=(g == NT - 1))
            P.op("vector", lambda e, tc=tc, hp=hp: e.tensor_add(
                out=vp[:, tc, :], in0=P.pspool[:, 2, 0:128],
                in1=bvb[:, hp * 128:(hp + 1) * 128]),
                reads=["psb2", "a_bvb"], writes=["a_vp"])

        for hh in range(2):
            h = hp * 2 + hh
            e0 = 64 * hh

            def emit_relw(ic, e0=e0):
                i0 = ic * 128
                s0w = 1920 - i0
                pi = ic % 2
                pr = ic % 4
                for rs in range(5):
                    w = 512 if rs < 4 else 128
                    bk_ = 4 + rs % 2
                    P.op("tensor", lambda e, rs=rs, w=w, bk_=bk_, e0=e0, i0=i0, s0w=s0w: e.matmul(
                        out=P.pspool[:, bk_, 0:w],
                        lhsT=qp[e0:e0 + 64, i0:i0 + 128],
                        rhs=tab2[e0:e0 + 64, s0w + rs * 512:s0w + rs * 512 + w],
                        start=True, stop=True),
                        reads=["a_qp", "a_tab2"], writes=[f"psb{bk_}"])
                    P.op("vector", lambda e, rs=rs, w=w, bk_=bk_, pi=pi: e.tensor_copy(
                        out=relw[:, pi, rs * 512:rs * 512 + w], in_=P.pspool[:, bk_, 0:w]),
                        reads=[f"psb{bk_}"], writes=[f"a_relw{pi}"])
                P.dma("sync", f"a_reld{pi}",
                      lambda e, pi=pi: e.dma_start(out=din["reld"][pi], in_=relw[:, pi, :]),
                      reads=[f"a_relw{pi}"], writes=[("dram", f"reld{pi}")])
                shear = bass.AP(tensor=din["reld"].tensor, offset=pi * 128 * W + 127,
                                ap=[[W - 1, 128], [1, T]])
                P.dma("sync", f"a_rpe{pr}",
                      lambda e, shear=shear, pr=pr: e.dma_start(
                          out=rpe[:, pr, :], in_=shear),
                      reads=[("dram", f"reld{pi}")], writes=[f"a_rpe{pr}"])

            def emit_scores(ic, e0=e0):
                i0 = ic * 128
                pi = ic % 2
                pr = ic % 4
                for js in range(4):
                    P.op("tensor", lambda e, js=js, i0=i0, e0=e0: e.matmul(
                        out=P.psb(js), lhsT=qp[e0:e0 + 64, i0:i0 + 128],
                        rhs=kp[e0:e0 + 64, js * 512:(js + 1) * 512],
                        start=True, stop=False),
                        reads=["a_qp", "a_kp"], writes=[f"psb{js}"], inc=False)
                    P.op("tensor", lambda e, js=js, pr=pr: e.matmul(
                        out=P.psb(js), lhsT=identbf[:],
                        rhs=rpe[:, pr, js * 512:(js + 1) * 512],
                        start=False, stop=True),
                        reads=["a_ident", f"a_rpe{pr}", f"psb{js}"], writes=[f"psb{js}"])
                    P.op("vector", lambda e, js=js, pi=pi: e.reduce_max(
                        out=mrow[:, pi, js:js + 1], in_=P.psb(js), axis=AX.X),
                        reads=[f"psb{js}"], writes=[f"a_mrow{pi}"])
                P.op("vector", lambda e, pi=pi: e.tensor_max(
                    out=mrow[:, pi, 4:5], in0=mrow[:, pi, 0:1], in1=mrow[:, pi, 1:2]),
                    reads=[f"a_mrow{pi}"], writes=[f"a_mrow{pi}"])
                P.op("vector", lambda e, pi=pi: e.tensor_max(
                    out=mrow[:, pi, 5:6], in0=mrow[:, pi, 2:3], in1=mrow[:, pi, 3:4]),
                    reads=[f"a_mrow{pi}"], writes=[f"a_mrow{pi}"])
                P.op("vector", lambda e, pi=pi: e.tensor_max(
                    out=mrow[:, pi, 4:5], in0=mrow[:, pi, 4:5], in1=mrow[:, pi, 5:6]),
                    reads=[f"a_mrow{pi}"], writes=[f"a_mrow{pi}"])
                P.op("vector", lambda e, pi=pi: e.tensor_scalar(
                    out=mrow[:, pi, 4:5], in0=mrow[:, pi, 4:5], scalar1=-1.0,
                    scalar2=None, op0=ALU.mult),
                    reads=[f"a_mrow{pi}"], writes=[f"a_mrow{pi}"])
                P.op("scalar", lambda e, pi=pi: e.activation(
                    out=praw[:, pi, :], in_=P.psw(0, 4), func=AF.Exp,
                    bias=mrow[:, pi, 4:5], accum_out=mrow[:, pi, 6:7]),
                    reads=["psb0", "psb1", "psb2", "psb3", f"a_mrow{pi}"],
                    writes=[f"a_praw{pi}", f"a_mrow{pi}"])

            def emit_post(ic):
                pi = ic % 2
                P.op("vector", lambda e, pi=pi: e.reciprocal(
                    out=mrow[:, pi, 7:8], in_=mrow[:, pi, 6:7]),
                    reads=[f"a_mrow{pi}"], writes=[f"a_mrow{pi}"])
                P.op("vector", lambda e, pi=pi: e.tensor_scalar(
                    out=praw[:, pi, :], in0=praw[:, pi, :], scalar1=mrow[:, pi, 7:8],
                    scalar2=None, op0=ALU.mult),
                    reads=[f"a_praw{pi}", f"a_mrow{pi}"], writes=[f"a_praw{pi}"])

            def emit_trans(ic):
                pi = ic % 2
                icq = ic % 4
                for kc in range(NIC):
                    tb = 6 + (kc // 4) % 2
                    P.op("tensor", lambda e, kc=kc, pi=pi, tb=tb: e.matmul(
                        out=P.pspool[:, tb, (kc % 4) * 128:(kc % 4) * 128 + 128],
                        lhsT=praw[:, pi, kc * 128:(kc + 1) * 128], rhs=identbf[:],
                        start=True, stop=True),
                        reads=[f"a_praw{pi}", "a_ident"], writes=[f"psb{tb}"],
                        inc=(kc % 4 == 3))
                    if kc % 4 == 3:
                        g4 = kc // 4
                        P.op("scalar", lambda e, g4=g4, icq=icq, tb=tb: e.copy(
                            out=ptb4[:, 4 * g4:4 * g4 + 4, icq * 128:icq * 128 + 128],
                            in_=P.pspool[:, tb, :].rearrange("p (a b) -> p a b", a=4)),
                            reads=[f"psb{tb}"], writes=["a_ptb4"])

            def emit_ctx(icg, e0=e0, hp=hp):
                for jc in range(NIC):
                    P.op("tensor", lambda e, jc=jc, e0=e0: e.matmul(
                        out=P.pspool[0:64, 6, 0:512], lhsT=vp[:, jc, e0:e0 + 64],
                        rhs=ptb4[:, jc, :],
                        start=(jc == 0), stop=(jc == NIC - 1)),
                        reads=["a_vp", "a_ptb4"], writes=["psb6"],
                        inc=(jc == NIC - 1))
                P.op("scalar", lambda e, icg=icg, e0=e0, hp=hp: e.copy(
                    out=ctx[e0:e0 + 64, hp, icg * 512:(icg + 1) * 512],
                    in_=P.pspool[0:64, 6, 0:512]),
                    reads=["psb6"], writes=["a_ctx"])

            for ic in range(-2, NIC + 1):
                if ic >= 1:
                    emit_post(ic - 1)
                if 0 <= ic < NIC:
                    emit_scores(ic)
                if ic + 2 < NIC:
                    emit_relw(ic + 2)
                if ic >= 1:
                    emit_trans(ic - 1)
                    if (ic - 1) % 4 == 3:
                        emit_ctx((ic - 1) // 4)

    P.dma("gpsimd", "a_wo",
          lambda e: e.dma_start(out=wo_sb[:], in_=din["woTt"][:]),
          reads=[("dram", "woTt"), "a_ptb4"], writes=["a_wo", "a_ptb4"])
    for ci, c0 in enumerate(range(0, T, TC)):
        for et in range(NT):
            bk_ = et % 2
            for g in range(NT):
                P.op("tensor", lambda e, g=g, et=et, ci=ci, bk_=bk_: e.matmul(
                    out=P.psb(bk_),
                    lhsT=wo_sb[:, g * 1024 + et * 128:g * 1024 + et * 128 + 128],
                    rhs=ctx[:, g, ci * 512:(ci + 1) * 512],
                    start=(g == 0), stop=(g == NT - 1)),
                    reads=["a_wo", "a_ctx"], writes=[f"psb{bk_}"], inc=(g == NT - 1))
            P.op("vector", lambda e, et=et, bk_=bk_: e.tensor_scalar(
                out=yc[:, et, :], in0=P.psb(bk_), scalar1=boc[:, et:et + 1],
                scalar2=None, op0=ALU.add),
                reads=[f"psb{bk_}", "a_boc"], writes=["a_yc"])
        emit_ln_stats(P, C, ln, yc, "a_yc", tmp, "a_tmp", NT)
        load_x_chunk(P, din["xin"], xin_key, xc, "a_xc", c0, TC)
        emit_ln_apply(P, C, ln, yc, "a_yc", xc, "a_xc", tmp, "a_tmp", NT,
                      add_to=xc, add_key="a_xc")
        store_x_chunk(P, dout["xout"], xout_key, xc, "a_xc", c0, TC, "st_xout")


def host_prep_l1attn(inputs):
    f32, c = np.float32, np.ascontiguousarray
    d = {}
    d["wqTt"] = _tile_lhs(np.asarray(inputs["L1_attn_wq"], f32).T)
    d["wkTt"] = _tile_lhs(np.asarray(inputs["L1_attn_wk"], f32).T)
    d["wvTt"] = _tile_lhs(np.asarray(inputs["L1_attn_wv"], f32).T)
    woT = np.asarray(inputs["L1_attn_wo"], f32).T
    d["woTt"] = c(woT.reshape(8, 128, 1024).transpose(1, 0, 2))
    d["bqc"] = c(np.asarray(inputs["L1_attn_bq"], f32).reshape(8, 128).T)
    d["bkc"] = c(np.asarray(inputs["L1_attn_bk"], f32).reshape(8, 128).T)
    d["boc"] = c(np.asarray(inputs["L1_attn_bo"], f32).reshape(8, 128).T)
    d["bvrow"] = c(np.asarray(inputs["L1_attn_bv"], f32).reshape(1, 1024))
    tab = np.asarray(inputs["L1_attn_rpe_table"], f32)[:, :2 * T - 1]
    d["tabTrev"] = c(tab[:, ::-1].transpose(0, 2, 1))
    d["identbf"] = np.eye(128, dtype=BF16NP)
    return d


L1ATTN_SHAPES = {
    "wqTt": ([8, 128, 8, 128], F32), "wkTt": ([8, 128, 8, 128], F32),
    "wvTt": ([8, 128, 8, 128], F32), "woTt": ([128, 8, 1024], F32),
    "bqc": ([128, 8], F32), "bkc": ([128, 8], F32), "boc": ([128, 8], F32),
    "bvrow": ([1, 1024], F32), "tabTrev": ([16, 64, 4095], F32),
    "identbf": ([128, 128], BF16),
}


def build_l1attn_program():
    nc = bass.Bass("TRN2", target_bir_lowering=False)
    din = {"xin": nc.dram_tensor("xin", [D, T], F32, kind="ExternalInput").ap()}
    for nm, (shp, dt) in L1ATTN_SHAPES.items():
        din[nm] = nc.dram_tensor(nm, shp, dt, kind="ExternalInput").ap()
    din["reld"] = nc.dram_tensor("reld", [2, 128, 2176], BF16, kind="Internal").ap()
    dout = {"xout": nc.dram_tensor("xout", [D, T], F32, kind="ExternalOutput").ap()}
    P = Prog(nc)
    P.make_pool()
    C = Common(P)
    build_l1attn(P, C, din, dout)
    P.final_wait("sync", ["st_xout"])
    P.build()
    return nc





PI = float(np.pi)


def build_l0attn(P: Prog, C: Common, din, dout, xin_key="dram_xin",
                 xout_key="dram_xout"):
    TCC = 512
    NCH = T // TCC
    CT = 17

    xc = P.sbuf("g_xc", [128, NT, TCC], F32)
    tmp = P.sbuf("g_tmp", [128, NT, TCC], F32)
    xhat = P.sbuf("g_xhat", [128, NT, TCC], BF16)
    wproj = P.sbuf("g_wproj", [128, NT, 2080], BF16)
    outw = [P.sbuf(f"g_outw{i}", [128, NT, 128], BF16) for i in range(2)]
    zbuf = P.sbuf("g_zbuf", [128, CT, TCC + 3], F32)
    z2c = P.sbuf("g_z2c", [128, 9, TCC], F32)
    gate = P.sbuf("g_gate", [128, NT, TCC], BF16)
    scr = P.sbuf("g_scr", [128, 2, TCC], F32)
    sqt = P.sbuf("g_sqt", [128, 2, TCC], F32)
    convw = P.sbuf("g_convw", [128, CT, 4], F32)
    convb = P.sbuf("g_convb", [128, CT], F32)
    thc = P.sbuf("g_thc", [128, 4, NT], F32)
    slc = P.sbuf("g_slc", [32, 4], F32)
    sscol = P.sbuf("g_sscol", [32, 1], F32)
    dlog = P.sbuf("g_dlog", [1, 4], F32)
    wcols = P.sbuf("g_wcols", [128, 4], F32)
    nscol = P.sbuf("g_nscol", [32, 8], F32)
    wre = P.sbuf("g_wre", [32, 8, 32], F32)
    bd = P.sbuf("g_bd", [128, 8, 128], BF16)
    selg = P.sbuf("g_selg", [32, NT, 128], F32)
    sel4 = P.sbuf("g_sel4", [4, 128], F32)
    o32 = P.sbuf("g_o32", [128, 4], F32)
    dist = P.sbuf("g_dist", [32, 512], F32)
    twc = P.sbuf("g_twc", [32, TCC], F32)
    pw = P.sbuf("g_pw", [32, TCC], F32)
    denc = P.sbuf("g_denc", [32, TCC], F32)
    sstage = P.sbuf("g_sstage", [4, 2, TCC], F32)
    dcarry = P.sbuf("g_dcarry", [32, 2], F32)
    carry = P.sbuf("g_carry", [128, 64], F32)
    numf = P.sbuf("g_numf", [128, 8, TCC], F32)
    numbf = P.sbuf("g_numbf", [128, 8, TCC], BF16)
    ygate = P.sbuf("g_ygate", [128, NT, TCC], BF16)
    zeros = P.sbuf("g_zeros", [128, TCC], F32)
    negpi = P.sbuf("g_negpi", [128, 1], F32)
    pihalf = P.sbuf("g_pihalf", [128, 1], F32)
    onecol = P.sbuf("g_onecol", [32, 1], F32)

    P.dma("sync", "g_convw", lambda e: e.dma_start(
        out=convw[:], in_=din["convT"].rearrange("(c p) j -> p c j", p=128)),
        reads=[("dram", "convT")], writes=["g_convw"])
    P.dma("sync", "g_convb", lambda e: e.dma_start(
        out=convb[:], in_=din["convb"][:]),
        reads=[("dram", "convb")], writes=["g_convb"])
    P.dma("sync", "g_thc", lambda e: e.dma_start(
        out=thc[:], in_=din["thetac"][:]),
        reads=[("dram", "thetac")], writes=["g_thc"])
    P.dma("sync", "g_slc", lambda e: e.dma_start(
        out=slc[:, 0:1], in_=din["slopes"][:]),
        reads=[("dram", "slopes")], writes=["g_slc"])
    P.dma("sync", "g_sscol", lambda e: e.dma_start(
        out=sscol[:], in_=din["sscol"][:]),
        reads=[("dram", "sscol")], writes=["g_sscol"])
    P.dma("sync", "g_dlog", lambda e: e.dma_start(
        out=dlog[0:1, 0:3], in_=din["dlog"][:]),
        reads=[("dram", "dlog")], writes=["g_dlog"])
    P.dma("sync", "g_nscol", lambda e: e.dma_start(
        out=nscol[:], in_=din["nsc"][:]),
        reads=[("dram", "nsc")], writes=["g_nscol"])
    P.dma("sync", "g_wre", lambda e: e.dma_start(
        out=wre[:], in_=din["wrim"].rearrange("r m h o -> h (r m) o")),
        reads=[("dram", "wrim")], writes=["g_wre"])
    P.dma("sync", "g_selg", lambda e: e.dma_start(
        out=selg[:], in_=din["selg"].rearrange("g k p -> k g p")),
        reads=[("dram", "selg")], writes=["g_selg"])
    P.dma("sync", "g_sel4", lambda e: e.dma_start(
        out=sel4[:], in_=din["sel4"][:]), reads=[("dram", "sel4")], writes=["g_sel4"])
    P.dma("sync", "g_o32", lambda e: e.dma_start(
        out=o32[:], in_=din["o32"][:]), reads=[("dram", "o32")], writes=["g_o32"])
    i32b = P.sbuf("g_i32", [32, 32], F32)
    P.dma("sync", "g_i32", lambda e: e.dma_start(
        out=i32b[:], in_=din["i32"][:]), reads=[("dram", "i32")], writes=["g_i32"])

    P.dma("gpsimd", "g_wproj", lambda e: e.dma_start(
        out=wproj[:], in_=din["inprojw"][:]),
        reads=[("dram", "inprojw")], writes=["g_wproj"])

    P.op("vector", lambda e: e.memset(zeros[:], 0.0), writes=["g_zeros"])
    P.op("vector", lambda e: e.memset(negpi[:], -PI), writes=["g_negpi"])
    P.op("vector", lambda e: e.memset(pihalf[:], 0.5 * PI), writes=["g_pihalf"])
    P.op("vector", lambda e: e.memset(onecol[:], 1.0), writes=["g_onecol"])
    P.op("vector", lambda e: e.memset(zbuf[:], 0.0), writes=["g_zbuf"])
    P.op("vector", lambda e: e.memset(carry[:], 0.0), writes=["g_carry"])
    P.op("vector", lambda e: e.memset(dcarry[:], 0.0), writes=["g_dcarry"])
    P.op("vector", lambda e: e.memset(bd[:], 0.0), writes=["g_bd"])

    P.op("scalar", lambda e: e.activation(
        out=slc[:, 2:3], in_=slc[:, 0:1], func=AF.Abs),
        reads=["g_slc"], writes=["g_slc"])
    P.op("scalar", lambda e: e.activation(
        out=slc[:, 2:3], in_=slc[:, 2:3], func=AF.Exp, scale=-1.0),
        reads=["g_slc"], writes=["g_slc"])
    P.op("scalar", lambda e: e.activation(
        out=slc[:, 2:3], in_=slc[:, 2:3], func=AF.Ln, bias=onecol[:]),
        reads=["g_slc", "g_onecol"], writes=["g_slc"])
    P.op("vector", lambda e: e.tensor_scalar(
        out=slc[:, 3:4], in0=slc[:, 0:1], scalar1=0.0, scalar2=None,
        op0=ALU.max), reads=["g_slc"], writes=["g_slc"])
    P.op("vector", lambda e: e.tensor_tensor(
        out=slc[:, 1:2], in0=slc[:, 2:3], in1=slc[:, 3:4], op=ALU.add),
        reads=["g_slc"], writes=["g_slc"])
    P.op("vector", lambda e: e.tensor_scalar(
        out=slc[:, 1:2], in0=slc[:, 1:2], scalar1=-1.0, scalar2=None,
        op0=ALU.mult), reads=["g_slc"], writes=["g_slc"])

    P.op("vector", lambda e: e.reduce_max(
        out=dlog[0:1, 3:4], in_=dlog[0:1, 0:3], axis=AX.X),
        reads=["g_dlog"], writes=["g_dlog"])
    P.op("vector", lambda e: e.tensor_scalar(
        out=dlog[0:1, 3:4], in0=dlog[0:1, 3:4], scalar1=-1.0, scalar2=None,
        op0=ALU.mult), reads=["g_dlog"], writes=["g_dlog"])
    P.op("scalar", lambda e: e.activation(
        out=dlog[0:1, 0:3], in_=dlog[0:1, 0:3], func=AF.Exp, bias=dlog[0:1, 3:4]),
        reads=["g_dlog"], writes=["g_dlog"])
    P.op("vector", lambda e: e.reduce_sum(
        out=dlog[0:1, 3:4], in_=dlog[0:1, 0:3], axis=AX.X),
        reads=["g_dlog"], writes=["g_dlog"])
    P.op("vector", lambda e: e.reciprocal(
        out=dlog[0:1, 3:4], in_=dlog[0:1, 3:4]), reads=["g_dlog"], writes=["g_dlog"])
    P.op("vector", lambda e: e.tensor_scalar(
        out=dlog[0:1, 0:3], in0=dlog[0:1, 0:3], scalar1=dlog[0:1, 3:4],
        scalar2=None, op0=ALU.mult), reads=["g_dlog"], writes=["g_dlog"])
    P.op("tensor", lambda e: e.matmul(
        out=P.pspool[:, 7, 0:3], lhsT=C.ones_row[:], rhs=dlog[0:1, 0:3],
        start=True, stop=True), reads=["g_dlog", "c_ones_row"], writes=["psb7"])
    P.op("vector", lambda e: e.tensor_copy(
        out=wcols[:, 0:3], in_=P.pspool[:, 7, 0:3]), reads=["psb7"], writes=["g_wcols"])
    P.op("vector", lambda e: e.tensor_scalar(
        out=wcols[:, 3:4], in0=wcols[:, 2:3], scalar1=-1.0, scalar2=None,
        op0=ALU.mult), reads=["g_wcols"], writes=["g_wcols"])

    for rim in range(8):
        P.op("vector", lambda e, rim=rim: e.tensor_scalar(
            out=wre[:, rim, :], in0=wre[:, rim, :], scalar1=nscol[:, rim:rim + 1],
            scalar2=None, op0=ALU.mult), reads=["g_wre", "g_nscol"], writes=["g_wre"])
    for rim in range(8):
        for kq in range(4):
            P.dma("gpsimd", "g_bd",
                  lambda e, rim=rim, kq=kq: e.dma_start(
                      out=bd[32 * kq:32 * kq + 32, rim, 32 * kq:32 * kq + 32],
                      in_=wre[:, rim, :]),
                  reads=["g_wre"], writes=["g_bd"])

    ln = alloc_ln(P, "g")

    for ci in range(NCH):
        c0 = ci * TCC
        load_x_chunk(P, din["xin"], xin_key, xc, "g_xc", c0, TCC)
        emit_ln_stats(P, C, ln, xc, "g_xc", tmp, "g_tmp", NT)
        emit_ln_apply(P, C, ln, xc, "g_xc", xhat, "g_xhat", tmp, "g_tmp", NT)

        for ct in range(CT):
            ncol = 128 if ct < 16 else 32
            bk_ = ct % 2
            for g in range(NT):
                P.op("tensor", lambda e, ct=ct, g=g, bk_=bk_, ncol=ncol: e.matmul(
                    out=P.pspool[0:ncol, bk_, :], lhsT=wproj[:, g, ct * 128:ct * 128 + ncol],
                    rhs=xhat[:, g, :], start=(g == 0), stop=(g == NT - 1)),
                    reads=["g_wproj", "g_xhat"], writes=[f"psb{bk_}"], inc=(g == NT - 1))
            P.op("scalar", lambda e, ct=ct, bk_=bk_, ncol=ncol: e.copy(
                out=zbuf[0:ncol, ct, 3:3 + TCC], in_=P.pspool[0:ncol, bk_, :]),
                reads=[f"psb{bk_}"], writes=["g_zbuf"])

        for ct in range(CT):
            nrow = 128 if ct < 16 else 32
            if ct < 8:
                dst, dstk = z2c, "g_z2c"
                dsl = (slice(0, nrow), ct, slice(0, TCC))
            elif ct < 16:
                dst, dstk = scr, f"g_scr{ct % 2}"
                dsl = (slice(0, nrow), ct % 2, slice(0, TCC))
            else:
                dst, dstk = z2c, "g_z2c"
                dsl = (slice(0, nrow), 8, slice(0, TCC))
            P.op("vector", lambda e, ct=ct, nrow=nrow, dst=dst, dsl=dsl: e.tensor_scalar(
                out=dst[dsl], in0=zbuf[0:nrow, ct, 0:TCC],
                scalar1=convw[0:nrow, ct, 0:1], scalar2=convb[0:nrow, ct:ct + 1],
                op0=ALU.mult, op1=ALU.add),
                reads=["g_zbuf", "g_convw", "g_convb"], writes=[dstk])
            for j in (1, 2, 3):
                P.op("vector", lambda e, ct=ct, j=j, nrow=nrow, dst=dst, dsl=dsl:
                     e.scalar_tensor_tensor(
                         out=dst[dsl], in0=zbuf[0:nrow, ct, j:j + TCC],
                         scalar=convw[0:nrow, ct, j:j + 1], in1=dst[dsl],
                         op0=ALU.mult, op1=ALU.add),
                     reads=["g_zbuf", "g_convw", dstk], writes=[dstk])
            if 8 <= ct < 16:
                P.op("scalar", lambda e, ct=ct: e.activation(
                    out=gate[:, ct - 8, :], in_=scr[:, ct % 2, :], func=AF.Silu),
                    reads=[f"g_scr{ct % 2}"], writes=["g_gate"])
        P.op("vector", lambda e: e.tensor_copy(
            out=zbuf[:, :, 0:3], in_=zbuf[:, :, TCC:TCC + 3]),
            reads=["g_zbuf"], writes=["g_zbuf"])

        P.dma("sync", "g_dist", lambda e, c0=c0: e.dma_start(
            out=dist[:], in_=din["dist"][:, c0:c0 + TCC]),
            reads=[("dram", "dist")], writes=["g_dist"])
        P.op("scalar", lambda e: e.activation(
            out=twc[:], in_=dist[:], func=AF.Exp, scale=slc[:, 1:2]),
            reads=["g_dist", "g_slc"], writes=["g_twc"])
        P.op("vector", lambda e: e.tensor_scalar(
            out=pw[:], in0=z2c[0:32, 8, :], scalar1=sscol[:], scalar2=20.0,
            op0=ALU.mult, op1=ALU.min), reads=["g_z2c", "g_sscol"], writes=["g_pw"])
        P.op("vector", lambda e: e.tensor_scalar(
            out=pw[:], in0=pw[:], scalar1=-20.0, scalar2=None,
            op0=ALU.max), reads=["g_pw"], writes=["g_pw"])
        P.op("scalar", lambda e: e.activation(
            out=pw[:], in_=pw[:], func=AF.Exp), reads=["g_pw"], writes=["g_pw"])
        P.op("vector", lambda e: e.tensor_mul(
            out=pw[:], in0=pw[:], in1=twc[:]), reads=["g_pw", "g_twc"], writes=["g_pw"])
        P.op("vector", lambda e: e.tensor_tensor_scan(
            out=denc[:], data0=pw[:], data1=zeros[0:32, :], initial=dcarry[:, 0:1],
            op0=ALU.add, op1=ALU.add),
            reads=["g_pw", "g_zeros", "g_dcarry"], writes=["g_denc"])
        P.op("vector", lambda e: e.tensor_copy(
            out=dcarry[:, 0:1], in_=denc[:, TCC - 1:TCC]),
            reads=["g_denc"], writes=["g_dcarry"])
        P.op("vector", lambda e: e.tensor_scalar(
            out=twc[:], in0=denc[:], scalar1=1e-4, scalar2=None,
            op0=ALU.max), reads=["g_denc"], writes=["g_twc"])
        P.op("vector", lambda e: e.tensor_mul(
            out=twc[:], in0=twc[:], in1=twc[:]), reads=["g_twc"], writes=["g_twc"])
        P.op("vector", lambda e: e.tensor_scalar(
            out=twc[:], in0=twc[:], scalar1=1e-5, scalar2=None,
            op0=ALU.mult), reads=["g_twc"], writes=["g_twc"])

        for g in range(NT):
            gp = g % 2
            sck = f"g_scr{gp}"
            P.op("tensor", lambda e, g=g: e.matmul(
                out=P.psb(6), lhsT=selg[:, g, :], rhs=pw[:],
                start=True, stop=True), reads=["g_selg", "g_pw"], writes=["psb6"])
            P.op("vector", lambda e, g=g, gp=gp: e.tensor_scalar(
                out=scr[:, gp, :], in0=z2c[:, g, :], scalar1=wcols[:, 3:4],
                scalar2=wcols[:, 1:2], op0=ALU.mult, op1=ALU.add),
                reads=["g_z2c", "g_wcols"], writes=[sck])
            P.op("vector", lambda e, g=g, gp=gp: e.tensor_mul(
                out=scr[:, gp, :], in0=scr[:, gp, :], in1=z2c[:, g, :]),
                reads=[sck, "g_z2c"], writes=[sck])
            P.op("vector", lambda e, gp=gp: e.tensor_scalar(
                out=scr[:, gp, :], in0=scr[:, gp, :], scalar1=wcols[:, 0:1],
                scalar2=None, op0=ALU.add), reads=[sck, "g_wcols"], writes=[sck])
            P.op("vector", lambda e, gp=gp: e.tensor_mul(
                out=scr[:, gp, :], in0=scr[:, gp, :], in1=P.psb(6)),
                reads=[sck, "psb6"], writes=[sck])
            for m in range(4):
                mp = m % 2
                tk = f"g_tmpA{mp}"
                sqk = f"g_sqt{mp}"
                t0, t1, t2 = 4 * mp + 0, 4 * mp + 1, 4 * mp + 2
                sq = sqt[:, mp, :]
                P.op("vector", lambda e, g=g, m=m, sq=sq: e.tensor_scalar(
                    out=sq, in0=z2c[:, g, :], scalar1=thc[:, m, g:g + 1],
                    scalar2=None, op0=ALU.mult),
                    reads=["g_z2c", "g_thc"], writes=[sqk])
                P.op("vector", lambda e, sq=sq, t2=t2: e.tensor_scalar(
                    out=tmp[:, t2, :], in0=sq, scalar1=PI, scalar2=None,
                    op0=ALU.is_gt), reads=[sqk], writes=[tk])
                P.op("vector", lambda e, sq=sq, t1=t1: e.tensor_scalar(
                    out=tmp[:, t1, :], in0=sq, scalar1=-PI, scalar2=None,
                    op0=ALU.is_lt), reads=[sqk], writes=[tk])
                P.op("vector", lambda e, t1=t1, t2=t2: e.tensor_sub(
                    out=tmp[:, t2, :], in0=tmp[:, t2, :], in1=tmp[:, t1, :]),
                    reads=[tk], writes=[tk])
                P.op("vector", lambda e, sq=sq, t1=t1, t2=t2: e.scalar_tensor_tensor(
                    out=tmp[:, t1, :], in0=tmp[:, t2, :], scalar=-2 * PI,
                    in1=sq, op0=ALU.mult, op1=ALU.add),
                    reads=[tk, sqk], writes=[tk])
                P.op("vector", lambda e, t1=t1: e.tensor_scalar(
                    out=tmp[:, t1, :], in0=tmp[:, t1, :], scalar1=PI,
                    scalar2=-PI, op0=ALU.min, op1=ALU.max),
                    reads=[tk], writes=[tk])
                P.op("scalar", lambda e, t0=t0, t1=t1: e.activation(
                    out=tmp[:, t0, :], in_=tmp[:, t1, :], func=AF.Sin),
                    reads=[tk], writes=[tk])
                P.op("scalar", lambda e, t1=t1, t2=t2: e.activation(
                    out=tmp[:, t2, :], in_=tmp[:, t1, :], func=AF.Sin, scale=0.5),
                    reads=[tk], writes=[tk])
                P.op("scalar", lambda e, t2=t2: e.activation(
                    out=tmp[:, t2, :], in_=tmp[:, t2, :], func=AF.Square),
                    reads=[tk], writes=[tk])
                P.op("vector", lambda e, t1=t1, t2=t2: e.tensor_scalar(
                    out=tmp[:, t1, :], in0=tmp[:, t2, :], scalar1=-2.0,
                    scalar2=1.0, op0=ALU.mult, op1=ALU.add),
                    reads=[tk], writes=[tk])
                P.op("vector", lambda e, t1=t1, gp=gp: e.tensor_mul(
                    out=tmp[:, t1, :], in0=tmp[:, t1, :], in1=scr[:, gp, :]),
                    reads=[tk, sck], writes=[tk])
                P.op("vector", lambda e, t0=t0, gp=gp: e.tensor_mul(
                    out=tmp[:, t0, :], in0=tmp[:, t0, :], in1=scr[:, gp, :]),
                    reads=[tk, sck], writes=[tk])
                for ri in range(2):
                    src = (t1, t0)[ri]
                    cc = g * 8 + m * 2 + ri
                    P.op("vector", lambda e, m=m, ri=ri, src=src, cc=cc: e.tensor_tensor_scan(
                        out=numf[:, 2 * m + ri, :], data0=tmp[:, src, :],
                        data1=zeros[:], initial=carry[:, cc:cc + 1],
                        op0=ALU.add, op1=ALU.add),
                        reads=[tk, "g_zeros", "g_carry"], writes=["g_numf"])
                    sqs = (3, 7)[ri]
                    P.op("scalar", lambda e, m=m, ri=ri, sqs=sqs: e.activation(
                        out=tmp[:, sqs, :], in_=numf[:, 2 * m + ri, :], func=AF.Square),
                        reads=["g_numf"], writes=[f"g_sqb{ri}"])
                    for sub in range(4):
                        first = (m == 0 and ri == 0 and sub == 0)
                        P.op("tensor", lambda e, sub=sub, sqs=sqs, first=first: e.matmul(
                            out=P.pspool[0:4, 7, sub * 128:(sub + 1) * 128],
                            lhsT=o32[:], rhs=tmp[:, sqs, sub * 128:(sub + 1) * 128],
                            start=first, stop=False),
                            reads=["g_o32", f"g_sqb{ri}"], writes=["psb7"],
                            inc=(sub == 3))
            P.op("vector", lambda e, g=g: e.tensor_copy(
                out=carry[:, g * 8:g * 8 + 8], in_=numf[:, 0:8, TCC - 1:TCC]),
                reads=["g_numf"], writes=["g_carry"])
            P.op("tensor", lambda e, g=g: e.matmul(
                out=P.pspool[0:4, 7, :], lhsT=i32b[:, 4 * g:4 * g + 4],
                rhs=twc[:], start=False, stop=True),
                reads=["g_i32", "g_twc", "psb7"], writes=["psb7"])
            P.op("scalar", lambda e: e.activation(
                out=sstage[:, 1, :], in_=P.pspool[0:4, 7, :], func=AF.Sqrt),
                reads=["psb7"], writes=["g_sstage"])
            P.op("vector", lambda e: e.reciprocal(
                out=sstage[:, 1, :], in_=sstage[:, 1, :]),
                reads=["g_sstage"], writes=["g_sstage"])
            P.op("tensor", lambda e: e.matmul(
                out=P.psb(6), lhsT=sel4[:], rhs=sstage[:, 1, :],
                start=True, stop=True), reads=["g_sel4", "g_sstage"], writes=["psb6"])
            for mm in range(8):
                P.op("vector", lambda e, mm=mm: e.tensor_mul(
                    out=numbf[:, mm, :], in0=numf[:, mm, :], in1=P.psb(6)),
                    reads=["g_numf", "psb6"], writes=["g_numbf"])
            for rim in range(8):
                ri, m = rim // 4, rim % 4
                P.op("tensor", lambda e, rim=rim, ri=ri, m=m: e.matmul(
                    out=P.psb(2), lhsT=bd[:, rim, :], rhs=numbf[:, 2 * m + ri, :],
                    start=(rim == 0), stop=(rim == 7)),
                    reads=["g_bd", "g_numbf"], writes=["psb2"], inc=(rim == 7))
            P.op("vector", lambda e, g=g: e.tensor_mul(
                out=ygate[:, g, :], in0=P.psb(2), in1=gate[:, g, :]),
                reads=["psb2", "g_gate"], writes=["g_ygate"])

        for et in range(NT):
            P.dma("gpsimd", f"g_outw{et % 2}",
                  lambda e, et=et: e.dma_start(
                      out=outw[et % 2][:], in_=din["outprojw"][et]),
                  reads=[("dram", "outprojw")], writes=[f"g_outw{et % 2}"])
            for g in range(NT):
                P.op("tensor", lambda e, et=et, g=g: e.matmul(
                    out=P.psb(3), lhsT=outw[et % 2][:, g, :], rhs=ygate[:, g, :],
                    start=(g == 0), stop=(g == NT - 1)),
                    reads=[f"g_outw{et % 2}", "g_ygate"], writes=["psb3"],
                    inc=(g == NT - 1))
            P.op("vector", lambda e, et=et: e.tensor_copy(
                out=numf[:, et, :], in_=P.psb(3)), reads=["psb3"], writes=["g_numf"])

        emit_ln_stats(P, C, ln, numf, "g_numf", tmp, "g_tmp", NT)
        load_x_chunk(P, din["xin"], xin_key, xc, "g_xc", c0, TCC)
        emit_ln_apply(P, C, ln, numf, "g_numf", xc, "g_xc", tmp, "g_tmp", NT,
                      add_to=xc, add_key="g_xc")
        store_x_chunk(P, dout["xout"], xout_key, xc, "g_xc", c0, TCC, "st_xout")


def host_prep_l0attn(inputs):
    f32, c = np.float32, np.ascontiguousarray
    d = {}
    ipw = np.asarray(inputs["L0_attn_in_proj_w"], f32)
    d["inprojw"] = c(ipw.reshape(8, 128, 2080).transpose(1, 0, 2))
    convT = np.zeros((2176, 4), f32)
    convT[:2080] = np.asarray(inputs["L0_attn_conv_w"], f32)[:, 0, :].T
    d["convT"] = convT
    convb = np.zeros((2176,), f32)
    convb[:2080] = np.asarray(inputs["L0_attn_conv_b"], f32)
    d["convb"] = c(convb.reshape(17, 128).T)
    th = np.asarray(inputs["L0_attn_theta"], f32).reshape(32, 32, 4)
    d["thetac"] = c(th.transpose(2, 0, 1).reshape(4, 8, 128).transpose(2, 0, 1))
    d["slopes"] = c(np.concatenate([
        np.asarray(inputs["L0_attn_decay_slopes"], f32),
        np.asarray(inputs["L0_attn_anchor_slopes"], f32)]).reshape(32, 1))
    d["sscol"] = c(np.asarray(inputs["L0_attn_score_scale"], f32).reshape(32, 1))
    d["dlog"] = c(np.asarray(inputs["L0_attn_deriv_logits"], f32).reshape(1, 3))
    ns = np.asarray(inputs["L0_attn_norm_scale"], f32)
    nsc = np.stack([ns[:128].reshape(32, 4).T, ns[128:].reshape(32, 4).T])
    d["nsc"] = c(nsc.reshape(8, 32).T)
    wr = np.asarray(inputs["L0_attn_W_re"], f32).reshape(32, 4, 32).transpose(1, 0, 2)
    wi = np.asarray(inputs["L0_attn_W_im"], f32).reshape(32, 4, 32).transpose(1, 0, 2)
    d["wrim"] = c(np.stack([wr, wi]))
    d["outprojw"] = _tile_lhs(np.asarray(inputs["L0_attn_out_proj_w"], f32))
    p = np.arange(128)
    d["sel4"] = c((p[None, :] // 32 == np.arange(4)[:, None]).astype(f32))
    d["o32"] = c(d["sel4"].T / 256.0)
    d["i32"] = np.eye(32, dtype=f32)
    selg = np.zeros((8, 32, 128), f32)
    for g in range(8):
        selg[g, g * 4 + p // 32, p] = 1.0
    d["selg"] = selg
    t = np.arange(2048, dtype=f32)
    dist = np.empty((32, 2048), f32)
    dist[:28] = 2047.0 - t
    dist[28:] = t
    d["dist"] = dist
    return d


L0ATTN_SHAPES = {
    "inprojw": ([128, 8, 2080], F32), "convT": ([2176, 4], F32),
    "convb": ([128, 17], F32), "thetac": ([128, 4, 8], F32),
    "slopes": ([32, 1], F32), "sscol": ([32, 1], F32), "dlog": ([1, 3], F32),
    "nsc": ([32, 8], F32), "wrim": ([2, 4, 32, 32], F32),
    "outprojw": ([8, 128, 8, 128], F32), "sel4": ([4, 128], F32),
    "o32": ([128, 4], F32), "selg": ([8, 32, 128], F32), "dist": ([32, 2048], F32),
    "i32": ([32, 32], F32),
}


def build_l0attn_program():
    nc = bass.Bass("TRN2", target_bir_lowering=False)
    din = {"xin": nc.dram_tensor("xin", [D, T], F32, kind="ExternalInput").ap()}
    for nm, (shp, dt) in L0ATTN_SHAPES.items():
        din[nm] = nc.dram_tensor(nm, shp, dt, kind="ExternalInput").ap()
    dout = {"xout": nc.dram_tensor("xout", [D, T], F32, kind="ExternalOutput").ap()}
    P = Prog(nc)
    P.make_pool()
    C = Common(P)
    build_l0attn(P, C, din, dout)
    P.final_wait("sync", ["st_xout"])
    P.build()
    return nc




def build_fused_program():
    nc = bass.Bass("TRN2", target_bir_lowering=False)
    din = {"xin": nc.dram_tensor("xin", [D, T], F32, kind="ExternalInput").ap()}
    for shapes in (L0ATTN_SHAPES,):
        for nm, (shp, dt) in shapes.items():
            din[nm] = nc.dram_tensor(nm, shp, dt, kind="ExternalInput").ap()
    for nm, (shp, dt) in L0FFN_SHAPES.items():
        din[nm] = nc.dram_tensor(nm, shp, dt, kind="ExternalInput").ap()
    for nm, (shp, dt) in L1ATTN_SHAPES.items():
        din[nm] = nc.dram_tensor(nm, shp, dt, kind="ExternalInput").ap()
    for nm, shp in L1FFN_SHAPES.items():
        din[nm] = nc.dram_tensor(nm, shp, F32, kind="ExternalInput").ap()
    din["hdram"] = nc.dram_tensor("hdram", [D, T], F32, kind="Internal").ap()
    din["reld"] = nc.dram_tensor("reld", [2, 128, 2176], BF16, kind="Internal").ap()
    xra = nc.dram_tensor("xra", [D, T], F32, kind="Internal").ap()
    xrb = nc.dram_tensor("xrb", [D, T], F32, kind="Internal").ap()
    xout = nc.dram_tensor("xout", [D, T], F32, kind="ExternalOutput").ap()

    P = Prog(nc)
    P.make_pool()
    C = Common(P)
    with P.scope():
        build_l0attn(P, C, din, {"xout": xra},
                     xin_key=("dram", "xin"), xout_key=("dram", "xra"))
    P.barrier()
    with P.scope():
        build_l0ffn(P, C, dict(din, xin=xra), {"xout": xrb},
                    xin_key=("dram", "xra"), xout_key=("dram", "xrb"))
    P.barrier()
    with P.scope():
        build_l1attn(P, C, dict(din, xin=xrb), {"xout": xra},
                     xin_key=("dram", "xrb"), xout_key=("dram", "xra2"))
    P.barrier()
    with P.scope():
        build_l1ffn(P, C, dict(din, xin=xra), {"xout": xout},
                    xin_key=("dram", "xra2"), xout_key=("dram", "xout"))
    P.final_wait("sync", ["st_xout"])
    P.build()
    return nc



_PROGS = {}


def _get_prog(name, builder):
    if name not in _PROGS:
        _PROGS[name] = builder()
    return _PROGS[name]


def kernel(**inputs):
    x = np.asarray(inputs["L0_attn_x"], np.float32)
    nc = _get_prog("fused", build_fused_program)
    hp = {}
    hp.update(host_prep_l0attn(inputs))
    hp.update(host_prep_l0ffn(inputs))
    hp.update(host_prep_l1attn(inputs))
    hp.update(host_prep_l1ffn(inputs))
    in_maps = []
    for i in range(NCORES):
        m = {"xin": np.ascontiguousarray(x[i].T)}
        m.update(hp)
        in_maps.append(m)
    res = run_bass_kernel_spmd(nc, in_maps, core_ids=list(range(NCORES)))
    out = np.stack([np.asarray(res.results[i]["xout"]).T for i in range(NCORES)])
    return out.astype(np.float32)


# version 25
# speedup vs baseline: 1.1461x; 1.0193x over previous
import numpy as np
import ml_dtypes
from contextlib import ExitStack

import concourse.bass as bass
import concourse.mybir as mybir
from concourse.bass_utils import run_bass_kernel_spmd

F32 = mybir.dt.float32
BF16 = mybir.dt.bfloat16
I32 = mybir.dt.int32
AF = mybir.ActivationFunctionType
ALU = mybir.AluOpType
AX = mybir.AxisListType
BF16NP = ml_dtypes.bfloat16

D = 1024
T = 2048
NCORES = 8
NT = D // 128
LN_EPS = 1e-5


ENGINES = ("sync", "tensor", "vector", "scalar", "gpsimd")


class Prog:
    def __init__(self, nc):
        self.nc = nc
        self.stack = ExitStack()
        self._alloc_stack = None
        self.sems = {}
        self.cnt = {}
        self.queue = {e: [] for e in ENGINES}
        self.last_w = {}
        self.readers = {}
        self.waited = {e: {} for e in ENGINES}
        self.bufs = {}

    def sem(self, name):
        if name not in self.sems:
            self.sems[name] = self.stack.enter_context(self.nc.semaphore(name))
            self.cnt[name] = 0
        return self.sems[name]

    def sbuf(self, name, shape, dtype):
        st = self._alloc_stack if self._alloc_stack is not None else self.stack
        t = st.enter_context(self.nc.sbuf_tensor(name, list(shape), dtype))
        self.bufs[name] = t
        return t

    def scope(self):
        from contextlib import contextmanager

        @contextmanager
        def _cm():
            st = ExitStack()
            old = self._alloc_stack
            self._alloc_stack = st
            try:
                yield
            finally:
                self._alloc_stack = old
                st.close()
        return _cm()

    def barrier(self):
        targets = {sname: v for sname, v in self.cnt.items() if v > 0}
        for eng in ENGINES:
            w = {}
            for sname, v in targets.items():
                if sname == eng:
                    continue
                if self.waited[eng].get(sname, -1) >= v:
                    continue
                w[sname] = v
            if w:
                self.queue[eng].append((w, None, None))
                for sname, v in w.items():
                    self.waited[eng][sname] = max(self.waited[eng].get(sname, -1), v)
        self.last_w.clear()
        self.readers.clear()

    def psum(self, name, shape, dtype=F32):
        t = self.stack.enter_context(self.nc.psum_tensor(name, list(shape), dtype))
        self.bufs[name] = t
        return t

    def make_pool(self):
        self.pspool = self.psum("pspool", [128, 8, 512], F32)

    def psb(self, bank):
        return self.pspool[:, bank, :]

    def psbk(self, b0, n=1):
        return [f"psb{i}" for i in range(b0, b0 + n)]

    def psw(self, b0, n):
        return self.pspool[:, b0:b0 + n, :].rearrange("p a b -> p (a b)")

    def _waits_for(self, eng, reads, writes):
        need = {}
        def add(swv):
            s, v = swv
            if need.get(s, 0) < v:
                need[s] = v
        for k in reads:
            if k in self.last_w:
                add(self.last_w[k])
        for k in writes:
            if k in self.last_w:
                add(self.last_w[k])
            for swv in self.readers.get(k, ()):
                add(swv)
        out = {}
        for s, v in need.items():
            if s == eng and v > self.cnt.get(eng, 0):
                continue
            if self.waited[eng].get(s, -1) >= v:
                continue
            out[s] = v
        return out

    def _record(self, eng_sem, val, reads, writes):
        for k in writes:
            self.last_w[k] = (eng_sem, val)
            self.readers[k] = []
        for k in reads:
            self.readers.setdefault(k, []).append((eng_sem, val))

    def op(self, eng, fn, reads=(), writes=(), inc=True):
        self.sem(eng)
        waits = self._waits_for(eng, reads, writes)
        for s, v in waits.items():
            self.waited[eng][s] = max(self.waited[eng].get(s, -1), v)
        if inc:
            self.cnt[eng] += 1
            incspec = (eng, 1)
            val = self.cnt[eng]
        else:
            incspec = None
            val = self.cnt[eng] + 1
        self._record(eng, val, reads, writes)
        self.queue[eng].append((dict(waits), fn, incspec))

    def dma(self, qeng, chan, fn, reads=(), writes=()):
        chan = "dq_" + chan
        self.sem(chan)
        self.sem(qeng)
        waits = self._waits_for(qeng, reads, writes)
        for s, v in waits.items():
            self.waited[qeng][s] = max(self.waited[qeng].get(s, -1), v)
        self.cnt[chan] += 16
        self._record(chan, self.cnt[chan], reads, writes)
        self.queue[qeng].append((dict(waits), fn, (chan, 16)))

    def final_wait(self, qeng, chans):
        waits = {}
        for c in chans:
            c = "dq_" + c
            if self.cnt.get(c, 0) > 0:
                waits[c] = self.cnt[c]
        self.queue[qeng].append((waits, None, None))

    def build(self):
        block = self.stack.enter_context(self.nc.Block())
        for eng_name in ENGINES:
            items = self.queue[eng_name]
            if not items:
                continue
            def make_body(items):
                def body(engine):
                    for waits, fn, incspec in items:
                        for s, v in waits.items():
                            engine.wait_ge(self.sems[s], v)
                        if fn is None:
                            continue
                        inst = fn(engine)
                        if incspec is not None:
                            s, by = incspec
                            inst.then_inc(self.sems[s], by)
                return body
            getattr(block, eng_name)(make_body(items))

    def close(self):
        self.stack.close()


class Common:

    def __init__(self, P: Prog):
        self.P = P
        self.ones_f = P.sbuf("c_ones_f", [128, 1], F32)
        self.ones_row = P.sbuf("c_ones_row", [1, 128], F32)
        self.eps_row = P.sbuf("c_eps_row", [1, 1], F32)
        P.op("vector", lambda v: v.memset(self.ones_f[:], 1.0), writes=["c_ones_f"])
        P.op("vector", lambda v: v.memset(self.ones_row[:], 1.0), writes=["c_ones_row"])
        P.op("vector", lambda v: v.memset(self.eps_row[:], LN_EPS), writes=["c_eps_row"])



def alloc_ln(P: Prog, tag):
    ln = {}
    ln["rows"] = P.sbuf(f"ln_rows_{tag}", [1, 3, 512], F32)
    ln["ps_a"] = P.pspool[0:1, 2, :]
    ln["ps_b"] = P.pspool[0:1, 3, :]
    ln["bc0"] = P.psb(4)
    ln["bc1"] = P.psb(5)
    ln["k"] = tag
    return ln


def emit_ln_stats(P, C, ln, src, src_key, sq, sq_key, ntile):
    k = ln["k"]
    rows, rk = ln["rows"], f"ln_rows_{k}"
    P.op("scalar", lambda e: e.activation(out=sq[:], in_=src[:], func=AF.Square),
         reads=[src_key], writes=[sq_key])
    for g in range(ntile):
        P.op("tensor", lambda e, g=g: e.matmul(
            out=ln["ps_a"], lhsT=C.ones_f[:], rhs=src[:, g, :],
            start=(g == 0), stop=(g == ntile - 1)),
            reads=[src_key, "c_ones_f"], writes=["psb2"], inc=(g == ntile - 1))
    for g in range(ntile):
        P.op("tensor", lambda e, g=g: e.matmul(
            out=ln["ps_b"], lhsT=C.ones_f[:], rhs=sq[:, g, :],
            start=(g == 0), stop=(g == ntile - 1)),
            reads=[sq_key, "c_ones_f"], writes=["psb3"], inc=(g == ntile - 1))
    nfeat = float(ntile * 128)
    P.op("vector", lambda e: e.tensor_scalar(
        out=rows[0:1, 0, :], in0=ln["ps_a"], scalar1=1.0 / nfeat, scalar2=None,
        op0=ALU.mult), reads=["psb2"], writes=[rk])
    P.op("vector", lambda e: e.tensor_scalar(
        out=rows[0:1, 1, :], in0=ln["ps_b"], scalar1=1.0 / nfeat, scalar2=None,
        op0=ALU.mult), reads=["psb3"], writes=[rk])
    P.op("vector", lambda e: e.tensor_tensor(
        out=rows[0:1, 2, :], in0=rows[0:1, 0, :], in1=rows[0:1, 0, :], op=ALU.mult),
        reads=[rk], writes=[rk])
    P.op("vector", lambda e: e.tensor_sub(
        out=rows[0:1, 1, :], in0=rows[0:1, 1, :], in1=rows[0:1, 2, :]),
        reads=[rk], writes=[rk])
    P.op("scalar", lambda e: e.activation(
        out=rows[0:1, 1, :], in_=rows[0:1, 1, :], func=AF.Sqrt, bias=C.eps_row[:]),
        reads=[rk, "c_eps_row"], writes=[rk])
    P.op("vector", lambda e: e.reciprocal(
        out=rows[0:1, 1, :], in_=rows[0:1, 1, :]), reads=[rk], writes=[rk])
    P.op("vector", lambda e: e.tensor_mul(
        out=rows[0:1, 2, :], in0=rows[0:1, 0, :], in1=rows[0:1, 1, :]),
        reads=[rk], writes=[rk])
    P.op("tensor", lambda e: e.matmul(
        out=ln["bc0"], lhsT=C.ones_row[:], rhs=rows[0:1, 1, :],
        start=True, stop=True), reads=[rk, "c_ones_row"], writes=["psb4"])
    P.op("tensor", lambda e: e.matmul(
        out=ln["bc1"], lhsT=C.ones_row[:], rhs=rows[0:1, 2, :],
        start=True, stop=True), reads=[rk, "c_ones_row"],
        writes=["psb5"], inc=True)


def emit_ln_apply(P, C, ln, src, src_key, dst, dst_key, tmp, tmp_key, ntile,
                  gamma_col=None, beta_col=None, gb_keys=(), add_to=None,
                  add_key=None):
    k = ln["k"]
    for g in range(ntile):
        P.op("vector", lambda e, g=g: e.tensor_mul(
            out=tmp[:, g, :], in0=src[:, g, :], in1=ln["bc0"]),
            reads=[src_key, "psb4"], writes=[tmp_key])
    if gamma_col is None and add_to is None:
        for g in range(ntile):
            P.op("vector", lambda e, g=g: e.tensor_sub(
                out=dst[:, g, :], in0=tmp[:, g, :], in1=ln["bc1"]),
                reads=[tmp_key, "psb5"], writes=[dst_key])
    elif gamma_col is not None:
        for g in range(ntile):
            P.op("vector", lambda e, g=g: e.tensor_sub(
                out=tmp[:, g, :], in0=tmp[:, g, :], in1=ln["bc1"]),
                reads=[tmp_key, "psb5"], writes=[tmp_key])
        for g in range(ntile):
            P.op("vector", lambda e, g=g: e.tensor_scalar(
                out=dst[:, g, :], in0=tmp[:, g, :],
                scalar1=gamma_col[:, g:g + 1], scalar2=beta_col[:, g:g + 1],
                op0=ALU.mult, op1=ALU.add),
                reads=[tmp_key] + list(gb_keys), writes=[dst_key])
    else:
        for g in range(ntile):
            P.op("vector", lambda e, g=g: e.tensor_sub(
                out=tmp[:, g, :], in0=tmp[:, g, :], in1=ln["bc1"]),
                reads=[tmp_key, "psb5"], writes=[tmp_key])
        for g in range(ntile):
            P.op("vector", lambda e, g=g: e.tensor_add(
                out=dst[:, g, :], in0=tmp[:, g, :], in1=add_to[:, g, :]),
                reads=[tmp_key, add_key], writes=[dst_key])


def load_x_chunk(P, xsrc_ap, xsrc_key, xc, xc_key, c0, TC):
    P.dma("sync", xc_key,
          lambda e: e.dma_start(
              out=xc[:], in_=xsrc_ap[:, c0:c0 + TC].rearrange("(g p) t -> p g t", p=128)),
          reads=[xsrc_key], writes=[xc_key])


def store_x_chunk(P, xdst_ap, xdst_key, xc, xc_key, c0, TC, chan):
    P.dma("sync", chan,
          lambda e: e.dma_start(
              out=xdst_ap[:, c0:c0 + TC].rearrange("(g p) t -> p g t", p=128), in_=xc[:]),
          reads=[xc_key], writes=[xdst_key])



def build_l1ffn(P: Prog, C: Common, din, dout, xin_key="dram_xin",
                xout_key="dram_xout"):
    NH = 32
    TC = 512

    s1c = P.sbuf("f_s1c", [128, NH], F32)
    s2c = P.sbuf("f_s2c", [128, NH], F32)
    sh1c = P.sbuf("f_sh1c", [128, NH], F32)
    sh2c = P.sbuf("f_sh2c", [128, NH], F32)
    s3c = P.sbuf("f_s3c", [128, NT], F32)
    for nm in ("s1c", "s2c", "sh1c", "sh2c", "s3c"):
        P.dma("sync", "f_" + nm,
              lambda e, nm=nm: e.dma_start(out=P.bufs["f_" + nm][:], in_=din[nm][:]),
              reads=[("dram", nm)], writes=["f_" + nm])

    xc = P.sbuf("f_xc", [128, NT, TC], F32)
    tmp = P.sbuf("f_tmp", [128, NT, TC], F32)
    xhat = P.sbuf("f_xhat", [128, NT, TC], BF16)
    t1 = P.sbuf("f_t1", [128, NH, TC], BF16)
    t2 = P.sbuf("f_t2", [128, NH, TC], BF16)
    y = P.sbuf("f_y", [128, NT, TC], F32)
    silu_t = P.sbuf("f_silu", [128, TC], F32)
    wbuf = [P.sbuf(f"f_wbuf{i}", [128, 32, 128], BF16) for i in range(2)]
    ps_mm = [P.psb(0), P.psb(1)]
    ln = alloc_ln(P, "f")

    for c0 in range(0, T, TC):
        load_x_chunk(P, din["xin"], xin_key, xc, "f_xc", c0, TC)
        emit_ln_stats(P, C, ln, xc, "f_xc", tmp, "f_tmp", NT)
        emit_ln_apply(P, C, ln, xc, "f_xc", xhat, "f_xhat", tmp, "f_tmp", NT)

        for ot in range(NH):
            wb, wbk = wbuf[ot % 2], f"f_wbuf{ot % 2}"
            P.dma("gpsimd", wbk + "a",
                  lambda e, ot=ot, wb=wb: e.dma_start(
                      out=wb[:, 0:8, :], in_=din["w1Tt"][ot]),
                  reads=[("dram", "w1Tt")], writes=[wbk + "a"])
            P.dma("gpsimd", wbk + "b",
                  lambda e, ot=ot, wb=wb: e.dma_start(
                      out=wb[:, 8:16, :], in_=din["w2Tt"][ot]),
                  reads=[("dram", "w2Tt")], writes=[wbk + "b"])
            for g in range(NT):
                P.op("tensor", lambda e, g=g, wb=wb: e.matmul(
                    out=ps_mm[0], lhsT=wb[:, g, :], rhs=xhat[:, g, :],
                    start=(g == 0), stop=(g == NT - 1)),
                    reads=[wbk + "a", "f_xhat"], writes=["psb0"], inc=(g == NT - 1))
            for g in range(NT):
                P.op("tensor", lambda e, g=g, wb=wb: e.matmul(
                    out=ps_mm[1], lhsT=wb[:, 8 + g, :], rhs=xhat[:, g, :],
                    start=(g == 0), stop=(g == NT - 1)),
                    reads=[wbk + "b", "f_xhat"], writes=["psb1"], inc=(g == NT - 1))
            P.op("scalar", lambda e, ot=ot: e.activation(
                out=silu_t[:], in_=ps_mm[0], func=AF.Silu, scale=s1c[:, ot:ot + 1]),
                reads=["psb0", "f_s1c"], writes=["f_silu"])
            P.op("vector", lambda e, ot=ot: e.scalar_tensor_tensor(
                out=t1[:, ot, :], in0=ps_mm[1], scalar=s2c[:, ot:ot + 1],
                in1=silu_t[:], op0=ALU.mult, op1=ALU.mult),
                reads=["psb1", "f_s2c", "f_silu"], writes=["f_t1"])

        for ot in range(NH):
            wb, wbk = wbuf[ot % 2], f"f_wbuf{ot % 2}"
            bk_ = ot % 2
            P.dma("gpsimd", wbk + "a",
                  lambda e, ot=ot, wb=wb: e.dma_start(
                      out=wb[:], in_=din["h1Tt"][ot]),
                  reads=[("dram", "h1Tt")], writes=[wbk + "a", wbk + "b"])
            for g in range(NH):
                P.op("tensor", lambda e, g=g, wb=wb, bk_=bk_: e.matmul(
                    out=ps_mm[bk_], lhsT=wb[:, g, :], rhs=t1[:, g, :],
                    start=(g == 0), stop=(g == NH - 1)),
                    reads=[wbk + "a", wbk + "b", "f_t1"], writes=[f"psb{bk_}"],
                    inc=(g == NH - 1))
            P.op("scalar", lambda e, ot=ot, bk_=bk_: e.activation(
                out=t2[:, ot, :], in_=ps_mm[bk_], func=AF.Gelu,
                scale=sh1c[:, ot:ot + 1]),
                reads=[f"psb{bk_}", "f_sh1c"], writes=["f_t2"])

        for ot in range(NH):
            wb, wbk = wbuf[ot % 2], f"f_wbuf{ot % 2}"
            bk_ = ot % 2
            P.dma("gpsimd", wbk + "a",
                  lambda e, ot=ot, wb=wb: e.dma_start(
                      out=wb[:], in_=din["h2Tt"][ot]),
                  reads=[("dram", "h2Tt")], writes=[wbk + "a", wbk + "b"])
            for g in range(NH):
                P.op("tensor", lambda e, g=g, wb=wb, bk_=bk_: e.matmul(
                    out=ps_mm[bk_], lhsT=wb[:, g, :], rhs=t2[:, g, :],
                    start=(g == 0), stop=(g == NH - 1)),
                    reads=[wbk + "a", wbk + "b", "f_t2"], writes=[f"psb{bk_}"],
                    inc=(g == NH - 1))
            P.op("scalar", lambda e, ot=ot, bk_=bk_: e.activation(
                out=t1[:, ot, :], in_=ps_mm[bk_], func=AF.Gelu,
                scale=sh2c[:, ot:ot + 1]),
                reads=[f"psb{bk_}", "f_sh2c"], writes=["f_t1"])

        for ot in range(NT):
            wb, wbk = wbuf[ot % 2], f"f_wbuf{ot % 2}"
            bk_ = ot % 2
            P.dma("gpsimd", wbk + "a",
                  lambda e, ot=ot, wb=wb: e.dma_start(
                      out=wb[:], in_=din["w3Tt"][ot]),
                  reads=[("dram", "w3Tt")], writes=[wbk + "a", wbk + "b"])
            for g in range(NH):
                P.op("tensor", lambda e, g=g, wb=wb, bk_=bk_: e.matmul(
                    out=ps_mm[bk_], lhsT=wb[:, g, :], rhs=t1[:, g, :],
                    start=(g == 0), stop=(g == NH - 1)),
                    reads=[wbk + "a", wbk + "b", "f_t1"], writes=[f"psb{bk_}"],
                    inc=(g == NH - 1))
            P.op("vector", lambda e, ot=ot, bk_=bk_: e.tensor_scalar(
                out=y[:, ot, :], in0=ps_mm[bk_], scalar1=s3c[:, ot:ot + 1],
                scalar2=None, op0=ALU.mult),
                reads=[f"psb{bk_}", "f_s3c"], writes=["f_y"])

        emit_ln_stats(P, C, ln, y, "f_y", tmp, "f_tmp", NT)
        emit_ln_apply(P, C, ln, y, "f_y", xc, "f_xc", tmp, "f_tmp", NT,
                      add_to=xc, add_key="f_xc")
        store_x_chunk(P, dout["xout"], xout_key, xc, "f_xc", c0, TC, "st_xout")


def _tile_lhs(wT):
    K, M = wT.shape
    return np.ascontiguousarray(
        wT.reshape(K // 128, 128, M // 128, 128).transpose(2, 1, 0, 3))


def host_prep_l1ffn(inputs):
    f32, c = np.float32, np.ascontiguousarray
    d = {}
    d["w1Tt"] = _tile_lhs(inputs["L1_ffn_w1_w"].T.astype(f32))
    d["w2Tt"] = _tile_lhs(inputs["L1_ffn_w2_w"].T.astype(f32))
    d["h1Tt"] = _tile_lhs(inputs["L1_ffn_h1_w"].T.astype(f32))
    d["h2Tt"] = _tile_lhs(inputs["L1_ffn_h2_w"].T.astype(f32))
    d["w3Tt"] = _tile_lhs(inputs["L1_ffn_w3_w"].T.astype(f32))
    d["s1c"] = c(np.asarray(inputs["L1_ffn_w1_s"], f32).reshape(32, 128).T)
    d["s2c"] = c(np.asarray(inputs["L1_ffn_w2_s"], f32).reshape(32, 128).T)
    d["sh1c"] = c(np.asarray(inputs["L1_ffn_h1_s"], f32).reshape(32, 128).T)
    d["sh2c"] = c(np.asarray(inputs["L1_ffn_h2_s"], f32).reshape(32, 128).T)
    d["s3c"] = c(np.asarray(inputs["L1_ffn_w3_s"], f32).reshape(8, 128).T)
    return d


L1FFN_SHAPES = {
    "w1Tt": [32, 128, 8, 128], "w2Tt": [32, 128, 8, 128],
    "h1Tt": [32, 128, 32, 128], "h2Tt": [32, 128, 32, 128],
    "w3Tt": [8, 128, 32, 128], "s1c": [128, 32],
    "s2c": [128, 32], "sh1c": [128, 32], "sh2c": [128, 32], "s3c": [128, 8],
}


def build_l1ffn_program():
    nc = bass.Bass("TRN2", target_bir_lowering=False)
    din = {"xin": nc.dram_tensor("xin", [D, T], F32, kind="ExternalInput").ap()}
    for nm, shp in L1FFN_SHAPES.items():
        din[nm] = nc.dram_tensor(nm, shp, F32, kind="ExternalInput").ap()
    dout = {"xout": nc.dram_tensor("xout", [D, T], F32, kind="ExternalOutput").ap()}
    P = Prog(nc)
    P.make_pool()
    C = Common(P)
    build_l1ffn(P, C, din, dout)
    P.final_wait("sync", ["st_xout"])
    P.build()
    return nc





def build_l0ffn(P: Prog, C: Common, din, dout, xin_key="dram_xin",
                xout_key="dram_xout"):
    TC = 512

    xc = P.sbuf("q_xc", [128, NT, TC], F32)
    tmp = P.sbuf("q_tmp", [128, NT, TC], F32)
    hbf = P.sbuf("q_hbf", [128, NT, T], BF16)
    y1bf = P.sbuf("q_y1bf", [128, NT, T], BF16)
    yc = P.sbuf("q_yc", [128, NT, TC], F32)
    qraw = P.sbuf("q_qraw", [128, 2, 1024], I32)
    sbc = P.sbuf("q_sbc", [128, 2, 1024], F32)
    dqs = P.sbuf("q_dqs", [128, 2, 1024], F32)
    wbf = [P.sbuf(f"q_wbf{i}", [128, NT, 1024], BF16) for i in range(2)]
    u = P.sbuf("q_u", [32, T], BF16)
    lat = P.sbuf("q_lat", [128, NT, 32], BF16)
    lbt = P.sbuf("q_lbt", [32, NT, 128], BF16)
    bcol = P.sbuf("q_bcol", [128, 2, NT], F32)
    gcol = P.sbuf("q_gcol", [128, NT], F32)
    becol = P.sbuf("q_becol", [128, NT], F32)
    ps_u = P.pspool[0:32, 6, :]
    ps_mm = [P.psb(0), P.psb(1)]
    ln = alloc_ln(P, "q")

    for ci, c0 in enumerate(range(0, T, TC)):
        load_x_chunk(P, din["xin"], xin_key, xc, "q_xc", c0, TC)
        emit_ln_stats(P, C, ln, xc, "q_xc", tmp, "q_tmp", NT)
        emit_ln_apply(P, C, ln, xc, "q_xc", yc, "q_yc", tmp, "q_tmp", NT)
        for g in range(NT):
            P.op("vector", lambda e, g=g, c0=c0: e.tensor_copy(
                out=hbf[:, g, c0:c0 + TC], in_=yc[:, g, :]),
                reads=["q_yc"], writes=["q_hbf"])
        store_x_chunk(P, din["hdram"], ("dram", "hdram"), yc, "q_yc", c0, TC, "st_h")

    def dequant_c(layer, wslot, c):
        wk = f"q_wbf{wslot}"
        cp = c % 2
        P.dma("sync", f"q_qraw{cp}",
              lambda e, layer=layer, c=c, cp=cp: e.dma_start(
                  out=qraw[:, cp, :], in_=din["qT"][layer, c * 128:(c + 1) * 128, :]),
              reads=[("dram", "qT")], writes=[f"q_qraw{cp}"])
        P.dma("sync", f"q_sbc{cp}",
              lambda e, layer=layer, c=c, cp=cp: e.dma_start(
                  out=sbc[:, cp, :],
                  in_=din["sT"][layer, c * 8:(c + 1) * 8, :][:, None, :]
                      .broadcast_to([8, 16, 1024])),
              reads=[("dram", "sT")], writes=[f"q_sbc{cp}"])
        P.op("vector", lambda e, cp=cp: e.tensor_scalar(
            out=dqs[:, cp, :], in0=qraw[:, cp, :],
            scalar1=1.0 / 7.5, scalar2=-1.0,
            op0=ALU.mult, op1=ALU.add),
            reads=[f"q_qraw{cp}"], writes=[f"q_dqs{cp}"])
        P.op("vector", lambda e, c=c, cp=cp, wslot=wslot: e.tensor_mul(
            out=wbf[wslot][:, c, :], in0=dqs[:, cp, :], in1=sbc[:, cp, :]),
            reads=[f"q_dqs{cp}", f"q_sbc{cp}"], writes=[wk])

    def dequant(layer, wslot):
        for c in range(NT):
            dequant_c(layer, wslot, c)

    def load_lora(layer):
        P.dma("gpsimd", "q_lat",
              lambda e, layer=layer: e.dma_start(
                  out=lat[:], in_=din["laTt"][layer]),
              reads=[("dram", "laTt")], writes=["q_lat"])
        P.dma("gpsimd", "q_lbt",
              lambda e, layer=layer: e.dma_start(
                  out=lbt[:], in_=din["lbT"][layer].rearrange("r (c p) -> r c p", p=128)),
              reads=[("dram", "lbT")], writes=["q_lbt"])

    def lora_mid(rhs, rhs_key):
        for sl in range(T // 512):
            for g in range(NT):
                P.op("tensor", lambda e, sl=sl, g=g: e.matmul(
                    out=ps_u, lhsT=lat[:, g, :], rhs=rhs[:, g, sl * 512:(sl + 1) * 512],
                    start=(g == 0), stop=(g == NT - 1)),
                    reads=["q_lat", rhs_key], writes=["psb6"], inc=(g == NT - 1))
            P.op("scalar", lambda e, sl=sl: e.copy(
                out=u[:, sl * 512:(sl + 1) * 512], in_=ps_u),
                reads=["psb6"], writes=["q_u"])

    def qlora_mm(wslot, rhs, rhs_key, c0, bslot):
        wk = f"q_wbf{wslot}"
        for sl in range(TC // 512):
            s0 = c0 + sl * 512
            for ot in range(NT):
                pm = ps_mm[ot % 2]
                pmk = f"psb{ot % 2}"
                for g in range(NT):
                    P.op("tensor", lambda e, g=g, ot=ot, s0=s0, pm=pm, wslot=wslot: e.matmul(
                        out=pm, lhsT=wbf[wslot][:, g, ot * 128:(ot + 1) * 128],
                        rhs=rhs[:, g, s0:s0 + 512],
                        start=(g == 0), stop=False),
                        reads=[wk, rhs_key], writes=[pmk], inc=False)
                P.op("tensor", lambda e, ot=ot, s0=s0, pm=pm: e.matmul(
                    out=pm, lhsT=lbt[:, ot, :], rhs=u[:, s0:s0 + 512],
                    start=False, stop=True),
                    reads=["q_lbt", "q_u", pmk], writes=[pmk], inc=True)
                P.op("vector", lambda e, ot=ot, sl=sl, pm=pm, bslot=bslot: e.tensor_scalar(
                    out=yc[:, ot, sl * 512:(sl + 1) * 512], in0=pm,
                    scalar1=bcol[:, bslot, ot:ot + 1], scalar2=None, op0=ALU.add),
                    reads=[pmk, "q_bcol"], writes=["q_yc"])

    for blk in range(3):
        l0, l1_ = 2 * blk, 2 * blk + 1
        if blk == 0:
            dequant(l0, 0)
        load_lora(l0)
        P.dma("gpsimd", "q_bcol",
              lambda e, l0=l0: e.dma_start(
                  out=bcol[:], in_=din["bcol"][l0:l0 + 2].rearrange("a p c -> p a c")),
              reads=[("dram", "bcol")], writes=["q_bcol"])
        P.dma("gpsimd", "q_gcol",
              lambda e, blk=blk: e.dma_start(out=gcol[:], in_=din["gcol"][blk]),
              reads=[("dram", "gcol")], writes=["q_gcol"])
        P.dma("gpsimd", "q_becol",
              lambda e, blk=blk: e.dma_start(out=becol[:], in_=din["becol"][blk]),
              reads=[("dram", "becol")], writes=["q_becol"])
        lora_mid(hbf, "q_hbf")
        for ci, c0 in enumerate(range(0, T, TC)):
            qlora_mm(0, hbf, "q_hbf", c0, 0)
            dequant_c(l1_, 1, 2 * ci)
            dequant_c(l1_, 1, 2 * ci + 1)
            emit_ln_stats(P, C, ln, yc, "q_yc", tmp, "q_tmp", NT)
            emit_ln_apply(P, C, ln, yc, "q_yc", yc, "q_yc", tmp, "q_tmp", NT,
                          gamma_col=gcol, beta_col=becol,
                          gb_keys=["q_gcol", "q_becol"])
            for g in range(NT):
                P.op("scalar", lambda e, g=g, c0=c0: e.activation(
                    out=y1bf[:, g, c0:c0 + TC], in_=yc[:, g, :], func=AF.Gelu),
                    reads=["q_yc"], writes=["q_y1bf"])
        load_lora(l1_)
        lora_mid(y1bf, "q_y1bf")
        last = blk == 2
        for ci, c0 in enumerate(range(0, T, TC)):
            qlora_mm(1, y1bf, "q_y1bf", c0, 1)
            if not last:
                dequant_c(2 * blk + 2, 0, 2 * ci)
                dequant_c(2 * blk + 2, 0, 2 * ci + 1)
            load_x_chunk(P, din["hdram"], ("dram", "hdram"), xc, "q_xc", c0, TC)
            for g in range(NT):
                P.op("vector", lambda e, g=g: e.tensor_add(
                    out=yc[:, g, :], in0=yc[:, g, :], in1=xc[:, g, :]),
                    reads=["q_yc", "q_xc"], writes=["q_yc"])
            if not last:
                for g in range(NT):
                    P.op("vector", lambda e, g=g, c0=c0: e.tensor_copy(
                        out=hbf[:, g, c0:c0 + TC], in_=yc[:, g, :]),
                        reads=["q_yc"], writes=["q_hbf"])
                store_x_chunk(P, din["hdram"], ("dram", "hdram"), yc, "q_yc", c0,
                              TC, "st_h")
            else:
                emit_ln_stats(P, C, ln, yc, "q_yc", tmp, "q_tmp", NT)
                load_x_chunk(P, din["xin"], xin_key, xc, "q_xc", c0, TC)
                emit_ln_apply(P, C, ln, yc, "q_yc", xc, "q_xc", tmp, "q_tmp", NT,
                              add_to=xc, add_key="q_xc")
                store_x_chunk(P, dout["xout"], xout_key, xc, "q_xc", c0, TC,
                              "st_xout")


def host_prep_l0ffn(inputs):
    f32, c = np.float32, np.ascontiguousarray
    d = {}
    d["qT"] = c(np.asarray(inputs["L0_ffn_qweights"], np.int32).transpose(0, 2, 1))
    d["sT"] = c(np.asarray(inputs["L0_ffn_scales"], f32).transpose(0, 2, 1))
    d["bcol"] = c(np.asarray(inputs["L0_ffn_biases"], f32).reshape(6, 8, 128)
                  .transpose(0, 2, 1))
    laT = np.asarray(inputs["L0_ffn_lora_a"], f32).transpose(0, 2, 1)
    d["laTt"] = c(laT.reshape(6, 8, 128, 32).transpose(0, 2, 1, 3))
    d["lbT"] = c(np.asarray(inputs["L0_ffn_lora_b"], f32).transpose(0, 2, 1))
    d["gcol"] = c(np.asarray(inputs["L0_ffn_ln_gamma"], f32).reshape(3, 8, 128)
                  .transpose(0, 2, 1))
    d["becol"] = c(np.asarray(inputs["L0_ffn_ln_beta"], f32).reshape(3, 8, 128)
                   .transpose(0, 2, 1))
    return d


L0FFN_SHAPES = {
    "qT": ([6, 1024, 1024], I32), "sT": ([6, 64, 1024], F32),
    "bcol": ([6, 128, 8], F32), "laTt": ([6, 128, 8, 32], F32),
    "lbT": ([6, 32, 1024], F32), "gcol": ([3, 128, 8], F32),
    "becol": ([3, 128, 8], F32),
}


def build_l0ffn_program():
    nc = bass.Bass("TRN2", target_bir_lowering=False)
    din = {"xin": nc.dram_tensor("xin", [D, T], F32, kind="ExternalInput").ap()}
    for nm, (shp, dt) in L0FFN_SHAPES.items():
        din[nm] = nc.dram_tensor(nm, shp, dt, kind="ExternalInput").ap()
    din["hdram"] = nc.dram_tensor("hdram", [D, T], F32, kind="Internal").ap()
    dout = {"xout": nc.dram_tensor("xout", [D, T], F32, kind="ExternalOutput").ap()}
    P = Prog(nc)
    P.make_pool()
    C = Common(P)
    build_l0ffn(P, C, din, dout)
    P.final_wait("sync", ["st_xout"])
    P.build()
    return nc





def build_l1attn(P: Prog, C: Common, din, dout, xin_key="dram_xin",
                 xout_key="dram_xout"):
    TC = 512
    NIC = T // 128
    W = 2176

    xc = P.sbuf("a_xc", [128, NT, TC], F32)
    tmp = P.sbuf("a_tmp", [128, NT, TC], F32)
    xhat = P.sbuf("a_xhat", [128, NT, T], BF16)
    wp3 = P.sbuf("a_wp3", [128, NT, 3, 128], BF16)
    qp = P.sbuf("a_qp", [128, T], BF16)
    kp = P.sbuf("a_kp", [128, T], BF16)
    vp = P.sbuf("a_vp", [128, NIC, 128], BF16)
    tab2 = P.sbuf("a_tab2", [128, 4096], BF16)
    relw = P.sbuf("a_relw", [128, 2, W], BF16)
    rpe = P.sbuf("a_rpe", [128, 4, T], BF16)
    praw = P.sbuf("a_praw", [128, 2, T], BF16)
    ptb4 = P.sbuf("a_ptb4", [128, NIC, 512], BF16)
    wo_sb = ptb4.rearrange("p a b -> p (a b)")
    mrow = P.sbuf("a_mrow", [128, 2, 8], F32)
    ctx = P.sbuf("a_ctx", [128, NT, T], BF16)
    yc = P.sbuf("a_yc", [128, NT, TC], F32)
    sfull = [yc[:, 0:4, :].rearrange("p a b -> p (a b)"),
             yc[:, 4:8, :].rearrange("p a b -> p (a b)")]
    bqc = P.sbuf("a_bqc", [128, 8], F32)
    bkc = P.sbuf("a_bkc", [128, 8], F32)
    boc = P.sbuf("a_boc", [128, 8], F32)
    bvb = P.sbuf("a_bvb", [128, 1024], F32)
    identbf = P.sbuf("a_ident", [128, 128], BF16)
    ln = alloc_ln(P, "a")

    for nm, buf in (("bqc", bqc), ("bkc", bkc), ("boc", boc)):
        P.dma("sync", "a_" + nm,
              lambda e, nm=nm, buf=buf: e.dma_start(out=buf[:], in_=din[nm][:]),
              reads=[("dram", nm)], writes=["a_" + nm])
    P.dma("sync", "a_bvb",
          lambda e: e.dma_start(out=bvb[:], in_=din["bvrow"][:].broadcast_to([128, 1024])),
          reads=[("dram", "bvrow")], writes=["a_bvb"])
    P.dma("sync", "a_ident",
          lambda e: e.dma_start(out=identbf[:], in_=din["identbf"][:]),
          reads=[("dram", "identbf")], writes=["a_ident"])
    P.op("vector", lambda e: e.memset(tab2[:], 0.0), writes=["a_tab2"])

    for c0 in range(0, T, TC):
        load_x_chunk(P, din["xin"], xin_key, xc, "a_xc", c0, TC)
        emit_ln_stats(P, C, ln, xc, "a_xc", tmp, "a_tmp", NT)
        emit_ln_apply(P, C, ln, xc, "a_xc",
                      xhat[:, :, c0:c0 + TC], "a_xhat", tmp, "a_tmp", NT)

    for hp in range(8):
        for wi, wnm in enumerate(("wqTt", "wkTt", "wvTt")):
            P.dma("gpsimd", "a_wp3",
                  lambda e, wi=wi, wnm=wnm, hp=hp: e.dma_start(
                      out=wp3[:, :, wi, :], in_=din[wnm][hp]),
                  reads=[("dram", wnm)], writes=["a_wp3"])
        for hh in range(2):
            h = hp * 2 + hh
            P.dma("gpsimd", "a_tab2",
                  lambda e, h=h, hh=hh: e.dma_start(
                      out=tab2[64 * hh:64 * hh + 64, 0:4095], in_=din["tabTrev"][h]),
                  reads=[("dram", "tabTrev")], writes=["a_tab2"])
        for ts in range(4):
            for g in range(NT):
                P.op("tensor", lambda e, g=g, ts=ts: e.matmul(
                    out=P.psb(0), lhsT=wp3[:, g, 0, :], rhs=xhat[:, g, ts * 512:(ts + 1) * 512],
                    start=(g == 0), stop=(g == NT - 1)),
                    reads=["a_wp3", "a_xhat"], writes=["psb0"], inc=(g == NT - 1))
            P.op("vector", lambda e, ts=ts, hp=hp: e.tensor_scalar(
                out=qp[:, ts * 512:(ts + 1) * 512], in0=P.psb(0),
                scalar1=bqc[:, hp:hp + 1], scalar2=None, op0=ALU.add),
                reads=["psb0", "a_bqc"], writes=["a_qp"])
            for g in range(NT):
                P.op("tensor", lambda e, g=g, ts=ts: e.matmul(
                    out=P.psb(1), lhsT=wp3[:, g, 1, :], rhs=xhat[:, g, ts * 512:(ts + 1) * 512],
                    start=(g == 0), stop=(g == NT - 1)),
                    reads=["a_wp3", "a_xhat"], writes=["psb1"], inc=(g == NT - 1))
            P.op("vector", lambda e, ts=ts, hp=hp: e.tensor_scalar(
                out=kp[:, ts * 512:(ts + 1) * 512], in0=P.psb(1),
                scalar1=bkc[:, hp:hp + 1], scalar2=0.125, op0=ALU.add, op1=ALU.mult),
                reads=["psb1", "a_bkc"], writes=["a_kp"])
        for tc in range(NIC):
            for g in range(NT):
                P.op("tensor", lambda e, g=g, tc=tc: e.matmul(
                    out=P.pspool[:, 2, 0:128], lhsT=xhat[:, g, tc * 128:(tc + 1) * 128],
                    rhs=wp3[:, g, 2, :],
                    start=(g == 0), stop=(g == NT - 1)),
                    reads=["a_wp3", "a_xhat"], writes=["psb2"], inc=(g == NT - 1))
            P.op("vector", lambda e, tc=tc, hp=hp: e.tensor_add(
                out=vp[:, tc, :], in0=P.pspool[:, 2, 0:128],
                in1=bvb[:, hp * 128:(hp + 1) * 128]),
                reads=["psb2", "a_bvb"], writes=["a_vp"])

        for hh in range(2):
            h = hp * 2 + hh
            e0 = 64 * hh

            def emit_relw(ic, e0=e0):
                i0 = ic * 128
                s0w = 1920 - i0
                pi = ic % 2
                pr = ic % 4
                for rs in range(5):
                    w = 512 if rs < 4 else 128
                    bk_ = 4 + rs % 2
                    P.op("tensor", lambda e, rs=rs, w=w, bk_=bk_, e0=e0, i0=i0, s0w=s0w: e.matmul(
                        out=P.pspool[:, bk_, 0:w],
                        lhsT=qp[e0:e0 + 64, i0:i0 + 128],
                        rhs=tab2[e0:e0 + 64, s0w + rs * 512:s0w + rs * 512 + w],
                        start=True, stop=True),
                        reads=["a_qp", "a_tab2"], writes=[f"psb{bk_}"])
                    P.op("vector", lambda e, rs=rs, w=w, bk_=bk_, pi=pi: e.tensor_copy(
                        out=relw[:, pi, rs * 512:rs * 512 + w], in_=P.pspool[:, bk_, 0:w]),
                        reads=[f"psb{bk_}"], writes=[f"a_relw{pi}"])
                P.dma("sync", f"a_reld{pi}",
                      lambda e, pi=pi: e.dma_start(out=din["reld"][pi], in_=relw[:, pi, :]),
                      reads=[f"a_relw{pi}"], writes=[("dram", f"reld{pi}")])
                shear = bass.AP(tensor=din["reld"].tensor, offset=pi * 128 * W + 127,
                                ap=[[W - 1, 128], [1, T]])
                P.dma("sync", f"a_rpe{pr}",
                      lambda e, shear=shear, pr=pr: e.dma_start(
                          out=rpe[:, pr, :], in_=shear),
                      reads=[("dram", f"reld{pi}")], writes=[f"a_rpe{pr}"])

            def emit_scores(ic, e0=e0):
                i0 = ic * 128
                pi = ic % 2
                pr = ic % 4
                for js in range(4):
                    P.op("tensor", lambda e, js=js, i0=i0, e0=e0: e.matmul(
                        out=P.psb(js), lhsT=qp[e0:e0 + 64, i0:i0 + 128],
                        rhs=kp[e0:e0 + 64, js * 512:(js + 1) * 512],
                        start=True, stop=False),
                        reads=["a_qp", "a_kp"], writes=[f"psb{js}"], inc=False)
                    P.op("tensor", lambda e, js=js, pr=pr: e.matmul(
                        out=P.psb(js), lhsT=identbf[:],
                        rhs=rpe[:, pr, js * 512:(js + 1) * 512],
                        start=False, stop=True),
                        reads=["a_ident", f"a_rpe{pr}", f"psb{js}"], writes=[f"psb{js}"])
                    P.op("vector", lambda e, js=js, pi=pi: e.reduce_max(
                        out=mrow[:, pi, js:js + 1], in_=P.psb(js), axis=AX.X),
                        reads=[f"psb{js}"], writes=[f"a_mrow{pi}"])
                P.op("vector", lambda e, pi=pi: e.tensor_max(
                    out=mrow[:, pi, 4:5], in0=mrow[:, pi, 0:1], in1=mrow[:, pi, 1:2]),
                    reads=[f"a_mrow{pi}"], writes=[f"a_mrow{pi}"])
                P.op("vector", lambda e, pi=pi: e.tensor_max(
                    out=mrow[:, pi, 5:6], in0=mrow[:, pi, 2:3], in1=mrow[:, pi, 3:4]),
                    reads=[f"a_mrow{pi}"], writes=[f"a_mrow{pi}"])
                P.op("vector", lambda e, pi=pi: e.tensor_max(
                    out=mrow[:, pi, 4:5], in0=mrow[:, pi, 4:5], in1=mrow[:, pi, 5:6]),
                    reads=[f"a_mrow{pi}"], writes=[f"a_mrow{pi}"])
                P.op("vector", lambda e, pi=pi: e.tensor_scalar(
                    out=mrow[:, pi, 4:5], in0=mrow[:, pi, 4:5], scalar1=-1.0,
                    scalar2=None, op0=ALU.mult),
                    reads=[f"a_mrow{pi}"], writes=[f"a_mrow{pi}"])
                P.op("scalar", lambda e, pi=pi: e.activation(
                    out=praw[:, pi, :], in_=P.psw(0, 4), func=AF.Exp,
                    bias=mrow[:, pi, 4:5], accum_out=mrow[:, pi, 6:7]),
                    reads=["psb0", "psb1", "psb2", "psb3", f"a_mrow{pi}"],
                    writes=[f"a_praw{pi}", f"a_mrow{pi}"])

            def emit_post(ic):
                pi = ic % 2
                P.op("vector", lambda e, pi=pi: e.reciprocal(
                    out=mrow[:, pi, 7:8], in_=mrow[:, pi, 6:7]),
                    reads=[f"a_mrow{pi}"], writes=[f"a_mrow{pi}"])
                P.op("vector", lambda e, pi=pi: e.tensor_scalar(
                    out=praw[:, pi, :], in0=praw[:, pi, :], scalar1=mrow[:, pi, 7:8],
                    scalar2=None, op0=ALU.mult),
                    reads=[f"a_praw{pi}", f"a_mrow{pi}"], writes=[f"a_praw{pi}"])

            def emit_trans(ic):
                pi = ic % 2
                icq = ic % 4
                for kc in range(NIC):
                    tb = 6 + (kc // 4) % 2
                    P.op("tensor", lambda e, kc=kc, pi=pi, tb=tb: e.matmul(
                        out=P.pspool[:, tb, (kc % 4) * 128:(kc % 4) * 128 + 128],
                        lhsT=praw[:, pi, kc * 128:(kc + 1) * 128], rhs=identbf[:],
                        start=True, stop=True),
                        reads=[f"a_praw{pi}", "a_ident"], writes=[f"psb{tb}"],
                        inc=(kc % 4 == 3))
                    if kc % 4 == 3:
                        g4 = kc // 4
                        P.op("scalar", lambda e, g4=g4, icq=icq, tb=tb: e.copy(
                            out=ptb4[:, 4 * g4:4 * g4 + 4, icq * 128:icq * 128 + 128],
                            in_=P.pspool[:, tb, :].rearrange("p (a b) -> p a b", a=4)),
                            reads=[f"psb{tb}"], writes=["a_ptb4"])

            def emit_ctx(icg, e0=e0, hp=hp):
                for jc in range(NIC):
                    P.op("tensor", lambda e, jc=jc, e0=e0: e.matmul(
                        out=P.pspool[0:64, 6, 0:512], lhsT=vp[:, jc, e0:e0 + 64],
                        rhs=ptb4[:, jc, :],
                        start=(jc == 0), stop=(jc == NIC - 1)),
                        reads=["a_vp", "a_ptb4"], writes=["psb6"],
                        inc=(jc == NIC - 1))
                P.op("scalar", lambda e, icg=icg, e0=e0, hp=hp: e.copy(
                    out=ctx[e0:e0 + 64, hp, icg * 512:(icg + 1) * 512],
                    in_=P.pspool[0:64, 6, 0:512]),
                    reads=["psb6"], writes=["a_ctx"])

            for ic in range(-2, NIC + 1):
                if ic >= 1:
                    emit_post(ic - 1)
                if 0 <= ic < NIC:
                    emit_scores(ic)
                if ic + 2 < NIC:
                    emit_relw(ic + 2)
                if ic >= 1:
                    emit_trans(ic - 1)
                    if (ic - 1) % 4 == 3:
                        emit_ctx((ic - 1) // 4)

    P.dma("gpsimd", "a_wo",
          lambda e: e.dma_start(out=wo_sb[:], in_=din["woTt"][:]),
          reads=[("dram", "woTt"), "a_ptb4"], writes=["a_wo", "a_ptb4"])
    for ci, c0 in enumerate(range(0, T, TC)):
        for et in range(NT):
            bk_ = et % 2
            for g in range(NT):
                P.op("tensor", lambda e, g=g, et=et, ci=ci, bk_=bk_: e.matmul(
                    out=P.psb(bk_),
                    lhsT=wo_sb[:, g * 1024 + et * 128:g * 1024 + et * 128 + 128],
                    rhs=ctx[:, g, ci * 512:(ci + 1) * 512],
                    start=(g == 0), stop=(g == NT - 1)),
                    reads=["a_wo", "a_ctx"], writes=[f"psb{bk_}"], inc=(g == NT - 1))
            P.op("vector", lambda e, et=et, bk_=bk_: e.tensor_scalar(
                out=yc[:, et, :], in0=P.psb(bk_), scalar1=boc[:, et:et + 1],
                scalar2=None, op0=ALU.add),
                reads=[f"psb{bk_}", "a_boc"], writes=["a_yc"])
        emit_ln_stats(P, C, ln, yc, "a_yc", tmp, "a_tmp", NT)
        load_x_chunk(P, din["xin"], xin_key, xc, "a_xc", c0, TC)
        emit_ln_apply(P, C, ln, yc, "a_yc", xc, "a_xc", tmp, "a_tmp", NT,
                      add_to=xc, add_key="a_xc")
        store_x_chunk(P, dout["xout"], xout_key, xc, "a_xc", c0, TC, "st_xout")


def host_prep_l1attn(inputs):
    f32, c = np.float32, np.ascontiguousarray
    d = {}
    d["wqTt"] = _tile_lhs(np.asarray(inputs["L1_attn_wq"], f32).T)
    d["wkTt"] = _tile_lhs(np.asarray(inputs["L1_attn_wk"], f32).T)
    d["wvTt"] = _tile_lhs(np.asarray(inputs["L1_attn_wv"], f32).T)
    woT = np.asarray(inputs["L1_attn_wo"], f32).T
    d["woTt"] = c(woT.reshape(8, 128, 1024).transpose(1, 0, 2))
    d["bqc"] = c(np.asarray(inputs["L1_attn_bq"], f32).reshape(8, 128).T)
    d["bkc"] = c(np.asarray(inputs["L1_attn_bk"], f32).reshape(8, 128).T)
    d["boc"] = c(np.asarray(inputs["L1_attn_bo"], f32).reshape(8, 128).T)
    d["bvrow"] = c(np.asarray(inputs["L1_attn_bv"], f32).reshape(1, 1024))
    tab = np.asarray(inputs["L1_attn_rpe_table"], f32)[:, :2 * T - 1]
    d["tabTrev"] = c(tab[:, ::-1].transpose(0, 2, 1))
    d["identbf"] = np.eye(128, dtype=BF16NP)
    return d


L1ATTN_SHAPES = {
    "wqTt": ([8, 128, 8, 128], F32), "wkTt": ([8, 128, 8, 128], F32),
    "wvTt": ([8, 128, 8, 128], F32), "woTt": ([128, 8, 1024], F32),
    "bqc": ([128, 8], F32), "bkc": ([128, 8], F32), "boc": ([128, 8], F32),
    "bvrow": ([1, 1024], F32), "tabTrev": ([16, 64, 4095], F32),
    "identbf": ([128, 128], BF16),
}


def build_l1attn_program():
    nc = bass.Bass("TRN2", target_bir_lowering=False)
    din = {"xin": nc.dram_tensor("xin", [D, T], F32, kind="ExternalInput").ap()}
    for nm, (shp, dt) in L1ATTN_SHAPES.items():
        din[nm] = nc.dram_tensor(nm, shp, dt, kind="ExternalInput").ap()
    din["reld"] = nc.dram_tensor("reld", [2, 128, 2176], BF16, kind="Internal").ap()
    dout = {"xout": nc.dram_tensor("xout", [D, T], F32, kind="ExternalOutput").ap()}
    P = Prog(nc)
    P.make_pool()
    C = Common(P)
    build_l1attn(P, C, din, dout)
    P.final_wait("sync", ["st_xout"])
    P.build()
    return nc





PI = float(np.pi)


def build_l0attn(P: Prog, C: Common, din, dout, xin_key="dram_xin",
                 xout_key="dram_xout"):
    TCC = 512
    NCH = T // TCC
    CT = 17

    xc = P.sbuf("g_xc", [128, NT, TCC], F32)
    tmp = P.sbuf("g_tmp", [128, NT, TCC], F32)
    xhat = P.sbuf("g_xhat", [128, NT, TCC], BF16)
    wproj = P.sbuf("g_wproj", [128, NT, 2080], BF16)
    outw = [P.sbuf(f"g_outw{i}", [128, NT, 128], BF16) for i in range(2)]
    zbuf = P.sbuf("g_zbuf", [128, CT, TCC + 3], F32)
    z2c = P.sbuf("g_z2c", [128, 9, TCC], F32)
    gate = P.sbuf("g_gate", [128, NT, TCC], BF16)
    scr = P.sbuf("g_scr", [128, 2, TCC], F32)
    sqt = P.sbuf("g_sqt", [128, 2, TCC], F32)
    convw = P.sbuf("g_convw", [128, CT, 4], F32)
    convb = P.sbuf("g_convb", [128, CT], F32)
    thc = P.sbuf("g_thc", [128, 4, NT], F32)
    slc = P.sbuf("g_slc", [32, 4], F32)
    sscol = P.sbuf("g_sscol", [32, 1], F32)
    dlog = P.sbuf("g_dlog", [1, 4], F32)
    wcols = P.sbuf("g_wcols", [128, 4], F32)
    nscol = P.sbuf("g_nscol", [32, 8], F32)
    wre = P.sbuf("g_wre", [32, 8, 32], F32)
    bd = P.sbuf("g_bd", [128, 8, 128], BF16)
    selg = P.sbuf("g_selg", [32, NT, 128], F32)
    sel4 = P.sbuf("g_sel4", [4, 128], F32)
    o32 = P.sbuf("g_o32", [128, 4], F32)
    dist = P.sbuf("g_dist", [32, 512], F32)
    twc = P.sbuf("g_twc", [32, TCC], F32)
    pw = P.sbuf("g_pw", [32, TCC], F32)
    denc = P.sbuf("g_denc", [32, TCC], F32)
    sstage = P.sbuf("g_sstage", [4, 2, TCC], F32)
    dcarry = P.sbuf("g_dcarry", [32, 2], F32)
    carry = P.sbuf("g_carry", [128, 64], F32)
    numf = P.sbuf("g_numf", [128, 8, TCC], F32)
    numbf = P.sbuf("g_numbf", [128, 8, TCC], BF16)
    ygate = P.sbuf("g_ygate", [128, NT, TCC], BF16)
    zeros = P.sbuf("g_zeros", [128, TCC], F32)
    negpi = P.sbuf("g_negpi", [128, 1], F32)
    pihalf = P.sbuf("g_pihalf", [128, 1], F32)
    onecol = P.sbuf("g_onecol", [32, 1], F32)

    P.dma("sync", "g_convw", lambda e: e.dma_start(
        out=convw[:], in_=din["convT"].rearrange("(c p) j -> p c j", p=128)),
        reads=[("dram", "convT")], writes=["g_convw"])
    P.dma("sync", "g_convb", lambda e: e.dma_start(
        out=convb[:], in_=din["convb"][:]),
        reads=[("dram", "convb")], writes=["g_convb"])
    P.dma("sync", "g_thc", lambda e: e.dma_start(
        out=thc[:], in_=din["thetac"][:]),
        reads=[("dram", "thetac")], writes=["g_thc"])
    P.dma("sync", "g_slc", lambda e: e.dma_start(
        out=slc[:, 0:1], in_=din["slopes"][:]),
        reads=[("dram", "slopes")], writes=["g_slc"])
    P.dma("sync", "g_sscol", lambda e: e.dma_start(
        out=sscol[:], in_=din["sscol"][:]),
        reads=[("dram", "sscol")], writes=["g_sscol"])
    P.dma("sync", "g_dlog", lambda e: e.dma_start(
        out=dlog[0:1, 0:3], in_=din["dlog"][:]),
        reads=[("dram", "dlog")], writes=["g_dlog"])
    P.dma("sync", "g_nscol", lambda e: e.dma_start(
        out=nscol[:], in_=din["nsc"][:]),
        reads=[("dram", "nsc")], writes=["g_nscol"])
    P.dma("sync", "g_wre", lambda e: e.dma_start(
        out=wre[:], in_=din["wrim"].rearrange("r m h o -> h (r m) o")),
        reads=[("dram", "wrim")], writes=["g_wre"])
    P.dma("sync", "g_selg", lambda e: e.dma_start(
        out=selg[:], in_=din["selg"].rearrange("g k p -> k g p")),
        reads=[("dram", "selg")], writes=["g_selg"])
    P.dma("sync", "g_sel4", lambda e: e.dma_start(
        out=sel4[:], in_=din["sel4"][:]), reads=[("dram", "sel4")], writes=["g_sel4"])
    P.dma("sync", "g_o32", lambda e: e.dma_start(
        out=o32[:], in_=din["o32"][:]), reads=[("dram", "o32")], writes=["g_o32"])
    i32b = P.sbuf("g_i32", [32, 32], F32)
    P.dma("sync", "g_i32", lambda e: e.dma_start(
        out=i32b[:], in_=din["i32"][:]), reads=[("dram", "i32")], writes=["g_i32"])

    P.dma("gpsimd", "g_wproj", lambda e: e.dma_start(
        out=wproj[:], in_=din["inprojw"][:]),
        reads=[("dram", "inprojw")], writes=["g_wproj"])

    P.op("vector", lambda e: e.memset(zeros[:], 0.0), writes=["g_zeros"])
    P.op("vector", lambda e: e.memset(negpi[:], -PI), writes=["g_negpi"])
    P.op("vector", lambda e: e.memset(pihalf[:], 0.5 * PI), writes=["g_pihalf"])
    P.op("vector", lambda e: e.memset(onecol[:], 1.0), writes=["g_onecol"])
    P.op("vector", lambda e: e.memset(zbuf[:], 0.0), writes=["g_zbuf"])
    P.op("vector", lambda e: e.memset(carry[:], 0.0), writes=["g_carry"])
    P.op("vector", lambda e: e.memset(dcarry[:], 0.0), writes=["g_dcarry"])
    P.op("vector", lambda e: e.memset(bd[:], 0.0), writes=["g_bd"])

    P.op("scalar", lambda e: e.activation(
        out=slc[:, 2:3], in_=slc[:, 0:1], func=AF.Abs),
        reads=["g_slc"], writes=["g_slc"])
    P.op("scalar", lambda e: e.activation(
        out=slc[:, 2:3], in_=slc[:, 2:3], func=AF.Exp, scale=-1.0),
        reads=["g_slc"], writes=["g_slc"])
    P.op("scalar", lambda e: e.activation(
        out=slc[:, 2:3], in_=slc[:, 2:3], func=AF.Ln, bias=onecol[:]),
        reads=["g_slc", "g_onecol"], writes=["g_slc"])
    P.op("vector", lambda e: e.tensor_scalar(
        out=slc[:, 3:4], in0=slc[:, 0:1], scalar1=0.0, scalar2=None,
        op0=ALU.max), reads=["g_slc"], writes=["g_slc"])
    P.op("vector", lambda e: e.tensor_tensor(
        out=slc[:, 1:2], in0=slc[:, 2:3], in1=slc[:, 3:4], op=ALU.add),
        reads=["g_slc"], writes=["g_slc"])
    P.op("vector", lambda e: e.tensor_scalar(
        out=slc[:, 1:2], in0=slc[:, 1:2], scalar1=-1.0, scalar2=None,
        op0=ALU.mult), reads=["g_slc"], writes=["g_slc"])

    P.op("vector", lambda e: e.reduce_max(
        out=dlog[0:1, 3:4], in_=dlog[0:1, 0:3], axis=AX.X),
        reads=["g_dlog"], writes=["g_dlog"])
    P.op("vector", lambda e: e.tensor_scalar(
        out=dlog[0:1, 3:4], in0=dlog[0:1, 3:4], scalar1=-1.0, scalar2=None,
        op0=ALU.mult), reads=["g_dlog"], writes=["g_dlog"])
    P.op("scalar", lambda e: e.activation(
        out=dlog[0:1, 0:3], in_=dlog[0:1, 0:3], func=AF.Exp, bias=dlog[0:1, 3:4]),
        reads=["g_dlog"], writes=["g_dlog"])
    P.op("vector", lambda e: e.reduce_sum(
        out=dlog[0:1, 3:4], in_=dlog[0:1, 0:3], axis=AX.X),
        reads=["g_dlog"], writes=["g_dlog"])
    P.op("vector", lambda e: e.reciprocal(
        out=dlog[0:1, 3:4], in_=dlog[0:1, 3:4]), reads=["g_dlog"], writes=["g_dlog"])
    P.op("vector", lambda e: e.tensor_scalar(
        out=dlog[0:1, 0:3], in0=dlog[0:1, 0:3], scalar1=dlog[0:1, 3:4],
        scalar2=None, op0=ALU.mult), reads=["g_dlog"], writes=["g_dlog"])
    P.op("tensor", lambda e: e.matmul(
        out=P.pspool[:, 7, 0:3], lhsT=C.ones_row[:], rhs=dlog[0:1, 0:3],
        start=True, stop=True), reads=["g_dlog", "c_ones_row"], writes=["psb7"])
    P.op("vector", lambda e: e.tensor_copy(
        out=wcols[:, 0:3], in_=P.pspool[:, 7, 0:3]), reads=["psb7"], writes=["g_wcols"])
    P.op("vector", lambda e: e.tensor_scalar(
        out=wcols[:, 3:4], in0=wcols[:, 2:3], scalar1=-1.0, scalar2=None,
        op0=ALU.mult), reads=["g_wcols"], writes=["g_wcols"])

    for rim in range(8):
        P.op("vector", lambda e, rim=rim: e.tensor_scalar(
            out=wre[:, rim, :], in0=wre[:, rim, :], scalar1=nscol[:, rim:rim + 1],
            scalar2=None, op0=ALU.mult), reads=["g_wre", "g_nscol"], writes=["g_wre"])
    for rim in range(8):
        for kq in range(4):
            P.dma("gpsimd", "g_bd",
                  lambda e, rim=rim, kq=kq: e.dma_start(
                      out=bd[32 * kq:32 * kq + 32, rim, 32 * kq:32 * kq + 32],
                      in_=wre[:, rim, :]),
                  reads=["g_wre"], writes=["g_bd"])

    ln = alloc_ln(P, "g")

    for ci in range(NCH):
        c0 = ci * TCC
        load_x_chunk(P, din["xin"], xin_key, xc, "g_xc", c0, TCC)
        emit_ln_stats(P, C, ln, xc, "g_xc", tmp, "g_tmp", NT)
        emit_ln_apply(P, C, ln, xc, "g_xc", xhat, "g_xhat", tmp, "g_tmp", NT)

        for ct in range(CT):
            ncol = 128 if ct < 16 else 32
            bk_ = ct % 2
            for g in range(NT):
                P.op("tensor", lambda e, ct=ct, g=g, bk_=bk_, ncol=ncol: e.matmul(
                    out=P.pspool[0:ncol, bk_, :], lhsT=wproj[:, g, ct * 128:ct * 128 + ncol],
                    rhs=xhat[:, g, :], start=(g == 0), stop=(g == NT - 1)),
                    reads=["g_wproj", "g_xhat"], writes=[f"psb{bk_}"], inc=(g == NT - 1))
            P.op("scalar", lambda e, ct=ct, bk_=bk_, ncol=ncol: e.copy(
                out=zbuf[0:ncol, ct, 3:3 + TCC], in_=P.pspool[0:ncol, bk_, :]),
                reads=[f"psb{bk_}"], writes=["g_zbuf"])

        for ct in range(CT):
            nrow = 128 if ct < 16 else 32
            if ct < 8:
                dst, dstk = z2c, "g_z2c"
                dsl = (slice(0, nrow), ct, slice(0, TCC))
            elif ct < 16:
                dst, dstk = scr, f"g_scr{ct % 2}"
                dsl = (slice(0, nrow), ct % 2, slice(0, TCC))
            else:
                dst, dstk = z2c, "g_z2c"
                dsl = (slice(0, nrow), 8, slice(0, TCC))
            P.op("vector", lambda e, ct=ct, nrow=nrow, dst=dst, dsl=dsl: e.tensor_scalar(
                out=dst[dsl], in0=zbuf[0:nrow, ct, 0:TCC],
                scalar1=convw[0:nrow, ct, 0:1], scalar2=convb[0:nrow, ct:ct + 1],
                op0=ALU.mult, op1=ALU.add),
                reads=["g_zbuf", "g_convw", "g_convb"], writes=[dstk])
            for j in (1, 2, 3):
                P.op("vector", lambda e, ct=ct, j=j, nrow=nrow, dst=dst, dsl=dsl:
                     e.scalar_tensor_tensor(
                         out=dst[dsl], in0=zbuf[0:nrow, ct, j:j + TCC],
                         scalar=convw[0:nrow, ct, j:j + 1], in1=dst[dsl],
                         op0=ALU.mult, op1=ALU.add),
                     reads=["g_zbuf", "g_convw", dstk], writes=[dstk])
            if 8 <= ct < 16:
                P.op("scalar", lambda e, ct=ct: e.activation(
                    out=gate[:, ct - 8, :], in_=scr[:, ct % 2, :], func=AF.Silu),
                    reads=[f"g_scr{ct % 2}"], writes=["g_gate"])
        P.op("vector", lambda e: e.tensor_copy(
            out=zbuf[:, :, 0:3], in_=zbuf[:, :, TCC:TCC + 3]),
            reads=["g_zbuf"], writes=["g_zbuf"])

        P.dma("sync", "g_dist", lambda e, c0=c0: e.dma_start(
            out=dist[:], in_=din["dist"][:, c0:c0 + TCC]),
            reads=[("dram", "dist")], writes=["g_dist"])
        P.op("scalar", lambda e: e.activation(
            out=twc[:], in_=dist[:], func=AF.Exp, scale=slc[:, 1:2]),
            reads=["g_dist", "g_slc"], writes=["g_twc"])
        P.op("vector", lambda e: e.tensor_scalar(
            out=pw[:], in0=z2c[0:32, 8, :], scalar1=sscol[:], scalar2=20.0,
            op0=ALU.mult, op1=ALU.min), reads=["g_z2c", "g_sscol"], writes=["g_pw"])
        P.op("vector", lambda e: e.tensor_scalar(
            out=pw[:], in0=pw[:], scalar1=-20.0, scalar2=None,
            op0=ALU.max), reads=["g_pw"], writes=["g_pw"])
        P.op("scalar", lambda e: e.activation(
            out=pw[:], in_=pw[:], func=AF.Exp), reads=["g_pw"], writes=["g_pw"])
        P.op("vector", lambda e: e.tensor_mul(
            out=pw[:], in0=pw[:], in1=twc[:]), reads=["g_pw", "g_twc"], writes=["g_pw"])
        P.op("vector", lambda e: e.tensor_tensor_scan(
            out=denc[:], data0=pw[:], data1=zeros[0:32, :], initial=dcarry[:, 0:1],
            op0=ALU.add, op1=ALU.add),
            reads=["g_pw", "g_zeros", "g_dcarry"], writes=["g_denc"])
        P.op("vector", lambda e: e.tensor_copy(
            out=dcarry[:, 0:1], in_=denc[:, TCC - 1:TCC]),
            reads=["g_denc"], writes=["g_dcarry"])
        P.op("vector", lambda e: e.tensor_scalar(
            out=twc[:], in0=denc[:], scalar1=1e-4, scalar2=None,
            op0=ALU.max), reads=["g_denc"], writes=["g_twc"])
        P.op("vector", lambda e: e.tensor_mul(
            out=twc[:], in0=twc[:], in1=twc[:]), reads=["g_twc"], writes=["g_twc"])
        P.op("vector", lambda e: e.tensor_scalar(
            out=twc[:], in0=twc[:], scalar1=1e-5, scalar2=None,
            op0=ALU.mult), reads=["g_twc"], writes=["g_twc"])

        for g in range(NT):
            gp = g % 2
            sck = f"g_scr{gp}"
            P.op("tensor", lambda e, g=g: e.matmul(
                out=P.psb(6), lhsT=selg[:, g, :], rhs=pw[:],
                start=True, stop=True), reads=["g_selg", "g_pw"], writes=["psb6"])
            P.op("vector", lambda e, g=g, gp=gp: e.tensor_scalar(
                out=scr[:, gp, :], in0=z2c[:, g, :], scalar1=wcols[:, 3:4],
                scalar2=wcols[:, 1:2], op0=ALU.mult, op1=ALU.add),
                reads=["g_z2c", "g_wcols"], writes=[sck])
            P.op("vector", lambda e, g=g, gp=gp: e.tensor_mul(
                out=scr[:, gp, :], in0=scr[:, gp, :], in1=z2c[:, g, :]),
                reads=[sck, "g_z2c"], writes=[sck])
            P.op("vector", lambda e, gp=gp: e.tensor_scalar(
                out=scr[:, gp, :], in0=scr[:, gp, :], scalar1=wcols[:, 0:1],
                scalar2=None, op0=ALU.add), reads=[sck, "g_wcols"], writes=[sck])
            P.op("vector", lambda e, gp=gp: e.tensor_mul(
                out=scr[:, gp, :], in0=scr[:, gp, :], in1=P.psb(6)),
                reads=[sck, "psb6"], writes=[sck])
            for m in range(4):
                mp = m % 2
                tk = f"g_tmpA{mp}"
                sqk = f"g_sqt{mp}"
                t0, t1, t2 = 4 * mp + 0, 4 * mp + 1, 4 * mp + 2
                sq = sqt[:, mp, :]
                P.op("vector", lambda e, g=g, m=m, sq=sq: e.tensor_scalar(
                    out=sq, in0=z2c[:, g, :], scalar1=thc[:, m, g:g + 1],
                    scalar2=None, op0=ALU.mult),
                    reads=["g_z2c", "g_thc"], writes=[sqk])
                P.op("vector", lambda e, sq=sq, t2=t2: e.tensor_scalar(
                    out=tmp[:, t2, :], in0=sq, scalar1=PI, scalar2=None,
                    op0=ALU.is_gt), reads=[sqk], writes=[tk])
                P.op("vector", lambda e, sq=sq, t1=t1: e.tensor_scalar(
                    out=tmp[:, t1, :], in0=sq, scalar1=-PI, scalar2=None,
                    op0=ALU.is_lt), reads=[sqk], writes=[tk])
                P.op("vector", lambda e, t1=t1, t2=t2: e.tensor_sub(
                    out=tmp[:, t2, :], in0=tmp[:, t2, :], in1=tmp[:, t1, :]),
                    reads=[tk], writes=[tk])
                P.op("vector", lambda e, sq=sq, t1=t1, t2=t2: e.scalar_tensor_tensor(
                    out=tmp[:, t1, :], in0=tmp[:, t2, :], scalar=-2 * PI,
                    in1=sq, op0=ALU.mult, op1=ALU.add),
                    reads=[tk, sqk], writes=[tk])
                P.op("vector", lambda e, t1=t1: e.tensor_scalar(
                    out=tmp[:, t1, :], in0=tmp[:, t1, :], scalar1=PI,
                    scalar2=-PI, op0=ALU.min, op1=ALU.max),
                    reads=[tk], writes=[tk])
                P.op("scalar", lambda e, t0=t0, t1=t1: e.activation(
                    out=tmp[:, t0, :], in_=tmp[:, t1, :], func=AF.Sin),
                    reads=[tk], writes=[tk])
                P.op("scalar", lambda e, t1=t1, t2=t2: e.activation(
                    out=tmp[:, t2, :], in_=tmp[:, t1, :], func=AF.Sin, scale=0.5),
                    reads=[tk], writes=[tk])
                P.op("scalar", lambda e, t2=t2: e.activation(
                    out=tmp[:, t2, :], in_=tmp[:, t2, :], func=AF.Square),
                    reads=[tk], writes=[tk])
                P.op("vector", lambda e, t1=t1, t2=t2: e.tensor_scalar(
                    out=tmp[:, t1, :], in0=tmp[:, t2, :], scalar1=-2.0,
                    scalar2=1.0, op0=ALU.mult, op1=ALU.add),
                    reads=[tk], writes=[tk])
                P.op("vector", lambda e, t1=t1, gp=gp: e.tensor_mul(
                    out=tmp[:, t1, :], in0=tmp[:, t1, :], in1=scr[:, gp, :]),
                    reads=[tk, sck], writes=[tk])
                P.op("vector", lambda e, t0=t0, gp=gp: e.tensor_mul(
                    out=tmp[:, t0, :], in0=tmp[:, t0, :], in1=scr[:, gp, :]),
                    reads=[tk, sck], writes=[tk])
                for ri in range(2):
                    src = (t1, t0)[ri]
                    cc = g * 8 + m * 2 + ri
                    P.op("vector", lambda e, m=m, ri=ri, src=src, cc=cc: e.tensor_tensor_scan(
                        out=numf[:, 2 * m + ri, :], data0=tmp[:, src, :],
                        data1=zeros[:], initial=carry[:, cc:cc + 1],
                        op0=ALU.add, op1=ALU.add),
                        reads=[tk, "g_zeros", "g_carry"], writes=["g_numf"])
                    sqs = (3, 7)[ri]
                    P.op("scalar", lambda e, m=m, ri=ri, sqs=sqs: e.activation(
                        out=tmp[:, sqs, :], in_=numf[:, 2 * m + ri, :], func=AF.Square),
                        reads=["g_numf"], writes=[f"g_sqb{ri}"])
                    for sub in range(4):
                        first = (m == 0 and ri == 0 and sub == 0)
                        P.op("tensor", lambda e, sub=sub, sqs=sqs, first=first: e.matmul(
                            out=P.pspool[0:4, 7, sub * 128:(sub + 1) * 128],
                            lhsT=o32[:], rhs=tmp[:, sqs, sub * 128:(sub + 1) * 128],
                            start=first, stop=False),
                            reads=["g_o32", f"g_sqb{ri}"], writes=["psb7"],
                            inc=(sub == 3))
            P.op("vector", lambda e, g=g: e.tensor_copy(
                out=carry[:, g * 8:g * 8 + 8], in_=numf[:, 0:8, TCC - 1:TCC]),
                reads=["g_numf"], writes=["g_carry"])
            P.op("tensor", lambda e, g=g: e.matmul(
                out=P.pspool[0:4, 7, :], lhsT=i32b[:, 4 * g:4 * g + 4],
                rhs=twc[:], start=False, stop=True),
                reads=["g_i32", "g_twc", "psb7"], writes=["psb7"])
            P.op("scalar", lambda e: e.activation(
                out=sstage[:, 1, :], in_=P.pspool[0:4, 7, :], func=AF.Sqrt),
                reads=["psb7"], writes=["g_sstage"])
            P.op("vector", lambda e: e.reciprocal(
                out=sstage[:, 1, :], in_=sstage[:, 1, :]),
                reads=["g_sstage"], writes=["g_sstage"])
            P.op("tensor", lambda e: e.matmul(
                out=P.psb(6), lhsT=sel4[:], rhs=sstage[:, 1, :],
                start=True, stop=True), reads=["g_sel4", "g_sstage"], writes=["psb6"])
            for mm in range(8):
                P.op("vector", lambda e, mm=mm: e.tensor_mul(
                    out=numbf[:, mm, :], in0=numf[:, mm, :], in1=P.psb(6)),
                    reads=["g_numf", "psb6"], writes=["g_numbf"])
            for rim in range(8):
                ri, m = rim // 4, rim % 4
                P.op("tensor", lambda e, rim=rim, ri=ri, m=m: e.matmul(
                    out=P.psb(2), lhsT=bd[:, rim, :], rhs=numbf[:, 2 * m + ri, :],
                    start=(rim == 0), stop=(rim == 7)),
                    reads=["g_bd", "g_numbf"], writes=["psb2"], inc=(rim == 7))
            P.op("vector", lambda e, g=g: e.tensor_mul(
                out=ygate[:, g, :], in0=P.psb(2), in1=gate[:, g, :]),
                reads=["psb2", "g_gate"], writes=["g_ygate"])

        for et in range(NT):
            P.dma("gpsimd", f"g_outw{et % 2}",
                  lambda e, et=et: e.dma_start(
                      out=outw[et % 2][:], in_=din["outprojw"][et]),
                  reads=[("dram", "outprojw")], writes=[f"g_outw{et % 2}"])
            for g in range(NT):
                P.op("tensor", lambda e, et=et, g=g: e.matmul(
                    out=P.psb(3), lhsT=outw[et % 2][:, g, :], rhs=ygate[:, g, :],
                    start=(g == 0), stop=(g == NT - 1)),
                    reads=[f"g_outw{et % 2}", "g_ygate"], writes=["psb3"],
                    inc=(g == NT - 1))
            P.op("vector", lambda e, et=et: e.tensor_copy(
                out=numf[:, et, :], in_=P.psb(3)), reads=["psb3"], writes=["g_numf"])

        emit_ln_stats(P, C, ln, numf, "g_numf", tmp, "g_tmp", NT)
        load_x_chunk(P, din["xin"], xin_key, xc, "g_xc", c0, TCC)
        emit_ln_apply(P, C, ln, numf, "g_numf", xc, "g_xc", tmp, "g_tmp", NT,
                      add_to=xc, add_key="g_xc")
        store_x_chunk(P, dout["xout"], xout_key, xc, "g_xc", c0, TCC, "st_xout")


def host_prep_l0attn(inputs):
    f32, c = np.float32, np.ascontiguousarray
    d = {}
    ipw = np.asarray(inputs["L0_attn_in_proj_w"], f32)
    d["inprojw"] = c(ipw.reshape(8, 128, 2080).transpose(1, 0, 2))
    convT = np.zeros((2176, 4), f32)
    convT[:2080] = np.asarray(inputs["L0_attn_conv_w"], f32)[:, 0, :].T
    d["convT"] = convT
    convb = np.zeros((2176,), f32)
    convb[:2080] = np.asarray(inputs["L0_attn_conv_b"], f32)
    d["convb"] = c(convb.reshape(17, 128).T)
    th = np.asarray(inputs["L0_attn_theta"], f32).reshape(32, 32, 4)
    d["thetac"] = c(th.transpose(2, 0, 1).reshape(4, 8, 128).transpose(2, 0, 1))
    d["slopes"] = c(np.concatenate([
        np.asarray(inputs["L0_attn_decay_slopes"], f32),
        np.asarray(inputs["L0_attn_anchor_slopes"], f32)]).reshape(32, 1))
    d["sscol"] = c(np.asarray(inputs["L0_attn_score_scale"], f32).reshape(32, 1))
    d["dlog"] = c(np.asarray(inputs["L0_attn_deriv_logits"], f32).reshape(1, 3))
    ns = np.asarray(inputs["L0_attn_norm_scale"], f32)
    nsc = np.stack([ns[:128].reshape(32, 4).T, ns[128:].reshape(32, 4).T])
    d["nsc"] = c(nsc.reshape(8, 32).T)
    wr = np.asarray(inputs["L0_attn_W_re"], f32).reshape(32, 4, 32).transpose(1, 0, 2)
    wi = np.asarray(inputs["L0_attn_W_im"], f32).reshape(32, 4, 32).transpose(1, 0, 2)
    d["wrim"] = c(np.stack([wr, wi]))
    d["outprojw"] = _tile_lhs(np.asarray(inputs["L0_attn_out_proj_w"], f32))
    p = np.arange(128)
    d["sel4"] = c((p[None, :] // 32 == np.arange(4)[:, None]).astype(f32))
    d["o32"] = c(d["sel4"].T / 256.0)
    d["i32"] = np.eye(32, dtype=f32)
    selg = np.zeros((8, 32, 128), f32)
    for g in range(8):
        selg[g, g * 4 + p // 32, p] = 1.0
    d["selg"] = selg
    t = np.arange(2048, dtype=f32)
    dist = np.empty((32, 2048), f32)
    dist[:28] = 2047.0 - t
    dist[28:] = t
    d["dist"] = dist
    return d


L0ATTN_SHAPES = {
    "inprojw": ([128, 8, 2080], F32), "convT": ([2176, 4], F32),
    "convb": ([128, 17], F32), "thetac": ([128, 4, 8], F32),
    "slopes": ([32, 1], F32), "sscol": ([32, 1], F32), "dlog": ([1, 3], F32),
    "nsc": ([32, 8], F32), "wrim": ([2, 4, 32, 32], F32),
    "outprojw": ([8, 128, 8, 128], F32), "sel4": ([4, 128], F32),
    "o32": ([128, 4], F32), "selg": ([8, 32, 128], F32), "dist": ([32, 2048], F32),
    "i32": ([32, 32], F32),
}


def build_l0attn_program():
    nc = bass.Bass("TRN2", target_bir_lowering=False)
    din = {"xin": nc.dram_tensor("xin", [D, T], F32, kind="ExternalInput").ap()}
    for nm, (shp, dt) in L0ATTN_SHAPES.items():
        din[nm] = nc.dram_tensor(nm, shp, dt, kind="ExternalInput").ap()
    dout = {"xout": nc.dram_tensor("xout", [D, T], F32, kind="ExternalOutput").ap()}
    P = Prog(nc)
    P.make_pool()
    C = Common(P)
    build_l0attn(P, C, din, dout)
    P.final_wait("sync", ["st_xout"])
    P.build()
    return nc




def build_fused_program():
    nc = bass.Bass("TRN2", target_bir_lowering=False)
    din = {"xin": nc.dram_tensor("xin", [D, T], F32, kind="ExternalInput").ap()}
    for shapes in (L0ATTN_SHAPES,):
        for nm, (shp, dt) in shapes.items():
            din[nm] = nc.dram_tensor(nm, shp, dt, kind="ExternalInput").ap()
    for nm, (shp, dt) in L0FFN_SHAPES.items():
        din[nm] = nc.dram_tensor(nm, shp, dt, kind="ExternalInput").ap()
    for nm, (shp, dt) in L1ATTN_SHAPES.items():
        din[nm] = nc.dram_tensor(nm, shp, dt, kind="ExternalInput").ap()
    for nm, shp in L1FFN_SHAPES.items():
        din[nm] = nc.dram_tensor(nm, shp, F32, kind="ExternalInput").ap()
    din["hdram"] = nc.dram_tensor("hdram", [D, T], F32, kind="Internal").ap()
    din["reld"] = nc.dram_tensor("reld", [2, 128, 2176], BF16, kind="Internal").ap()
    xra = nc.dram_tensor("xra", [D, T], F32, kind="Internal").ap()
    xrb = nc.dram_tensor("xrb", [D, T], F32, kind="Internal").ap()
    xout = nc.dram_tensor("xout", [D, T], F32, kind="ExternalOutput").ap()

    P = Prog(nc)
    P.make_pool()
    C = Common(P)
    with P.scope():
        build_l0attn(P, C, din, {"xout": xra},
                     xin_key=("dram", "xin"), xout_key=("dram", "xra"))
    P.barrier()
    with P.scope():
        build_l0ffn(P, C, dict(din, xin=xra), {"xout": xrb},
                    xin_key=("dram", "xra"), xout_key=("dram", "xrb"))
    P.barrier()
    with P.scope():
        build_l1attn(P, C, dict(din, xin=xrb), {"xout": xra},
                     xin_key=("dram", "xrb"), xout_key=("dram", "xra2"))
    P.barrier()
    with P.scope():
        build_l1ffn(P, C, dict(din, xin=xra), {"xout": xout},
                    xin_key=("dram", "xra2"), xout_key=("dram", "xout"))
    P.final_wait("sync", ["st_xout"])
    P.build()
    return nc



_PROGS = {}


def _get_prog(name, builder):
    if name not in _PROGS:
        _PROGS[name] = builder()
    return _PROGS[name]


def kernel(**inputs):
    x = np.asarray(inputs["L0_attn_x"], np.float32)
    nc = _get_prog("fused", build_fused_program)
    hp = {}
    hp.update(host_prep_l0attn(inputs))
    hp.update(host_prep_l0ffn(inputs))
    hp.update(host_prep_l1attn(inputs))
    hp.update(host_prep_l1ffn(inputs))
    in_maps = []
    for i in range(NCORES):
        m = {"xin": np.ascontiguousarray(x[i].T)}
        m.update(hp)
        in_maps.append(m)
    res = run_bass_kernel_spmd(nc, in_maps, core_ids=list(range(NCORES)))
    out = np.stack([np.asarray(res.results[i]["xout"]).T for i in range(NCORES)])
    return out.astype(np.float32)
